# version 01
# baseline (speedup 1.0000x reference)
import numpy as np
import concourse.bass as bass
import concourse.mybir as mybir
from concourse.bass_utils import run_bass_kernel_spmd
import ml_dtypes

F32 = mybir.dt.float32
BF16 = mybir.dt.bfloat16
AF = mybir.ActivationFunctionType
ALU = mybir.AluOpType
AX = mybir.AxisListType

B, S, D = 2, 4096, 2048
NCORE = 8
TQ = S // 4
CHUNK = 64
H0, HKV0, DH = 16, 4, 128
bf16 = ml_dtypes.bfloat16


class Sched:

    ENGINES = ("t", "v", "s", "g", "y")

    def __init__(self, nc):
        self.nc = nc
        self.sem = {}
        self.cnt = {}
        self.last = {}
        self.unmarked = {}
        self.pending = {k: {} for k in self.ENGINES}
        self._ctx = []

    def open(self):
        nc = self.nc
        for k in self.ENGINES + ("gd", "yd", "d0", "d1", "d2", "d3", "d4", "d5", "d6", "d7"):
            cm = nc.semaphore("sem_" + k)
            self.sem[k] = cm.__enter__()
            self._ctx.append(cm)
            self.cnt[k] = 0
        blk = nc.Block()
        self.block = blk.__enter__()
        self._ctx.append(blk)
        return self

    def close(self):
        for cm in reversed(self._ctx):
            cm.__exit__(None, None, None)

    def _engine(self, k):
        nc = self.nc
        return {"t": nc.tensor, "v": nc.vector, "s": nc.scalar,
                "g": nc.gpsimd, "y": nc.sync}[k]

    def _flush(self, k):
        e = self._engine(k)
        for sk, val in sorted(self.pending[k].items()):
            if val > 0:
                e.wait_ge(self.sem[sk], val)
        self.pending[k] = {}

    def _record(self, k, inst):
        self.last[k] = inst
        self.unmarked[k] = self.unmarked.get(k, 0) + 1

    def mark(self, k):
        if self.unmarked.get(k, 0) > 0:
            self.last[k].then_inc(self.sem[k], 1)
            self.cnt[k] += 1
            self.unmarked[k] = 0
        return self.cnt[k]

    def wait(self, waiter, prod, val=None):
        if val is None:
            val = self.cnt[prod]
        if val <= 0:
            return
        cur = self.pending[waiter].get(prod, 0)
        self.pending[waiter][prod] = max(cur, val)

    def barrier(self, engines=("t", "v", "s")):
        for k in engines:
            self.mark(k)
        allk = list(engines) + ["gd", "yd", "d0", "d1", "d2", "d3", "d4", "d5", "d6", "d7"]
        for k in engines:
            for p in allk:
                if p == k:
                    continue
                self.wait(k, p, self.cnt[p])

    def dma(self, q, out, in_, key=None, fence=False, **kw):
        if fence:
            semk = key or (q + "d")
            self.wait(q, semk, self.cnt[semk])
        self._flush(q)
        inst = self._engine(q).dma_start(out=out, in_=in_, **kw)
        semk = key or (q + "d")
        inst.then_inc(self.sem[semk], 16)
        self.cnt[semk] += 16
        return self.cnt[semk]

    def mm(self, out, lhsT, rhs, start, stop):
        self._flush("t")
        inst = self.nc.tensor.matmul(out, lhsT, rhs, start=start, stop=stop)
        self._record("t", inst)
        return inst

    def trans(self, out, in_, ident):
        self._flush("t")
        inst = self.nc.tensor.transpose(out, in_, ident)
        self._record("t", inst)
        return inst

    def _selfsync(self, k):
        if self.unmarked.get(k, 0) > 0:
            self.mark(k)
        self.pending[k][k] = max(self.pending[k].get(k, 0), self.cnt[k])

    def vop(self, name, *a, **kw):
        self._selfsync("v")
        self._flush("v")
        inst = getattr(self.nc.vector, name)(*a, **kw)
        self._record("v", inst)
        return inst

    def sop(self, name, *a, **kw):
        self._selfsync("s")
        self._flush("s")
        inst = getattr(self.nc.scalar, name)(*a, **kw)
        self._record("s", inst)
        return inst

    def gop(self, name, *a, **kw):
        self._selfsync("g")
        self._flush("g")
        inst = getattr(self.nc.gpsimd, name)(*a, **kw)
        self._record("g", inst)
        return inst

    def init_consts(self):
        self._cf = self.nc.alloc_sbuf_tensor("constsF", [128, 16], F32)
        self._cb = self.nc.alloc_sbuf_tensor("constsB", [128, 16], BF16)
        self._cmap = {}
        self._cn = {F32: 0, BF16: 0}

    def const(self, value, dtype=F32, parts=128):
        key = (dtype, float(value))
        if key not in self._cmap:
            t = self._cf if dtype == F32 else self._cb
            i = self._cn[dtype]
            self._cn[dtype] += 1
            self.vop("memset", t[:, i:i + 1], float(value))
            self._cmap[key] = t[:, i:i + 1]
        return self._cmap[key][0:parts, :]


def emit_ln(sc, xs, xln, tmp, stats, pool, ones_bf, ones1_f32, nchunks, T, eps=1e-5):
    assert T == 1024
    for c in range(nchunks):
        xc = xs[:, c * T:(c + 1) * T]
        sc.sop("activation", tmp[:, T:2 * T], xc, AF.Square)
        sc.vop("tensor_copy", tmp[:, 0:T], xc)
        sc.mark("s"); sc.mark("v")
        sc.wait("t", "s"); sc.wait("t", "v")
        for h in range(2):
            sl = slice(h * 512, (h + 1) * 512)
            sc.mm(pool[h][0:1, :], ones_bf[:, 0:1], tmp[:, 0:T][:, sl], start=(c == 0), stop=(c == nchunks - 1))
            sc.mm(pool[2 + h][0:1, :], ones_bf[:, 0:1], tmp[:, T:2 * T][:, sl], start=(c == 0), stop=(c == nchunks - 1))
        sc.mark("t")
        sc.wait("s", "t"); sc.wait("v", "t")
    sc.mark("t")
    sc.wait("v", "t")
    sumx = stats[0:1, 0:T]
    var_t = stats[0:1, T:2 * T]
    rstd = stats[0:1, 2 * T:3 * T]
    mr = stats[0:1, 3 * T:4 * T]
    for h in range(2):
        sl = slice(h * 512, (h + 1) * 512)
        sc.vop("tensor_copy", sumx[:, sl], pool[h][0:1, :])
        sc.vop("tensor_copy", var_t[:, sl], pool[2 + h][0:1, :])
    sc.vop("scalar_tensor_tensor", mr, sumx, 1.0, sumx, ALU.mult, ALU.mult)
    sc.vop("scalar_tensor_tensor", var_t, mr, -1.0 / D, var_t, ALU.mult, ALU.add)
    sc.mark("v")
    sc.wait("s", "v")
    sc.sop("activation", rstd, var_t, AF.Ln, scale=1.0 / D, bias=sc.const(eps, parts=1))
    sc.sop("activation", rstd, rstd, AF.Exp, scale=-0.5)
    sc.mark("s")
    sc.wait("v", "s")
    sc.vop("scalar_tensor_tensor", mr, sumx, 1.0 / D, rstd, ALU.mult, ALU.mult)
    sc.mark("v")
    sc.wait("t", "v")
    for h in range(2):
        sc.mm(pool[4 + h][:, :], ones1_f32[0:1, :], rstd[:, h * 512:(h + 1) * 512], start=True, stop=True)
        sc.mm(pool[6 + h][:, :], ones1_f32[0:1, :], mr[:, h * 512:(h + 1) * 512], start=True, stop=True)
    sc.mark("t")
    sc.wait("v", "t")
    ftmp = stats[0:1, 0:1]
    for c in range(nchunks):
        for h in range(2):
            sl = slice(h * 512, (h + 1) * 512)
            xc = xs[:, c * T:(c + 1) * T][:, sl]
            oc = xln[:, c * T:(c + 1) * T][:, sl]
            sc.vop("tensor_mul", tmp[:, 0:512], xc, pool[4 + h][:, :])
            sc.vop("tensor_sub", oc, tmp[:, 0:512], pool[6 + h][:, :])
    sc.mark("v")


def _host_fm(a2d, nch):
    T, F = a2d.shape
    assert F == nch * 128
    return np.ascontiguousarray(a2d.T.reshape(nch, 128, T))


def _fm_ap(h, nch, T):
    return bass.AP(h, 0, [[T, 128], [128 * T, nch], [1, T]])


def build_launchA():
    nc = bass.Bass("TRN2", target_bir_lowering=False)
    x_in = nc.dram_tensor("x", [16, 128, TQ], F32, kind="ExternalInput")
    w_in = nc.dram_tensor("wqkv", [8, 16, 128, 384], F32, kind="ExternalInput").ap()
    cos_in = nc.dram_tensor("cos", [128, TQ], F32, kind="ExternalInput").ap()
    sin_in = nc.dram_tensor("sin", [128, TQ], F32, kind="ExternalInput").ap()
    rotp_in = nc.dram_tensor("rotp", [128, 128], F32, kind="ExternalInput").ap()
    q_out = nc.dram_tensor("q", [16, 128, TQ], BF16, kind="ExternalOutput").ap()
    k_out = nc.dram_tensor("k", [4, 128, TQ], BF16, kind="ExternalOutput").ap()
    v_out = nc.dram_tensor("v", [4, 128, TQ], BF16, kind="ExternalOutput").ap()

    xs = nc.alloc_sbuf_tensor("xs", [128, 16 * TQ], F32)
    xln = nc.alloc_sbuf_tensor("xln", [128, 16 * TQ], BF16)
    tmp = nc.alloc_sbuf_tensor("tmp", [128, 2 * TQ], BF16)
    stats = nc.alloc_sbuf_tensor("stats", [1, 4 * TQ], F32)
    cos_t = nc.alloc_sbuf_tensor("cos_t", [128, TQ], BF16)
    sin_t = nc.alloc_sbuf_tensor("sin_t", [128, TQ], BF16)
    rotp = nc.alloc_sbuf_tensor("rotp_sb", [128, 128], BF16)
    ones_bf = nc.alloc_sbuf_tensor("ones_bf", [128, 1], BF16)
    ones1_f32 = nc.alloc_sbuf_tensor("ones1_f32", [1, 128], F32)
    wbuf = [nc.alloc_sbuf_tensor(f"wbuf{i}", [128, 384], BF16) for i in range(2)]
    out_sb = nc.alloc_sbuf_tensor("out_sb", [128, 3 * TQ], BF16)

    pool = [nc.alloc_psum_tensor(f"pool{i}", [128, 512], F32) for i in range(8)]

    sc = Sched(nc).open()
    sc.init_consts()
    sc.const(1e-5, parts=1)

    sc.dma("y", xs[:], _fm_ap(x_in, 16, TQ))
    sc.dma("g", cos_t[:], cos_in)
    sc.dma("g", sin_t[:], sin_in)
    sc.dma("g", rotp[:], rotp_in)
    sc.vop("memset", ones_bf[:], 1.0)
    sc.vop("memset", ones1_f32[:], 1.0)
    sc.mark("v")
    sc.wait("t", "v")
    sc.wait("s", "v")
    sc.wait("s", "yd", 16)
    sc.wait("v", "yd", 16)

    emit_ln(sc, xs[:], xln[:], tmp[:], stats[:], pool, ones_bf, ones1_f32, 16, TQ)

    sc.barrier()
    sc.wait("t", "gd", sc.cnt["gd"])
    for grp in range(8):
        for d in range(16):
            buf = wbuf[d % 2]
            c = sc.dma("g", buf[:], w_in[grp, d], key=f"d{d % 2}")
            sc.wait("t", f"d{d % 2}", c)
            for oc3 in range(3):
                for th in range(2):
                    sc.mm(pool[oc3 * 2 + th][:, :],
                          buf[:, oc3 * 128:(oc3 + 1) * 128],
                          xln[:, d * TQ + th * 512: d * TQ + (th + 1) * 512],
                          start=(d == 0), stop=(d == 15))
            if d >= 1:
                sc.mark("t")
                sc.wait("g", "t")
        tmark = sc.mark("t")
        sc.wait("v", "t", tmark)
        sc.wait("s", "t", tmark)
        for oc3 in range(3):
            oc = grp * 3 + oc3
            dst = out_sb[:, oc3 * TQ:(oc3 + 1) * TQ]
            p0 = pool[oc3 * 2][:, :]
            p1 = pool[oc3 * 2 + 1][:, :]
            if oc < 20:
                sc.vop("tensor_copy", tmp[:, 0:512], p0)
                sc.vop("tensor_copy", tmp[:, 512:1024], p1)
                sc.mark("v")
                sc.wait("t", "v")
                for th in range(2):
                    sc.mm(pool[6 + th][:, :], rotp[:, :], tmp[:, th * 512:(th + 1) * 512], start=True, stop=True)
                sc.mark("t")
                sc.wait("v", "t")
                sc.wait("s", "t")
                sc.vop("tensor_mul", dst, tmp[:, 0:TQ], cos_t[:, :])
                for th in range(2):
                    sc.sop("activation", tmp[:, TQ + th * 512:TQ + (th + 1) * 512], pool[6 + th][:, :], AF.Copy)
                sc.mark("s")
                sc.wait("v", "s")
                sc.vop("tensor_mul", tmp[:, TQ:2 * TQ], tmp[:, TQ:2 * TQ], sin_t[:, :])
                sc.vop("tensor_add", dst, dst, tmp[:, TQ:2 * TQ])
                sc.mark("v")
                sc.wait("t", "v")
            else:
                sc.vop("tensor_copy", dst[:, 0:512], p0)
                sc.vop("tensor_copy", dst[:, 512:1024], p1)
        vmark = sc.mark("v")
        sc.wait("t", "v", vmark)
        sc.wait("y", "v", vmark)
        for oc3 in range(3):
            oc = grp * 3 + oc3
            src = out_sb[:, oc3 * TQ:(oc3 + 1) * TQ]
            if oc < 16:
                sc.dma("y", q_out[oc], src)
            elif oc < 20:
                sc.dma("y", k_out[oc - 16], src)
            else:
                sc.dma("y", v_out[oc - 20], src)
        sc.wait("v", "yd", sc.cnt["yd"])

    sc._flush("y")
    nc.sync.wait_ge(sc.sem["yd"], sc.cnt["yd"])
    sc.close()
    return nc


def host_prep_A(x_full, wq, wk, wv):
    wcat = np.concatenate([wq.T, wk.T, wv.T], axis=1)
    w_arr = np.ascontiguousarray(
        wcat.reshape(16, 128, 8, 3, 128).transpose(2, 0, 1, 3, 4)).reshape(8, 16, 128, 384)
    d = np.arange(64, dtype=np.float64)
    inv = 1.0 / (10000.0 ** (2 * d / 128.0))
    pos = np.arange(S, dtype=np.float64)
    fr = pos[:, None] * inv[None, :]
    cos = np.cos(fr).astype(np.float32)
    sin = np.sin(fr).astype(np.float32)
    cos_full = np.ascontiguousarray(np.concatenate([cos, cos], axis=1).T)
    sin_full = np.ascontiguousarray(np.concatenate([sin, sin], axis=1).T)
    rotp = np.zeros((128, 128), np.float32)
    for r in range(64):
        rotp[r + 64, r] = -1.0
    for r in range(64, 128):
        rotp[r - 64, r] = 1.0
    maps = []
    for core in range(NCORE):
        b, quarter = core // 4, core % 4
        sl = slice(quarter * TQ, (quarter + 1) * TQ)
        maps.append({
            "x": _host_fm(x_full[b, sl], 16).astype(np.float32),
            "wqkv": w_arr,
            "cos": np.ascontiguousarray(cos_full[:, sl]),
            "sin": np.ascontiguousarray(sin_full[:, sl]),
            "rotp": rotp,
        })
    return maps


def build_launchB():
    nc = bass.Bass("TRN2", target_bir_lowering=False)
    NU = 4
    qf_in = nc.dram_tensor("qf", [NU, 128, S], BF16, kind="ExternalInput")
    kf_in = nc.dram_tensor("kf", [NU, 128, S], BF16, kind="ExternalInput")
    ktm_in = nc.dram_tensor("ktm", [NU, 64, 128, 128], BF16, kind="ExternalInput")
    vtm_in = nc.dram_tensor("vtm", [NU, 64, 128, 128], BF16, kind="ExternalInput")
    diagT_in = nc.dram_tensor("diagT", [NU, 64, 64], F32, kind="ExternalInput")
    qdecT_in = nc.dram_tensor("qdecT", [NU, 128, 64], BF16, kind="ExternalInput")
    kdec_in = nc.dram_tensor("kdec", [NU, 128, 1], F32, kind="ExternalInput")
    blkc_in = nc.dram_tensor("blkc", [NU, 128, 1], F32, kind="ExternalInput")
    o_out = nc.dram_tensor("o", [NU, 128, S], BF16, kind="ExternalOutput")

    NC64 = S // 64
    qf = [nc.alloc_sbuf_tensor(f"qf{i}", [128, S], BF16) for i in range(2)]
    kf = [nc.alloc_sbuf_tensor(f"kf{i}", [128, S], BF16) for i in range(2)]
    ktm = [nc.alloc_sbuf_tensor(f"ktm{i}", [128, 64 * 128], BF16) for i in range(2)]
    vtm = [nc.alloc_sbuf_tensor(f"vtm{i}", [128, 64 * 128], BF16) for i in range(2)]
    diagT = [nc.alloc_sbuf_tensor(f"diagT{i}", [64, 64], F32) for i in range(2)]
    qdecT = [nc.alloc_sbuf_tensor(f"qdecT{i}", [128, 64], BF16) for i in range(2)]
    kdec = [nc.alloc_sbuf_tensor(f"kdec{i}", [128, 1], F32) for i in range(2)]
    blkc = [nc.alloc_sbuf_tensor(f"blkc{i}", [128, 1], F32) for i in range(2)]
    o_sb = [nc.alloc_sbuf_tensor(f"osb{i}", [128, S], BF16) for i in range(2)]

    kv = nc.alloc_sbuf_tensor("kv", [128, 128], F32)
    kv_bf = nc.alloc_sbuf_tensor("kv_bf", [128, 128], BF16)
    scT = nc.alloc_sbuf_tensor("scT", [128, 64], BF16)
    qt = nc.alloc_sbuf_tensor("qt", [128, 64], BF16)
    vt = nc.alloc_sbuf_tensor("vt", [128, 128], BF16)

    pool = [nc.alloc_psum_tensor(f"pool{i}", [128, 512], F32) for i in range(4)]

    sc = Sched(nc).open()
    sc.init_consts()
    sc.vop("memset", scT[:], 0.0)
    sc.mark("v")

    def load_unit(u, slot):
        ks = f"d{slot}"
        sc.dma("y", qf[slot][:], qf_in.ap()[u], key=ks)
        sc.dma("y", kf[slot][:], kf_in.ap()[u], key=ks)
        sc.dma("y", ktm[slot][:], bass.AP(ktm_in, u * 64 * 128 * 128, [[128, 128], [128 * 128, 64], [1, 128]]), key=ks)
        sc.dma("y", vtm[slot][:], bass.AP(vtm_in, u * 64 * 128 * 128, [[128, 128], [128 * 128, 64], [1, 128]]), key=ks)
        sc.dma("g", diagT[slot][:], diagT_in.ap()[u], key=f"d{2 + slot}")
        sc.dma("g", qdecT[slot][:], qdecT_in.ap()[u], key=f"d{2 + slot}")
        sc.dma("g", kdec[slot][:], kdec_in.ap()[u], key=f"d{2 + slot}")
        sc.dma("g", blkc[slot][:], blkc_in.ap()[u], key=f"d{2 + slot}")
        return sc.cnt[ks], sc.cnt[f"d{2 + slot}"]

    loads = {}
    done_v = {}
    loads[0] = load_unit(0, 0)
    for u in range(NU):
        slot = u % 2
        if u + 1 < NU:
            if u >= 1:
                sc.wait("g", "v", done_v[u - 1])
            loads[u + 1] = load_unit(u + 1, 1 - slot)
        ydc, gdc = loads[u]
        ks, kg = f"d{slot}", f"d{2 + slot}"
        sc.wait("t", ks, ydc); sc.wait("t", kg, gdc)
        sc.wait("v", ks, ydc); sc.wait("v", kg, gdc)
        sc.vop("memset", kv[:], 0.0)
        sc.vop("memset", kv_bf[:], 0.0)
        sc.mark("v")
        sc.wait("t", "v")
        for c in range(NC64):
            t0, t1 = c * 64, (c + 1) * 64
            sc.mm(pool[0][0:64, 0:64], kf[slot][:, t0:t1], qf[slot][:, t0:t1], start=True, stop=True)
            sc.mark("t")
            sc.wait("v", "t")
            sc.vop("tensor_mul", scT[0:64, :], pool[0][0:64, 0:64], diagT[slot][:, :])
            sc.vop("tensor_mul", qt[:], qf[slot][:, t0:t1], qdecT[slot][:, :])
            sc.vop("tensor_scalar_mul", vt[:], vtm[slot][:, c * 128:(c + 1) * 128], kdec[slot][:, 0:1])
            sc.mark("v")
            sc.wait("t", "v")
            sc.mm(pool[1][:, 0:64], vtm[slot][:, c * 128:(c + 1) * 128], scT[:, :], start=True, stop=False)
            sc.mm(pool[1][:, 0:64], kv_bf[:], qt[:], start=False, stop=True)
            sc.mm(pool[2][:, 0:128], ktm[slot][:, c * 128:(c + 1) * 128], vt[:], start=True, stop=True)
            sc.mark("t")
            sc.wait("v", "t")
            sc.vop("scalar_tensor_tensor", kv[:], kv[:], blkc[slot][:, 0:1], pool[2][:, 0:128], ALU.mult, ALU.add)
            sc.vop("tensor_copy", o_sb[slot][:, t0:t1], pool[1][:, 0:64])
            sc.mark("v")
            sc.wait("s", "v")
            sc.sop("activation", kv_bf[:], kv[:], AF.Copy)
            sc.mark("s")
            sc.wait("t", "s")
            sc.wait("t", "v")
        vm = sc.mark("v")
        done_v[u] = vm
        sc.wait("y", "v", vm)
        sc.dma("y", o_out.ap()[u], o_sb[slot][:])
        sc.wait("y", "yd", sc.cnt["yd"])

    sc._flush("y")
    nc.sync.wait_ge(sc.sem["yd"], sc.cnt["yd"])
    sc.close()
    return nc


def host_prep_B(q_all, k_all, v_all):
    lam_all = np.exp2(-8.0 * np.arange(1, 17) / 16.0).astype(np.float64)
    r = np.arange(64, dtype=np.float64)
    maps = []
    for core in range(NCORE):
        units = [4 * core + i for i in range(4)]
        qf = np.stack([q_all[u // 16, u % 16] for u in units])
        kf = np.stack([k_all[u // 16, (u % 16) // 4] for u in units])
        ktm = np.zeros((4, 64, 128, 128), bf16)
        vtm = np.zeros((4, 64, 128, 128), bf16)
        diagT = np.zeros((4, 64, 64), np.float32)
        qdecT = np.zeros((4, 128, 64), np.float32)
        kdec = np.zeros((4, 128, 1), np.float32)
        blkc = np.zeros((4, 128, 1), np.float32)
        for i, u in enumerate(units):
            b, h = u // 16, u % 16
            lam = lam_all[h]
            kfm = k_all[b, h // 4]
            vfm = v_all[b, h // 4]
            ktm[i, :, 0:64, :] = kfm.T.reshape(64, 64, 128)
            vtm[i, :, 0:64, :] = vfm.T.reshape(64, 64, 128)
            diff = r[:, None] - r[None, :]
            diagT[i] = np.where(diff >= 0, np.exp(-lam * diff), 0.0).T.astype(np.float32)
            qdecT[i] = np.broadcast_to(np.exp(-lam * (r + 1.0)).astype(np.float32), (128, 64))
            kdec[i, 0:64, 0] = np.exp(-lam * (63.0 - r)).astype(np.float32)
            blkc[i, :, 0] = np.exp(-lam * 64.0)
        maps.append({
            "qf": qf, "kf": kf, "ktm": ktm, "vtm": vtm,
            "diagT": diagT, "qdecT": qdecT.astype(bf16), "kdec": kdec, "blkc": blkc,
        })
    return maps


def build_launchC():
    nc = bass.Bass("TRN2", target_bir_lowering=False)
    o_in = nc.dram_tensor("o_in", [16, 128, TQ], BF16, kind="ExternalInput")
    x_in = nc.dram_tensor("x", [16, 128, TQ], F32, kind="ExternalInput")
    woT_in = nc.dram_tensor("woT", [16, 16, 128, 128], F32, kind="ExternalInput")
    rl_in = nc.dram_tensor("rl", [16, 128, 8], F32, kind="ExternalInput")
    rsc_in = nc.dram_tensor("rscale", [128, 16], F32, kind="ExternalInput")
    pes_in = nc.dram_tensor("pes", [8, 1], F32, kind="ExternalInput")
    rowsel_in = nc.dram_tensor("rowsel", [8, 8, 128], F32, kind="ExternalInput")
    ident_in = nc.dram_tensor("identf", [128, 128], F32, kind="ExternalInput")
    wg_in = nc.dram_tensor("wgate", [8, 16, 16, 128, 256], F32, kind="ExternalInput")
    wl_in = nc.dram_tensor("wlin", [8, 16, 16, 128, 128], F32, kind="ExternalInput")
    x2_out = nc.dram_tensor("x2", [16, 128, TQ], F32, kind="ExternalOutput")
    dbgw_out = nc.dram_tensor("dbgw", [8, TQ], F32, kind="ExternalOutput")
    dbgx1_out = nc.dram_tensor("dbgx1", [16, 128, TQ], F32, kind="ExternalOutput")
    dbglg_out = nc.dram_tensor("dbglg", [8, 128, 8], F32, kind="ExternalOutput")

    xs = nc.alloc_sbuf_tensor("xs", [128, 16 * TQ], F32)
    o_sb = nc.alloc_sbuf_tensor("o_sb", [128, 16 * TQ], BF16)
    xln = nc.alloc_sbuf_tensor("xln", [128, 16 * TQ], BF16)
    tmp = nc.alloc_sbuf_tensor("tmp", [128, 2 * TQ], BF16)
    stats = nc.alloc_sbuf_tensor("stats", [1, 4 * TQ], F32)
    ones_bf = nc.alloc_sbuf_tensor("ones_bf", [128, 1], BF16)
    ones1_f32 = nc.alloc_sbuf_tensor("ones1_f32", [1, 128], F32)
    identf = nc.alloc_sbuf_tensor("identf_sb", [128, 128], F32)
    rlp = nc.alloc_sbuf_tensor("rlp", [128, 16 * 8], BF16)
    rl_sb = nc.alloc_sbuf_tensor("rl_sb", [128, 16 * 8], F32)
    rsc_sb = nc.alloc_sbuf_tensor("rsc_sb", [128, 16], F32)
    pes_sb = nc.alloc_sbuf_tensor("pes_sb", [8, 1], F32)
    rowsel = nc.alloc_sbuf_tensor("rowsel_sb", [8, 8 * 128], BF16)
    lg = nc.alloc_sbuf_tensor("lg", [128, 8], F32)
    m1 = nc.alloc_sbuf_tensor("m1", [128, 4], F32)
    eqm = nc.alloc_sbuf_tensor("eqm", [128, 8], F32)
    ind = nc.alloc_sbuf_tensor("ind", [128, 8], F32)
    ee = nc.alloc_sbuf_tensor("ee", [128, 8], F32)
    w_tm = nc.alloc_sbuf_tensor("w_tm", [128, 8], F32)
    rs_sb = nc.alloc_sbuf_tensor("rs_sb", [128, 1], F32)
    W_fm = nc.alloc_sbuf_tensor("W_fm", [8, TQ], BF16)
    W_dbg = nc.alloc_sbuf_tensor("W_dbg", [8, TQ], F32)
    wbc = nc.alloc_sbuf_tensor("wbc", [128, TQ], BF16)
    hg = nc.alloc_sbuf_tensor("hg", [128, TQ], BF16)
    hu = nc.alloc_sbuf_tensor("hu", [128, TQ], BF16)
    wov = [nc.alloc_sbuf_tensor(f"wov{i}", [128, 16 * 128], BF16) for i in range(2)]
    wgb = [nc.alloc_sbuf_tensor(f"wgb{i}", [128, 16 * 256], BF16) for i in range(2)]
    wlb = [nc.alloc_sbuf_tensor(f"wlb{i}", [128, 16 * 128], BF16) for i in range(2)]

    pool = [nc.alloc_psum_tensor(f"pool{i}", [128, 512], F32) for i in range(8)]

    sc = Sched(nc).open()
    sc.init_consts()
    sc.const(1e-5, parts=1)
    sc.const(1e-6, parts=1)
    sc.const(-0.5 * float(np.log(2048.0)))
    sc.vop("memset", ones_bf[:], 1.0)
    sc.vop("memset", ones1_f32[:], 1.0)
    sc.mark("v")

    sc.dma("y", xs[:], _fm_ap(x_in, 16, TQ))
    sc.dma("y", o_sb[:], _fm_ap(o_in, 16, TQ))
    ld0 = sc.cnt["yd"]
    sc.dma("g", rl_sb[:], bass.AP(rl_in, 0, [[8, 128], [128 * 8, 16], [1, 8]]))
    sc.dma("g", rsc_sb[:], rsc_in.ap())
    sc.dma("g", pes_sb[:], pes_in.ap())
    sc.dma("g", rowsel[:], bass.AP(rowsel_in, 0, [[128, 8], [8 * 128, 8], [1, 128]]))
    sc.dma("g", identf[:], ident_in.ap())
    gld0 = sc.cnt["gd"]

    wo_ld = {}
    wo_ld[0] = sc.dma("g", wov[0][:], bass.AP(woT_in, 0, [[128, 128], [128 * 128, 16], [1, 128]]), key="d0")
    for ds in range(16):
        slot = ds % 2
        if ds + 1 < 16:
            if ds >= 1:
                sc.wait("g", "t", sc.cnt["t"])
            wo_ld[ds + 1] = sc.dma("g", wov[1 - slot][:],
                                   bass.AP(woT_in, (ds + 1) * 16 * 128 * 128, [[128, 128], [128 * 128, 16], [1, 128]]),
                                   key=f"d{1 - slot}")
        sc.wait("t", f"d{slot}", wo_ld[ds])
        sc.wait("t", "yd", ld0)
        sc.wait("v", "yd", ld0)
        for th in range(2):
            for ec in range(16):
                sc.mm(pool[th][:, :], wov[slot][:, ec * 128:(ec + 1) * 128],
                      o_sb[:, ec * TQ + th * 512: ec * TQ + (th + 1) * 512],
                      start=(ec == 0), stop=(ec == 15))
        tm = sc.mark("t")
        sc.wait("v", "t", tm)
        for th in range(2):
            sl = slice(th * 512, (th + 1) * 512)
            sc.vop("tensor_add", xs[:, ds * TQ:(ds + 1) * TQ][:, sl], xs[:, ds * TQ:(ds + 1) * TQ][:, sl], pool[th][:, :])
        sc.mark("v")
        sc.wait("t", "v")

    sc.barrier()
    sc.wait("y", "v", sc.cnt["v"])
    sc.dma("y", _fm_ap(dbgx1_out, 16, TQ), xs[:])
    sc.wait("v", "yd", sc.cnt["yd"])
    emit_ln(sc, xs[:], xln[:], tmp[:], stats[:], pool, ones_bf, ones1_f32, 16, TQ)
    sc.barrier()

    sc.wait("v", "gd", gld0)
    sc.wait("t", "gd", gld0)
    sc.wait("s", "gd", gld0)
    for d in range(16):
        sc.vop("tensor_scalar_mul", rlp[:, d * 8:(d + 1) * 8], rl_sb[:, d * 8:(d + 1) * 8], rsc_sb[:, d:d + 1])
    sc.mark("v")
    sc.wait("t", "v"); sc.wait("s", "v")
    for tt in range(8):
        tsl = slice(tt * 128, (tt + 1) * 128)
        for d in range(16):
            sc.sop("activation", tmp[:, d * 128:(d + 1) * 128], xln[:, d * TQ:(d + 1) * TQ][:, tsl], AF.Square)
        sm = sc.mark("s")
        sc.wait("t", "s", sm)
        for d in range(16):
            sc.mm(pool[2][:, 0:1], tmp[:, d * 128:(d + 1) * 128], ones_bf[:, 0:1], start=(d == 0), stop=(d == 15))
            sc.mm(pool[3][:, 0:8], xln[:, d * TQ:(d + 1) * TQ][:, tsl], rlp[:, d * 8:(d + 1) * 8], start=(d == 0), stop=(d == 15))
        tm = sc.mark("t")
        sc.wait("s", "t", tm); sc.wait("v", "t", tm)
        sc.sop("activation", rs_sb[:], pool[2][:, 0:1], AF.Ln, scale=1.0 / D, bias=sc.const(1e-6))
        sc.sop("activation", rs_sb[:], rs_sb[:], AF.Exp, scale=-0.5,
               bias=sc.const(-0.5 * float(np.log(2048.0))))
        sc.mark("s")
        sc.wait("v", "s")
        sc.vop("tensor_scalar_mul", lg[:], pool[3][:, 0:8], rs_sb[:, 0:1])
        sc.mark("v"); sc.wait("y", "v", sc.cnt["v"])
        sc.dma("y", dbglg_out.ap()[tt], lg[:])
        sc.wait("v", "yd", sc.cnt["yd"])
        sc.vop("reduce_max", m1[:, 0:1], lg[:], axis=AX.X)
        sc.vop("tensor_scalar_mul", m1[:, 1:2], m1[:, 0:1], -1.0)
        sc.vop("tensor_single_scalar", eqm[:], lg[:], m1[:, 0:1], ALU.is_ge)
        sc.vop("scalar_tensor_tensor", eqm[:], eqm[:], -1e30, lg[:], ALU.mult, ALU.add)
        sc.vop("reduce_max", m1[:, 2:3], eqm[:], axis=AX.X)
        sc.vop("tensor_single_scalar", ind[:], lg[:], m1[:, 2:3], ALU.is_ge)
        sc.mark("v")
        sc.wait("s", "v")
        sc.sop("activation", ee[:], lg[:], AF.Exp, bias=m1[:, 1:2])
        sc.mark("s")
        sc.wait("v", "s")
        sc.vop("tensor_mul", ee[:], ee[:], ind[:])
        sc.vop("reduce_sum", m1[:, 3:4], ee[:], axis=AX.X)
        sc.vop("reciprocal", m1[:, 3:4], m1[:, 3:4])
        sc.vop("tensor_scalar_mul", w_tm[:], ee[:], m1[:, 3:4])
        sc.mark("v")
        sc.wait("t", "v")
        sc.trans(pool[5][0:8, 0:128], w_tm[:, 0:8], identf[:, :])
        tm = sc.mark("t")
        sc.wait("v", "t", tm)
        sc.vop("tensor_scalar_mul", W_fm[0:8, tsl], pool[5][0:8, 0:128], pes_sb[:, 0:1])
        sc.vop("tensor_scalar_mul", W_dbg[0:8, tsl], pool[5][0:8, 0:128], pes_sb[:, 0:1])
        sc.mark("v")
        sc.wait("t", "v")

    vm = sc.mark("v")
    sc.wait("y", "v", vm)
    sc.dma("y", dbgw_out.ap(), W_dbg[0:8, :])

    wg_ld = {}
    wl_ld = {}
    wg_ld[(0, 0)] = sc.dma("g", wgb[0][:], bass.AP(wg_in, 0, [[256, 128], [128 * 256, 16], [1, 256]]), key="d4")
    prev_t_for_slot = {0: 0, 1: 0}
    prev_wl_t = {0: 0, 1: 0}
    for e in range(8):
        sc.wait("t", "v")
        for th in range(2):
            sc.mm(pool[5][:, :], rowsel[0:8, e * 128:(e + 1) * 128], W_fm[0:8, th * 512:(th + 1) * 512], start=True, stop=True)
            tm = sc.mark("t")
            sc.wait("v", "t", tm)
            sc.vop("tensor_copy", wbc[:, th * 512:(th + 1) * 512], pool[5][:, :])
            sc.mark("v")
            sc.wait("t", "v")
        for hc in range(16):
            slot = (e * 16 + hc) % 2
            nslot = 1 - slot
            ne, nhc = (e, hc + 1) if hc + 1 < 16 else (e + 1, 0)
            if ne < 8:
                sc.wait("g", "t", prev_t_for_slot[nslot])
                wg_ld[(ne, nhc)] = sc.dma("g", wgb[nslot][:],
                                          bass.AP(wg_in, (ne * 16 + nhc) * 16 * 128 * 256, [[256, 128], [128 * 256, 16], [1, 256]]),
                                          key=f"d{4 + nslot}")
            sc.wait("t", f"d{4 + slot}", wg_ld[(e, hc)])
            for d in range(16):
                for th in range(2):
                    sc.mm(pool[th][:, :], wgb[slot][:, d * 256:d * 256 + 128],
                          xln[:, d * TQ + th * 512: d * TQ + (th + 1) * 512], start=(d == 0), stop=(d == 15))
                    sc.mm(pool[2 + th][:, :], wgb[slot][:, d * 256 + 128:d * 256 + 256],
                          xln[:, d * TQ + th * 512: d * TQ + (th + 1) * 512], start=(d == 0), stop=(d == 15))
            tm = sc.mark("t")
            prev_t_for_slot[slot] = tm
            sc.wait("s", "t", tm); sc.wait("v", "t", tm)
            for th in range(2):
                sl = slice(th * 512, (th + 1) * 512)
                sc.sop("activation", hg[:, sl], pool[th][:, :], AF.Gelu_apprx_tanh)
                sc.vop("tensor_mul", hu[:, sl], pool[2 + th][:, :], wbc[:, sl])
            sc.mark("s"); sc.mark("v")
            sc.wait("v", "s")
            sc.vop("tensor_mul", o_sb[:, hc * TQ:(hc + 1) * TQ], hg[:], hu[:])
            sc.mark("v")
            sc.wait("t", "v")
        for fs in range(16):
            slot = (e * 16 + fs) % 2
            nslot = 1 - slot
            ne, nfs = (e, fs + 1) if fs + 1 < 16 else (e + 1, 0)
            if ne < 8:
                sc.wait("g", "t", prev_wl_t[nslot])
                wl_ld[(ne, nfs)] = sc.dma("g", wlb[nslot][:],
                                          bass.AP(wl_in, (ne * 16 + nfs) * 16 * 128 * 128, [[128, 128], [128 * 128, 16], [1, 128]]),
                                          key=f"d{6 + nslot}")
            if (e, fs) not in wl_ld:
                wl_ld[(e, fs)] = sc.dma("g", wlb[slot][:],
                                        bass.AP(wl_in, (e * 16 + fs) * 16 * 128 * 128, [[128, 128], [128 * 128, 16], [1, 128]]),
                                        key=f"d{6 + slot}")
            sc.wait("t", f"d{6 + slot}", wl_ld[(e, fs)])
            for th in range(2):
                for hc in range(16):
                    sc.mm(pool[4 + th][:, :], wlb[slot][:, hc * 128:(hc + 1) * 128],
                          o_sb[:, hc * TQ + th * 512: hc * TQ + th * 512 + 512],
                          start=(hc == 0), stop=(hc == 15))
            tm = sc.mark("t")
            prev_wl_t[slot] = tm
            sc.wait("v", "t", tm)
            for th in range(2):
                sl = slice(th * 512, (th + 1) * 512)
                dst = xs[:, fs * TQ:(fs + 1) * TQ][:, sl]
                sc.vop("tensor_add", dst, dst, pool[4 + th][:, :])
            sc.mark("v")
            sc.wait("t", "v")

    vm = sc.mark("v")
    sc.wait("y", "v", vm)
    sc.dma("y", _fm_ap(x2_out, 16, TQ), xs[:])
    sc._flush("y")
    nc.sync.wait_ge(sc.sem["yd"], sc.cnt["yd"])
    sc.close()
    return nc


def host_prep_C(o_all, x_full, wo, router_logits, gating_einsum, linear, pes, rscale):
    woT = np.ascontiguousarray(wo.T.reshape(16, 128, 16, 128).transpose(2, 0, 1, 3))
    rl = np.ascontiguousarray(router_logits.reshape(16, 128, 8)).astype(np.float32)
    rsc = np.ascontiguousarray(rscale.reshape(16, 128).T).astype(np.float32)
    pes_a = pes.reshape(8, 1).astype(np.float32)
    rowsel = np.zeros((8, 8, 128), np.float32)
    for e in range(8):
        rowsel[e, e, :] = 1.0
    identf = np.eye(128, dtype=np.float32)
    wg_flat = np.transpose(gating_einsum, (0, 3, 1, 2)).reshape(8, 2048, 4096)
    wg = np.ascontiguousarray(
        wg_flat.reshape(8, 16, 128, 2, 16, 128).transpose(0, 4, 1, 2, 3, 5)).reshape(8, 16, 16, 128, 256)
    wl = np.ascontiguousarray(
        linear.reshape(8, 16, 128, 16, 128).transpose(0, 3, 1, 2, 4))
    maps = []
    for core in range(NCORE):
        b, quarter = core // 4, core % 4
        sl = slice(quarter * TQ, (quarter + 1) * TQ)
        o_q = np.ascontiguousarray(o_all[b, :, :, sl])
        maps.append({
            "o_in": o_q,
            "x": _host_fm(x_full[b, sl], 16).astype(np.float32),
            "woT": woT, "rl": rl, "rscale": rsc, "pes": pes_a,
            "rowsel": rowsel, "identf": identf,
            "wgate": wg, "wlin": wl,
        })
    return maps


TE = TQ + 3

def emit_ln2(sc, xe, xlne, tmp, stats, pool, ones_bf, ones1_f32, nchunks, eps=1e-5):
    tiles = [(0, 512), (512, 1024), (1024, 1027)]
    for c in range(nchunks):
        xc = xe[:, c * TE:(c + 1) * TE]
        sc.sop("activation", tmp[:, TE:2 * TE], xc, AF.Square)
        sc.vop("tensor_copy", tmp[:, 0:TE], xc)
        sc.mark("s"); sc.mark("v")
        sc.wait("t", "s"); sc.wait("t", "v")
        for i, (a, b_) in enumerate(tiles):
            w = b_ - a
            sc.mm(pool[i][0:1, 0:w], ones_bf[:, 0:1], tmp[:, a:b_], start=(c == 0), stop=(c == nchunks - 1))
            sc.mm(pool[3 + i][0:1, 0:w], ones_bf[:, 0:1], tmp[:, TE + a:TE + b_], start=(c == 0), stop=(c == nchunks - 1))
        sc.mark("t")
        sc.wait("s", "t"); sc.wait("v", "t")
    sc.mark("t")
    sc.wait("v", "t")
    sumx = stats[0:1, 0:TE]
    var_t = stats[0:1, TE:2 * TE]
    rstd = stats[0:1, 2 * TE:3 * TE]
    mr = stats[0:1, 3 * TE:4 * TE]
    for i, (a, b_) in enumerate(tiles):
        w = b_ - a
        sc.vop("tensor_copy", sumx[:, a:b_], pool[i][0:1, 0:w])
        sc.vop("tensor_copy", var_t[:, a:b_], pool[3 + i][0:1, 0:w])
    sc.vop("scalar_tensor_tensor", mr, sumx, 1.0, sumx, ALU.mult, ALU.mult)
    sc.vop("scalar_tensor_tensor", var_t, mr, -1.0 / D, var_t, ALU.mult, ALU.add)
    sc.mark("v")
    sc.wait("s", "v")
    sc.sop("activation", rstd, var_t, AF.Ln, scale=1.0 / D, bias=sc.const(eps, parts=1))
    sc.sop("activation", rstd, rstd, AF.Exp, scale=-0.5)
    sc.mark("s")
    sc.wait("v", "s")
    sc.vop("scalar_tensor_tensor", mr, sumx, 1.0 / D, rstd, ALU.mult, ALU.mult)
    sc.mark("v")
    for i, (a, b_) in enumerate(tiles):
        w = b_ - a
        sc.wait("t", "v")
        sc.mm(pool[6][:, 0:w], ones1_f32[0:1, :], rstd[:, a:b_], start=True, stop=True)
        sc.mm(pool[7][:, 0:w], ones1_f32[0:1, :], mr[:, a:b_], start=True, stop=True)
        sc.mark("t")
        sc.wait("v", "t")
        for c in range(nchunks):
            xc = xe[:, c * TE + a:c * TE + b_]
            oc = xlne[:, c * TE + a:c * TE + b_]
            sc.vop("tensor_mul", tmp[:, 0:w], xc, pool[6][:, 0:w])
            sc.vop("tensor_sub", oc, tmp[:, 0:w], pool[7][:, 0:w])
        sc.mark("v")
        sc.wait("t", "v")


def build_launchD():
    nc = bass.Bass("TRN2", target_bir_lowering=False)
    x2_in = nc.dram_tensor("x2", [16, 128, TQ], F32, kind="ExternalInput")
    halo_in = nc.dram_tensor("halo", [16, 128, 3], F32, kind="ExternalInput")
    w_in = nc.dram_tensor("wcat", [24, 16, 128, 256], F32, kind="ExternalInput")
    wba_in = nc.dram_tensor("wba", [128, 16 * 24], F32, kind="ExternalInput")
    convw_in = nc.dram_tensor("convw", [128, 40 * 4], F32, kind="ExternalInput")
    alog_in = nc.dram_tensor("alog", [8, 1], F32, kind="ExternalInput")
    dtb_in = nc.dram_tensor("dtb", [8, 1], F32, kind="ExternalInput")
    qn_out = nc.dram_tensor("qn", [8, 128, TQ], BF16, kind="ExternalOutput")
    kn_out = nc.dram_tensor("kn", [16, 128, TQ], BF16, kind="ExternalOutput")
    vv_out = nc.dram_tensor("vv", [16, 128, TQ], BF16, kind="ExternalOutput")
    gate_out = nc.dram_tensor("gate", [8, 128, TQ], BF16, kind="ExternalOutput")
    beta_out = nc.dram_tensor("beta", [16, TQ], F32, kind="ExternalOutput")
    lc_out = nc.dram_tensor("lc", [8, TQ], F32, kind="ExternalOutput")

    xe = nc.alloc_sbuf_tensor("xe", [128, 16 * TE], F32)
    xlne = nc.alloc_sbuf_tensor("xlne", [128, 16 * TE], BF16)
    tmp = nc.alloc_sbuf_tensor("tmp", [128, 2 * TE], BF16)
    stats = nc.alloc_sbuf_tensor("stats", [1, 4 * TE], F32)
    ones_bf = nc.alloc_sbuf_tensor("ones_bf", [128, 1], BF16)
    ones1_f32 = nc.alloc_sbuf_tensor("ones1_f32", [1, 128], F32)
    ones8 = nc.alloc_sbuf_tensor("ones8", [16, 64], F32)
    wbuf = [nc.alloc_sbuf_tensor(f"wbuf{i}", [128, 256], BF16) for i in range(2)]
    wba = nc.alloc_sbuf_tensor("wba_sb", [128, 16 * 24], BF16)
    convw = nc.alloc_sbuf_tensor("convw_sb", [128, 40 * 4], F32)
    alog = nc.alloc_sbuf_tensor("alog_sb", [8, 1], F32)
    dtb = nc.alloc_sbuf_tensor("dtb_sb", [8, 1], F32)
    xq = nc.alloc_sbuf_tensor("xq", [128, 2 * TE], BF16)
    cv = nc.alloc_sbuf_tensor("cv", [128, 2 * TQ], BF16)
    nrm = nc.alloc_sbuf_tensor("nrm", [1, 2 * TQ], F32)
    outstage = nc.alloc_sbuf_tensor("outstage", [128, 2 * TQ], BF16)
    zsb = nc.alloc_sbuf_tensor("zsb", [16, TQ], F32)
    sp1 = nc.alloc_sbuf_tensor("sp1", [16, TQ], F32)
    sp2 = nc.alloc_sbuf_tensor("sp2", [16, TQ], F32)
    beta_sb = nc.alloc_sbuf_tensor("beta_sb", [16, TQ], F32)
    lc_sb = nc.alloc_sbuf_tensor("lc_sb", [16, TQ], F32)
    asc = nc.alloc_sbuf_tensor("asc", [8, 2], F32)

    pool = [nc.alloc_psum_tensor(f"pool{i}", [128, 512], F32) for i in range(8)]

    sc = Sched(nc).open()
    sc.init_consts()
    sc.const(1e-5, parts=1)
    sc.const(1.0)
    sc.vop("memset", ones_bf[:], 1.0)
    sc.vop("memset", ones1_f32[:], 1.0)
    sc.vop("memset", ones8[:], 1.0)
    sc.mark("v")
    sc.wait("t", "v"); sc.wait("s", "v")

    sc.dma("y", bass.AP(xe, 3, [[16 * TE, 128], [TE, 16], [1, TQ]]),
           bass.AP(x2_in, 0, [[TQ, 128], [128 * TQ, 16], [1, TQ]]))
    sc.dma("y", bass.AP(xe, 0, [[16 * TE, 128], [TE, 16], [1, 3]]),
           bass.AP(halo_in, 0, [[3, 128], [128 * 3, 16], [1, 3]]))
    ld0 = sc.cnt["yd"]
    sc.dma("g", wba[:], wba_in.ap())
    sc.dma("g", convw[:], convw_in.ap())
    sc.dma("g", alog[:], alog_in.ap())
    sc.dma("g", dtb[:], dtb_in.ap())
    gld0 = sc.cnt["gd"]
    sc.wait("t", "yd", ld0); sc.wait("v", "yd", ld0); sc.wait("s", "yd", ld0)

    emit_ln2(sc, xe[:], xlne[:], tmp[:], stats[:], pool, ones_bf, ones1_f32, 16)
    sc.barrier()
    sc.wait("v", "gd", gld0); sc.wait("t", "gd", gld0); sc.wait("s", "gd", gld0)

    RSQ = 1.0 / float(np.sqrt(128.0))
    slab_ld = {}
    prev_t = {0: 0, 1: 0}
    slab_ld[0] = sc.dma("g", wbuf[0][:], w_in.ap()[0, 0], key="d0")
    for grp in range(24):
        has_conv = grp < 20
        for d in range(16):
            gi = grp * 16 + d
            slot = gi % 2
            if gi + 1 < 24 * 16:
                nslot = (gi + 1) % 2
                sc.wait("g", "t", prev_t[nslot])
                slab_ld[gi + 1] = sc.dma("g", wbuf[nslot][:], w_in.ap()[(gi + 1) // 16, (gi + 1) % 16], key=f"d{nslot}")
            sc.wait("t", f"d{slot}", slab_ld[gi])
            for oc2 in range(2):
                lh = wbuf[slot][:, oc2 * 128:(oc2 + 1) * 128]
                xc = xlne[:, d * TE:(d + 1) * TE]
                sc.mm(pool[oc2 * 2][:, :], lh, xc[:, 3:515], start=(d == 0), stop=(d == 15))
                sc.mm(pool[oc2 * 2 + 1][:, :], lh, xc[:, 515:1027], start=(d == 0), stop=(d == 15))
                if has_conv:
                    sc.mm(pool[6 + oc2][:, 0:3], lh, xc[:, 0:3], start=(d == 0), stop=(d == 15))
            prev_t[slot] = sc.mark("t")
        tm = sc.mark("t")
        sc.wait("v", "t", tm); sc.wait("s", "t", tm)
        for oc2 in range(2):
            oc = grp * 2 + oc2
            if has_conv:
                xqs = xq[:, oc2 * TE:(oc2 + 1) * TE]
                sc.vop("tensor_copy", xqs[:, 0:3], pool[6 + oc2][:, 0:3])
                sc.vop("tensor_copy", xqs[:, 3:515], pool[oc2 * 2][:, :])
                sc.vop("tensor_copy", xqs[:, 515:1027], pool[oc2 * 2 + 1][:, :])
                cvs = cv[:, oc2 * TQ:(oc2 + 1) * TQ]
                wcol = convw[:, oc * 4:(oc + 1) * 4]
                sc.vop("tensor_scalar_mul", cvs, xqs[:, 0:TQ], wcol[:, 0:1])
                for tau in range(1, 4):
                    sc.vop("scalar_tensor_tensor", cvs, xqs[:, tau:tau + TQ], wcol[:, tau:tau + 1], cvs, ALU.mult, ALU.add)
                sc.mark("v")
                sc.wait("s", "v")
                sc.sop("activation", cvs, cvs, AF.Silu)
                sc.mark("s")
                sc.wait("v", "s"); sc.wait("t", "s")
            else:
                sc.vop("tensor_copy", outstage[:, oc2 * TQ:oc2 * TQ + 512], pool[oc2 * 2][:, :])
                sc.vop("tensor_copy", outstage[:, oc2 * TQ + 512:(oc2 + 1) * TQ], pool[oc2 * 2 + 1][:, :])
                sc.mark("v")
                sc.wait("t", "v")
        if has_conv:
            for oc2 in range(2):
                oc = grp * 2 + oc2
                cvs = cv[:, oc2 * TQ:(oc2 + 1) * TQ]
                dst = outstage[:, oc2 * TQ:(oc2 + 1) * TQ]
                if oc < 24:
                    sc.sop("activation", tmp[:, 0:TQ], cvs, AF.Square)
                    sc.mark("s")
                    sc.wait("t", "s")
                    sc.mm(pool[4][0:1, :], ones_bf[:, 0:1], tmp[:, 0:512], start=True, stop=True)
                    sc.mm(pool[5][0:1, :], ones_bf[:, 0:1], tmp[:, 512:1024], start=True, stop=True)
                    tm = sc.mark("t")
                    sc.wait("s", "t", tm)
                    nr = nrm[0:1, oc2 * TQ:(oc2 + 1) * TQ]
                    sc.sop("activation", nr[:, 0:512], pool[4][0:1, :], AF.Sqrt)
                    sc.sop("activation", nr[:, 512:1024], pool[5][0:1, :], AF.Sqrt)
                    sc.mark("s")
                    sc.wait("v", "s")
                    sc.vop("tensor_scalar_max", nr, nr, 1e-12)
                    sc.vop("reciprocal", nr, nr)
                    if oc < 8:
                        sc.vop("tensor_scalar_mul", nr, nr, RSQ)
                    sc.mark("v")
                    sc.wait("t", "v")
                    sc.mm(pool[4][:, :], ones1_f32[0:1, :], nr[:, 0:512], start=True, stop=True)
                    sc.mm(pool[5][:, :], ones1_f32[0:1, :], nr[:, 512:1024], start=True, stop=True)
                    tm = sc.mark("t")
                    sc.wait("v", "t", tm)
                    sc.vop("tensor_mul", dst[:, 0:512], cvs[:, 0:512], pool[4][:, :])
                    sc.vop("tensor_mul", dst[:, 512:1024], cvs[:, 512:1024], pool[5][:, :])
                    sc.mark("v")
                    sc.wait("t", "v")
                else:
                    sc.vop("tensor_copy", dst, cvs)
                    sc.mark("v")
        vm = sc.mark("v")
        sc.wait("y", "v", vm)
        sc.wait("y", "s", sc.cnt["s"])
        for oc2 in range(2):
            oc = grp * 2 + oc2
            src = outstage[:, oc2 * TQ:(oc2 + 1) * TQ]
            if oc < 8:
                sc.dma("y", qn_out.ap()[oc], src)
            elif oc < 24:
                sc.dma("y", kn_out.ap()[oc - 8], src)
            elif oc < 40:
                sc.dma("y", vv_out.ap()[oc - 24], src)
            else:
                sc.dma("y", gate_out.ap()[oc - 40], src)
        sc.wait("v", "yd", sc.cnt["yd"])
        sc.wait("s", "yd", sc.cnt["yd"])
        sc.wait("t", "v", vm)

    sc.barrier()
    for th in range(2):
        sl = slice(th * 512, (th + 1) * 512)
        for d in range(16):
            xc = xlne[:, d * TE:(d + 1) * TE]
            sc.mm(pool[th][0:16, :], wba[:, d * 24:d * 24 + 16], xc[:, 3 + th * 512:3 + (th + 1) * 512], start=(d == 0), stop=(d == 15))
            sc.mm(pool[2 + th][0:8, :], wba[:, d * 24 + 16:d * 24 + 24], xc[:, 3 + th * 512:3 + (th + 1) * 512], start=(d == 0), stop=(d == 15))
    tm = sc.mark("t")
    sc.wait("s", "t", tm); sc.wait("v", "t", tm)
    for th in range(2):
        sl = slice(th * 512, (th + 1) * 512)
        sc.sop("activation", beta_sb[:, sl], pool[th][0:16, :], AF.Sigmoid)
        sc.vop("tensor_scalar_add", zsb[0:8, sl], pool[2 + th][0:8, :], dtb[0:8, 0:1])
    sc.mark("s"); sc.mark("v")
    sc.wait("v", "s")
    sc.vop("tensor_scalar_mul", beta_sb[:], beta_sb[:], 2.0)
    sc.mark("v")
    sc.wait("s", "v")
    sc.sop("activation", sp1[0:8, :], zsb[0:8, :], AF.Abs)
    sc.sop("activation", sp1[0:8, :], sp1[0:8, :], AF.Exp, scale=-1.0)
    sc.sop("activation", sp1[0:8, :], sp1[0:8, :], AF.Ln, bias=sc.const(1.0, parts=8))
    sc.sop("activation", asc[:, 0:1], alog[:, 0:1], AF.Exp)
    sc.mark("s")
    sc.wait("v", "s")
    sc.vop("tensor_relu", sp2[0:8, :], zsb[0:8, :])
    sc.vop("tensor_add", sp1[0:8, :], sp1[0:8, :], sp2[0:8, :])
    sc.vop("tensor_scalar_mul", asc[:, 1:2], asc[:, 0:1], -1.0)
    sc.vop("tensor_scalar_mul", sp1[0:8, :], sp1[0:8, :], asc[:, 1:2])
    for ch in range(16):
        sl = slice(ch * 64, (ch + 1) * 64)
        sc.vop("tensor_tensor_scan", lc_sb[0:8, sl], ones8[0:8, :], sp1[0:8, sl], 0.0, ALU.mult, ALU.add)
    vm = sc.mark("v")
    sc.wait("y", "v", vm)
    sc.dma("y", beta_out.ap(), beta_sb[:])
    sc.dma("y", lc_out.ap(), lc_sb[0:8, :])

    sc._flush("y")
    nc.sync.wait_ge(sc.sem["yd"], sc.cnt["yd"])
    sc.close()
    return nc


def host_prep_D(x2_all, Wq, Wk, Wv, Wg, Wb, Wa, A_log, dt_bias, cqw, ckw, cvw):
    wcat = np.concatenate([Wq, Wk, Wv, Wg], axis=1)
    w_arr = np.ascontiguousarray(
        wcat.reshape(16, 128, 24, 2, 128).transpose(2, 0, 1, 3, 4)).reshape(24, 16, 128, 256)
    wba = np.ascontiguousarray(
        np.concatenate([Wb, Wa], axis=1).reshape(16, 128, 24).transpose(1, 0, 2)).reshape(128, 16 * 24)
    convw = np.ascontiguousarray(
        np.concatenate([cqw[:, 0, :], ckw[:, 0, :], cvw[:, 0, :]], axis=0)
        .reshape(40, 128, 4).transpose(1, 0, 2)).reshape(128, 160)
    maps = []
    for core in range(NCORE):
        b, quarter = core // 4, core % 4
        if quarter == 0:
            halo = np.zeros((16, 128, 3), np.float32)
        else:
            halo = np.ascontiguousarray(x2_all[core - 1][:, :, TQ - 3:TQ])
        maps.append({
            "x2": x2_all[core], "halo": halo, "wcat": w_arr, "wba": wba,
            "convw": convw, "alog": A_log.reshape(8, 1).astype(np.float32),
            "dtb": dt_bias.reshape(8, 1).astype(np.float32),
        })
    return maps


def build_launchE():
    nc = bass.Bass("TRN2", target_bir_lowering=False)
    NU = 2
    NCH = 64
    qf_in = nc.dram_tensor("qf", [NU, 128, S], BF16, kind="ExternalInput")
    kf_in = nc.dram_tensor("kf", [NU, 128, 2 * S], BF16, kind="ExternalInput")
    ktm_in = nc.dram_tensor("ktm", [NU, NCH, 128, 128], BF16, kind="ExternalInput")
    vtm_in = nc.dram_tensor("vtm", [NU, NCH, 128, 128], BF16, kind="ExternalInput")
    gate_in = nc.dram_tensor("gate", [NU, 128, S], BF16, kind="ExternalInput")
    bc_in = nc.dram_tensor("beta_c", [NU, 128, NCH], F32, kind="ExternalInput")
    lc2_in = nc.dram_tensor("lc2", [NU, 128, NCH], F32, kind="ExternalInput")
    lcq_in = nc.dram_tensor("lcq", [NU, 128, NCH], F32, kind="ExternalInput")
    lcn_in = nc.dram_tensor("lcn", [NU, 128, NCH], F32, kind="ExternalInput")
    identf_in = nc.dram_tensor("identf", [128, 128], F32, kind="ExternalInput")
    normw_in = nc.dram_tensor("normw", [128, 1], F32, kind="ExternalInput")
    maskm_in = nc.dram_tensor("maskm", [128, 128], F32, kind="ExternalInput")
    maskmt_in = nc.dram_tensor("maskmt", [128, 128], F32, kind="ExternalInput")
    maskl_in = nc.dram_tensor("maskl", [128, 64], F32, kind="ExternalInput")
    identb_in = nc.dram_tensor("identb", [128, 128], BF16, kind="ExternalInput")
    of_out = nc.dram_tensor("of", [NU, 128, S], BF16, kind="ExternalOutput")

    qf = [nc.alloc_sbuf_tensor(f"qf{i}", [128, S], BF16) for i in range(2)]
    kf = [nc.alloc_sbuf_tensor(f"kf{i}", [128, 2 * S], BF16) for i in range(2)]
    ktm = [nc.alloc_sbuf_tensor(f"ktm{i}", [128, NCH * 128], BF16) for i in range(2)]
    vtm = [nc.alloc_sbuf_tensor(f"vtm{i}", [128, NCH * 128], BF16) for i in range(2)]
    gate = [nc.alloc_sbuf_tensor(f"gate{i}", [128, S], BF16) for i in range(2)]
    bcw = [nc.alloc_sbuf_tensor(f"bcw{i}", [128, NCH], F32) for i in range(2)]
    lc2w = [nc.alloc_sbuf_tensor(f"lc2w{i}", [128, NCH], F32) for i in range(2)]
    lcqw = [nc.alloc_sbuf_tensor(f"lcqw{i}", [128, NCH], F32) for i in range(2)]
    lcnw = [nc.alloc_sbuf_tensor(f"lcnw{i}", [128, NCH], F32) for i in range(2)]
    rows_sb = nc.alloc_sbuf_tensor("rows_sb", [1, 320], F32)
    identf = nc.alloc_sbuf_tensor("identf_sb", [128, 128], F32)
    o_sb = [nc.alloc_sbuf_tensor(f"osb{i}", [128, S], BF16) for i in range(2)]
    normw = nc.alloc_sbuf_tensor("normw_sb", [128, 1], F32)
    maskm = nc.alloc_sbuf_tensor("maskm_sb", [128, 128], F32)
    maskmt = nc.alloc_sbuf_tensor("maskmt_sb", [128, 128], F32)
    maskl = nc.alloc_sbuf_tensor("maskl_sb", [128, 64], F32)
    identb = nc.alloc_sbuf_tensor("identb_sb", [128, 128], BF16)
    ones1_f32 = nc.alloc_sbuf_tensor("ones1_f32", [1, 128], F32)
    ones_bf = nc.alloc_sbuf_tensor("ones_bf", [128, 1], BF16)

    S_st = nc.alloc_sbuf_tensor("S_st", [128, 128], F32)
    S_bf = nc.alloc_sbuf_tensor("S_bf", [128, 128], BF16)
    t1 = nc.alloc_sbuf_tensor("t1", [128, 128], F32)
    t2 = nc.alloc_sbuf_tensor("t2", [128, 128], F32)
    Rr = nc.alloc_sbuf_tensor("Rr", [128, 128], F32)
    Ff = [nc.alloc_sbuf_tensor(f"Ff{i}", [128, 128], BF16) for i in range(2)]
    Hh = [nc.alloc_sbuf_tensor(f"Hh{i}", [128, 128], BF16) for i in range(2)]
    acc = nc.alloc_sbuf_tensor("acc", [128, 128], BF16)
    tbt = nc.alloc_sbuf_tensor("tbt", [128, 128], BF16)
    w0 = nc.alloc_sbuf_tensor("w0", [128, 128], BF16)
    lt = nc.alloc_sbuf_tensor("lt", [128, 64], BF16)
    qbar = nc.alloc_sbuf_tensor("qbar", [128, 64], BF16)
    xb = nc.alloc_sbuf_tensor("xb", [128, 128], BF16)
    uu = nc.alloc_sbuf_tensor("uu", [128, 128], BF16)
    khat = nc.alloc_sbuf_tensor("khat", [128, 128], BF16)
    scr = nc.alloc_sbuf_tensor("scr", [128, 8], F32)
    tmp = nc.alloc_sbuf_tensor("tmp", [128, 1024], BF16)
    nrm = nc.alloc_sbuf_tensor("nrm", [1, 512], F32)

    pool = [nc.alloc_psum_tensor(f"pool{i}", [128, 512], F32) for i in range(8)]

    sc = Sched(nc).open()
    sc.init_consts()
    sc.const(1e-5, parts=1)
    sc.vop("memset", ones1_f32[:], 1.0)
    sc.vop("memset", ones_bf[:], 1.0)
    sc.mark("v")

    sc.dma("g", normw[:], normw_in.ap())
    sc.dma("g", maskm[:], maskm_in.ap())
    sc.dma("g", maskmt[:], maskmt_in.ap())
    sc.dma("g", maskl[:], maskl_in.ap())
    sc.dma("g", identb[:], identb_in.ap())
    sc.dma("g", identf[:], identf_in.ap())
    gld0 = sc.cnt["gd"]
    sc.wait("t", "gd", gld0); sc.wait("v", "gd", gld0); sc.wait("s", "gd", gld0)

    def load_unit(u, slot):
        ks = f"d{slot}"
        sc.dma("y", qf[slot][:], qf_in.ap()[u], key=ks)
        sc.dma("y", kf[slot][:], kf_in.ap()[u], key=ks)
        sc.dma("y", ktm[slot][:], bass.AP(ktm_in, u * NCH * 128 * 128, [[128, 128], [128 * 128, NCH], [1, 128]]), key=ks)
        sc.dma("y", vtm[slot][:], bass.AP(vtm_in, u * NCH * 128 * 128, [[128, 128], [128 * 128, NCH], [1, 128]]), key=ks)
        sc.dma("y", gate[slot][:], gate_in.ap()[u], key=ks)
        kg = f"d{2 + slot}"
        sc.dma("g", bcw[slot][:], bc_in.ap()[u], key=kg)
        sc.dma("g", lc2w[slot][:], lc2_in.ap()[u], key=kg)
        sc.dma("g", lcqw[slot][:], lcq_in.ap()[u], key=kg)
        sc.dma("g", lcnw[slot][:], lcn_in.ap()[u], key=kg)
        return sc.cnt[ks], sc.cnt[kg]

    loads = {0: load_unit(0, 0)}
    done_v = {}
    for u in range(NU):
        slot = u % 2
        if u + 1 < NU:
            loads[u + 1] = load_unit(u + 1, 1 - slot)
        ydc, gdc = loads[u]
        ks, kg = f"d{slot}", f"d{2 + slot}"
        sc.wait("t", ks, ydc); sc.wait("t", kg, gdc)
        sc.wait("v", ks, ydc); sc.wait("v", kg, gdc)
        sc.wait("s", ks, ydc); sc.wait("s", kg, gdc)
        sc.vop("memset", S_st[:], 0.0)
        sc.vop("memset", S_bf[:], 0.0)
        sc.mark("v")
        sc.wait("t", "v")
        for c in range(NCH):
            kfc = kf[slot][:, c * 128:(c + 1) * 128]
            qfc = qf[slot][:, c * 64:(c + 1) * 64]
            ktc = ktm[slot][:, c * 128:(c + 1) * 128]
            vtc = vtm[slot][:, c * 128:(c + 1) * 128]
            bcc = bcw[slot][:, c:c + 1]
            lc2c = lc2w[slot][:, c:c + 1]
            lcnc = lcnw[slot][:, c:c + 1]
            sc.wait("v", "s", sc.cnt["s"])
            sc.vop("tensor_sub", scr[:, 0:1], lcnc, lc2c)
            sc.vop("tensor_scalar_mul", scr[:, 1:2], bcc, -1.0)
            sc.mark("v"); sc.wait("s", "v")
            sc.sop("activation", scr[:, 2:3], lc2c, AF.Exp)
            sc.sop("activation", scr[:, 3:4], scr[:, 0:1], AF.Exp)
            sc.sop("activation", scr[:, 4:5], lcnc, AF.Exp)
            sc.mark("s")
            sc.trans(pool[2][0:1, 128:256], lc2c, identf[:, :])
            sc.trans(pool[2][0:1, 256:384], bcc, identf[:, :])
            sc.trans(pool[2][0:1, 384:448], lcqw[slot][0:64, c:c + 1], identf[0:64, 0:64])
            tm = sc.mark("t")
            sc.wait("v", "t", tm)
            sc.vop("tensor_copy", rows_sb[0:1, 0:128], pool[2][0:1, 128:256])
            sc.vop("tensor_copy", rows_sb[0:1, 128:256], pool[2][0:1, 256:384])
            sc.vop("tensor_copy", rows_sb[0:1, 256:320], pool[2][0:1, 384:448])
            sc.mark("v"); sc.wait("t", "v")
            sc.mm(pool[0][:, 0:128], kfc, kfc, start=True, stop=True)
            sc.mm(pool[1][:, 0:128], ones1_f32[0:1, :], rows_sb[0:1, 0:128], start=True, stop=True)
            sc.mm(pool[1][:, 128:256], ones1_f32[0:1, :], rows_sb[0:1, 128:256], start=True, stop=True)
            sc.mm(pool[1][:, 256:320], ones1_f32[0:1, :], rows_sb[0:1, 256:320], start=True, stop=True)
            tm = sc.mark("t")
            sc.wait("v", "t", tm)
            sc.vop("tensor_scalar", t1[:], pool[1][:, 0:128], lc2c, None, ALU.subtract)
            sc.vop("tensor_add", t2[:], t1[:], maskm[:])
            sc.mark("v"); sc.wait("s", "v")
            sc.sop("activation", Rr[:], t2[:], AF.Exp, scale=-1.0)
            sc.mark("s"); sc.wait("v", "s")
            sc.vop("tensor_mul", Rr[:], Rr[:], pool[0][:, 0:128])
            sc.vop("tensor_scalar_mul", Ff[0][:], Rr[:], scr[:, 1:2])
            sc.vop("tensor_add", t2[:], t1[:], maskmt[:])
            sc.mark("v"); sc.wait("s", "v")
            sc.sop("activation", Rr[:], t2[:], AF.Exp)
            sc.mark("s"); sc.wait("v", "s")
            sc.vop("tensor_mul", Rr[:], Rr[:], pool[0][:, 0:128])
            sc.vop("scalar_tensor_tensor", Hh[0][:], Rr[:], -1.0, pool[1][:, 128:256], ALU.mult, ALU.mult)
            sc.vop("tensor_add", acc[:], Hh[0][:], identb[:])
            sc.vop("tensor_scalar", t1[:, 0:64], pool[1][:, 256:320], lc2c, None, ALU.subtract)
            sc.vop("tensor_add", t1[:, 0:64], t1[:, 0:64], maskl[:])
            sc.mark("v"); sc.wait("s", "v")
            sc.sop("activation", t2[:, 0:64], t1[:, 0:64], AF.Exp)
            sc.sop("activation", t2[:, 64:128], pool[1][:, 256:320][:, 0:64], AF.Exp)
            sc.mark("s")
            sc.wait("t", "s"); sc.wait("t", "v")
            sc.mm(pool[2][:, 0:64], kfc, qfc, start=True, stop=True)
            cur = 0
            pidx = [3, 4]
            for kstep in range(1, 7):
                nxt = 1 - cur
                sc.mm(pool[pidx[0]][:, 0:128], Hh[cur][:], Ff[cur][:], start=True, stop=True)
                sc.mm(pool[pidx[1]][:, 0:128], Ff[cur][:], Hh[cur][:], start=True, stop=True)
                tm = sc.mark("t")
                sc.wait("v", "t", tm); sc.wait("s", "t", tm)
                sc.vop("tensor_copy", Ff[nxt][:], pool[pidx[0]][:, 0:128])
                sc.sop("activation", Hh[nxt][:], pool[pidx[1]][:, 0:128], AF.Copy)
                sc.mark("v"); sc.mark("s")
                sc.wait("t", "v"); sc.wait("t", "s")
                sc.mm(pool[5][:, 0:128], Ff[nxt][:], acc[:], start=True, stop=True)
                tm = sc.mark("t")
                sc.wait("v", "t", tm)
                sc.vop("tensor_add", acc[:], acc[:], pool[5][:, 0:128])
                sc.mark("v"); sc.wait("t", "v")
                cur = nxt
            sc.vop("tensor_mul", lt[:], t2[:, 0:64], pool[2][:, 0:64])
            sc.vop("tensor_scalar_mul", tbt[:], acc[:], bcc)
            sc.vop("tensor_mul", qbar[:], qfc, t2[:, 64:128])
            sc.mark("v"); sc.wait("t", "v")
            sc.mm(pool[6][:, 0:128], tbt[:], vtc, start=True, stop=True)
            sc.mm(pool[7][:, 0:128], kfc, S_bf[:], start=True, stop=True)
            tm = sc.mark("t")
            sc.wait("v", "t", tm)
            sc.vop("tensor_copy", w0[:], pool[6][:, 0:128])
            sc.vop("tensor_scalar_mul", xb[:], pool[7][:, 0:128], scr[:, 2:3])
            sc.vop("tensor_scalar_mul", khat[:], ktc, scr[:, 3:4])
            sc.mark("v"); sc.wait("t", "v")
            sc.mm(pool[6][:, 0:128], tbt[:], xb[:], start=True, stop=True)
            tm = sc.mark("t")
            sc.wait("v", "t", tm)
            sc.vop("tensor_sub", uu[:], w0[:], pool[6][:, 0:128])
            sc.mark("v"); sc.wait("t", "v")
            sc.mm(pool[7][:, 0:64], S_bf[:], qbar[:], start=True, stop=False)
            sc.mm(pool[7][:, 0:64], uu[:], lt[:], start=False, stop=True)
            sc.mm(pool[6][:, 0:128], khat[:], uu[:], start=True, stop=True)
            tm = sc.mark("t")
            sc.wait("v", "t", tm)
            sc.vop("tensor_copy", o_sb[slot][:, c * 64:(c + 1) * 64], pool[7][:, 0:64])
            sc.vop("scalar_tensor_tensor", S_st[:], S_st[:], scr[:, 4:5], pool[6][:, 0:128], ALU.mult, ALU.add)
            sc.mark("v"); sc.wait("s", "v")
            sc.sop("activation", S_bf[:], S_st[:], AF.Copy)
            sc.mark("s")
            sc.wait("t", "s"); sc.wait("t", "v")
        for blk in range(8):
            bs = slice(blk * 512, (blk + 1) * 512)
            sc.sop("activation", tmp[:, 0:512], o_sb[slot][:, bs], AF.Square)
            sc.mark("s"); sc.wait("t", "s")
            sc.mm(pool[0][0:1, :], ones_bf[:, 0:1], tmp[:, 0:512], start=True, stop=True)
            tm = sc.mark("t")
            sc.wait("s", "t", tm)
            sc.sop("activation", nrm[0:1, :], pool[0][0:1, :], AF.Ln, scale=1.0 / 128.0, bias=sc.const(1e-5, parts=1))
            sc.sop("activation", nrm[0:1, :], nrm[0:1, :], AF.Exp, scale=-0.5)
            sc.mark("s"); sc.wait("t", "s")
            sc.mm(pool[1][:, :], ones1_f32[0:1, :], nrm[0:1, :], start=True, stop=True)
            tm = sc.mark("t")
            sc.wait("s", "t", tm); sc.wait("v", "t", tm)
            sc.sop("activation", tmp[:, 512:1024], gate[slot][:, bs], AF.Silu)
            sc.mark("s"); sc.wait("v", "s")
            sc.vop("tensor_scalar_mul", tmp[:, 512:1024], tmp[:, 512:1024], normw[:, 0:1])
            sc.vop("tensor_mul", tmp[:, 0:512], o_sb[slot][:, bs], pool[1][:, :])
            sc.vop("tensor_mul", o_sb[slot][:, bs], tmp[:, 0:512], tmp[:, 512:1024])
            sc.mark("v"); sc.wait("t", "v"); sc.wait("s", "v")
        vm = sc.mark("v")
        done_v[u] = vm
        sc.wait("y", "v", vm)
        sc.dma("y", of_out.ap()[u], o_sb[slot][:])
        sc.wait("y", "yd", sc.cnt["yd"])

    sc._flush("y")
    nc.sync.wait_ge(sc.sem["yd"], sc.cnt["yd"])
    sc.close()
    return nc


def host_prep_E(qn_all, kn_all, vv_all, gate_all, beta_all, lc_all, norm_w):
    NCH = 64
    def full(arrs, b, idx):
        return np.concatenate([arrs[b * 4 + q][idx] for q in range(4)], axis=-1)
    maskm = np.where(np.tril(np.ones((128, 128)), -1) > 0, 0.0, 1e30).astype(np.float32)
    maskmt = np.where(np.triu(np.ones((128, 128)), 1) > 0, 0.0, -1e30).astype(np.float32)
    j_idx = np.arange(128)[:, None]
    t_idx = np.arange(64)[None, :]
    maskl = np.where(j_idx <= 2 * t_idx + 1, 0.0, -1e30).astype(np.float32)
    identb = np.eye(128, dtype=np.float32).astype(bf16)
    maps = []
    for core in range(NCORE):
        m = {"normw": norm_w.reshape(128, 1).astype(np.float32), "maskm": maskm, "maskmt": maskmt, "maskl": maskl, "identb": identb}
        qfs, kfs, ktms, vtms, gates = [], [], [], [], []
        bcs, lc2s, lcqs, lcns = [], [], [], []
        for i in range(2):
            u = 2 * core + i
            b, h = u // 8, u % 8
            q_full = full(qn_all, b, h)
            k0 = full(kn_all, b, h); k1 = full(kn_all, b, 8 + h)
            v0 = full(vv_all, b, h); v1 = full(vv_all, b, 8 + h)
            kv = np.stack([k0, k1], axis=-1).reshape(128, 2 * S)
            vv = np.stack([v0, v1], axis=-1).reshape(128, 2 * S)
            g_full = full(gate_all, b, h)
            be0 = full(beta_all, b, h); be1 = full(beta_all, b, 8 + h)
            bv = np.stack([be0, be1], axis=-1).reshape(2 * S)
            lc = full(lc_all, b, h)
            lcv = np.repeat(lc, 2)
            qfs.append(q_full); kfs.append(kv)
            ktms.append(np.ascontiguousarray(kv.T).reshape(NCH, 128, 128))
            vtms.append(np.ascontiguousarray(vv.T).reshape(NCH, 128, 128))
            gates.append(g_full)
            bcs.append(np.ascontiguousarray(bv.reshape(NCH, 128).T).astype(np.float32))
            lc2s.append(np.ascontiguousarray(lcv.reshape(NCH, 128).T).astype(np.float32))
            lcq_c = np.zeros((128, NCH), np.float32)
            lcq_c[0:64, :] = lc.reshape(NCH, 64).T
            lcqs.append(lcq_c)
            lcns.append(np.ascontiguousarray(np.broadcast_to(
                lcv.reshape(NCH, 128)[:, 127], (128, NCH))).astype(np.float32))
        m["identf"] = np.eye(128, dtype=np.float32)
        m.update({
            "qf": np.stack(qfs), "kf": np.stack(kfs), "ktm": np.stack(ktms),
            "vtm": np.stack(vtms), "gate": np.stack(gates),
            "beta_c": np.stack(bcs), "lc2": np.stack(lc2s), "lcq": np.stack(lcqs),
            "lcn": np.stack(lcns),
        })
        maps.append(m)
    return maps


def build_launchF():
    nc = bass.Bass("TRN2", target_bir_lowering=False)
    of_in = nc.dram_tensor("of_in", [8, 128, TQ], BF16, kind="ExternalInput")
    x2_in = nc.dram_tensor("x2", [16, 128, TQ], F32, kind="ExternalInput")
    wo1_in = nc.dram_tensor("wo1", [16, 8, 128, 128], F32, kind="ExternalInput")
    rw_in = nc.dram_tensor("rw", [16, 128, 8], F32, kind="ExternalInput")
    rowsel_in = nc.dram_tensor("rowsel", [8, 8, 128], F32, kind="ExternalInput")
    ident_in = nc.dram_tensor("identf", [128, 128], F32, kind="ExternalInput")
    sgw_in = nc.dram_tensor("sgw", [128, 16], F32, kind="ExternalInput")
    wgu_in = nc.dram_tensor("wgu", [8, 11, 16, 128, 256], F32, kind="ExternalInput")
    wd_in = nc.dram_tensor("wd", [8, 16, 11, 128, 128], F32, kind="ExternalInput")
    swgu_in = nc.dram_tensor("swgu", [44, 16, 128, 256], F32, kind="ExternalInput")
    swd_in = nc.dram_tensor("swd", [16, 44, 128, 128], F32, kind="ExternalInput")
    out_out = nc.dram_tensor("out", [16, 128, TQ], F32, kind="ExternalOutput")

    xs = nc.alloc_sbuf_tensor("xs", [128, 16 * TQ], F32)
    o_sb = nc.alloc_sbuf_tensor("o_sb", [128, 8 * TQ], BF16)
    xln = nc.alloc_sbuf_tensor("xln", [128, 16 * TQ], BF16)
    tmp = nc.alloc_sbuf_tensor("tmp", [128, 2 * TQ], BF16)
    stats = nc.alloc_sbuf_tensor("stats", [1, 4 * TQ], F32)
    ones_bf = nc.alloc_sbuf_tensor("ones_bf", [128, 1], BF16)
    ones1_f32 = nc.alloc_sbuf_tensor("ones1_f32", [1, 128], F32)
    identf = nc.alloc_sbuf_tensor("identf_sb", [128, 128], F32)
    rw_sb = nc.alloc_sbuf_tensor("rw_sb", [128, 16 * 8], BF16)
    rowsel = nc.alloc_sbuf_tensor("rowsel_sb", [8, 8 * 128], BF16)
    sgw_sb = nc.alloc_sbuf_tensor("sgw_sb", [128, 16], BF16)
    lg = nc.alloc_sbuf_tensor("lg", [128, 8], F32)
    m1 = nc.alloc_sbuf_tensor("m1", [128, 4], F32)
    eqm = nc.alloc_sbuf_tensor("eqm", [128, 8], F32)
    ind = nc.alloc_sbuf_tensor("ind", [128, 8], F32)
    ee = nc.alloc_sbuf_tensor("ee", [128, 8], F32)
    w_tm = nc.alloc_sbuf_tensor("w_tm", [128, 8], F32)
    W_fm = nc.alloc_sbuf_tensor("W_fm", [8, TQ], BF16)
    wbc = nc.alloc_sbuf_tensor("wbc", [128, TQ], BF16)
    sg_b = nc.alloc_sbuf_tensor("sg_b", [128, TQ], BF16)
    hg = nc.alloc_sbuf_tensor("hg", [128, TQ], BF16)
    hu = nc.alloc_sbuf_tensor("hu", [128, TQ], BF16)
    hsh = nc.alloc_sbuf_tensor("hsh", [128, 44 * 512], BF16)
    he = hsh
    wov = [nc.alloc_sbuf_tensor(f"wov{i}", [128, 8 * 128], BF16) for i in range(2)]
    _osb_addr = nc.lookup_mloc(o_sb).addr
    wgb = [nc.alloc_sbuf_tensor_at(f"wgb{i}", [128, 16 * 256], BF16, offset=_osb_addr + i * 16 * 256 * 2) for i in range(2)]
    wlb = [nc.alloc_sbuf_tensor(f"wlb{i}", [128, 11 * 128], BF16) for i in range(2)]
    _st_addr = nc.lookup_mloc(stats).addr
    swb = [nc.alloc_sbuf_tensor_at(f"swb{i}", [128, 16 * 256], BF16, offset=_st_addr + i * 16 * 256 * 2) for i in range(2)]
    sdb = [nc.alloc_sbuf_tensor_at("sdb0", [128, 44 * 128], BF16, offset=_osb_addr)]

    pool = [nc.alloc_psum_tensor(f"pool{i}", [128, 512], F32) for i in range(8)]

    sc = Sched(nc).open()
    sc.init_consts()
    sc.const(1e-5, parts=1)
    sc.vop("memset", ones_bf[:], 1.0)
    sc.vop("memset", ones1_f32[:], 1.0)
    sc.mark("v")

    sc.dma("y", xs[:], _fm_ap(x2_in, 16, TQ))
    sc.dma("y", o_sb[:], _fm_ap(of_in, 8, TQ))
    ld0 = sc.cnt["yd"]
    sc.dma("g", rw_sb[:], bass.AP(rw_in, 0, [[8, 128], [128 * 8, 16], [1, 8]]))
    sc.dma("g", rowsel[:], bass.AP(rowsel_in, 0, [[128, 8], [8 * 128, 8], [1, 128]]))
    sc.dma("g", identf[:], ident_in.ap())
    sc.dma("g", sgw_sb[:], sgw_in.ap())
    gld0 = sc.cnt["gd"]

    wo_ld = {}
    wo_ld[0] = sc.dma("g", wov[0][:], bass.AP(wo1_in, 0, [[128, 128], [128 * 128, 8], [1, 128]]), key="d0")
    for ds in range(16):
        slot = ds % 2
        if ds + 1 < 16:
            if ds >= 1:
                sc.wait("g", "t", sc.cnt["t"])
            wo_ld[ds + 1] = sc.dma("g", wov[1 - slot][:],
                                   bass.AP(wo1_in, (ds + 1) * 8 * 128 * 128, [[128, 128], [128 * 128, 8], [1, 128]]),
                                   key=f"d{1 - slot}")
        sc.wait("t", f"d{slot}", wo_ld[ds])
        sc.wait("t", "yd", ld0)
        sc.wait("v", "yd", ld0)
        for th in range(2):
            for ec in range(8):
                sc.mm(pool[th][:, :], wov[slot][:, ec * 128:(ec + 1) * 128],
                      o_sb[:, ec * TQ + th * 512: ec * TQ + (th + 1) * 512],
                      start=(ec == 0), stop=(ec == 7))
        tm = sc.mark("t")
        sc.wait("v", "t", tm)
        for th in range(2):
            sl = slice(th * 512, (th + 1) * 512)
            sc.vop("tensor_add", xs[:, ds * TQ:(ds + 1) * TQ][:, sl], xs[:, ds * TQ:(ds + 1) * TQ][:, sl], pool[th][:, :])
        sc.mark("v")
        sc.wait("t", "v")

    sc.barrier()
    emit_ln(sc, xs[:], xln[:], tmp[:], stats[:], pool, ones_bf, ones1_f32, 16, TQ)
    sc.barrier()

    sc.wait("v", "gd", gld0); sc.wait("t", "gd", gld0); sc.wait("s", "gd", gld0)
    for tt in range(8):
        tsl = slice(tt * 128, (tt + 1) * 128)
        for d in range(16):
            sc.mm(pool[3][:, 0:8], xln[:, d * TQ:(d + 1) * TQ][:, tsl], rw_sb[:, d * 8:(d + 1) * 8], start=(d == 0), stop=(d == 15))
        tm = sc.mark("t")
        sc.wait("v", "t", tm)
        sc.vop("tensor_copy", lg[:], pool[3][:, 0:8])
        sc.vop("reduce_max", m1[:, 0:1], lg[:], axis=AX.X)
        sc.vop("tensor_scalar_mul", m1[:, 1:2], m1[:, 0:1], -1.0)
        sc.vop("tensor_single_scalar", eqm[:], lg[:], m1[:, 0:1], ALU.is_ge)
        sc.vop("scalar_tensor_tensor", eqm[:], eqm[:], -1e30, lg[:], ALU.mult, ALU.add)
        sc.vop("reduce_max", m1[:, 2:3], eqm[:], axis=AX.X)
        sc.vop("tensor_single_scalar", ind[:], lg[:], m1[:, 2:3], ALU.is_ge)
        sc.mark("v")
        sc.wait("s", "v")
        sc.sop("activation", ee[:], lg[:], AF.Exp, bias=m1[:, 1:2])
        sc.mark("s")
        sc.wait("v", "s")
        sc.vop("tensor_mul", ee[:], ee[:], ind[:])
        sc.vop("reduce_sum", m1[:, 3:4], ee[:], axis=AX.X)
        sc.vop("reciprocal", m1[:, 3:4], m1[:, 3:4])
        sc.vop("tensor_scalar_mul", w_tm[:], ee[:], m1[:, 3:4])
        sc.mark("v")
        sc.wait("t", "v")
        sc.trans(pool[5][0:8, 0:128], w_tm[:, 0:8], identf[:, :])
        tm = sc.mark("t")
        sc.wait("v", "t", tm)
        sc.vop("tensor_copy", W_fm[0:8, tsl], pool[5][0:8, 0:128])
        sc.mark("v")
        sc.wait("t", "v")

    sgrow = nc.alloc_sbuf_tensor("sgrow", [1, TQ], F32)
    for th in range(2):
        pp = pool[2] if th == 0 else pool[4]
        for d in range(16):
            sc.mm(pp[0:1, 0:512], sgw_sb[:, d:d + 1],
                  xln[:, d * TQ + th * 512:d * TQ + th * 512 + 512], start=(d == 0), stop=(d == 15))
    tm = sc.mark("t")
    sc.wait("s", "t", tm)
    sc.sop("activation", sgrow[0:1, 0:512], pool[2][0:1, 0:512], AF.Sigmoid)
    sc.sop("activation", sgrow[0:1, 512:1024], pool[4][0:1, 0:512], AF.Sigmoid)
    sc.mark("s")
    sc.wait("t", "s")
    for th in range(2):
        pp = pool[2] if th == 0 else pool[4]
        sc.mm(pp[:, :], ones1_f32[0:1, :], sgrow[0:1, th * 512:(th + 1) * 512], start=True, stop=True)
    tm = sc.mark("t")
    sc.wait("v", "t", tm)
    sc.vop("tensor_copy", sg_b[:, 0:512], pool[2][:, :])
    sc.vop("tensor_copy", sg_b[:, 512:1024], pool[4][:, :])
    sc.mark("v")
    sc.wait("t", "v")

    sw_ld = {}
    sd_ld = {}
    prev_sw_t = {0: 0, 1: 0}
    prev_sd_t = {0: 0, 1: 0}
    sc.wait("g", "t", sc.cnt["t"]); sc.wait("g", "s", sc.cnt["s"]); sc.wait("g", "v", sc.cnt["v"])
    sw_ld[0] = sc.dma("g", swb[0][:], bass.AP(swgu_in, 0, [[256, 128], [128 * 256, 16], [1, 256]]), key="d4")
    for th in range(2):
        tsl = slice(th * 512, (th + 1) * 512)
        for ic in range(44):
            gi = th * 44 + ic
            slot = gi % 2
            if gi + 1 < 88:
                nslot = (gi + 1) % 2
                ngi = gi + 1
                nic = ngi % 44
                sc.wait("g", "t", prev_sw_t[nslot])
                sw_ld[ngi] = sc.dma("g", swb[nslot][:],
                                    bass.AP(swgu_in, nic * 16 * 128 * 256, [[256, 128], [128 * 256, 16], [1, 256]]),
                                    key=f"d{4 + nslot}")
            sc.wait("t", f"d{4 + slot}", sw_ld[gi])
            for d in range(16):
                sc.mm(pool[0][:, :], swb[slot][:, d * 256:d * 256 + 128], xln[:, d * TQ:(d + 1) * TQ][:, tsl],
                      start=(d == 0), stop=(d == 15))
                sc.mm(pool[1][:, :], swb[slot][:, d * 256 + 128:d * 256 + 256], xln[:, d * TQ:(d + 1) * TQ][:, tsl],
                      start=(d == 0), stop=(d == 15))
            tm = sc.mark("t")
            prev_sw_t[slot] = tm
            sc.wait("s", "t", tm); sc.wait("v", "t", tm)
            sc.sop("activation", hg[:, 0:512], pool[0][:, :], AF.Silu)
            sc.vop("tensor_mul", hu[:, 0:512], pool[1][:, :], sg_b[:, tsl])
            sc.mark("s"); sc.mark("v")
            sc.wait("v", "s")
            sc.vop("tensor_mul", hsh[:, ic * 512:(ic + 1) * 512], hg[:, 0:512], hu[:, 0:512])
            sc.mark("v")
            sc.wait("t", "v"); sc.wait("s", "v")
        for fs in range(16):
            gi2 = th * 16 + fs
            sc.wait("g", "t", sc.cnt["t"])
            cnt = sc.dma("g", sdb[0][:], bass.AP(swd_in, fs * 44 * 128 * 128, [[128, 128], [128 * 128, 44], [1, 128]]),
                         key="d6", fence=True)
            sc.wait("t", "d6", cnt)
            for ic in range(44):
                sc.mm(pool[4 + (fs % 2)][:, :], sdb[0][:, ic * 128:(ic + 1) * 128], hsh[:, ic * 512:(ic + 1) * 512],
                      start=(ic == 0), stop=(ic == 43))
            tm = sc.mark("t")
            sc.wait("v", "t", tm)
            dst = xs[:, fs * TQ:(fs + 1) * TQ][:, tsl]
            sc.vop("tensor_add", dst, dst, pool[4 + (fs % 2)][:, :])
            sc.mark("v")
            sc.wait("t", "v")

    wg_ld = {}
    wl_ld = {}
    prev_wg_t = {0: 0, 1: 0}
    prev_wl_t = {0: 0, 1: 0}
    sc.wait("g", "t", sc.cnt["t"])
    wg_ld[(0, 0)] = sc.dma("g", wgb[0][:], bass.AP(wgu_in, 0, [[256, 128], [128 * 256, 16], [1, 256]]), key="d0", fence=True)
    _fenced_d1 = [False]
    for e in range(8):
        sc.wait("t", "v")
        for th in range(2):
            sc.mm(pool[5][:, :], rowsel[0:8, e * 128:(e + 1) * 128], W_fm[0:8, th * 512:(th + 1) * 512], start=True, stop=True)
            tm = sc.mark("t")
            sc.wait("v", "t", tm)
            sc.vop("tensor_copy", wbc[:, th * 512:(th + 1) * 512], pool[5][:, :])
            sc.mark("v")
            sc.wait("t", "v")
        for ic in range(11):
            gi = e * 11 + ic
            slot = gi % 2
            ne, nic = (e, ic + 1) if ic + 1 < 11 else (e + 1, 0)
            if ne < 8:
                nslot = (gi + 1) % 2
                sc.wait("g", "t", prev_wg_t[nslot])
                fence = nslot == 1 and not _fenced_d1[0]
                if fence:
                    _fenced_d1[0] = True
                wg_ld[(ne, nic)] = sc.dma("g", wgb[nslot][:],
                                          bass.AP(wgu_in, (ne * 11 + nic) * 16 * 128 * 256, [[256, 128], [128 * 256, 16], [1, 256]]),
                                          key=f"d{nslot}", fence=fence)
            sc.wait("t", f"d{slot}", wg_ld[(e, ic)])
            for d in range(16):
                for th in range(2):
                    sc.mm(pool[th][:, :], wgb[slot][:, d * 256:d * 256 + 128],
                          xln[:, d * TQ + th * 512: d * TQ + (th + 1) * 512], start=(d == 0), stop=(d == 15))
                    sc.mm(pool[2 + th][:, :], wgb[slot][:, d * 256 + 128:d * 256 + 256],
                          xln[:, d * TQ + th * 512: d * TQ + (th + 1) * 512], start=(d == 0), stop=(d == 15))
            tm = sc.mark("t")
            prev_wg_t[slot] = tm
            sc.wait("s", "t", tm); sc.wait("v", "t", tm)
            for th in range(2):
                sl = slice(th * 512, (th + 1) * 512)
                sc.sop("activation", hg[:, sl], pool[th][:, :], AF.Silu)
                sc.vop("tensor_mul", hu[:, sl], pool[2 + th][:, :], wbc[:, sl])
            sc.mark("s"); sc.mark("v")
            sc.wait("v", "s")
            sc.vop("tensor_mul", he[:, ic * TQ:(ic + 1) * TQ], hg[:], hu[:])
            sc.mark("v")
            sc.wait("t", "v")
        for fs in range(16):
            gi2 = e * 16 + fs
            slot = gi2 % 2
            ne, nfs = (e, fs + 1) if fs + 1 < 16 else (e + 1, 0)
            if ne < 8:
                nslot = (gi2 + 1) % 2
                sc.wait("g", "t", prev_wl_t[nslot])
                wl_ld[(ne, nfs)] = sc.dma("g", wlb[nslot][:],
                                          bass.AP(wd_in, (ne * 16 + nfs) * 11 * 128 * 128, [[128, 128], [128 * 128, 11], [1, 128]]),
                                          key=f"d{2 + nslot}")
            if (e, fs) not in wl_ld:
                wl_ld[(e, fs)] = sc.dma("g", wlb[slot][:],
                                        bass.AP(wd_in, (e * 16 + fs) * 11 * 128 * 128, [[128, 128], [128 * 128, 11], [1, 128]]),
                                        key=f"d{2 + slot}")
            sc.wait("t", f"d{2 + slot}", wl_ld[(e, fs)])
            for th in range(2):
                for ic in range(11):
                    sc.mm(pool[4 + th][:, :], wlb[slot][:, ic * 128:(ic + 1) * 128],
                          he[:, ic * TQ + th * 512: ic * TQ + th * 512 + 512],
                          start=(ic == 0), stop=(ic == 10))
            tm = sc.mark("t")
            prev_wl_t[slot] = tm
            sc.wait("v", "t", tm)
            for th in range(2):
                sl = slice(th * 512, (th + 1) * 512)
                dst = xs[:, fs * TQ:(fs + 1) * TQ][:, sl]
                sc.vop("tensor_add", dst, dst, pool[4 + th][:, :])
            sc.mark("v")
            sc.wait("t", "v")

    vm = sc.mark("v")
    sc.wait("y", "v", vm)
    sc.dma("y", _fm_ap(out_out, 16, TQ), xs[:])
    sc._flush("y")
    nc.sync.wait_ge(sc.sem["yd"], sc.cnt["yd"])
    sc.close()
    return nc


def host_prep_F(of_all, x2_all, Wo, router_w, w_gate, w_up, w_down, sw_gate, sw_up, sw_down, sgw):
    wo1 = np.ascontiguousarray(Wo.reshape(8, 128, 16, 128).transpose(2, 0, 1, 3))
    rw = np.ascontiguousarray(router_w.T.reshape(16, 128, 8)).astype(np.float32)
    rowsel = np.zeros((8, 8, 128), np.float32)
    for e in range(8):
        rowsel[e, e, :] = 1.0
    identf = np.eye(128, dtype=np.float32)
    sgw_a = np.ascontiguousarray(sgw.reshape(16, 128).T).astype(np.float32)
    wgu = np.ascontiguousarray(np.concatenate(
        [w_gate.reshape(8, 16, 128, 11, 128), w_up.reshape(8, 16, 128, 11, 128)],
        axis=4).transpose(0, 3, 1, 2, 4)).reshape(8, 11, 16, 128, 256)
    wd = np.ascontiguousarray(w_down.reshape(8, 11, 128, 16, 128).transpose(0, 3, 1, 2, 4))
    swgu = np.ascontiguousarray(np.concatenate(
        [sw_gate.reshape(16, 128, 44, 128), sw_up.reshape(16, 128, 44, 128)],
        axis=3).transpose(2, 0, 1, 3)).reshape(44, 16, 128, 256)
    swd = np.ascontiguousarray(sw_down.reshape(44, 128, 16, 128).transpose(2, 0, 1, 3))
    of_by_bh = {}
    for core in range(NCORE):
        for i in range(2):
            u = 2 * core + i
            of_by_bh[(u // 8, u % 8)] = of_all[core][i]
    maps = []
    for core in range(NCORE):
        b, quarter = core // 4, core % 4
        sl = slice(quarter * TQ, (quarter + 1) * TQ)
        of_q = np.stack([of_by_bh[(b, h)][:, sl] for h in range(8)])
        maps.append({
            "of_in": np.ascontiguousarray(of_q), "x2": x2_all[core],
            "wo1": wo1, "rw": rw, "rowsel": rowsel, "identf": identf, "sgw": sgw_a,
            "wgu": wgu, "wd": wd, "swgu": swgu, "swd": swd,
        })
    return maps


def kernel(**inputs):
    inp = inputs
    mapsA = host_prep_A(np.asarray(inp["L0_attn_x"]), np.asarray(inp["L0_attn_wq"]),
                        np.asarray(inp["L0_attn_wk"]), np.asarray(inp["L0_attn_wv"]))
    resA = run_bass_kernel_spmd(build_launchA(), mapsA, core_ids=list(range(NCORE))).results
    q_all = np.stack([np.concatenate([np.asarray(resA[b * 4 + q]["q"]) for q in range(4)], axis=2) for b in range(2)])
    k_all = np.stack([np.concatenate([np.asarray(resA[b * 4 + q]["k"]) for q in range(4)], axis=2) for b in range(2)])
    v_all = np.stack([np.concatenate([np.asarray(resA[b * 4 + q]["v"]) for q in range(4)], axis=2) for b in range(2)])
    mapsB = host_prep_B(q_all, k_all, v_all)
    resB = run_bass_kernel_spmd(build_launchB(), mapsB, core_ids=list(range(NCORE))).results
    o_all = np.zeros((2, 16, 128, S), bf16)
    for u in range(32):
        o_all[u // 16, u % 16] = np.asarray(resB[u // 4]["o"])[u % 4]
    mapsC = host_prep_C(o_all, np.asarray(inp["L0_attn_x"]), np.asarray(inp["L0_attn_wo"]),
                        np.asarray(inp["L0_ffn_router_logits"]), np.asarray(inp["L0_ffn_gating_einsum"]),
                        np.asarray(inp["L0_ffn_linear"]), np.asarray(inp["L0_ffn_per_expert_scale"]),
                        np.asarray(inp["L0_ffn_router_scale"]))
    resC = run_bass_kernel_spmd(build_launchC(), mapsC, core_ids=list(range(NCORE))).results
    x2_all = [np.asarray(resC[c]["x2"]) for c in range(NCORE)]
    mapsD = host_prep_D(x2_all, np.asarray(inp["L1_attn_Wq"]), np.asarray(inp["L1_attn_Wk"]),
                        np.asarray(inp["L1_attn_Wv"]), np.asarray(inp["L1_attn_Wg"]),
                        np.asarray(inp["L1_attn_Wb"]), np.asarray(inp["L1_attn_Wa"]),
                        np.asarray(inp["L1_attn_A_log"]), np.asarray(inp["L1_attn_dt_bias"]),
                        np.asarray(inp["L1_attn_conv_q_w"]), np.asarray(inp["L1_attn_conv_k_w"]),
                        np.asarray(inp["L1_attn_conv_v_w"]))
    resD = run_bass_kernel_spmd(build_launchD(), mapsD, core_ids=list(range(NCORE))).results
    mapsE = host_prep_E([np.asarray(resD[c]["qn"]) for c in range(NCORE)],
                        [np.asarray(resD[c]["kn"]) for c in range(NCORE)],
                        [np.asarray(resD[c]["vv"]) for c in range(NCORE)],
                        [np.asarray(resD[c]["gate"]) for c in range(NCORE)],
                        [np.asarray(resD[c]["beta"]) for c in range(NCORE)],
                        [np.asarray(resD[c]["lc"]) for c in range(NCORE)],
                        np.asarray(inp["L1_attn_norm_w"]))
    resE = run_bass_kernel_spmd(build_launchE(), mapsE, core_ids=list(range(NCORE))).results
    mapsF = host_prep_F([np.asarray(resE[c]["of"]) for c in range(NCORE)], x2_all,
                        np.asarray(inp["L1_attn_Wo"]), np.asarray(inp["L1_ffn_router_w"]),
                        np.asarray(inp["L1_ffn_w_gate"]), np.asarray(inp["L1_ffn_w_up"]),
                        np.asarray(inp["L1_ffn_w_down"]), np.asarray(inp["L1_ffn_sw_gate"]),
                        np.asarray(inp["L1_ffn_sw_up"]), np.asarray(inp["L1_ffn_sw_down"]),
                        np.asarray(inp["L1_ffn_shared_gate_w"]))
    resF = run_bass_kernel_spmd(build_launchF(), mapsF, core_ids=list(range(NCORE))).results
    out = np.zeros((B, S, D), np.float32)
    for core in range(NCORE):
        b, q = core // 4, core % 4
        out[b, q * TQ:(q + 1) * TQ] = np.asarray(resF[core]["out"]).reshape(D, TQ).T
    return out


# version 02
# speedup vs baseline: 1.1895x; 1.1895x over previous
import numpy as np
import concourse.bass as bass
import concourse.mybir as mybir
from concourse.bass_utils import run_bass_kernel_spmd
import ml_dtypes

F32 = mybir.dt.float32
BF16 = mybir.dt.bfloat16
AF = mybir.ActivationFunctionType
ALU = mybir.AluOpType
AX = mybir.AxisListType

B, S, D = 2, 4096, 2048
NCORE = 8
TQ = S // 4
CHUNK = 64
H0, HKV0, DH = 16, 4, 128
bf16 = ml_dtypes.bfloat16


class Sched:

    ENGINES = ("t", "v", "s", "g", "y")

    def __init__(self, nc):
        self.nc = nc
        self.sem = {}
        self.cnt = {}
        self.last = {}
        self.unmarked = {}
        self.pending = {k: {} for k in self.ENGINES}
        self._ctx = []

    def open(self):
        nc = self.nc
        for k in self.ENGINES + ("gd", "yd", "d0", "d1", "d2", "d3", "d4", "d5", "d6", "d7"):
            cm = nc.semaphore("sem_" + k)
            self.sem[k] = cm.__enter__()
            self._ctx.append(cm)
            self.cnt[k] = 0
        blk = nc.Block()
        self.block = blk.__enter__()
        self._ctx.append(blk)
        return self

    def close(self):
        for cm in reversed(self._ctx):
            cm.__exit__(None, None, None)

    def _engine(self, k):
        nc = self.nc
        return {"t": nc.tensor, "v": nc.vector, "s": nc.scalar,
                "g": nc.gpsimd, "y": nc.sync}[k]

    def _flush(self, k):
        e = self._engine(k)
        for sk, val in sorted(self.pending[k].items()):
            if val > 0:
                e.wait_ge(self.sem[sk], val)
        self.pending[k] = {}

    def _record(self, k, inst):
        self.last[k] = inst
        self.unmarked[k] = self.unmarked.get(k, 0) + 1

    def mark(self, k):
        if self.unmarked.get(k, 0) > 0:
            self.last[k].then_inc(self.sem[k], 1)
            self.cnt[k] += 1
            self.unmarked[k] = 0
        return self.cnt[k]

    def wait(self, waiter, prod, val=None):
        if val is None:
            val = self.cnt[prod]
        if val <= 0:
            return
        cur = self.pending[waiter].get(prod, 0)
        self.pending[waiter][prod] = max(cur, val)

    def barrier(self, engines=("t", "v", "s")):
        for k in engines:
            self.mark(k)
        allk = list(engines) + ["gd", "yd", "d0", "d1", "d2", "d3", "d4", "d5", "d6", "d7"]
        for k in engines:
            for p in allk:
                if p == k:
                    continue
                self.wait(k, p, self.cnt[p])

    def dma(self, q, out, in_, key=None, fence=False, **kw):
        if fence:
            semk = key or (q + "d")
            self.wait(q, semk, self.cnt[semk])
        self._flush(q)
        inst = self._engine(q).dma_start(out=out, in_=in_, **kw)
        semk = key or (q + "d")
        inst.then_inc(self.sem[semk], 16)
        self.cnt[semk] += 16
        return self.cnt[semk]

    def mm(self, out, lhsT, rhs, start, stop):
        self._flush("t")
        inst = self.nc.tensor.matmul(out, lhsT, rhs, start=start, stop=stop)
        self._record("t", inst)
        return inst

    def trans(self, out, in_, ident):
        self._flush("t")
        inst = self.nc.tensor.transpose(out, in_, ident)
        self._record("t", inst)
        return inst

    def _selfsync(self, k):
        if self.unmarked.get(k, 0) > 0:
            self.mark(k)
        self.pending[k][k] = max(self.pending[k].get(k, 0), self.cnt[k])

    def vop(self, name, *a, **kw):
        self._selfsync("v")
        self._flush("v")
        inst = getattr(self.nc.vector, name)(*a, **kw)
        self._record("v", inst)
        return inst

    def sop(self, name, *a, **kw):
        self._selfsync("s")
        self._flush("s")
        inst = getattr(self.nc.scalar, name)(*a, **kw)
        self._record("s", inst)
        return inst

    def gop(self, name, *a, **kw):
        self._selfsync("g")
        self._flush("g")
        inst = getattr(self.nc.gpsimd, name)(*a, **kw)
        self._record("g", inst)
        return inst

    def init_consts(self):
        self._cf = self.nc.alloc_sbuf_tensor("constsF", [128, 16], F32)
        self._cb = self.nc.alloc_sbuf_tensor("constsB", [128, 16], BF16)
        self._cmap = {}
        self._cn = {F32: 0, BF16: 0}

    def const(self, value, dtype=F32, parts=128):
        key = (dtype, float(value))
        if key not in self._cmap:
            t = self._cf if dtype == F32 else self._cb
            i = self._cn[dtype]
            self._cn[dtype] += 1
            self.vop("memset", t[:, i:i + 1], float(value))
            self._cmap[key] = t[:, i:i + 1]
        return self._cmap[key][0:parts, :]


def emit_ln(sc, xs, xln, tmp, stats, pool, ones_bf, ones1_f32, nchunks, T, eps=1e-5):
    assert T == 1024
    for c in range(nchunks):
        xc = xs[:, c * T:(c + 1) * T]
        sc.sop("activation", tmp[:, T:2 * T], xc, AF.Square)
        sc.vop("tensor_copy", tmp[:, 0:T], xc)
        sc.mark("s"); sc.mark("v")
        sc.wait("t", "s"); sc.wait("t", "v")
        for h in range(2):
            sl = slice(h * 512, (h + 1) * 512)
            sc.mm(pool[h][0:1, :], ones_bf[:, 0:1], tmp[:, 0:T][:, sl], start=(c == 0), stop=(c == nchunks - 1))
            sc.mm(pool[2 + h][0:1, :], ones_bf[:, 0:1], tmp[:, T:2 * T][:, sl], start=(c == 0), stop=(c == nchunks - 1))
        sc.mark("t")
        sc.wait("s", "t"); sc.wait("v", "t")
    sc.mark("t")
    sc.wait("v", "t")
    sumx = stats[0:1, 0:T]
    var_t = stats[0:1, T:2 * T]
    rstd = stats[0:1, 2 * T:3 * T]
    mr = stats[0:1, 3 * T:4 * T]
    for h in range(2):
        sl = slice(h * 512, (h + 1) * 512)
        sc.vop("tensor_copy", sumx[:, sl], pool[h][0:1, :])
        sc.vop("tensor_copy", var_t[:, sl], pool[2 + h][0:1, :])
    sc.vop("scalar_tensor_tensor", mr, sumx, 1.0, sumx, ALU.mult, ALU.mult)
    sc.vop("scalar_tensor_tensor", var_t, mr, -1.0 / D, var_t, ALU.mult, ALU.add)
    sc.mark("v")
    sc.wait("s", "v")
    sc.sop("activation", rstd, var_t, AF.Ln, scale=1.0 / D, bias=sc.const(eps, parts=1))
    sc.sop("activation", rstd, rstd, AF.Exp, scale=-0.5)
    sc.mark("s")
    sc.wait("v", "s")
    sc.vop("scalar_tensor_tensor", mr, sumx, 1.0 / D, rstd, ALU.mult, ALU.mult)
    sc.mark("v")
    sc.wait("t", "v")
    for h in range(2):
        sc.mm(pool[4 + h][:, :], ones1_f32[0:1, :], rstd[:, h * 512:(h + 1) * 512], start=True, stop=True)
        sc.mm(pool[6 + h][:, :], ones1_f32[0:1, :], mr[:, h * 512:(h + 1) * 512], start=True, stop=True)
    sc.mark("t")
    sc.wait("v", "t")
    ftmp = stats[0:1, 0:1]
    for c in range(nchunks):
        for h in range(2):
            sl = slice(h * 512, (h + 1) * 512)
            xc = xs[:, c * T:(c + 1) * T][:, sl]
            oc = xln[:, c * T:(c + 1) * T][:, sl]
            sc.vop("tensor_mul", tmp[:, 0:512], xc, pool[4 + h][:, :])
            sc.vop("tensor_sub", oc, tmp[:, 0:512], pool[6 + h][:, :])
    sc.mark("v")


def _host_fm(a2d, nch):
    T, F = a2d.shape
    assert F == nch * 128
    return np.ascontiguousarray(a2d.T.reshape(nch, 128, T))


def _fm_ap(h, nch, T):
    return bass.AP(h, 0, [[T, 128], [128 * T, nch], [1, T]])


def build_launchA():
    nc = bass.Bass("TRN2", target_bir_lowering=False)
    x_in = nc.dram_tensor("x", [16, 128, TQ], F32, kind="ExternalInput")
    w_in = nc.dram_tensor("wqkv", [8, 16, 128, 384], F32, kind="ExternalInput").ap()
    cos_in = nc.dram_tensor("cos", [128, TQ], F32, kind="ExternalInput").ap()
    sin_in = nc.dram_tensor("sin", [128, TQ], F32, kind="ExternalInput").ap()
    rotp_in = nc.dram_tensor("rotp", [128, 128], F32, kind="ExternalInput").ap()
    q_out = nc.dram_tensor("q", [16, 128, TQ], BF16, kind="ExternalOutput").ap()
    k_out = nc.dram_tensor("k", [4, 128, TQ], BF16, kind="ExternalOutput").ap()
    v_out = nc.dram_tensor("v", [4, 128, TQ], BF16, kind="ExternalOutput").ap()

    xs = nc.alloc_sbuf_tensor("xs", [128, 16 * TQ], F32)
    xln = nc.alloc_sbuf_tensor("xln", [128, 16 * TQ], BF16)
    tmp = nc.alloc_sbuf_tensor("tmp", [128, 2 * TQ], BF16)
    stats = nc.alloc_sbuf_tensor("stats", [1, 4 * TQ], F32)
    cos_t = nc.alloc_sbuf_tensor("cos_t", [128, TQ], BF16)
    sin_t = nc.alloc_sbuf_tensor("sin_t", [128, TQ], BF16)
    rotp = nc.alloc_sbuf_tensor("rotp_sb", [128, 128], BF16)
    ones_bf = nc.alloc_sbuf_tensor("ones_bf", [128, 1], BF16)
    ones1_f32 = nc.alloc_sbuf_tensor("ones1_f32", [1, 128], F32)
    wbuf = [nc.alloc_sbuf_tensor(f"wbuf{i}", [128, 384], BF16) for i in range(2)]
    out_sb = nc.alloc_sbuf_tensor("out_sb", [128, 3 * TQ], BF16)

    pool = [nc.alloc_psum_tensor(f"pool{i}", [128, 512], F32) for i in range(8)]

    sc = Sched(nc).open()
    sc.init_consts()
    sc.const(1e-5, parts=1)

    sc.dma("y", xs[:], _fm_ap(x_in, 16, TQ))
    sc.dma("g", cos_t[:], cos_in)
    sc.dma("g", sin_t[:], sin_in)
    sc.dma("g", rotp[:], rotp_in)
    sc.vop("memset", ones_bf[:], 1.0)
    sc.vop("memset", ones1_f32[:], 1.0)
    sc.mark("v")
    sc.wait("t", "v")
    sc.wait("s", "v")
    sc.wait("s", "yd", 16)
    sc.wait("v", "yd", 16)

    emit_ln(sc, xs[:], xln[:], tmp[:], stats[:], pool, ones_bf, ones1_f32, 16, TQ)

    sc.barrier()
    sc.wait("t", "gd", sc.cnt["gd"])
    for grp in range(8):
        for d in range(16):
            buf = wbuf[d % 2]
            c = sc.dma("g", buf[:], w_in[grp, d], key=f"d{d % 2}")
            sc.wait("t", f"d{d % 2}", c)
            for oc3 in range(3):
                for th in range(2):
                    sc.mm(pool[oc3 * 2 + th][:, :],
                          buf[:, oc3 * 128:(oc3 + 1) * 128],
                          xln[:, d * TQ + th * 512: d * TQ + (th + 1) * 512],
                          start=(d == 0), stop=(d == 15))
            if d >= 1:
                sc.mark("t")
                sc.wait("g", "t")
        tmark = sc.mark("t")
        sc.wait("v", "t", tmark)
        sc.wait("s", "t", tmark)
        for oc3 in range(3):
            oc = grp * 3 + oc3
            dst = out_sb[:, oc3 * TQ:(oc3 + 1) * TQ]
            p0 = pool[oc3 * 2][:, :]
            p1 = pool[oc3 * 2 + 1][:, :]
            if oc < 20:
                sc.vop("tensor_copy", tmp[:, 0:512], p0)
                sc.vop("tensor_copy", tmp[:, 512:1024], p1)
                sc.mark("v")
                sc.wait("t", "v")
                for th in range(2):
                    sc.mm(pool[6 + th][:, :], rotp[:, :], tmp[:, th * 512:(th + 1) * 512], start=True, stop=True)
                sc.mark("t")
                sc.wait("v", "t")
                sc.wait("s", "t")
                sc.vop("tensor_mul", dst, tmp[:, 0:TQ], cos_t[:, :])
                for th in range(2):
                    sc.sop("activation", tmp[:, TQ + th * 512:TQ + (th + 1) * 512], pool[6 + th][:, :], AF.Copy)
                sc.mark("s")
                sc.wait("v", "s")
                sc.vop("tensor_mul", tmp[:, TQ:2 * TQ], tmp[:, TQ:2 * TQ], sin_t[:, :])
                sc.vop("tensor_add", dst, dst, tmp[:, TQ:2 * TQ])
                sc.mark("v")
                sc.wait("t", "v")
            else:
                sc.vop("tensor_copy", dst[:, 0:512], p0)
                sc.vop("tensor_copy", dst[:, 512:1024], p1)
        vmark = sc.mark("v")
        sc.wait("t", "v", vmark)
        sc.wait("y", "v", vmark)
        for oc3 in range(3):
            oc = grp * 3 + oc3
            src = out_sb[:, oc3 * TQ:(oc3 + 1) * TQ]
            if oc < 16:
                sc.dma("y", q_out[oc], src)
            elif oc < 20:
                sc.dma("y", k_out[oc - 16], src)
            else:
                sc.dma("y", v_out[oc - 20], src)
        sc.wait("v", "yd", sc.cnt["yd"])

    sc._flush("y")
    nc.sync.wait_ge(sc.sem["yd"], sc.cnt["yd"])
    sc.close()
    return nc


def host_prep_A(x_full, wq, wk, wv):
    wcat = np.concatenate([wq.T, wk.T, wv.T], axis=1)
    w_arr = np.ascontiguousarray(
        wcat.reshape(16, 128, 8, 3, 128).transpose(2, 0, 1, 3, 4)).reshape(8, 16, 128, 384)
    d = np.arange(64, dtype=np.float64)
    inv = 1.0 / (10000.0 ** (2 * d / 128.0))
    pos = np.arange(S, dtype=np.float64)
    fr = pos[:, None] * inv[None, :]
    cos = np.cos(fr).astype(np.float32)
    sin = np.sin(fr).astype(np.float32)
    cos_full = np.ascontiguousarray(np.concatenate([cos, cos], axis=1).T)
    sin_full = np.ascontiguousarray(np.concatenate([sin, sin], axis=1).T)
    rotp = np.zeros((128, 128), np.float32)
    for r in range(64):
        rotp[r + 64, r] = -1.0
    for r in range(64, 128):
        rotp[r - 64, r] = 1.0
    maps = []
    for core in range(NCORE):
        b, quarter = core // 4, core % 4
        sl = slice(quarter * TQ, (quarter + 1) * TQ)
        maps.append({
            "x": _host_fm(x_full[b, sl], 16).astype(np.float32),
            "wqkv": w_arr,
            "cos": np.ascontiguousarray(cos_full[:, sl]),
            "sin": np.ascontiguousarray(sin_full[:, sl]),
            "rotp": rotp,
        })
    return maps


def build_launchB():
    nc = bass.Bass("TRN2", target_bir_lowering=False)
    NU = 4
    qf_in = nc.dram_tensor("qf", [NU, 128, S], BF16, kind="ExternalInput")
    kf_in = nc.dram_tensor("kf", [NU, 128, S], BF16, kind="ExternalInput")
    ktm_in = nc.dram_tensor("ktm", [NU, 64, 128, 128], BF16, kind="ExternalInput")
    vtm_in = nc.dram_tensor("vtm", [NU, 64, 128, 128], BF16, kind="ExternalInput")
    diagT_in = nc.dram_tensor("diagT", [NU, 64, 64], F32, kind="ExternalInput")
    qdecT_in = nc.dram_tensor("qdecT", [NU, 128, 64], BF16, kind="ExternalInput")
    kdec_in = nc.dram_tensor("kdec", [NU, 128, 1], F32, kind="ExternalInput")
    blkc_in = nc.dram_tensor("blkc", [NU, 128, 1], F32, kind="ExternalInput")
    o_out = nc.dram_tensor("o", [NU, 128, S], BF16, kind="ExternalOutput")

    NC64 = S // 64
    qf = [nc.alloc_sbuf_tensor(f"qf{i}", [128, S], BF16) for i in range(2)]
    kf = [nc.alloc_sbuf_tensor(f"kf{i}", [128, S], BF16) for i in range(2)]
    ktm = [nc.alloc_sbuf_tensor(f"ktm{i}", [128, 64 * 128], BF16) for i in range(2)]
    vtm = [nc.alloc_sbuf_tensor(f"vtm{i}", [128, 64 * 128], BF16) for i in range(2)]
    diagT = [nc.alloc_sbuf_tensor(f"diagT{i}", [64, 64], F32) for i in range(2)]
    qdecT = [nc.alloc_sbuf_tensor(f"qdecT{i}", [128, 64], BF16) for i in range(2)]
    kdec = [nc.alloc_sbuf_tensor(f"kdec{i}", [128, 1], F32) for i in range(2)]
    blkc = [nc.alloc_sbuf_tensor(f"blkc{i}", [128, 1], F32) for i in range(2)]
    o_sb = [nc.alloc_sbuf_tensor(f"osb{i}", [128, S], BF16) for i in range(2)]

    kv = nc.alloc_sbuf_tensor("kv", [128, 128], F32)
    kv_bf = nc.alloc_sbuf_tensor("kv_bf", [128, 128], BF16)
    scT = nc.alloc_sbuf_tensor("scT", [128, 64], BF16)
    qt = nc.alloc_sbuf_tensor("qt", [128, 64], BF16)
    vt = nc.alloc_sbuf_tensor("vt", [128, 128], BF16)

    pool = [nc.alloc_psum_tensor(f"pool{i}", [128, 512], F32) for i in range(4)]

    sc = Sched(nc).open()
    sc.init_consts()
    sc.vop("memset", scT[:], 0.0)
    sc.mark("v")

    def load_unit(u, slot):
        ks = f"d{slot}"
        sc.dma("y", qf[slot][:], qf_in.ap()[u], key=ks)
        sc.dma("y", kf[slot][:], kf_in.ap()[u], key=ks)
        sc.dma("y", ktm[slot][:], bass.AP(ktm_in, u * 64 * 128 * 128, [[128, 128], [128 * 128, 64], [1, 128]]), key=ks)
        sc.dma("y", vtm[slot][:], bass.AP(vtm_in, u * 64 * 128 * 128, [[128, 128], [128 * 128, 64], [1, 128]]), key=ks)
        sc.dma("g", diagT[slot][:], diagT_in.ap()[u], key=f"d{2 + slot}")
        sc.dma("g", qdecT[slot][:], qdecT_in.ap()[u], key=f"d{2 + slot}")
        sc.dma("g", kdec[slot][:], kdec_in.ap()[u], key=f"d{2 + slot}")
        sc.dma("g", blkc[slot][:], blkc_in.ap()[u], key=f"d{2 + slot}")
        return sc.cnt[ks], sc.cnt[f"d{2 + slot}"]

    loads = {}
    done_v = {}
    loads[0] = load_unit(0, 0)
    for u in range(NU):
        slot = u % 2
        if u + 1 < NU:
            if u >= 1:
                sc.wait("g", "v", done_v[u - 1])
            loads[u + 1] = load_unit(u + 1, 1 - slot)
        ydc, gdc = loads[u]
        ks, kg = f"d{slot}", f"d{2 + slot}"
        sc.wait("t", ks, ydc); sc.wait("t", kg, gdc)
        sc.wait("v", ks, ydc); sc.wait("v", kg, gdc)
        sc.vop("memset", kv[:], 0.0)
        sc.vop("memset", kv_bf[:], 0.0)
        sc.mark("v")
        sc.wait("t", "v")
        for c in range(NC64):
            t0, t1 = c * 64, (c + 1) * 64
            sc.mm(pool[0][0:64, 0:64], kf[slot][:, t0:t1], qf[slot][:, t0:t1], start=True, stop=True)
            sc.mark("t")
            sc.wait("v", "t")
            sc.vop("tensor_mul", scT[0:64, :], pool[0][0:64, 0:64], diagT[slot][:, :])
            sc.vop("tensor_mul", qt[:], qf[slot][:, t0:t1], qdecT[slot][:, :])
            sc.vop("tensor_scalar_mul", vt[:], vtm[slot][:, c * 128:(c + 1) * 128], kdec[slot][:, 0:1])
            sc.mark("v")
            sc.wait("t", "v")
            sc.mm(pool[1][:, 0:64], vtm[slot][:, c * 128:(c + 1) * 128], scT[:, :], start=True, stop=False)
            sc.mm(pool[1][:, 0:64], kv_bf[:], qt[:], start=False, stop=True)
            sc.mm(pool[2][:, 0:128], ktm[slot][:, c * 128:(c + 1) * 128], vt[:], start=True, stop=True)
            sc.mark("t")
            sc.wait("v", "t")
            sc.vop("scalar_tensor_tensor", kv[:], kv[:], blkc[slot][:, 0:1], pool[2][:, 0:128], ALU.mult, ALU.add)
            sc.vop("tensor_copy", o_sb[slot][:, t0:t1], pool[1][:, 0:64])
            sc.mark("v")
            sc.wait("s", "v")
            sc.sop("activation", kv_bf[:], kv[:], AF.Copy)
            sc.mark("s")
            sc.wait("t", "s")
            sc.wait("t", "v")
        vm = sc.mark("v")
        done_v[u] = vm
        sc.wait("y", "v", vm)
        sc.dma("y", o_out.ap()[u], o_sb[slot][:])
        sc.wait("y", "yd", sc.cnt["yd"])

    sc._flush("y")
    nc.sync.wait_ge(sc.sem["yd"], sc.cnt["yd"])
    sc.close()
    return nc


def host_prep_B(q_all, k_all, v_all):
    lam_all = np.exp2(-8.0 * np.arange(1, 17) / 16.0).astype(np.float64)
    r = np.arange(64, dtype=np.float64)
    maps = []
    for core in range(NCORE):
        units = [4 * core + i for i in range(4)]
        qf = np.stack([q_all[u // 16, u % 16] for u in units])
        kf = np.stack([k_all[u // 16, (u % 16) // 4] for u in units])
        ktm = np.zeros((4, 64, 128, 128), bf16)
        vtm = np.zeros((4, 64, 128, 128), bf16)
        diagT = np.zeros((4, 64, 64), np.float32)
        qdecT = np.zeros((4, 128, 64), np.float32)
        kdec = np.zeros((4, 128, 1), np.float32)
        blkc = np.zeros((4, 128, 1), np.float32)
        for i, u in enumerate(units):
            b, h = u // 16, u % 16
            lam = lam_all[h]
            kfm = k_all[b, h // 4]
            vfm = v_all[b, h // 4]
            ktm[i, :, 0:64, :] = kfm.T.reshape(64, 64, 128)
            vtm[i, :, 0:64, :] = vfm.T.reshape(64, 64, 128)
            diff = r[:, None] - r[None, :]
            diagT[i] = np.where(diff >= 0, np.exp(-lam * diff), 0.0).T.astype(np.float32)
            qdecT[i] = np.broadcast_to(np.exp(-lam * (r + 1.0)).astype(np.float32), (128, 64))
            kdec[i, 0:64, 0] = np.exp(-lam * (63.0 - r)).astype(np.float32)
            blkc[i, :, 0] = np.exp(-lam * 64.0)
        maps.append({
            "qf": qf, "kf": kf, "ktm": ktm, "vtm": vtm,
            "diagT": diagT, "qdecT": qdecT.astype(bf16), "kdec": kdec, "blkc": blkc,
        })
    return maps


def build_launchC():
    nc = bass.Bass("TRN2", target_bir_lowering=False)
    o_in = nc.dram_tensor("o_in", [16, 128, TQ], BF16, kind="ExternalInput")
    x_in = nc.dram_tensor("x", [16, 128, TQ], F32, kind="ExternalInput")
    woT_in = nc.dram_tensor("woT", [16, 16, 128, 128], F32, kind="ExternalInput")
    rl_in = nc.dram_tensor("rl", [16, 128, 8], F32, kind="ExternalInput")
    rsc_in = nc.dram_tensor("rscale", [128, 16], F32, kind="ExternalInput")
    pes_in = nc.dram_tensor("pes", [8, 1], F32, kind="ExternalInput")
    rowsel_in = nc.dram_tensor("rowsel", [8, 8, 128], F32, kind="ExternalInput")
    ident_in = nc.dram_tensor("identf", [128, 128], F32, kind="ExternalInput")
    wg_in = nc.dram_tensor("wgate", [8, 16, 16, 128, 256], F32, kind="ExternalInput")
    wl_in = nc.dram_tensor("wlin", [8, 16, 16, 128, 128], F32, kind="ExternalInput")
    x2_out = nc.dram_tensor("x2", [16, 128, TQ], F32, kind="ExternalOutput")
    dbgw_out = nc.dram_tensor("dbgw", [8, TQ], F32, kind="ExternalOutput")
    dbgx1_out = nc.dram_tensor("dbgx1", [16, 128, TQ], F32, kind="ExternalOutput")
    dbglg_out = nc.dram_tensor("dbglg", [8, 128, 8], F32, kind="ExternalOutput")

    xs = nc.alloc_sbuf_tensor("xs", [128, 16 * TQ], F32)
    o_sb = nc.alloc_sbuf_tensor("o_sb", [128, 16 * TQ], BF16)
    xln = nc.alloc_sbuf_tensor("xln", [128, 16 * TQ], BF16)
    tmp = nc.alloc_sbuf_tensor("tmp", [128, 2 * TQ], BF16)
    stats = nc.alloc_sbuf_tensor("stats", [1, 4 * TQ], F32)
    ones_bf = nc.alloc_sbuf_tensor("ones_bf", [128, 1], BF16)
    ones1_f32 = nc.alloc_sbuf_tensor("ones1_f32", [1, 128], F32)
    identf = nc.alloc_sbuf_tensor("identf_sb", [128, 128], F32)
    rlp = nc.alloc_sbuf_tensor("rlp", [128, 16 * 8], BF16)
    rl_sb = nc.alloc_sbuf_tensor("rl_sb", [128, 16 * 8], F32)
    rsc_sb = nc.alloc_sbuf_tensor("rsc_sb", [128, 16], F32)
    pes_sb = nc.alloc_sbuf_tensor("pes_sb", [8, 1], F32)
    rowsel = nc.alloc_sbuf_tensor("rowsel_sb", [8, 8 * 128], BF16)
    lg = nc.alloc_sbuf_tensor("lg", [128, 8], F32)
    m1 = nc.alloc_sbuf_tensor("m1", [128, 4], F32)
    eqm = nc.alloc_sbuf_tensor("eqm", [128, 8], F32)
    ind = nc.alloc_sbuf_tensor("ind", [128, 8], F32)
    ee = nc.alloc_sbuf_tensor("ee", [128, 8], F32)
    w_tm = nc.alloc_sbuf_tensor("w_tm", [128, 8], F32)
    rs_sb = nc.alloc_sbuf_tensor("rs_sb", [128, 1], F32)
    W_fm = nc.alloc_sbuf_tensor("W_fm", [8, TQ], BF16)
    W_dbg = nc.alloc_sbuf_tensor("W_dbg", [8, TQ], F32)
    wbc = nc.alloc_sbuf_tensor("wbc", [128, TQ], BF16)
    hg = nc.alloc_sbuf_tensor("hg", [128, TQ], BF16)
    hu = nc.alloc_sbuf_tensor("hu", [128, TQ], BF16)
    wov = [nc.alloc_sbuf_tensor(f"wov{i}", [128, 16 * 128], BF16) for i in range(2)]
    wgb = [nc.alloc_sbuf_tensor(f"wgb{i}", [128, 16 * 256], BF16) for i in range(2)]
    wlb = [nc.alloc_sbuf_tensor(f"wlb{i}", [128, 16 * 128], BF16) for i in range(2)]

    pool = [nc.alloc_psum_tensor(f"pool{i}", [128, 512], F32) for i in range(8)]

    sc = Sched(nc).open()
    sc.init_consts()
    sc.const(1e-5, parts=1)
    sc.const(1e-6, parts=1)
    sc.const(-0.5 * float(np.log(2048.0)))
    sc.vop("memset", ones_bf[:], 1.0)
    sc.vop("memset", ones1_f32[:], 1.0)
    sc.mark("v")

    sc.dma("y", xs[:], _fm_ap(x_in, 16, TQ))
    sc.dma("y", o_sb[:], _fm_ap(o_in, 16, TQ))
    ld0 = sc.cnt["yd"]
    sc.dma("g", rl_sb[:], bass.AP(rl_in, 0, [[8, 128], [128 * 8, 16], [1, 8]]))
    sc.dma("g", rsc_sb[:], rsc_in.ap())
    sc.dma("g", pes_sb[:], pes_in.ap())
    sc.dma("g", rowsel[:], bass.AP(rowsel_in, 0, [[128, 8], [8 * 128, 8], [1, 128]]))
    sc.dma("g", identf[:], ident_in.ap())
    gld0 = sc.cnt["gd"]

    wo_ld = {}
    wo_ld[0] = sc.dma("g", wov[0][:], bass.AP(woT_in, 0, [[128, 128], [128 * 128, 16], [1, 128]]), key="d0")
    for ds in range(16):
        slot = ds % 2
        if ds + 1 < 16:
            if ds >= 1:
                sc.wait("g", "t", sc.cnt["t"])
            wo_ld[ds + 1] = sc.dma("g", wov[1 - slot][:],
                                   bass.AP(woT_in, (ds + 1) * 16 * 128 * 128, [[128, 128], [128 * 128, 16], [1, 128]]),
                                   key=f"d{1 - slot}")
        sc.wait("t", f"d{slot}", wo_ld[ds])
        sc.wait("t", "yd", ld0)
        sc.wait("v", "yd", ld0)
        for th in range(2):
            for ec in range(16):
                sc.mm(pool[th][:, :], wov[slot][:, ec * 128:(ec + 1) * 128],
                      o_sb[:, ec * TQ + th * 512: ec * TQ + (th + 1) * 512],
                      start=(ec == 0), stop=(ec == 15))
            tm = sc.mark("t")
            sc.wait("v", "t", tm)
            sl = slice(th * 512, (th + 1) * 512)
            sc.vop("tensor_add", xs[:, ds * TQ:(ds + 1) * TQ][:, sl], xs[:, ds * TQ:(ds + 1) * TQ][:, sl], pool[th][:, :])
            sc.mark("v")
        sc.wait("t", "v")

    sc.barrier()
    sc.wait("y", "v", sc.cnt["v"])
    sc.dma("y", _fm_ap(dbgx1_out, 16, TQ), xs[:])
    sc.wait("v", "yd", sc.cnt["yd"])
    emit_ln(sc, xs[:], xln[:], tmp[:], stats[:], pool, ones_bf, ones1_f32, 16, TQ)
    sc.barrier()

    sc.wait("v", "gd", gld0)
    sc.wait("t", "gd", gld0)
    sc.wait("s", "gd", gld0)
    for d in range(16):
        sc.vop("tensor_scalar_mul", rlp[:, d * 8:(d + 1) * 8], rl_sb[:, d * 8:(d + 1) * 8], rsc_sb[:, d:d + 1])
    sc.mark("v")
    sc.wait("t", "v"); sc.wait("s", "v")
    for tt in range(8):
        tsl = slice(tt * 128, (tt + 1) * 128)
        for d in range(16):
            sc.sop("activation", tmp[:, d * 128:(d + 1) * 128], xln[:, d * TQ:(d + 1) * TQ][:, tsl], AF.Square)
        sm = sc.mark("s")
        sc.wait("t", "s", sm)
        for d in range(16):
            sc.mm(pool[2][:, 0:1], tmp[:, d * 128:(d + 1) * 128], ones_bf[:, 0:1], start=(d == 0), stop=(d == 15))
            sc.mm(pool[3][:, 0:8], xln[:, d * TQ:(d + 1) * TQ][:, tsl], rlp[:, d * 8:(d + 1) * 8], start=(d == 0), stop=(d == 15))
        tm = sc.mark("t")
        sc.wait("s", "t", tm); sc.wait("v", "t", tm)
        sc.sop("activation", rs_sb[:], pool[2][:, 0:1], AF.Ln, scale=1.0 / D, bias=sc.const(1e-6))
        sc.sop("activation", rs_sb[:], rs_sb[:], AF.Exp, scale=-0.5,
               bias=sc.const(-0.5 * float(np.log(2048.0))))
        sc.mark("s")
        sc.wait("v", "s")
        sc.vop("tensor_scalar_mul", lg[:], pool[3][:, 0:8], rs_sb[:, 0:1])
        sc.mark("v"); sc.wait("y", "v", sc.cnt["v"])
        sc.dma("y", dbglg_out.ap()[tt], lg[:])
        sc.wait("v", "yd", sc.cnt["yd"])
        sc.vop("reduce_max", m1[:, 0:1], lg[:], axis=AX.X)
        sc.vop("tensor_scalar_mul", m1[:, 1:2], m1[:, 0:1], -1.0)
        sc.vop("tensor_single_scalar", eqm[:], lg[:], m1[:, 0:1], ALU.is_ge)
        sc.vop("scalar_tensor_tensor", eqm[:], eqm[:], -1e30, lg[:], ALU.mult, ALU.add)
        sc.vop("reduce_max", m1[:, 2:3], eqm[:], axis=AX.X)
        sc.vop("tensor_single_scalar", ind[:], lg[:], m1[:, 2:3], ALU.is_ge)
        sc.mark("v")
        sc.wait("s", "v")
        sc.sop("activation", ee[:], lg[:], AF.Exp, bias=m1[:, 1:2])
        sc.mark("s")
        sc.wait("v", "s")
        sc.vop("tensor_mul", ee[:], ee[:], ind[:])
        sc.vop("reduce_sum", m1[:, 3:4], ee[:], axis=AX.X)
        sc.vop("reciprocal", m1[:, 3:4], m1[:, 3:4])
        sc.vop("tensor_scalar_mul", w_tm[:], ee[:], m1[:, 3:4])
        sc.mark("v")
        sc.wait("t", "v")
        sc.trans(pool[5][0:8, 0:128], w_tm[:, 0:8], identf[:, :])
        tm = sc.mark("t")
        sc.wait("v", "t", tm)
        sc.vop("tensor_scalar_mul", W_fm[0:8, tsl], pool[5][0:8, 0:128], pes_sb[:, 0:1])
        sc.vop("tensor_scalar_mul", W_dbg[0:8, tsl], pool[5][0:8, 0:128], pes_sb[:, 0:1])
        sc.mark("v")
        sc.wait("t", "v")

    vm = sc.mark("v")
    sc.wait("y", "v", vm)
    sc.dma("y", dbgw_out.ap(), W_dbg[0:8, :])

    wg_ld = {}
    wl_ld = {}
    wg_ld[(0, 0)] = sc.dma("g", wgb[0][:], bass.AP(wg_in, 0, [[256, 128], [128 * 256, 16], [1, 256]]), key="d4")
    prev_t_for_slot = {0: 0, 1: 0}
    prev_wl_t = {0: 0, 1: 0}
    for e in range(8):
        sc.wait("t", "v")
        for th in range(2):
            sc.mm(pool[5][:, :], rowsel[0:8, e * 128:(e + 1) * 128], W_fm[0:8, th * 512:(th + 1) * 512], start=True, stop=True)
            tm = sc.mark("t")
            sc.wait("v", "t", tm)
            sc.vop("tensor_copy", wbc[:, th * 512:(th + 1) * 512], pool[5][:, :])
            sc.mark("v")
            sc.wait("t", "v")
        for hc in range(16):
            slot = (e * 16 + hc) % 2
            nslot = 1 - slot
            ne, nhc = (e, hc + 1) if hc + 1 < 16 else (e + 1, 0)
            if ne < 8:
                sc.wait("g", "t", prev_t_for_slot[nslot])
                wg_ld[(ne, nhc)] = sc.dma("g", wgb[nslot][:],
                                          bass.AP(wg_in, (ne * 16 + nhc) * 16 * 128 * 256, [[256, 128], [128 * 256, 16], [1, 256]]),
                                          key=f"d{4 + nslot}")
            sc.wait("t", f"d{4 + slot}", wg_ld[(e, hc)])
            tms = {}
            for th in range(2):
                for d in range(16):
                    sc.mm(pool[th][:, :], wgb[slot][:, d * 256:d * 256 + 128],
                          xln[:, d * TQ + th * 512: d * TQ + (th + 1) * 512], start=(d == 0), stop=(d == 15))
                    sc.mm(pool[2 + th][:, :], wgb[slot][:, d * 256 + 128:d * 256 + 256],
                          xln[:, d * TQ + th * 512: d * TQ + (th + 1) * 512], start=(d == 0), stop=(d == 15))
                tms[th] = sc.mark("t")
                sc.wait("s", "t", tms[th]); sc.wait("v", "t", tms[th])
                sl = slice(th * 512, (th + 1) * 512)
                sc.sop("activation", hg[:, sl], pool[th][:, :], AF.Gelu_apprx_tanh)
                sc.vop("tensor_mul", hu[:, sl], pool[2 + th][:, :], wbc[:, sl])
                sc.mark("s"); sc.mark("v")
                sc.wait("v", "s")
                sc.vop("tensor_mul", o_sb[:, hc * TQ + th * 512:hc * TQ + th * 512 + 512], hg[:, sl], hu[:, sl])
                sc.mark("v")
            prev_t_for_slot[slot] = tms[1]
            sc.wait("t", "v")
        for fs in range(16):
            slot = (e * 16 + fs) % 2
            nslot = 1 - slot
            ne, nfs = (e, fs + 1) if fs + 1 < 16 else (e + 1, 0)
            if ne < 8:
                sc.wait("g", "t", prev_wl_t[nslot])
                wl_ld[(ne, nfs)] = sc.dma("g", wlb[nslot][:],
                                          bass.AP(wl_in, (ne * 16 + nfs) * 16 * 128 * 128, [[128, 128], [128 * 128, 16], [1, 128]]),
                                          key=f"d{6 + nslot}")
            if (e, fs) not in wl_ld:
                wl_ld[(e, fs)] = sc.dma("g", wlb[slot][:],
                                        bass.AP(wl_in, (e * 16 + fs) * 16 * 128 * 128, [[128, 128], [128 * 128, 16], [1, 128]]),
                                        key=f"d{6 + slot}")
            sc.wait("t", f"d{6 + slot}", wl_ld[(e, fs)])
            tms = {}
            for th in range(2):
                for hc in range(16):
                    sc.mm(pool[4 + th][:, :], wlb[slot][:, hc * 128:(hc + 1) * 128],
                          o_sb[:, hc * TQ + th * 512: hc * TQ + th * 512 + 512],
                          start=(hc == 0), stop=(hc == 15))
                tms[th] = sc.mark("t")
                sc.wait("v", "t", tms[th])
                sl = slice(th * 512, (th + 1) * 512)
                dst = xs[:, fs * TQ:(fs + 1) * TQ][:, sl]
                sc.vop("tensor_add", dst, dst, pool[4 + th][:, :])
                sc.mark("v")
            prev_wl_t[slot] = tms[1]
            sc.wait("t", "v")

    vm = sc.mark("v")
    sc.wait("y", "v", vm)
    sc.dma("y", _fm_ap(x2_out, 16, TQ), xs[:])
    sc._flush("y")
    nc.sync.wait_ge(sc.sem["yd"], sc.cnt["yd"])
    sc.close()
    return nc


def host_prep_C(o_all, x_full, wo, router_logits, gating_einsum, linear, pes, rscale):
    woT = np.ascontiguousarray(wo.T.reshape(16, 128, 16, 128).transpose(2, 0, 1, 3))
    rl = np.ascontiguousarray(router_logits.reshape(16, 128, 8)).astype(np.float32)
    rsc = np.ascontiguousarray(rscale.reshape(16, 128).T).astype(np.float32)
    pes_a = pes.reshape(8, 1).astype(np.float32)
    rowsel = np.zeros((8, 8, 128), np.float32)
    for e in range(8):
        rowsel[e, e, :] = 1.0
    identf = np.eye(128, dtype=np.float32)
    wg_flat = np.transpose(gating_einsum, (0, 3, 1, 2)).reshape(8, 2048, 4096)
    wg = np.ascontiguousarray(
        wg_flat.reshape(8, 16, 128, 2, 16, 128).transpose(0, 4, 1, 2, 3, 5)).reshape(8, 16, 16, 128, 256)
    wl = np.ascontiguousarray(
        linear.reshape(8, 16, 128, 16, 128).transpose(0, 3, 1, 2, 4))
    maps = []
    for core in range(NCORE):
        b, quarter = core // 4, core % 4
        sl = slice(quarter * TQ, (quarter + 1) * TQ)
        o_q = np.ascontiguousarray(o_all[b, :, :, sl])
        maps.append({
            "o_in": o_q,
            "x": _host_fm(x_full[b, sl], 16).astype(np.float32),
            "woT": woT, "rl": rl, "rscale": rsc, "pes": pes_a,
            "rowsel": rowsel, "identf": identf,
            "wgate": wg, "wlin": wl,
        })
    return maps


TE = TQ + 3

def emit_ln2(sc, xe, xlne, tmp, stats, pool, ones_bf, ones1_f32, nchunks, eps=1e-5):
    tiles = [(0, 512), (512, 1024), (1024, 1027)]
    for c in range(nchunks):
        xc = xe[:, c * TE:(c + 1) * TE]
        sc.sop("activation", tmp[:, TE:2 * TE], xc, AF.Square)
        sc.vop("tensor_copy", tmp[:, 0:TE], xc)
        sc.mark("s"); sc.mark("v")
        sc.wait("t", "s"); sc.wait("t", "v")
        for i, (a, b_) in enumerate(tiles):
            w = b_ - a
            sc.mm(pool[i][0:1, 0:w], ones_bf[:, 0:1], tmp[:, a:b_], start=(c == 0), stop=(c == nchunks - 1))
            sc.mm(pool[3 + i][0:1, 0:w], ones_bf[:, 0:1], tmp[:, TE + a:TE + b_], start=(c == 0), stop=(c == nchunks - 1))
        sc.mark("t")
        sc.wait("s", "t"); sc.wait("v", "t")
    sc.mark("t")
    sc.wait("v", "t")
    sumx = stats[0:1, 0:TE]
    var_t = stats[0:1, TE:2 * TE]
    rstd = stats[0:1, 2 * TE:3 * TE]
    mr = stats[0:1, 3 * TE:4 * TE]
    for i, (a, b_) in enumerate(tiles):
        w = b_ - a
        sc.vop("tensor_copy", sumx[:, a:b_], pool[i][0:1, 0:w])
        sc.vop("tensor_copy", var_t[:, a:b_], pool[3 + i][0:1, 0:w])
    sc.vop("scalar_tensor_tensor", mr, sumx, 1.0, sumx, ALU.mult, ALU.mult)
    sc.vop("scalar_tensor_tensor", var_t, mr, -1.0 / D, var_t, ALU.mult, ALU.add)
    sc.mark("v")
    sc.wait("s", "v")
    sc.sop("activation", rstd, var_t, AF.Ln, scale=1.0 / D, bias=sc.const(eps, parts=1))
    sc.sop("activation", rstd, rstd, AF.Exp, scale=-0.5)
    sc.mark("s")
    sc.wait("v", "s")
    sc.vop("scalar_tensor_tensor", mr, sumx, 1.0 / D, rstd, ALU.mult, ALU.mult)
    sc.mark("v")
    for i, (a, b_) in enumerate(tiles):
        w = b_ - a
        sc.wait("t", "v")
        sc.mm(pool[6][:, 0:w], ones1_f32[0:1, :], rstd[:, a:b_], start=True, stop=True)
        sc.mm(pool[7][:, 0:w], ones1_f32[0:1, :], mr[:, a:b_], start=True, stop=True)
        sc.mark("t")
        sc.wait("v", "t")
        for c in range(nchunks):
            xc = xe[:, c * TE + a:c * TE + b_]
            oc = xlne[:, c * TE + a:c * TE + b_]
            sc.vop("tensor_mul", tmp[:, 0:w], xc, pool[6][:, 0:w])
            sc.vop("tensor_sub", oc, tmp[:, 0:w], pool[7][:, 0:w])
        sc.mark("v")
        sc.wait("t", "v")


def build_launchD():
    nc = bass.Bass("TRN2", target_bir_lowering=False)
    x2_in = nc.dram_tensor("x2", [16, 128, TQ], F32, kind="ExternalInput")
    halo_in = nc.dram_tensor("halo", [16, 128, 3], F32, kind="ExternalInput")
    w_in = nc.dram_tensor("wcat", [24, 16, 128, 256], F32, kind="ExternalInput")
    wba_in = nc.dram_tensor("wba", [128, 16 * 24], F32, kind="ExternalInput")
    convw_in = nc.dram_tensor("convw", [128, 40 * 4], F32, kind="ExternalInput")
    alog_in = nc.dram_tensor("alog", [8, 1], F32, kind="ExternalInput")
    dtb_in = nc.dram_tensor("dtb", [8, 1], F32, kind="ExternalInput")
    qn_out = nc.dram_tensor("qn", [8, 128, TQ], BF16, kind="ExternalOutput")
    kn_out = nc.dram_tensor("kn", [16, 128, TQ], BF16, kind="ExternalOutput")
    vv_out = nc.dram_tensor("vv", [16, 128, TQ], BF16, kind="ExternalOutput")
    gate_out = nc.dram_tensor("gate", [8, 128, TQ], BF16, kind="ExternalOutput")
    beta_out = nc.dram_tensor("beta", [16, TQ], F32, kind="ExternalOutput")
    lc_out = nc.dram_tensor("lc", [8, TQ], F32, kind="ExternalOutput")

    xe = nc.alloc_sbuf_tensor("xe", [128, 16 * TE], F32)
    xlne = nc.alloc_sbuf_tensor("xlne", [128, 16 * TE], BF16)
    tmp = nc.alloc_sbuf_tensor("tmp", [128, 2 * TE], BF16)
    stats = nc.alloc_sbuf_tensor("stats", [1, 4 * TE], F32)
    ones_bf = nc.alloc_sbuf_tensor("ones_bf", [128, 1], BF16)
    ones1_f32 = nc.alloc_sbuf_tensor("ones1_f32", [1, 128], F32)
    ones8 = nc.alloc_sbuf_tensor("ones8", [16, 64], F32)
    wbuf = [nc.alloc_sbuf_tensor(f"wbuf{i}", [128, 256], BF16) for i in range(2)]
    wba = nc.alloc_sbuf_tensor("wba_sb", [128, 16 * 24], BF16)
    convw = nc.alloc_sbuf_tensor("convw_sb", [128, 40 * 4], F32)
    alog = nc.alloc_sbuf_tensor("alog_sb", [8, 1], F32)
    dtb = nc.alloc_sbuf_tensor("dtb_sb", [8, 1], F32)
    xq = nc.alloc_sbuf_tensor("xq", [128, 2 * TE], BF16)
    cv = nc.alloc_sbuf_tensor("cv", [128, 2 * TQ], BF16)
    nrm = nc.alloc_sbuf_tensor("nrm", [1, 2 * TQ], F32)
    outstage = nc.alloc_sbuf_tensor("outstage", [128, 2 * TQ], BF16)
    zsb = nc.alloc_sbuf_tensor("zsb", [16, TQ], F32)
    sp1 = nc.alloc_sbuf_tensor("sp1", [16, TQ], F32)
    sp2 = nc.alloc_sbuf_tensor("sp2", [16, TQ], F32)
    beta_sb = nc.alloc_sbuf_tensor("beta_sb", [16, TQ], F32)
    lc_sb = nc.alloc_sbuf_tensor("lc_sb", [16, TQ], F32)
    asc = nc.alloc_sbuf_tensor("asc", [8, 2], F32)

    pool = [nc.alloc_psum_tensor(f"pool{i}", [128, 512], F32) for i in range(8)]

    sc = Sched(nc).open()
    sc.init_consts()
    sc.const(1e-5, parts=1)
    sc.const(1.0)
    sc.vop("memset", ones_bf[:], 1.0)
    sc.vop("memset", ones1_f32[:], 1.0)
    sc.vop("memset", ones8[:], 1.0)
    sc.mark("v")
    sc.wait("t", "v"); sc.wait("s", "v")

    sc.dma("y", bass.AP(xe, 3, [[16 * TE, 128], [TE, 16], [1, TQ]]),
           bass.AP(x2_in, 0, [[TQ, 128], [128 * TQ, 16], [1, TQ]]))
    sc.dma("y", bass.AP(xe, 0, [[16 * TE, 128], [TE, 16], [1, 3]]),
           bass.AP(halo_in, 0, [[3, 128], [128 * 3, 16], [1, 3]]))
    ld0 = sc.cnt["yd"]
    sc.dma("g", wba[:], wba_in.ap())
    sc.dma("g", convw[:], convw_in.ap())
    sc.dma("g", alog[:], alog_in.ap())
    sc.dma("g", dtb[:], dtb_in.ap())
    gld0 = sc.cnt["gd"]
    sc.wait("t", "yd", ld0); sc.wait("v", "yd", ld0); sc.wait("s", "yd", ld0)

    emit_ln2(sc, xe[:], xlne[:], tmp[:], stats[:], pool, ones_bf, ones1_f32, 16)
    sc.barrier()
    sc.wait("v", "gd", gld0); sc.wait("t", "gd", gld0); sc.wait("s", "gd", gld0)

    RSQ = 1.0 / float(np.sqrt(128.0))
    slab_ld = {}
    prev_t = {0: 0, 1: 0}
    slab_ld[0] = sc.dma("g", wbuf[0][:], w_in.ap()[0, 0], key="d0")
    for grp in range(24):
        has_conv = grp < 20
        for d in range(16):
            gi = grp * 16 + d
            slot = gi % 2
            if gi + 1 < 24 * 16:
                nslot = (gi + 1) % 2
                sc.wait("g", "t", prev_t[nslot])
                slab_ld[gi + 1] = sc.dma("g", wbuf[nslot][:], w_in.ap()[(gi + 1) // 16, (gi + 1) % 16], key=f"d{nslot}")
            sc.wait("t", f"d{slot}", slab_ld[gi])
            for oc2 in range(2):
                lh = wbuf[slot][:, oc2 * 128:(oc2 + 1) * 128]
                xc = xlne[:, d * TE:(d + 1) * TE]
                sc.mm(pool[oc2 * 2][:, :], lh, xc[:, 3:515], start=(d == 0), stop=(d == 15))
                sc.mm(pool[oc2 * 2 + 1][:, :], lh, xc[:, 515:1027], start=(d == 0), stop=(d == 15))
                if has_conv:
                    sc.mm(pool[6 + oc2][:, 0:3], lh, xc[:, 0:3], start=(d == 0), stop=(d == 15))
            prev_t[slot] = sc.mark("t")
        tm = sc.mark("t")
        sc.wait("v", "t", tm); sc.wait("s", "t", tm)
        for oc2 in range(2):
            oc = grp * 2 + oc2
            if has_conv:
                xqs = xq[:, oc2 * TE:(oc2 + 1) * TE]
                sc.vop("tensor_copy", xqs[:, 0:3], pool[6 + oc2][:, 0:3])
                sc.vop("tensor_copy", xqs[:, 3:515], pool[oc2 * 2][:, :])
                sc.vop("tensor_copy", xqs[:, 515:1027], pool[oc2 * 2 + 1][:, :])
                cvs = cv[:, oc2 * TQ:(oc2 + 1) * TQ]
                wcol = convw[:, oc * 4:(oc + 1) * 4]
                sc.vop("tensor_scalar_mul", cvs, xqs[:, 0:TQ], wcol[:, 0:1])
                for tau in range(1, 4):
                    sc.vop("scalar_tensor_tensor", cvs, xqs[:, tau:tau + TQ], wcol[:, tau:tau + 1], cvs, ALU.mult, ALU.add)
                sc.mark("v")
                sc.wait("s", "v")
                sc.sop("activation", cvs, cvs, AF.Silu)
                sc.mark("s")
                sc.wait("v", "s"); sc.wait("t", "s")
            else:
                sc.vop("tensor_copy", outstage[:, oc2 * TQ:oc2 * TQ + 512], pool[oc2 * 2][:, :])
                sc.vop("tensor_copy", outstage[:, oc2 * TQ + 512:(oc2 + 1) * TQ], pool[oc2 * 2 + 1][:, :])
                sc.mark("v")
                sc.wait("t", "v")
        if has_conv:
            for oc2 in range(2):
                oc = grp * 2 + oc2
                cvs = cv[:, oc2 * TQ:(oc2 + 1) * TQ]
                dst = outstage[:, oc2 * TQ:(oc2 + 1) * TQ]
                if oc < 24:
                    sc.sop("activation", tmp[:, 0:TQ], cvs, AF.Square)
                    sc.mark("s")
                    sc.wait("t", "s")
                    sc.mm(pool[4][0:1, :], ones_bf[:, 0:1], tmp[:, 0:512], start=True, stop=True)
                    sc.mm(pool[5][0:1, :], ones_bf[:, 0:1], tmp[:, 512:1024], start=True, stop=True)
                    tm = sc.mark("t")
                    sc.wait("s", "t", tm)
                    nr = nrm[0:1, oc2 * TQ:(oc2 + 1) * TQ]
                    sc.sop("activation", nr[:, 0:512], pool[4][0:1, :], AF.Sqrt)
                    sc.sop("activation", nr[:, 512:1024], pool[5][0:1, :], AF.Sqrt)
                    sc.mark("s")
                    sc.wait("v", "s")
                    sc.vop("tensor_scalar_max", nr, nr, 1e-12)
                    sc.vop("reciprocal", nr, nr)
                    if oc < 8:
                        sc.vop("tensor_scalar_mul", nr, nr, RSQ)
                    sc.mark("v")
                    sc.wait("t", "v")
                    sc.mm(pool[4][:, :], ones1_f32[0:1, :], nr[:, 0:512], start=True, stop=True)
                    sc.mm(pool[5][:, :], ones1_f32[0:1, :], nr[:, 512:1024], start=True, stop=True)
                    tm = sc.mark("t")
                    sc.wait("v", "t", tm)
                    sc.vop("tensor_mul", dst[:, 0:512], cvs[:, 0:512], pool[4][:, :])
                    sc.vop("tensor_mul", dst[:, 512:1024], cvs[:, 512:1024], pool[5][:, :])
                    sc.mark("v")
                    sc.wait("t", "v")
                else:
                    sc.vop("tensor_copy", dst, cvs)
                    sc.mark("v")
        vm = sc.mark("v")
        sc.wait("y", "v", vm)
        sc.wait("y", "s", sc.cnt["s"])
        for oc2 in range(2):
            oc = grp * 2 + oc2
            src = outstage[:, oc2 * TQ:(oc2 + 1) * TQ]
            if oc < 8:
                sc.dma("y", qn_out.ap()[oc], src)
            elif oc < 24:
                sc.dma("y", kn_out.ap()[oc - 8], src)
            elif oc < 40:
                sc.dma("y", vv_out.ap()[oc - 24], src)
            else:
                sc.dma("y", gate_out.ap()[oc - 40], src)
        sc.wait("v", "yd", sc.cnt["yd"])
        sc.wait("s", "yd", sc.cnt["yd"])
        sc.wait("t", "v", vm)

    sc.barrier()
    for th in range(2):
        sl = slice(th * 512, (th + 1) * 512)
        for d in range(16):
            xc = xlne[:, d * TE:(d + 1) * TE]
            sc.mm(pool[th][0:16, :], wba[:, d * 24:d * 24 + 16], xc[:, 3 + th * 512:3 + (th + 1) * 512], start=(d == 0), stop=(d == 15))
            sc.mm(pool[2 + th][0:8, :], wba[:, d * 24 + 16:d * 24 + 24], xc[:, 3 + th * 512:3 + (th + 1) * 512], start=(d == 0), stop=(d == 15))
    tm = sc.mark("t")
    sc.wait("s", "t", tm); sc.wait("v", "t", tm)
    for th in range(2):
        sl = slice(th * 512, (th + 1) * 512)
        sc.sop("activation", beta_sb[:, sl], pool[th][0:16, :], AF.Sigmoid)
        sc.vop("tensor_scalar_add", zsb[0:8, sl], pool[2 + th][0:8, :], dtb[0:8, 0:1])
    sc.mark("s"); sc.mark("v")
    sc.wait("v", "s")
    sc.vop("tensor_scalar_mul", beta_sb[:], beta_sb[:], 2.0)
    sc.mark("v")
    sc.wait("s", "v")
    sc.sop("activation", sp1[0:8, :], zsb[0:8, :], AF.Abs)
    sc.sop("activation", sp1[0:8, :], sp1[0:8, :], AF.Exp, scale=-1.0)
    sc.sop("activation", sp1[0:8, :], sp1[0:8, :], AF.Ln, bias=sc.const(1.0, parts=8))
    sc.sop("activation", asc[:, 0:1], alog[:, 0:1], AF.Exp)
    sc.mark("s")
    sc.wait("v", "s")
    sc.vop("tensor_relu", sp2[0:8, :], zsb[0:8, :])
    sc.vop("tensor_add", sp1[0:8, :], sp1[0:8, :], sp2[0:8, :])
    sc.vop("tensor_scalar_mul", asc[:, 1:2], asc[:, 0:1], -1.0)
    sc.vop("tensor_scalar_mul", sp1[0:8, :], sp1[0:8, :], asc[:, 1:2])
    for ch in range(16):
        sl = slice(ch * 64, (ch + 1) * 64)
        sc.vop("tensor_tensor_scan", lc_sb[0:8, sl], ones8[0:8, :], sp1[0:8, sl], 0.0, ALU.mult, ALU.add)
    vm = sc.mark("v")
    sc.wait("y", "v", vm)
    sc.dma("y", beta_out.ap(), beta_sb[:])
    sc.dma("y", lc_out.ap(), lc_sb[0:8, :])

    sc._flush("y")
    nc.sync.wait_ge(sc.sem["yd"], sc.cnt["yd"])
    sc.close()
    return nc


def host_prep_D(x2_all, Wq, Wk, Wv, Wg, Wb, Wa, A_log, dt_bias, cqw, ckw, cvw):
    wcat = np.concatenate([Wq, Wk, Wv, Wg], axis=1)
    w_arr = np.ascontiguousarray(
        wcat.reshape(16, 128, 24, 2, 128).transpose(2, 0, 1, 3, 4)).reshape(24, 16, 128, 256)
    wba = np.ascontiguousarray(
        np.concatenate([Wb, Wa], axis=1).reshape(16, 128, 24).transpose(1, 0, 2)).reshape(128, 16 * 24)
    convw = np.ascontiguousarray(
        np.concatenate([cqw[:, 0, :], ckw[:, 0, :], cvw[:, 0, :]], axis=0)
        .reshape(40, 128, 4).transpose(1, 0, 2)).reshape(128, 160)
    maps = []
    for core in range(NCORE):
        b, quarter = core // 4, core % 4
        if quarter == 0:
            halo = np.zeros((16, 128, 3), np.float32)
        else:
            halo = np.ascontiguousarray(x2_all[core - 1][:, :, TQ - 3:TQ])
        maps.append({
            "x2": x2_all[core], "halo": halo, "wcat": w_arr, "wba": wba,
            "convw": convw, "alog": A_log.reshape(8, 1).astype(np.float32),
            "dtb": dt_bias.reshape(8, 1).astype(np.float32),
        })
    return maps


def build_launchE():
    nc = bass.Bass("TRN2", target_bir_lowering=False)
    NU = 2
    NCH = 64
    qf_in = nc.dram_tensor("qf", [NU, 128, S], BF16, kind="ExternalInput")
    kf_in = nc.dram_tensor("kf", [NU, 128, 2 * S], BF16, kind="ExternalInput")
    ktm_in = nc.dram_tensor("ktm", [NU, NCH, 128, 128], BF16, kind="ExternalInput")
    vtm_in = nc.dram_tensor("vtm", [NU, NCH, 128, 128], BF16, kind="ExternalInput")
    gate_in = nc.dram_tensor("gate", [NU, 128, S], BF16, kind="ExternalInput")
    bc_in = nc.dram_tensor("beta_c", [NU, 128, NCH], F32, kind="ExternalInput")
    lc2_in = nc.dram_tensor("lc2", [NU, 128, NCH], F32, kind="ExternalInput")
    lcq_in = nc.dram_tensor("lcq", [NU, 128, NCH], F32, kind="ExternalInput")
    lcn_in = nc.dram_tensor("lcn", [NU, 128, NCH], F32, kind="ExternalInput")
    identf_in = nc.dram_tensor("identf", [128, 128], F32, kind="ExternalInput")
    normw_in = nc.dram_tensor("normw", [128, 1], F32, kind="ExternalInput")
    maskm_in = nc.dram_tensor("maskm", [128, 128], F32, kind="ExternalInput")
    maskmt_in = nc.dram_tensor("maskmt", [128, 128], F32, kind="ExternalInput")
    maskl_in = nc.dram_tensor("maskl", [128, 64], F32, kind="ExternalInput")
    identb_in = nc.dram_tensor("identb", [128, 128], BF16, kind="ExternalInput")
    of_out = nc.dram_tensor("of", [NU, 128, S], BF16, kind="ExternalOutput")

    qf = [nc.alloc_sbuf_tensor(f"qf{i}", [128, S], BF16) for i in range(2)]
    kf = [nc.alloc_sbuf_tensor(f"kf{i}", [128, 2 * S], BF16) for i in range(2)]
    ktm = [nc.alloc_sbuf_tensor(f"ktm{i}", [128, NCH * 128], BF16) for i in range(2)]
    vtm = [nc.alloc_sbuf_tensor(f"vtm{i}", [128, NCH * 128], BF16) for i in range(2)]
    gate = [nc.alloc_sbuf_tensor(f"gate{i}", [128, S], BF16) for i in range(2)]
    bcw = [nc.alloc_sbuf_tensor(f"bcw{i}", [128, NCH], F32) for i in range(2)]
    lc2w = [nc.alloc_sbuf_tensor(f"lc2w{i}", [128, NCH], F32) for i in range(2)]
    lcqw = [nc.alloc_sbuf_tensor(f"lcqw{i}", [128, NCH], F32) for i in range(2)]
    lcnw = [nc.alloc_sbuf_tensor(f"lcnw{i}", [128, NCH], F32) for i in range(2)]
    rows_sb = nc.alloc_sbuf_tensor("rows_sb", [1, 320], F32)
    identf = nc.alloc_sbuf_tensor("identf_sb", [128, 128], F32)
    o_sb = [nc.alloc_sbuf_tensor(f"osb{i}", [128, S], BF16) for i in range(2)]
    normw = nc.alloc_sbuf_tensor("normw_sb", [128, 1], F32)
    maskm = nc.alloc_sbuf_tensor("maskm_sb", [128, 128], F32)
    maskmt = nc.alloc_sbuf_tensor("maskmt_sb", [128, 128], F32)
    maskl = nc.alloc_sbuf_tensor("maskl_sb", [128, 64], F32)
    identb = nc.alloc_sbuf_tensor("identb_sb", [128, 128], BF16)
    ones1_f32 = nc.alloc_sbuf_tensor("ones1_f32", [1, 128], F32)
    ones_bf = nc.alloc_sbuf_tensor("ones_bf", [128, 1], BF16)

    S_st = nc.alloc_sbuf_tensor("S_st", [128, 128], F32)
    S_bf = nc.alloc_sbuf_tensor("S_bf", [128, 128], BF16)
    t1 = nc.alloc_sbuf_tensor("t1", [128, 128], F32)
    t2 = nc.alloc_sbuf_tensor("t2", [128, 128], F32)
    Rr = nc.alloc_sbuf_tensor("Rr", [128, 128], F32)
    Ff = [nc.alloc_sbuf_tensor(f"Ff{i}", [128, 128], BF16) for i in range(2)]
    Hh = [nc.alloc_sbuf_tensor(f"Hh{i}", [128, 128], BF16) for i in range(2)]
    acc = nc.alloc_sbuf_tensor("acc", [128, 128], BF16)
    tbt = nc.alloc_sbuf_tensor("tbt", [128, 128], BF16)
    w0 = nc.alloc_sbuf_tensor("w0", [128, 128], BF16)
    lt = nc.alloc_sbuf_tensor("lt", [128, 64], BF16)
    qbar = nc.alloc_sbuf_tensor("qbar", [128, 64], BF16)
    xb = nc.alloc_sbuf_tensor("xb", [128, 128], BF16)
    uu = nc.alloc_sbuf_tensor("uu", [128, 128], BF16)
    khat = nc.alloc_sbuf_tensor("khat", [128, 128], BF16)
    scr = nc.alloc_sbuf_tensor("scr", [128, 8], F32)
    tmp = nc.alloc_sbuf_tensor("tmp", [128, 1024], BF16)
    nrm = nc.alloc_sbuf_tensor("nrm", [1, 512], F32)

    pool = [nc.alloc_psum_tensor(f"pool{i}", [128, 512], F32) for i in range(8)]

    sc = Sched(nc).open()
    sc.init_consts()
    sc.const(1e-5, parts=1)
    sc.vop("memset", ones1_f32[:], 1.0)
    sc.vop("memset", ones_bf[:], 1.0)
    sc.mark("v")

    sc.dma("g", normw[:], normw_in.ap())
    sc.dma("g", maskm[:], maskm_in.ap())
    sc.dma("g", maskmt[:], maskmt_in.ap())
    sc.dma("g", maskl[:], maskl_in.ap())
    sc.dma("g", identb[:], identb_in.ap())
    sc.dma("g", identf[:], identf_in.ap())
    gld0 = sc.cnt["gd"]
    sc.wait("t", "gd", gld0); sc.wait("v", "gd", gld0); sc.wait("s", "gd", gld0)

    def load_unit(u, slot):
        ks = f"d{slot}"
        sc.dma("y", qf[slot][:], qf_in.ap()[u], key=ks)
        sc.dma("y", kf[slot][:], kf_in.ap()[u], key=ks)
        sc.dma("y", ktm[slot][:], bass.AP(ktm_in, u * NCH * 128 * 128, [[128, 128], [128 * 128, NCH], [1, 128]]), key=ks)
        sc.dma("y", vtm[slot][:], bass.AP(vtm_in, u * NCH * 128 * 128, [[128, 128], [128 * 128, NCH], [1, 128]]), key=ks)
        sc.dma("y", gate[slot][:], gate_in.ap()[u], key=ks)
        kg = f"d{2 + slot}"
        sc.dma("g", bcw[slot][:], bc_in.ap()[u], key=kg)
        sc.dma("g", lc2w[slot][:], lc2_in.ap()[u], key=kg)
        sc.dma("g", lcqw[slot][:], lcq_in.ap()[u], key=kg)
        sc.dma("g", lcnw[slot][:], lcn_in.ap()[u], key=kg)
        return sc.cnt[ks], sc.cnt[kg]

    loads = {0: load_unit(0, 0)}
    done_v = {}
    for u in range(NU):
        slot = u % 2
        if u + 1 < NU:
            loads[u + 1] = load_unit(u + 1, 1 - slot)
        ydc, gdc = loads[u]
        ks, kg = f"d{slot}", f"d{2 + slot}"
        sc.wait("t", ks, ydc); sc.wait("t", kg, gdc)
        sc.wait("v", ks, ydc); sc.wait("v", kg, gdc)
        sc.wait("s", ks, ydc); sc.wait("s", kg, gdc)
        sc.vop("memset", S_st[:], 0.0)
        sc.vop("memset", S_bf[:], 0.0)
        sc.mark("v")
        sc.wait("t", "v")
        for c in range(NCH):
            kfc = kf[slot][:, c * 128:(c + 1) * 128]
            qfc = qf[slot][:, c * 64:(c + 1) * 64]
            ktc = ktm[slot][:, c * 128:(c + 1) * 128]
            vtc = vtm[slot][:, c * 128:(c + 1) * 128]
            bcc = bcw[slot][:, c:c + 1]
            lc2c = lc2w[slot][:, c:c + 1]
            lcnc = lcnw[slot][:, c:c + 1]
            sc.wait("v", "s", sc.cnt["s"])
            sc.vop("tensor_sub", scr[:, 0:1], lcnc, lc2c)
            sc.vop("tensor_scalar_mul", scr[:, 1:2], bcc, -1.0)
            sc.mark("v"); sc.wait("s", "v")
            sc.sop("activation", scr[:, 2:3], lc2c, AF.Exp)
            sc.sop("activation", scr[:, 3:4], scr[:, 0:1], AF.Exp)
            sc.sop("activation", scr[:, 4:5], lcnc, AF.Exp)
            sc.mark("s")
            sc.trans(pool[2][0:1, 128:256], lc2c, identf[:, :])
            sc.trans(pool[2][0:1, 256:384], bcc, identf[:, :])
            sc.trans(pool[2][0:1, 384:448], lcqw[slot][0:64, c:c + 1], identf[0:64, 0:64])
            tm = sc.mark("t")
            sc.wait("v", "t", tm)
            sc.vop("tensor_copy", rows_sb[0:1, 0:128], pool[2][0:1, 128:256])
            sc.vop("tensor_copy", rows_sb[0:1, 128:256], pool[2][0:1, 256:384])
            sc.vop("tensor_copy", rows_sb[0:1, 256:320], pool[2][0:1, 384:448])
            sc.mark("v"); sc.wait("t", "v")
            sc.mm(pool[0][:, 0:128], kfc, kfc, start=True, stop=True)
            sc.mm(pool[1][:, 0:128], ones1_f32[0:1, :], rows_sb[0:1, 0:128], start=True, stop=True)
            sc.mm(pool[1][:, 128:256], ones1_f32[0:1, :], rows_sb[0:1, 128:256], start=True, stop=True)
            sc.mm(pool[1][:, 256:320], ones1_f32[0:1, :], rows_sb[0:1, 256:320], start=True, stop=True)
            tm = sc.mark("t")
            sc.wait("v", "t", tm)
            sc.vop("tensor_scalar", t1[:], pool[1][:, 0:128], lc2c, None, ALU.subtract)
            sc.vop("tensor_add", t2[:], t1[:], maskm[:])
            sc.mark("v"); sc.wait("s", "v")
            sc.sop("activation", Rr[:], t2[:], AF.Exp, scale=-1.0)
            sc.mark("s"); sc.wait("v", "s")
            sc.vop("tensor_mul", Rr[:], Rr[:], pool[0][:, 0:128])
            sc.vop("tensor_scalar_mul", Ff[0][:], Rr[:], scr[:, 1:2])
            sc.vop("tensor_add", t2[:], t1[:], maskmt[:])
            sc.mark("v"); sc.wait("s", "v")
            sc.sop("activation", Rr[:], t2[:], AF.Exp)
            sc.mark("s"); sc.wait("v", "s")
            sc.vop("tensor_mul", Rr[:], Rr[:], pool[0][:, 0:128])
            sc.vop("scalar_tensor_tensor", Hh[0][:], Rr[:], -1.0, pool[1][:, 128:256], ALU.mult, ALU.mult)
            sc.vop("tensor_add", acc[:], Hh[0][:], identb[:])
            sc.vop("tensor_scalar", t1[:, 0:64], pool[1][:, 256:320], lc2c, None, ALU.subtract)
            sc.vop("tensor_add", t1[:, 0:64], t1[:, 0:64], maskl[:])
            sc.mark("v"); sc.wait("s", "v")
            sc.sop("activation", t2[:, 0:64], t1[:, 0:64], AF.Exp)
            sc.sop("activation", t2[:, 64:128], pool[1][:, 256:320][:, 0:64], AF.Exp)
            sc.mark("s")
            sc.wait("t", "s"); sc.wait("t", "v")
            sc.mm(pool[2][:, 0:64], kfc, qfc, start=True, stop=True)
            cur = 0
            pidx = [3, 4]
            for kstep in range(1, 7):
                nxt = 1 - cur
                sc.mm(pool[pidx[0]][:, 0:128], Hh[cur][:], Ff[cur][:], start=True, stop=True)
                sc.mm(pool[pidx[1]][:, 0:128], Ff[cur][:], Hh[cur][:], start=True, stop=True)
                tm = sc.mark("t")
                sc.wait("v", "t", tm); sc.wait("s", "t", tm)
                sc.vop("tensor_copy", Ff[nxt][:], pool[pidx[0]][:, 0:128])
                sc.sop("activation", Hh[nxt][:], pool[pidx[1]][:, 0:128], AF.Copy)
                sc.mark("v"); sc.mark("s")
                sc.wait("t", "v"); sc.wait("t", "s")
                sc.mm(pool[5][:, 0:128], Ff[nxt][:], acc[:], start=True, stop=True)
                tm = sc.mark("t")
                sc.wait("v", "t", tm)
                sc.vop("tensor_add", acc[:], acc[:], pool[5][:, 0:128])
                sc.mark("v"); sc.wait("t", "v")
                cur = nxt
            sc.vop("tensor_mul", lt[:], t2[:, 0:64], pool[2][:, 0:64])
            sc.vop("tensor_scalar_mul", tbt[:], acc[:], bcc)
            sc.vop("tensor_mul", qbar[:], qfc, t2[:, 64:128])
            sc.mark("v"); sc.wait("t", "v")
            sc.mm(pool[6][:, 0:128], tbt[:], vtc, start=True, stop=True)
            sc.mm(pool[7][:, 0:128], kfc, S_bf[:], start=True, stop=True)
            tm = sc.mark("t")
            sc.wait("v", "t", tm)
            sc.vop("tensor_copy", w0[:], pool[6][:, 0:128])
            sc.vop("tensor_scalar_mul", xb[:], pool[7][:, 0:128], scr[:, 2:3])
            sc.vop("tensor_scalar_mul", khat[:], ktc, scr[:, 3:4])
            sc.mark("v"); sc.wait("t", "v")
            sc.mm(pool[6][:, 0:128], tbt[:], xb[:], start=True, stop=True)
            tm = sc.mark("t")
            sc.wait("v", "t", tm)
            sc.vop("tensor_sub", uu[:], w0[:], pool[6][:, 0:128])
            sc.mark("v"); sc.wait("t", "v")
            sc.mm(pool[7][:, 0:64], S_bf[:], qbar[:], start=True, stop=False)
            sc.mm(pool[7][:, 0:64], uu[:], lt[:], start=False, stop=True)
            sc.mm(pool[6][:, 0:128], khat[:], uu[:], start=True, stop=True)
            tm = sc.mark("t")
            sc.wait("v", "t", tm)
            sc.vop("tensor_copy", o_sb[slot][:, c * 64:(c + 1) * 64], pool[7][:, 0:64])
            sc.vop("scalar_tensor_tensor", S_st[:], S_st[:], scr[:, 4:5], pool[6][:, 0:128], ALU.mult, ALU.add)
            sc.mark("v"); sc.wait("s", "v")
            sc.sop("activation", S_bf[:], S_st[:], AF.Copy)
            sc.mark("s")
            sc.wait("t", "s"); sc.wait("t", "v")
        for blk in range(8):
            bs = slice(blk * 512, (blk + 1) * 512)
            sc.sop("activation", tmp[:, 0:512], o_sb[slot][:, bs], AF.Square)
            sc.mark("s"); sc.wait("t", "s")
            sc.mm(pool[0][0:1, :], ones_bf[:, 0:1], tmp[:, 0:512], start=True, stop=True)
            tm = sc.mark("t")
            sc.wait("s", "t", tm)
            sc.sop("activation", nrm[0:1, :], pool[0][0:1, :], AF.Ln, scale=1.0 / 128.0, bias=sc.const(1e-5, parts=1))
            sc.sop("activation", nrm[0:1, :], nrm[0:1, :], AF.Exp, scale=-0.5)
            sc.mark("s"); sc.wait("t", "s")
            sc.mm(pool[1][:, :], ones1_f32[0:1, :], nrm[0:1, :], start=True, stop=True)
            tm = sc.mark("t")
            sc.wait("s", "t", tm); sc.wait("v", "t", tm)
            sc.sop("activation", tmp[:, 512:1024], gate[slot][:, bs], AF.Silu)
            sc.mark("s"); sc.wait("v", "s")
            sc.vop("tensor_scalar_mul", tmp[:, 512:1024], tmp[:, 512:1024], normw[:, 0:1])
            sc.vop("tensor_mul", tmp[:, 0:512], o_sb[slot][:, bs], pool[1][:, :])
            sc.vop("tensor_mul", o_sb[slot][:, bs], tmp[:, 0:512], tmp[:, 512:1024])
            sc.mark("v"); sc.wait("t", "v"); sc.wait("s", "v")
        vm = sc.mark("v")
        done_v[u] = vm
        sc.wait("y", "v", vm)
        sc.dma("y", of_out.ap()[u], o_sb[slot][:])
        sc.wait("y", "yd", sc.cnt["yd"])

    sc._flush("y")
    nc.sync.wait_ge(sc.sem["yd"], sc.cnt["yd"])
    sc.close()
    return nc


def host_prep_E(qn_all, kn_all, vv_all, gate_all, beta_all, lc_all, norm_w):
    NCH = 64
    def full(arrs, b, idx):
        return np.concatenate([arrs[b * 4 + q][idx] for q in range(4)], axis=-1)
    maskm = np.where(np.tril(np.ones((128, 128)), -1) > 0, 0.0, 1e30).astype(np.float32)
    maskmt = np.where(np.triu(np.ones((128, 128)), 1) > 0, 0.0, -1e30).astype(np.float32)
    j_idx = np.arange(128)[:, None]
    t_idx = np.arange(64)[None, :]
    maskl = np.where(j_idx <= 2 * t_idx + 1, 0.0, -1e30).astype(np.float32)
    identb = np.eye(128, dtype=np.float32).astype(bf16)
    maps = []
    for core in range(NCORE):
        m = {"normw": norm_w.reshape(128, 1).astype(np.float32), "maskm": maskm, "maskmt": maskmt, "maskl": maskl, "identb": identb}
        qfs, kfs, ktms, vtms, gates = [], [], [], [], []
        bcs, lc2s, lcqs, lcns = [], [], [], []
        for i in range(2):
            u = 2 * core + i
            b, h = u // 8, u % 8
            q_full = full(qn_all, b, h)
            k0 = full(kn_all, b, h); k1 = full(kn_all, b, 8 + h)
            v0 = full(vv_all, b, h); v1 = full(vv_all, b, 8 + h)
            kv = np.stack([k0, k1], axis=-1).reshape(128, 2 * S)
            vv = np.stack([v0, v1], axis=-1).reshape(128, 2 * S)
            g_full = full(gate_all, b, h)
            be0 = full(beta_all, b, h); be1 = full(beta_all, b, 8 + h)
            bv = np.stack([be0, be1], axis=-1).reshape(2 * S)
            lc = full(lc_all, b, h)
            lcv = np.repeat(lc, 2)
            qfs.append(q_full); kfs.append(kv)
            ktms.append(np.ascontiguousarray(kv.T).reshape(NCH, 128, 128))
            vtms.append(np.ascontiguousarray(vv.T).reshape(NCH, 128, 128))
            gates.append(g_full)
            bcs.append(np.ascontiguousarray(bv.reshape(NCH, 128).T).astype(np.float32))
            lc2s.append(np.ascontiguousarray(lcv.reshape(NCH, 128).T).astype(np.float32))
            lcq_c = np.zeros((128, NCH), np.float32)
            lcq_c[0:64, :] = lc.reshape(NCH, 64).T
            lcqs.append(lcq_c)
            lcns.append(np.ascontiguousarray(np.broadcast_to(
                lcv.reshape(NCH, 128)[:, 127], (128, NCH))).astype(np.float32))
        m["identf"] = np.eye(128, dtype=np.float32)
        m.update({
            "qf": np.stack(qfs), "kf": np.stack(kfs), "ktm": np.stack(ktms),
            "vtm": np.stack(vtms), "gate": np.stack(gates),
            "beta_c": np.stack(bcs), "lc2": np.stack(lc2s), "lcq": np.stack(lcqs),
            "lcn": np.stack(lcns),
        })
        maps.append(m)
    return maps


def build_launchF():
    nc = bass.Bass("TRN2", target_bir_lowering=False)
    of_in = nc.dram_tensor("of_in", [8, 128, TQ], BF16, kind="ExternalInput")
    x2_in = nc.dram_tensor("x2", [16, 128, TQ], F32, kind="ExternalInput")
    wo1_in = nc.dram_tensor("wo1", [16, 8, 128, 128], F32, kind="ExternalInput")
    rw_in = nc.dram_tensor("rw", [16, 128, 8], F32, kind="ExternalInput")
    rowsel_in = nc.dram_tensor("rowsel", [8, 8, 128], F32, kind="ExternalInput")
    ident_in = nc.dram_tensor("identf", [128, 128], F32, kind="ExternalInput")
    sgw_in = nc.dram_tensor("sgw", [128, 16], F32, kind="ExternalInput")
    wgu_in = nc.dram_tensor("wgu", [8, 11, 16, 128, 256], F32, kind="ExternalInput")
    wd_in = nc.dram_tensor("wd", [8, 16, 11, 128, 128], F32, kind="ExternalInput")
    swgu_in = nc.dram_tensor("swgu", [44, 16, 128, 256], F32, kind="ExternalInput")
    swd_in = nc.dram_tensor("swd", [16, 44, 128, 128], F32, kind="ExternalInput")
    out_out = nc.dram_tensor("out", [16, 128, TQ], F32, kind="ExternalOutput")

    xs = nc.alloc_sbuf_tensor("xs", [128, 16 * TQ], F32)
    o_sb = nc.alloc_sbuf_tensor("o_sb", [128, 8 * TQ], BF16)
    xln = nc.alloc_sbuf_tensor("xln", [128, 16 * TQ], BF16)
    tmp = nc.alloc_sbuf_tensor("tmp", [128, 2 * TQ], BF16)
    stats = nc.alloc_sbuf_tensor("stats", [1, 4 * TQ], F32)
    ones_bf = nc.alloc_sbuf_tensor("ones_bf", [128, 1], BF16)
    ones1_f32 = nc.alloc_sbuf_tensor("ones1_f32", [1, 128], F32)
    identf = nc.alloc_sbuf_tensor("identf_sb", [128, 128], F32)
    rw_sb = nc.alloc_sbuf_tensor("rw_sb", [128, 16 * 8], BF16)
    rowsel = nc.alloc_sbuf_tensor("rowsel_sb", [8, 8 * 128], BF16)
    sgw_sb = nc.alloc_sbuf_tensor("sgw_sb", [128, 16], BF16)
    lg = nc.alloc_sbuf_tensor("lg", [128, 8], F32)
    m1 = nc.alloc_sbuf_tensor("m1", [128, 4], F32)
    eqm = nc.alloc_sbuf_tensor("eqm", [128, 8], F32)
    ind = nc.alloc_sbuf_tensor("ind", [128, 8], F32)
    ee = nc.alloc_sbuf_tensor("ee", [128, 8], F32)
    w_tm = nc.alloc_sbuf_tensor("w_tm", [128, 8], F32)
    W_fm = nc.alloc_sbuf_tensor("W_fm", [8, TQ], BF16)
    wbc = nc.alloc_sbuf_tensor("wbc", [128, TQ], BF16)
    sg_b = nc.alloc_sbuf_tensor("sg_b", [128, TQ], BF16)
    hg = nc.alloc_sbuf_tensor("hg", [128, TQ], BF16)
    hu = nc.alloc_sbuf_tensor("hu", [128, TQ], BF16)
    hsh = nc.alloc_sbuf_tensor("hsh", [128, 44 * 512], BF16)
    he = hsh
    wov = [nc.alloc_sbuf_tensor(f"wov{i}", [128, 8 * 128], BF16) for i in range(2)]
    _osb_addr = nc.lookup_mloc(o_sb).addr
    wgb = [nc.alloc_sbuf_tensor_at(f"wgb{i}", [128, 16 * 256], BF16, offset=_osb_addr + i * 16 * 256 * 2) for i in range(2)]
    wlb = [nc.alloc_sbuf_tensor(f"wlb{i}", [128, 11 * 128], BF16) for i in range(2)]
    _st_addr = nc.lookup_mloc(stats).addr
    swb = [nc.alloc_sbuf_tensor_at(f"swb{i}", [128, 16 * 256], BF16, offset=_st_addr + i * 16 * 256 * 2) for i in range(2)]
    sdb = [nc.alloc_sbuf_tensor_at("sdb0", [128, 44 * 128], BF16, offset=_osb_addr)]

    pool = [nc.alloc_psum_tensor(f"pool{i}", [128, 512], F32) for i in range(8)]

    sc = Sched(nc).open()
    sc.init_consts()
    sc.const(1e-5, parts=1)
    sc.vop("memset", ones_bf[:], 1.0)
    sc.vop("memset", ones1_f32[:], 1.0)
    sc.mark("v")

    sc.dma("y", xs[:], _fm_ap(x2_in, 16, TQ))
    sc.dma("y", o_sb[:], _fm_ap(of_in, 8, TQ))
    ld0 = sc.cnt["yd"]
    sc.dma("g", rw_sb[:], bass.AP(rw_in, 0, [[8, 128], [128 * 8, 16], [1, 8]]))
    sc.dma("g", rowsel[:], bass.AP(rowsel_in, 0, [[128, 8], [8 * 128, 8], [1, 128]]))
    sc.dma("g", identf[:], ident_in.ap())
    sc.dma("g", sgw_sb[:], sgw_in.ap())
    gld0 = sc.cnt["gd"]

    wo_ld = {}
    wo_ld[0] = sc.dma("g", wov[0][:], bass.AP(wo1_in, 0, [[128, 128], [128 * 128, 8], [1, 128]]), key="d0")
    for ds in range(16):
        slot = ds % 2
        if ds + 1 < 16:
            if ds >= 1:
                sc.wait("g", "t", sc.cnt["t"])
            wo_ld[ds + 1] = sc.dma("g", wov[1 - slot][:],
                                   bass.AP(wo1_in, (ds + 1) * 8 * 128 * 128, [[128, 128], [128 * 128, 8], [1, 128]]),
                                   key=f"d{1 - slot}")
        sc.wait("t", f"d{slot}", wo_ld[ds])
        sc.wait("t", "yd", ld0)
        sc.wait("v", "yd", ld0)
        for th in range(2):
            for ec in range(8):
                sc.mm(pool[th][:, :], wov[slot][:, ec * 128:(ec + 1) * 128],
                      o_sb[:, ec * TQ + th * 512: ec * TQ + (th + 1) * 512],
                      start=(ec == 0), stop=(ec == 7))
            tm = sc.mark("t")
            sc.wait("v", "t", tm)
            sl = slice(th * 512, (th + 1) * 512)
            sc.vop("tensor_add", xs[:, ds * TQ:(ds + 1) * TQ][:, sl], xs[:, ds * TQ:(ds + 1) * TQ][:, sl], pool[th][:, :])
            sc.mark("v")
        sc.wait("t", "v")

    sc.barrier()
    emit_ln(sc, xs[:], xln[:], tmp[:], stats[:], pool, ones_bf, ones1_f32, 16, TQ)
    sc.barrier()

    sc.wait("v", "gd", gld0); sc.wait("t", "gd", gld0); sc.wait("s", "gd", gld0)
    for tt in range(8):
        tsl = slice(tt * 128, (tt + 1) * 128)
        for d in range(16):
            sc.mm(pool[3][:, 0:8], xln[:, d * TQ:(d + 1) * TQ][:, tsl], rw_sb[:, d * 8:(d + 1) * 8], start=(d == 0), stop=(d == 15))
        tm = sc.mark("t")
        sc.wait("v", "t", tm)
        sc.vop("tensor_copy", lg[:], pool[3][:, 0:8])
        sc.vop("reduce_max", m1[:, 0:1], lg[:], axis=AX.X)
        sc.vop("tensor_scalar_mul", m1[:, 1:2], m1[:, 0:1], -1.0)
        sc.vop("tensor_single_scalar", eqm[:], lg[:], m1[:, 0:1], ALU.is_ge)
        sc.vop("scalar_tensor_tensor", eqm[:], eqm[:], -1e30, lg[:], ALU.mult, ALU.add)
        sc.vop("reduce_max", m1[:, 2:3], eqm[:], axis=AX.X)
        sc.vop("tensor_single_scalar", ind[:], lg[:], m1[:, 2:3], ALU.is_ge)
        sc.mark("v")
        sc.wait("s", "v")
        sc.sop("activation", ee[:], lg[:], AF.Exp, bias=m1[:, 1:2])
        sc.mark("s")
        sc.wait("v", "s")
        sc.vop("tensor_mul", ee[:], ee[:], ind[:])
        sc.vop("reduce_sum", m1[:, 3:4], ee[:], axis=AX.X)
        sc.vop("reciprocal", m1[:, 3:4], m1[:, 3:4])
        sc.vop("tensor_scalar_mul", w_tm[:], ee[:], m1[:, 3:4])
        sc.mark("v")
        sc.wait("t", "v")
        sc.trans(pool[5][0:8, 0:128], w_tm[:, 0:8], identf[:, :])
        tm = sc.mark("t")
        sc.wait("v", "t", tm)
        sc.vop("tensor_copy", W_fm[0:8, tsl], pool[5][0:8, 0:128])
        sc.mark("v")
        sc.wait("t", "v")

    sgrow = nc.alloc_sbuf_tensor("sgrow", [1, TQ], F32)
    for th in range(2):
        pp = pool[2] if th == 0 else pool[4]
        for d in range(16):
            sc.mm(pp[0:1, 0:512], sgw_sb[:, d:d + 1],
                  xln[:, d * TQ + th * 512:d * TQ + th * 512 + 512], start=(d == 0), stop=(d == 15))
    tm = sc.mark("t")
    sc.wait("s", "t", tm)
    sc.sop("activation", sgrow[0:1, 0:512], pool[2][0:1, 0:512], AF.Sigmoid)
    sc.sop("activation", sgrow[0:1, 512:1024], pool[4][0:1, 0:512], AF.Sigmoid)
    sc.mark("s")
    sc.wait("t", "s")
    for th in range(2):
        pp = pool[2] if th == 0 else pool[4]
        sc.mm(pp[:, :], ones1_f32[0:1, :], sgrow[0:1, th * 512:(th + 1) * 512], start=True, stop=True)
    tm = sc.mark("t")
    sc.wait("v", "t", tm)
    sc.vop("tensor_copy", sg_b[:, 0:512], pool[2][:, :])
    sc.vop("tensor_copy", sg_b[:, 512:1024], pool[4][:, :])
    sc.mark("v")
    sc.wait("t", "v")

    sw_ld = {}
    sd_ld = {}
    prev_sw_t = {0: 0, 1: 0}
    prev_sd_t = {0: 0, 1: 0}
    shared_vm = {0: 0, 1: 0}
    shared_sm = {0: 0, 1: 0}
    sc.wait("g", "t", sc.cnt["t"]); sc.wait("g", "s", sc.cnt["s"]); sc.wait("g", "v", sc.cnt["v"])
    sw_ld[0] = sc.dma("g", swb[0][:], bass.AP(swgu_in, 0, [[256, 128], [128 * 256, 16], [1, 256]]), key="d4")
    for th in range(2):
        tsl = slice(th * 512, (th + 1) * 512)
        for ic in range(44):
            gi = th * 44 + ic
            slot = gi % 2
            if gi + 1 < 88:
                nslot = (gi + 1) % 2
                ngi = gi + 1
                nic = ngi % 44
                sc.wait("g", "t", prev_sw_t[nslot])
                sw_ld[ngi] = sc.dma("g", swb[nslot][:],
                                    bass.AP(swgu_in, nic * 16 * 128 * 256, [[256, 128], [128 * 256, 16], [1, 256]]),
                                    key=f"d{4 + nslot}")
            sc.wait("t", f"d{4 + slot}", sw_ld[gi])
            pg, pu = (pool[0], pool[1]) if ic % 2 == 0 else (pool[2], pool[3])
            sc.wait("t", "v", shared_vm[ic % 2]); sc.wait("t", "s", shared_sm[ic % 2])
            for d in range(16):
                sc.mm(pg[:, :], swb[slot][:, d * 256:d * 256 + 128], xln[:, d * TQ:(d + 1) * TQ][:, tsl],
                      start=(d == 0), stop=(d == 15))
                sc.mm(pu[:, :], swb[slot][:, d * 256 + 128:d * 256 + 256], xln[:, d * TQ:(d + 1) * TQ][:, tsl],
                      start=(d == 0), stop=(d == 15))
            tm = sc.mark("t")
            prev_sw_t[slot] = tm
            sc.wait("s", "t", tm); sc.wait("v", "t", tm)
            hsl = slice((ic % 2) * 512, (ic % 2) * 512 + 512)
            sc.sop("activation", hg[:, hsl], pg[:, :], AF.Silu)
            sc.vop("tensor_mul", hu[:, hsl], pu[:, :], sg_b[:, tsl])
            shared_sm[ic % 2] = sc.mark("s")
            sc.mark("v")
            sc.wait("v", "s")
            sc.vop("tensor_mul", hsh[:, ic * 512:(ic + 1) * 512], hg[:, hsl], hu[:, hsl])
            shared_vm[ic % 2] = sc.mark("v")
        sdown_vm = {0: 0, 1: 0}
        for fs in range(16):
            gi2 = th * 16 + fs
            sc.wait("g", "t", sc.cnt["t"])
            cnt = sc.dma("g", sdb[0][:], bass.AP(swd_in, fs * 44 * 128 * 128, [[128, 128], [128 * 128, 44], [1, 128]]),
                         key="d6", fence=True)
            sc.wait("t", "d6", cnt)
            sc.wait("t", "v", sdown_vm[fs % 2])
            for ic in range(44):
                sc.mm(pool[4 + (fs % 2)][:, :], sdb[0][:, ic * 128:(ic + 1) * 128], hsh[:, ic * 512:(ic + 1) * 512],
                      start=(ic == 0), stop=(ic == 43))
            tm = sc.mark("t")
            sc.wait("v", "t", tm)
            dst = xs[:, fs * TQ:(fs + 1) * TQ][:, tsl]
            sc.vop("tensor_add", dst, dst, pool[4 + (fs % 2)][:, :])
            sdown_vm[fs % 2] = sc.mark("v")

    wg_ld = {}
    wl_ld = {}
    prev_wg_t = {0: 0, 1: 0}
    prev_wl_t = {0: 0, 1: 0}
    sc.wait("g", "t", sc.cnt["t"])
    wg_ld[(0, 0)] = sc.dma("g", wgb[0][:], bass.AP(wgu_in, 0, [[256, 128], [128 * 256, 16], [1, 256]]), key="d0", fence=True)
    _fenced_d1 = [False]
    for e in range(8):
        sc.wait("t", "v")
        for th in range(2):
            sc.mm(pool[5][:, :], rowsel[0:8, e * 128:(e + 1) * 128], W_fm[0:8, th * 512:(th + 1) * 512], start=True, stop=True)
            tm = sc.mark("t")
            sc.wait("v", "t", tm)
            sc.vop("tensor_copy", wbc[:, th * 512:(th + 1) * 512], pool[5][:, :])
            sc.mark("v")
            sc.wait("t", "v")
        for ic in range(11):
            gi = e * 11 + ic
            slot = gi % 2
            ne, nic = (e, ic + 1) if ic + 1 < 11 else (e + 1, 0)
            if ne < 8:
                nslot = (gi + 1) % 2
                sc.wait("g", "t", prev_wg_t[nslot])
                fence = nslot == 1 and not _fenced_d1[0]
                if fence:
                    _fenced_d1[0] = True
                wg_ld[(ne, nic)] = sc.dma("g", wgb[nslot][:],
                                          bass.AP(wgu_in, (ne * 11 + nic) * 16 * 128 * 256, [[256, 128], [128 * 256, 16], [1, 256]]),
                                          key=f"d{nslot}", fence=fence)
            sc.wait("t", f"d{slot}", wg_ld[(e, ic)])
            tms = {}
            for th in range(2):
                for d in range(16):
                    sc.mm(pool[th][:, :], wgb[slot][:, d * 256:d * 256 + 128],
                          xln[:, d * TQ + th * 512: d * TQ + (th + 1) * 512], start=(d == 0), stop=(d == 15))
                    sc.mm(pool[2 + th][:, :], wgb[slot][:, d * 256 + 128:d * 256 + 256],
                          xln[:, d * TQ + th * 512: d * TQ + (th + 1) * 512], start=(d == 0), stop=(d == 15))
                tms[th] = sc.mark("t")
                sc.wait("s", "t", tms[th]); sc.wait("v", "t", tms[th])
                sl = slice(th * 512, (th + 1) * 512)
                sc.sop("activation", hg[:, sl], pool[th][:, :], AF.Silu)
                sc.vop("tensor_mul", hu[:, sl], pool[2 + th][:, :], wbc[:, sl])
                sc.mark("s"); sc.mark("v")
                sc.wait("v", "s")
                sc.vop("tensor_mul", he[:, ic * TQ + th * 512:ic * TQ + th * 512 + 512], hg[:, sl], hu[:, sl])
                sc.mark("v")
            prev_wg_t[slot] = tms[1]
            sc.wait("t", "v")
        for fs in range(16):
            gi2 = e * 16 + fs
            slot = gi2 % 2
            ne, nfs = (e, fs + 1) if fs + 1 < 16 else (e + 1, 0)
            if ne < 8:
                nslot = (gi2 + 1) % 2
                sc.wait("g", "t", prev_wl_t[nslot])
                wl_ld[(ne, nfs)] = sc.dma("g", wlb[nslot][:],
                                          bass.AP(wd_in, (ne * 16 + nfs) * 11 * 128 * 128, [[128, 128], [128 * 128, 11], [1, 128]]),
                                          key=f"d{2 + nslot}")
            if (e, fs) not in wl_ld:
                wl_ld[(e, fs)] = sc.dma("g", wlb[slot][:],
                                        bass.AP(wd_in, (e * 16 + fs) * 11 * 128 * 128, [[128, 128], [128 * 128, 11], [1, 128]]),
                                        key=f"d{2 + slot}")
            sc.wait("t", f"d{2 + slot}", wl_ld[(e, fs)])
            tms = {}
            for th in range(2):
                for ic in range(11):
                    sc.mm(pool[4 + th][:, :], wlb[slot][:, ic * 128:(ic + 1) * 128],
                          he[:, ic * TQ + th * 512: ic * TQ + th * 512 + 512],
                          start=(ic == 0), stop=(ic == 10))
                tms[th] = sc.mark("t")
                sc.wait("v", "t", tms[th])
                sl = slice(th * 512, (th + 1) * 512)
                dst = xs[:, fs * TQ:(fs + 1) * TQ][:, sl]
                sc.vop("tensor_add", dst, dst, pool[4 + th][:, :])
                sc.mark("v")
            prev_wl_t[slot] = tms[1]
            sc.wait("t", "v")

    vm = sc.mark("v")
    sc.wait("y", "v", vm)
    sc.dma("y", _fm_ap(out_out, 16, TQ), xs[:])
    sc._flush("y")
    nc.sync.wait_ge(sc.sem["yd"], sc.cnt["yd"])
    sc.close()
    return nc


def host_prep_F(of_all, x2_all, Wo, router_w, w_gate, w_up, w_down, sw_gate, sw_up, sw_down, sgw):
    wo1 = np.ascontiguousarray(Wo.reshape(8, 128, 16, 128).transpose(2, 0, 1, 3))
    rw = np.ascontiguousarray(router_w.T.reshape(16, 128, 8)).astype(np.float32)
    rowsel = np.zeros((8, 8, 128), np.float32)
    for e in range(8):
        rowsel[e, e, :] = 1.0
    identf = np.eye(128, dtype=np.float32)
    sgw_a = np.ascontiguousarray(sgw.reshape(16, 128).T).astype(np.float32)
    wgu = np.ascontiguousarray(np.concatenate(
        [w_gate.reshape(8, 16, 128, 11, 128), w_up.reshape(8, 16, 128, 11, 128)],
        axis=4).transpose(0, 3, 1, 2, 4)).reshape(8, 11, 16, 128, 256)
    wd = np.ascontiguousarray(w_down.reshape(8, 11, 128, 16, 128).transpose(0, 3, 1, 2, 4))
    swgu = np.ascontiguousarray(np.concatenate(
        [sw_gate.reshape(16, 128, 44, 128), sw_up.reshape(16, 128, 44, 128)],
        axis=3).transpose(2, 0, 1, 3)).reshape(44, 16, 128, 256)
    swd = np.ascontiguousarray(sw_down.reshape(44, 128, 16, 128).transpose(2, 0, 1, 3))
    of_by_bh = {}
    for core in range(NCORE):
        for i in range(2):
            u = 2 * core + i
            of_by_bh[(u // 8, u % 8)] = of_all[core][i]
    maps = []
    for core in range(NCORE):
        b, quarter = core // 4, core % 4
        sl = slice(quarter * TQ, (quarter + 1) * TQ)
        of_q = np.stack([of_by_bh[(b, h)][:, sl] for h in range(8)])
        maps.append({
            "of_in": np.ascontiguousarray(of_q), "x2": x2_all[core],
            "wo1": wo1, "rw": rw, "rowsel": rowsel, "identf": identf, "sgw": sgw_a,
            "wgu": wgu, "wd": wd, "swgu": swgu, "swd": swd,
        })
    return maps


def kernel(**inputs):
    inp = inputs
    mapsA = host_prep_A(np.asarray(inp["L0_attn_x"]), np.asarray(inp["L0_attn_wq"]),
                        np.asarray(inp["L0_attn_wk"]), np.asarray(inp["L0_attn_wv"]))
    resA = run_bass_kernel_spmd(build_launchA(), mapsA, core_ids=list(range(NCORE))).results
    q_all = np.stack([np.concatenate([np.asarray(resA[b * 4 + q]["q"]) for q in range(4)], axis=2) for b in range(2)])
    k_all = np.stack([np.concatenate([np.asarray(resA[b * 4 + q]["k"]) for q in range(4)], axis=2) for b in range(2)])
    v_all = np.stack([np.concatenate([np.asarray(resA[b * 4 + q]["v"]) for q in range(4)], axis=2) for b in range(2)])
    mapsB = host_prep_B(q_all, k_all, v_all)
    resB = run_bass_kernel_spmd(build_launchB(), mapsB, core_ids=list(range(NCORE))).results
    o_all = np.zeros((2, 16, 128, S), bf16)
    for u in range(32):
        o_all[u // 16, u % 16] = np.asarray(resB[u // 4]["o"])[u % 4]
    mapsC = host_prep_C(o_all, np.asarray(inp["L0_attn_x"]), np.asarray(inp["L0_attn_wo"]),
                        np.asarray(inp["L0_ffn_router_logits"]), np.asarray(inp["L0_ffn_gating_einsum"]),
                        np.asarray(inp["L0_ffn_linear"]), np.asarray(inp["L0_ffn_per_expert_scale"]),
                        np.asarray(inp["L0_ffn_router_scale"]))
    resC = run_bass_kernel_spmd(build_launchC(), mapsC, core_ids=list(range(NCORE))).results
    x2_all = [np.asarray(resC[c]["x2"]) for c in range(NCORE)]
    mapsD = host_prep_D(x2_all, np.asarray(inp["L1_attn_Wq"]), np.asarray(inp["L1_attn_Wk"]),
                        np.asarray(inp["L1_attn_Wv"]), np.asarray(inp["L1_attn_Wg"]),
                        np.asarray(inp["L1_attn_Wb"]), np.asarray(inp["L1_attn_Wa"]),
                        np.asarray(inp["L1_attn_A_log"]), np.asarray(inp["L1_attn_dt_bias"]),
                        np.asarray(inp["L1_attn_conv_q_w"]), np.asarray(inp["L1_attn_conv_k_w"]),
                        np.asarray(inp["L1_attn_conv_v_w"]))
    resD = run_bass_kernel_spmd(build_launchD(), mapsD, core_ids=list(range(NCORE))).results
    mapsE = host_prep_E([np.asarray(resD[c]["qn"]) for c in range(NCORE)],
                        [np.asarray(resD[c]["kn"]) for c in range(NCORE)],
                        [np.asarray(resD[c]["vv"]) for c in range(NCORE)],
                        [np.asarray(resD[c]["gate"]) for c in range(NCORE)],
                        [np.asarray(resD[c]["beta"]) for c in range(NCORE)],
                        [np.asarray(resD[c]["lc"]) for c in range(NCORE)],
                        np.asarray(inp["L1_attn_norm_w"]))
    resE = run_bass_kernel_spmd(build_launchE(), mapsE, core_ids=list(range(NCORE))).results
    mapsF = host_prep_F([np.asarray(resE[c]["of"]) for c in range(NCORE)], x2_all,
                        np.asarray(inp["L1_attn_Wo"]), np.asarray(inp["L1_ffn_router_w"]),
                        np.asarray(inp["L1_ffn_w_gate"]), np.asarray(inp["L1_ffn_w_up"]),
                        np.asarray(inp["L1_ffn_w_down"]), np.asarray(inp["L1_ffn_sw_gate"]),
                        np.asarray(inp["L1_ffn_sw_up"]), np.asarray(inp["L1_ffn_sw_down"]),
                        np.asarray(inp["L1_ffn_shared_gate_w"]))
    resF = run_bass_kernel_spmd(build_launchF(), mapsF, core_ids=list(range(NCORE))).results
    out = np.zeros((B, S, D), np.float32)
    for core in range(NCORE):
        b, q = core // 4, core % 4
        out[b, q * TQ:(q + 1) * TQ] = np.asarray(resF[core]["out"]).reshape(D, TQ).T
    return out


# version 04
# speedup vs baseline: 1.4009x; 1.1777x over previous
import numpy as np
import concourse.bass as bass
import concourse.mybir as mybir
from concourse.bass_utils import run_bass_kernel_spmd
import ml_dtypes

F32 = mybir.dt.float32
BF16 = mybir.dt.bfloat16
AF = mybir.ActivationFunctionType
ALU = mybir.AluOpType
AX = mybir.AxisListType

B, S, D = 2, 4096, 2048
NCORE = 8
TQ = S // 4
CHUNK = 64
H0, HKV0, DH = 16, 4, 128
bf16 = ml_dtypes.bfloat16


class Sched:

    ENGINES = ("t", "v", "s", "g", "y")

    def __init__(self, nc):
        self.nc = nc
        self.sem = {}
        self.cnt = {}
        self.last = {}
        self.unmarked = {}
        self.pending = {k: {} for k in self.ENGINES}
        self._ctx = []

    def open(self):
        nc = self.nc
        for k in self.ENGINES + ("gd", "yd", "d0", "d1", "d2", "d3", "d4", "d5", "d6", "d7"):
            cm = nc.semaphore("sem_" + k)
            self.sem[k] = cm.__enter__()
            self._ctx.append(cm)
            self.cnt[k] = 0
        blk = nc.Block()
        self.block = blk.__enter__()
        self._ctx.append(blk)
        return self

    def close(self):
        for cm in reversed(self._ctx):
            cm.__exit__(None, None, None)

    def _engine(self, k):
        nc = self.nc
        return {"t": nc.tensor, "v": nc.vector, "s": nc.scalar,
                "g": nc.gpsimd, "y": nc.sync}[k]

    def _flush(self, k):
        e = self._engine(k)
        for sk, val in sorted(self.pending[k].items()):
            if val > 0:
                e.wait_ge(self.sem[sk], val)
        self.pending[k] = {}

    def _record(self, k, inst):
        self.last[k] = inst
        self.unmarked[k] = self.unmarked.get(k, 0) + 1

    def mark(self, k):
        if self.unmarked.get(k, 0) > 0:
            self.last[k].then_inc(self.sem[k], 1)
            self.cnt[k] += 1
            self.unmarked[k] = 0
        return self.cnt[k]

    def wait(self, waiter, prod, val=None):
        if val is None:
            val = self.cnt[prod]
        if val <= 0:
            return
        cur = self.pending[waiter].get(prod, 0)
        self.pending[waiter][prod] = max(cur, val)

    def barrier(self, engines=("t", "v", "s")):
        for k in engines:
            self.mark(k)
        allk = list(engines) + ["gd", "yd", "d0", "d1", "d2", "d3", "d4", "d5", "d6", "d7"]
        for k in engines:
            for p in allk:
                if p == k:
                    continue
                self.wait(k, p, self.cnt[p])

    def dma(self, q, out, in_, key=None, fence=False, **kw):
        if fence:
            semk = key or (q + "d")
            self.wait(q, semk, self.cnt[semk])
        self._flush(q)
        inst = self._engine(q).dma_start(out=out, in_=in_, **kw)
        semk = key or (q + "d")
        inst.then_inc(self.sem[semk], 16)
        self.cnt[semk] += 16
        return self.cnt[semk]

    def mm(self, out, lhsT, rhs, start, stop):
        self._flush("t")
        inst = self.nc.tensor.matmul(out, lhsT, rhs, start=start, stop=stop)
        self._record("t", inst)
        return inst

    def trans(self, out, in_, ident):
        self._flush("t")
        inst = self.nc.tensor.transpose(out, in_, ident)
        self._record("t", inst)
        return inst

    def __init2__(self):
        pass

    def _selfsync(self, k):
        if not getattr(self, "SELFSYNC", True):
            return
        if self.unmarked.get(k, 0) > 0:
            self.mark(k)
        self.pending[k][k] = max(self.pending[k].get(k, 0), self.cnt[k])

    def vop(self, name, *a, **kw):
        self._selfsync("v")
        self._flush("v")
        inst = getattr(self.nc.vector, name)(*a, **kw)
        self._record("v", inst)
        return inst

    def sop(self, name, *a, **kw):
        self._selfsync("s")
        self._flush("s")
        inst = getattr(self.nc.scalar, name)(*a, **kw)
        self._record("s", inst)
        return inst

    def gop(self, name, *a, **kw):
        self._selfsync("g")
        self._flush("g")
        inst = getattr(self.nc.gpsimd, name)(*a, **kw)
        self._record("g", inst)
        return inst

    def init_consts(self):
        self._cf = self.nc.alloc_sbuf_tensor("constsF", [128, 16], F32)
        self._cb = self.nc.alloc_sbuf_tensor("constsB", [128, 16], BF16)
        self._cmap = {}
        self._cn = {F32: 0, BF16: 0}

    def const(self, value, dtype=F32, parts=128):
        key = (dtype, float(value))
        if key not in self._cmap:
            t = self._cf if dtype == F32 else self._cb
            i = self._cn[dtype]
            self._cn[dtype] += 1
            self.vop("memset", t[:, i:i + 1], float(value))
            self._cmap[key] = t[:, i:i + 1]
        return self._cmap[key][0:parts, :]


def emit_ln(sc, xs, xln, tmp, stats, pool, ones_bf, ones1_f32, nchunks, T, eps=1e-5):
    assert T == 1024
    for c in range(nchunks):
        xc = xs[:, c * T:(c + 1) * T]
        sc.sop("activation", tmp[:, T:2 * T], xc, AF.Square)
        sc.vop("tensor_copy", tmp[:, 0:T], xc)
        sc.mark("s"); sc.mark("v")
        sc.wait("t", "s"); sc.wait("t", "v")
        for h in range(2):
            sl = slice(h * 512, (h + 1) * 512)
            sc.mm(pool[h][0:1, :], ones_bf[:, 0:1], tmp[:, 0:T][:, sl], start=(c == 0), stop=(c == nchunks - 1))
            sc.mm(pool[2 + h][0:1, :], ones_bf[:, 0:1], tmp[:, T:2 * T][:, sl], start=(c == 0), stop=(c == nchunks - 1))
        sc.mark("t")
        sc.wait("s", "t"); sc.wait("v", "t")
    sc.mark("t")
    sc.wait("v", "t")
    sumx = stats[0:1, 0:T]
    var_t = stats[0:1, T:2 * T]
    rstd = stats[0:1, 2 * T:3 * T]
    mr = stats[0:1, 3 * T:4 * T]
    for h in range(2):
        sl = slice(h * 512, (h + 1) * 512)
        sc.vop("tensor_copy", sumx[:, sl], pool[h][0:1, :])
        sc.vop("tensor_copy", var_t[:, sl], pool[2 + h][0:1, :])
    sc.vop("scalar_tensor_tensor", mr, sumx, 1.0, sumx, ALU.mult, ALU.mult)
    sc.vop("scalar_tensor_tensor", var_t, mr, -1.0 / D, var_t, ALU.mult, ALU.add)
    sc.mark("v")
    sc.wait("s", "v")
    sc.sop("activation", rstd, var_t, AF.Ln, scale=1.0 / D, bias=sc.const(eps, parts=1))
    sc.sop("activation", rstd, rstd, AF.Exp, scale=-0.5)
    sc.mark("s")
    sc.wait("v", "s")
    sc.vop("scalar_tensor_tensor", mr, sumx, 1.0 / D, rstd, ALU.mult, ALU.mult)
    sc.mark("v")
    sc.wait("t", "v")
    for h in range(2):
        sc.mm(pool[4 + h][:, :], ones1_f32[0:1, :], rstd[:, h * 512:(h + 1) * 512], start=True, stop=True)
        sc.mm(pool[6 + h][:, :], ones1_f32[0:1, :], mr[:, h * 512:(h + 1) * 512], start=True, stop=True)
    sc.mark("t")
    sc.wait("v", "t")
    ftmp = stats[0:1, 0:1]
    for c in range(nchunks):
        for h in range(2):
            sl = slice(h * 512, (h + 1) * 512)
            xc = xs[:, c * T:(c + 1) * T][:, sl]
            oc = xln[:, c * T:(c + 1) * T][:, sl]
            sc.vop("tensor_mul", tmp[:, 0:512], xc, pool[4 + h][:, :])
            sc.vop("tensor_sub", oc, tmp[:, 0:512], pool[6 + h][:, :])
    sc.mark("v")


def _host_fm(a2d, nch):
    T, F = a2d.shape
    assert F == nch * 128
    return np.ascontiguousarray(a2d.T.reshape(nch, 128, T))


def _fm_ap(h, nch, T):
    return bass.AP(h, 0, [[T, 128], [128 * T, nch], [1, T]])


def build_launchA():
    nc = bass.Bass("TRN2", target_bir_lowering=False)
    x_in = nc.dram_tensor("x", [16, 128, TQ], F32, kind="ExternalInput")
    w_in = nc.dram_tensor("wqkv", [8, 16, 128, 384], F32, kind="ExternalInput").ap()
    cos_in = nc.dram_tensor("cos", [128, TQ], F32, kind="ExternalInput").ap()
    sin_in = nc.dram_tensor("sin", [128, TQ], F32, kind="ExternalInput").ap()
    rotp_in = nc.dram_tensor("rotp", [128, 128], F32, kind="ExternalInput").ap()
    q_out = nc.dram_tensor("q", [16, 128, TQ], BF16, kind="ExternalOutput").ap()
    k_out = nc.dram_tensor("k", [4, 128, TQ], BF16, kind="ExternalOutput").ap()
    v_out = nc.dram_tensor("v", [4, 128, TQ], BF16, kind="ExternalOutput").ap()

    xs = nc.alloc_sbuf_tensor("xs", [128, 16 * TQ], F32)
    xln = nc.alloc_sbuf_tensor("xln", [128, 16 * TQ], BF16)
    tmp = nc.alloc_sbuf_tensor("tmp", [128, 2 * TQ], BF16)
    stats = nc.alloc_sbuf_tensor("stats", [1, 4 * TQ], F32)
    cos_t = nc.alloc_sbuf_tensor("cos_t", [128, TQ], BF16)
    sin_t = nc.alloc_sbuf_tensor("sin_t", [128, TQ], BF16)
    rotp = nc.alloc_sbuf_tensor("rotp_sb", [128, 128], BF16)
    ones_bf = nc.alloc_sbuf_tensor("ones_bf", [128, 1], BF16)
    ones1_f32 = nc.alloc_sbuf_tensor("ones1_f32", [1, 128], F32)
    wbuf = [nc.alloc_sbuf_tensor(f"wbuf{i}", [128, 384], BF16) for i in range(2)]
    out_sb = nc.alloc_sbuf_tensor("out_sb", [128, 3 * TQ], BF16)

    pool = [nc.alloc_psum_tensor(f"pool{i}", [128, 512], F32) for i in range(8)]

    sc = Sched(nc).open()
    sc.init_consts()
    sc.const(1e-5, parts=1)

    sc.dma("y", xs[:], _fm_ap(x_in, 16, TQ))
    sc.dma("g", cos_t[:], cos_in)
    sc.dma("g", sin_t[:], sin_in)
    sc.dma("g", rotp[:], rotp_in)
    sc.vop("memset", ones_bf[:], 1.0)
    sc.vop("memset", ones1_f32[:], 1.0)
    sc.mark("v")
    sc.wait("t", "v")
    sc.wait("s", "v")
    sc.wait("s", "yd", 16)
    sc.wait("v", "yd", 16)

    emit_ln(sc, xs[:], xln[:], tmp[:], stats[:], pool, ones_bf, ones1_f32, 16, TQ)

    sc.barrier()
    sc.wait("t", "gd", sc.cnt["gd"])
    for grp in range(8):
        for d in range(16):
            buf = wbuf[d % 2]
            c = sc.dma("g", buf[:], w_in[grp, d], key=f"d{d % 2}")
            sc.wait("t", f"d{d % 2}", c)
            for oc3 in range(3):
                for th in range(2):
                    sc.mm(pool[oc3 * 2 + th][:, :],
                          buf[:, oc3 * 128:(oc3 + 1) * 128],
                          xln[:, d * TQ + th * 512: d * TQ + (th + 1) * 512],
                          start=(d == 0), stop=(d == 15))
            if d >= 1:
                sc.mark("t")
                sc.wait("g", "t")
        tmark = sc.mark("t")
        sc.wait("v", "t", tmark)
        sc.wait("s", "t", tmark)
        for oc3 in range(3):
            oc = grp * 3 + oc3
            dst = out_sb[:, oc3 * TQ:(oc3 + 1) * TQ]
            p0 = pool[oc3 * 2][:, :]
            p1 = pool[oc3 * 2 + 1][:, :]
            if oc < 20:
                sc.vop("tensor_copy", tmp[:, 0:512], p0)
                sc.vop("tensor_copy", tmp[:, 512:1024], p1)
                sc.mark("v")
                sc.wait("t", "v")
                for th in range(2):
                    sc.mm(pool[6 + th][:, :], rotp[:, :], tmp[:, th * 512:(th + 1) * 512], start=True, stop=True)
                sc.mark("t")
                sc.wait("v", "t")
                sc.wait("s", "t")
                sc.vop("tensor_mul", dst, tmp[:, 0:TQ], cos_t[:, :])
                for th in range(2):
                    sc.sop("activation", tmp[:, TQ + th * 512:TQ + (th + 1) * 512], pool[6 + th][:, :], AF.Copy)
                sc.mark("s")
                sc.wait("v", "s")
                sc.vop("tensor_mul", tmp[:, TQ:2 * TQ], tmp[:, TQ:2 * TQ], sin_t[:, :])
                sc.vop("tensor_add", dst, dst, tmp[:, TQ:2 * TQ])
                sc.mark("v")
                sc.wait("t", "v")
            else:
                sc.vop("tensor_copy", dst[:, 0:512], p0)
                sc.vop("tensor_copy", dst[:, 512:1024], p1)
        vmark = sc.mark("v")
        sc.wait("t", "v", vmark)
        sc.wait("y", "v", vmark)
        for oc3 in range(3):
            oc = grp * 3 + oc3
            src = out_sb[:, oc3 * TQ:(oc3 + 1) * TQ]
            if oc < 16:
                sc.dma("y", q_out[oc], src)
            elif oc < 20:
                sc.dma("y", k_out[oc - 16], src)
            else:
                sc.dma("y", v_out[oc - 20], src)
        sc.wait("v", "yd", sc.cnt["yd"])

    sc._flush("y")
    nc.sync.wait_ge(sc.sem["yd"], sc.cnt["yd"])
    sc.close()
    return nc


def host_prep_A(x_full, wq, wk, wv):
    wcat = np.concatenate([wq.T, wk.T, wv.T], axis=1)
    w_arr = np.ascontiguousarray(
        wcat.reshape(16, 128, 8, 3, 128).transpose(2, 0, 1, 3, 4)).reshape(8, 16, 128, 384)
    d = np.arange(64, dtype=np.float64)
    inv = 1.0 / (10000.0 ** (2 * d / 128.0))
    pos = np.arange(S, dtype=np.float64)
    fr = pos[:, None] * inv[None, :]
    cos = np.cos(fr).astype(np.float32)
    sin = np.sin(fr).astype(np.float32)
    cos_full = np.ascontiguousarray(np.concatenate([cos, cos], axis=1).T)
    sin_full = np.ascontiguousarray(np.concatenate([sin, sin], axis=1).T)
    rotp = np.zeros((128, 128), np.float32)
    for r in range(64):
        rotp[r + 64, r] = -1.0
    for r in range(64, 128):
        rotp[r - 64, r] = 1.0
    maps = []
    for core in range(NCORE):
        b, quarter = core // 4, core % 4
        sl = slice(quarter * TQ, (quarter + 1) * TQ)
        maps.append({
            "x": _host_fm(x_full[b, sl], 16).astype(np.float32),
            "wqkv": w_arr,
            "cos": np.ascontiguousarray(cos_full[:, sl]),
            "sin": np.ascontiguousarray(sin_full[:, sl]),
            "rotp": rotp,
        })
    return maps


def build_launchB():
    nc = bass.Bass("TRN2", target_bir_lowering=False)
    NU = 4
    qf_in = nc.dram_tensor("qf", [NU, 128, S], BF16, kind="ExternalInput")
    kf_in = nc.dram_tensor("kf", [NU, 128, S], BF16, kind="ExternalInput")
    ktm_in = nc.dram_tensor("ktm", [NU, 64, 128, 128], BF16, kind="ExternalInput")
    vtm_in = nc.dram_tensor("vtm", [NU, 64, 128, 128], BF16, kind="ExternalInput")
    diagT_in = nc.dram_tensor("diagT", [NU, 64, 64], F32, kind="ExternalInput")
    qdecT_in = nc.dram_tensor("qdecT", [NU, 128, 64], BF16, kind="ExternalInput")
    kdec_in = nc.dram_tensor("kdec", [NU, 128, 1], F32, kind="ExternalInput")
    blkc_in = nc.dram_tensor("blkc", [NU, 128, 1], F32, kind="ExternalInput")
    o_out = nc.dram_tensor("o", [NU, 128, S], BF16, kind="ExternalOutput")

    NC64 = S // 64
    qf = [nc.alloc_sbuf_tensor(f"qf{i}", [128, S], BF16) for i in range(2)]
    kf = [nc.alloc_sbuf_tensor(f"kf{i}", [128, S], BF16) for i in range(2)]
    ktm = [nc.alloc_sbuf_tensor(f"ktm{i}", [128, 64 * 128], BF16) for i in range(2)]
    vtm = [nc.alloc_sbuf_tensor(f"vtm{i}", [128, 64 * 128], BF16) for i in range(2)]
    diagT = [nc.alloc_sbuf_tensor(f"diagT{i}", [64, 64], F32) for i in range(2)]
    qdecT = [nc.alloc_sbuf_tensor(f"qdecT{i}", [128, 64], BF16) for i in range(2)]
    kdec = [nc.alloc_sbuf_tensor(f"kdec{i}", [128, 1], F32) for i in range(2)]
    blkc = [nc.alloc_sbuf_tensor(f"blkc{i}", [128, 1], F32) for i in range(2)]
    o_sb = [nc.alloc_sbuf_tensor(f"osb{i}", [128, S], BF16) for i in range(2)]

    kv = nc.alloc_sbuf_tensor("kv", [128, 128], F32)
    kv_bf = nc.alloc_sbuf_tensor("kv_bf", [128, 128], BF16)
    scT = nc.alloc_sbuf_tensor("scT", [128, 64], BF16)
    qt = nc.alloc_sbuf_tensor("qt", [128, 64], BF16)
    vt = nc.alloc_sbuf_tensor("vt", [128, 128], BF16)

    pool = [nc.alloc_psum_tensor(f"pool{i}", [128, 512], F32) for i in range(4)]

    sc = Sched(nc).open()
    sc.init_consts()
    sc.vop("memset", scT[:], 0.0)
    sc.mark("v")

    def load_unit(u, slot):
        ks = f"d{slot}"
        sc.dma("y", qf[slot][:], qf_in.ap()[u], key=ks)
        sc.dma("y", kf[slot][:], kf_in.ap()[u], key=ks)
        sc.dma("y", ktm[slot][:], bass.AP(ktm_in, u * 64 * 128 * 128, [[128, 128], [128 * 128, 64], [1, 128]]), key=ks)
        sc.dma("y", vtm[slot][:], bass.AP(vtm_in, u * 64 * 128 * 128, [[128, 128], [128 * 128, 64], [1, 128]]), key=ks)
        sc.dma("g", diagT[slot][:], diagT_in.ap()[u], key=f"d{2 + slot}")
        sc.dma("g", qdecT[slot][:], qdecT_in.ap()[u], key=f"d{2 + slot}")
        sc.dma("g", kdec[slot][:], kdec_in.ap()[u], key=f"d{2 + slot}")
        sc.dma("g", blkc[slot][:], blkc_in.ap()[u], key=f"d{2 + slot}")
        return sc.cnt[ks], sc.cnt[f"d{2 + slot}"]

    loads = {}
    done_v = {}
    loads[0] = load_unit(0, 0)
    for u in range(NU):
        slot = u % 2
        if u + 1 < NU:
            if u >= 1:
                sc.wait("g", "v", done_v[u - 1])
            loads[u + 1] = load_unit(u + 1, 1 - slot)
        ydc, gdc = loads[u]
        ks, kg = f"d{slot}", f"d{2 + slot}"
        sc.wait("t", ks, ydc); sc.wait("t", kg, gdc)
        sc.wait("v", ks, ydc); sc.wait("v", kg, gdc)
        sc.vop("memset", kv[:], 0.0)
        sc.vop("memset", kv_bf[:], 0.0)
        sc.mark("v")
        sc.wait("t", "v")
        for c in range(NC64):
            t0, t1 = c * 64, (c + 1) * 64
            sc.mm(pool[0][0:64, 0:64], kf[slot][:, t0:t1], qf[slot][:, t0:t1], start=True, stop=True)
            sc.mark("t")
            sc.wait("v", "t")
            sc.vop("tensor_mul", scT[0:64, :], pool[0][0:64, 0:64], diagT[slot][:, :])
            sc.vop("tensor_mul", qt[:], qf[slot][:, t0:t1], qdecT[slot][:, :])
            sc.vop("tensor_scalar_mul", vt[:], vtm[slot][:, c * 128:(c + 1) * 128], kdec[slot][:, 0:1])
            sc.mark("v")
            sc.wait("t", "v")
            sc.mm(pool[1][:, 0:64], vtm[slot][:, c * 128:(c + 1) * 128], scT[:, :], start=True, stop=False)
            sc.mm(pool[1][:, 0:64], kv_bf[:], qt[:], start=False, stop=True)
            sc.mm(pool[2][:, 0:128], ktm[slot][:, c * 128:(c + 1) * 128], vt[:], start=True, stop=True)
            sc.mark("t")
            sc.wait("v", "t")
            sc.vop("scalar_tensor_tensor", kv[:], kv[:], blkc[slot][:, 0:1], pool[2][:, 0:128], ALU.mult, ALU.add)
            sc.vop("tensor_copy", o_sb[slot][:, t0:t1], pool[1][:, 0:64])
            sc.mark("v")
            sc.wait("s", "v")
            sc.sop("activation", kv_bf[:], kv[:], AF.Copy)
            sc.mark("s")
            sc.wait("t", "s")
            sc.wait("t", "v")
        vm = sc.mark("v")
        done_v[u] = vm
        sc.wait("y", "v", vm)
        sc.dma("y", o_out.ap()[u], o_sb[slot][:])
        sc.wait("y", "yd", sc.cnt["yd"])

    sc._flush("y")
    nc.sync.wait_ge(sc.sem["yd"], sc.cnt["yd"])
    sc.close()
    return nc


def host_prep_B(q_all, k_all, v_all):
    lam_all = np.exp2(-8.0 * np.arange(1, 17) / 16.0).astype(np.float64)
    r = np.arange(64, dtype=np.float64)
    maps = []
    for core in range(NCORE):
        units = [4 * core + i for i in range(4)]
        qf = np.stack([q_all[u // 16, u % 16] for u in units])
        kf = np.stack([k_all[u // 16, (u % 16) // 4] for u in units])
        ktm = np.zeros((4, 64, 128, 128), bf16)
        vtm = np.zeros((4, 64, 128, 128), bf16)
        diagT = np.zeros((4, 64, 64), np.float32)
        qdecT = np.zeros((4, 128, 64), np.float32)
        kdec = np.zeros((4, 128, 1), np.float32)
        blkc = np.zeros((4, 128, 1), np.float32)
        for i, u in enumerate(units):
            b, h = u // 16, u % 16
            lam = lam_all[h]
            kfm = k_all[b, h // 4]
            vfm = v_all[b, h // 4]
            ktm[i, :, 0:64, :] = kfm.T.reshape(64, 64, 128)
            vtm[i, :, 0:64, :] = vfm.T.reshape(64, 64, 128)
            diff = r[:, None] - r[None, :]
            diagT[i] = np.where(diff >= 0, np.exp(-lam * diff), 0.0).T.astype(np.float32)
            qdecT[i] = np.broadcast_to(np.exp(-lam * (r + 1.0)).astype(np.float32), (128, 64))
            kdec[i, 0:64, 0] = np.exp(-lam * (63.0 - r)).astype(np.float32)
            blkc[i, :, 0] = np.exp(-lam * 64.0)
        maps.append({
            "qf": qf, "kf": kf, "ktm": ktm, "vtm": vtm,
            "diagT": diagT, "qdecT": qdecT.astype(bf16), "kdec": kdec, "blkc": blkc,
        })
    return maps


def build_launchC():
    nc = bass.Bass("TRN2", target_bir_lowering=False)
    o_in = nc.dram_tensor("o_in", [16, 128, TQ], BF16, kind="ExternalInput")
    x_in = nc.dram_tensor("x", [16, 128, TQ], F32, kind="ExternalInput")
    woT_in = nc.dram_tensor("woT", [16, 16, 128, 128], F32, kind="ExternalInput")
    rl_in = nc.dram_tensor("rl", [16, 128, 8], F32, kind="ExternalInput")
    rsc_in = nc.dram_tensor("rscale", [128, 16], F32, kind="ExternalInput")
    pes_in = nc.dram_tensor("pes", [8, 1], F32, kind="ExternalInput")
    rowsel_in = nc.dram_tensor("rowsel", [8, 8, 128], F32, kind="ExternalInput")
    ident_in = nc.dram_tensor("identf", [128, 128], F32, kind="ExternalInput")
    wg_in = nc.dram_tensor("wgate", [8, 16, 16, 128, 256], F32, kind="ExternalInput")
    wl_in = nc.dram_tensor("wlin", [8, 16, 16, 128, 128], F32, kind="ExternalInput")
    x2_out = nc.dram_tensor("x2", [16, 128, TQ], F32, kind="ExternalOutput")
    dbgw_out = nc.dram_tensor("dbgw", [8, TQ], F32, kind="ExternalOutput")
    dbgx1_out = nc.dram_tensor("dbgx1", [16, 128, TQ], F32, kind="ExternalOutput")
    dbglg_out = nc.dram_tensor("dbglg", [8, 128, 8], F32, kind="ExternalOutput")

    xs = nc.alloc_sbuf_tensor("xs", [128, 16 * TQ], F32)
    o_sb = nc.alloc_sbuf_tensor("o_sb", [128, 16 * TQ], BF16)
    xln = nc.alloc_sbuf_tensor("xln", [128, 16 * TQ], BF16)
    tmp = nc.alloc_sbuf_tensor("tmp", [128, 2 * TQ], BF16)
    stats = nc.alloc_sbuf_tensor("stats", [1, 4 * TQ], F32)
    ones_bf = nc.alloc_sbuf_tensor("ones_bf", [128, 1], BF16)
    ones1_f32 = nc.alloc_sbuf_tensor("ones1_f32", [1, 128], F32)
    identf = nc.alloc_sbuf_tensor("identf_sb", [128, 128], F32)
    rlp = nc.alloc_sbuf_tensor("rlp", [128, 16 * 8], BF16)
    rl_sb = nc.alloc_sbuf_tensor("rl_sb", [128, 16 * 8], F32)
    rsc_sb = nc.alloc_sbuf_tensor("rsc_sb", [128, 16], F32)
    pes_sb = nc.alloc_sbuf_tensor("pes_sb", [8, 1], F32)
    rowsel = nc.alloc_sbuf_tensor("rowsel_sb", [8, 8 * 128], BF16)
    lg = nc.alloc_sbuf_tensor("lg", [128, 8], F32)
    m1 = nc.alloc_sbuf_tensor("m1", [128, 4], F32)
    eqm = nc.alloc_sbuf_tensor("eqm", [128, 8], F32)
    ind = nc.alloc_sbuf_tensor("ind", [128, 8], F32)
    ee = nc.alloc_sbuf_tensor("ee", [128, 8], F32)
    w_tm = nc.alloc_sbuf_tensor("w_tm", [128, 8], F32)
    rs_sb = nc.alloc_sbuf_tensor("rs_sb", [128, 1], F32)
    W_fm = nc.alloc_sbuf_tensor("W_fm", [8, TQ], BF16)
    W_dbg = nc.alloc_sbuf_tensor("W_dbg", [8, TQ], F32)
    wbc = nc.alloc_sbuf_tensor("wbc", [128, TQ], BF16)
    hg = nc.alloc_sbuf_tensor("hg", [128, TQ], BF16)
    hu = nc.alloc_sbuf_tensor("hu", [128, TQ], BF16)
    wov = [nc.alloc_sbuf_tensor(f"wov{i}", [128, 16 * 128], BF16) for i in range(2)]
    wgb = [nc.alloc_sbuf_tensor(f"wgb{i}", [128, 16 * 256], BF16) for i in range(2)]
    wlb = [nc.alloc_sbuf_tensor(f"wlb{i}", [128, 16 * 128], BF16) for i in range(2)]

    pool = [nc.alloc_psum_tensor(f"pool{i}", [128, 512], F32) for i in range(8)]

    sc = Sched(nc).open()
    sc.init_consts()
    sc.const(1e-5, parts=1)
    sc.const(1e-6, parts=1)
    sc.const(-0.5 * float(np.log(2048.0)))
    sc.vop("memset", ones_bf[:], 1.0)
    sc.vop("memset", ones1_f32[:], 1.0)
    sc.mark("v")

    sc.dma("y", xs[:], _fm_ap(x_in, 16, TQ))
    sc.dma("y", o_sb[:], _fm_ap(o_in, 16, TQ))
    ld0 = sc.cnt["yd"]
    sc.dma("g", rl_sb[:], bass.AP(rl_in, 0, [[8, 128], [128 * 8, 16], [1, 8]]))
    sc.dma("g", rsc_sb[:], rsc_in.ap())
    sc.dma("g", pes_sb[:], pes_in.ap())
    sc.dma("g", rowsel[:], bass.AP(rowsel_in, 0, [[128, 8], [8 * 128, 8], [1, 128]]))
    sc.dma("g", identf[:], ident_in.ap())
    gld0 = sc.cnt["gd"]

    wo_ld = {}
    wo_ld[0] = sc.dma("g", wov[0][:], bass.AP(woT_in, 0, [[128, 128], [128 * 128, 16], [1, 128]]), key="d0")
    for ds in range(16):
        slot = ds % 2
        if ds + 1 < 16:
            if ds >= 1:
                sc.wait("g", "t", sc.cnt["t"])
            wo_ld[ds + 1] = sc.dma("g", wov[1 - slot][:],
                                   bass.AP(woT_in, (ds + 1) * 16 * 128 * 128, [[128, 128], [128 * 128, 16], [1, 128]]),
                                   key=f"d{1 - slot}")
        sc.wait("t", f"d{slot}", wo_ld[ds])
        sc.wait("t", "yd", ld0)
        sc.wait("v", "yd", ld0)
        for th in range(2):
            for ec in range(16):
                sc.mm(pool[th][:, :], wov[slot][:, ec * 128:(ec + 1) * 128],
                      o_sb[:, ec * TQ + th * 512: ec * TQ + (th + 1) * 512],
                      start=(ec == 0), stop=(ec == 15))
            tm = sc.mark("t")
            sc.wait("v", "t", tm)
            sl = slice(th * 512, (th + 1) * 512)
            sc.vop("tensor_add", xs[:, ds * TQ:(ds + 1) * TQ][:, sl], xs[:, ds * TQ:(ds + 1) * TQ][:, sl], pool[th][:, :])
            sc.mark("v")
        sc.wait("t", "v")

    sc.barrier()
    sc.wait("y", "v", sc.cnt["v"])
    sc.dma("y", _fm_ap(dbgx1_out, 16, TQ), xs[:])
    sc.wait("v", "yd", sc.cnt["yd"])
    emit_ln(sc, xs[:], xln[:], tmp[:], stats[:], pool, ones_bf, ones1_f32, 16, TQ)
    sc.barrier()

    sc.wait("v", "gd", gld0)
    sc.wait("t", "gd", gld0)
    sc.wait("s", "gd", gld0)
    for d in range(16):
        sc.vop("tensor_scalar_mul", rlp[:, d * 8:(d + 1) * 8], rl_sb[:, d * 8:(d + 1) * 8], rsc_sb[:, d:d + 1])
    sc.mark("v")
    sc.wait("t", "v"); sc.wait("s", "v")
    for tt in range(8):
        tsl = slice(tt * 128, (tt + 1) * 128)
        for d in range(16):
            sc.sop("activation", tmp[:, d * 128:(d + 1) * 128], xln[:, d * TQ:(d + 1) * TQ][:, tsl], AF.Square)
        sm = sc.mark("s")
        sc.wait("t", "s", sm)
        for d in range(16):
            sc.mm(pool[2][:, 0:1], tmp[:, d * 128:(d + 1) * 128], ones_bf[:, 0:1], start=(d == 0), stop=(d == 15))
            sc.mm(pool[3][:, 0:8], xln[:, d * TQ:(d + 1) * TQ][:, tsl], rlp[:, d * 8:(d + 1) * 8], start=(d == 0), stop=(d == 15))
        tm = sc.mark("t")
        sc.wait("s", "t", tm); sc.wait("v", "t", tm)
        sc.sop("activation", rs_sb[:], pool[2][:, 0:1], AF.Ln, scale=1.0 / D, bias=sc.const(1e-6))
        sc.sop("activation", rs_sb[:], rs_sb[:], AF.Exp, scale=-0.5,
               bias=sc.const(-0.5 * float(np.log(2048.0))))
        sc.mark("s")
        sc.wait("v", "s")
        sc.vop("tensor_scalar_mul", lg[:], pool[3][:, 0:8], rs_sb[:, 0:1])
        sc.mark("v"); sc.wait("y", "v", sc.cnt["v"])
        sc.dma("y", dbglg_out.ap()[tt], lg[:])
        sc.wait("v", "yd", sc.cnt["yd"])
        sc.vop("reduce_max", m1[:, 0:1], lg[:], axis=AX.X)
        sc.vop("tensor_scalar_mul", m1[:, 1:2], m1[:, 0:1], -1.0)
        sc.vop("tensor_single_scalar", eqm[:], lg[:], m1[:, 0:1], ALU.is_ge)
        sc.vop("scalar_tensor_tensor", eqm[:], eqm[:], -1e30, lg[:], ALU.mult, ALU.add)
        sc.vop("reduce_max", m1[:, 2:3], eqm[:], axis=AX.X)
        sc.vop("tensor_single_scalar", ind[:], lg[:], m1[:, 2:3], ALU.is_ge)
        sc.mark("v")
        sc.wait("s", "v")
        sc.sop("activation", ee[:], lg[:], AF.Exp, bias=m1[:, 1:2])
        sc.mark("s")
        sc.wait("v", "s")
        sc.vop("tensor_mul", ee[:], ee[:], ind[:])
        sc.vop("reduce_sum", m1[:, 3:4], ee[:], axis=AX.X)
        sc.vop("reciprocal", m1[:, 3:4], m1[:, 3:4])
        sc.vop("tensor_scalar_mul", w_tm[:], ee[:], m1[:, 3:4])
        sc.mark("v")
        sc.wait("t", "v")
        sc.trans(pool[5][0:8, 0:128], w_tm[:, 0:8], identf[:, :])
        tm = sc.mark("t")
        sc.wait("v", "t", tm)
        sc.vop("tensor_scalar_mul", W_fm[0:8, tsl], pool[5][0:8, 0:128], pes_sb[:, 0:1])
        sc.vop("tensor_scalar_mul", W_dbg[0:8, tsl], pool[5][0:8, 0:128], pes_sb[:, 0:1])
        sc.mark("v")
        sc.wait("t", "v")

    vm = sc.mark("v")
    sc.wait("y", "v", vm)
    sc.dma("y", dbgw_out.ap(), W_dbg[0:8, :])

    wg_ld = {}
    wl_ld = {}
    wg_ld[(0, 0)] = sc.dma("g", wgb[0][:], bass.AP(wg_in, 0, [[256, 128], [128 * 256, 16], [1, 256]]), key="d4")
    prev_t_for_slot = {0: 0, 1: 0}
    prev_wl_t = {0: 0, 1: 0}
    for e in range(8):
        sc.wait("t", "v")
        for th in range(2):
            sc.mm(pool[5][:, :], rowsel[0:8, e * 128:(e + 1) * 128], W_fm[0:8, th * 512:(th + 1) * 512], start=True, stop=True)
            tm = sc.mark("t")
            sc.wait("v", "t", tm)
            sc.vop("tensor_copy", wbc[:, th * 512:(th + 1) * 512], pool[5][:, :])
            sc.mark("v")
            sc.wait("t", "v")
        for hc in range(16):
            slot = (e * 16 + hc) % 2
            nslot = 1 - slot
            ne, nhc = (e, hc + 1) if hc + 1 < 16 else (e + 1, 0)
            if ne < 8:
                sc.wait("g", "t", prev_t_for_slot[nslot])
                wg_ld[(ne, nhc)] = sc.dma("g", wgb[nslot][:],
                                          bass.AP(wg_in, (ne * 16 + nhc) * 16 * 128 * 256, [[256, 128], [128 * 256, 16], [1, 256]]),
                                          key=f"d{4 + nslot}")
            sc.wait("t", f"d{4 + slot}", wg_ld[(e, hc)])
            tms = {}
            for th in range(2):
                for d in range(16):
                    sc.mm(pool[th][:, :], wgb[slot][:, d * 256:d * 256 + 128],
                          xln[:, d * TQ + th * 512: d * TQ + (th + 1) * 512], start=(d == 0), stop=(d == 15))
                    sc.mm(pool[2 + th][:, :], wgb[slot][:, d * 256 + 128:d * 256 + 256],
                          xln[:, d * TQ + th * 512: d * TQ + (th + 1) * 512], start=(d == 0), stop=(d == 15))
                tms[th] = sc.mark("t")
                sc.wait("s", "t", tms[th]); sc.wait("v", "t", tms[th])
                sl = slice(th * 512, (th + 1) * 512)
                sc.sop("activation", hg[:, sl], pool[th][:, :], AF.Gelu_apprx_tanh)
                sc.vop("tensor_mul", hu[:, sl], pool[2 + th][:, :], wbc[:, sl])
                sc.mark("s"); sc.mark("v")
                sc.wait("v", "s")
                sc.vop("tensor_mul", o_sb[:, hc * TQ + th * 512:hc * TQ + th * 512 + 512], hg[:, sl], hu[:, sl])
                sc.mark("v")
            prev_t_for_slot[slot] = tms[1]
            sc.wait("t", "v")
        for fs in range(16):
            slot = (e * 16 + fs) % 2
            nslot = 1 - slot
            ne, nfs = (e, fs + 1) if fs + 1 < 16 else (e + 1, 0)
            if ne < 8:
                sc.wait("g", "t", prev_wl_t[nslot])
                wl_ld[(ne, nfs)] = sc.dma("g", wlb[nslot][:],
                                          bass.AP(wl_in, (ne * 16 + nfs) * 16 * 128 * 128, [[128, 128], [128 * 128, 16], [1, 128]]),
                                          key=f"d{6 + nslot}")
            if (e, fs) not in wl_ld:
                wl_ld[(e, fs)] = sc.dma("g", wlb[slot][:],
                                        bass.AP(wl_in, (e * 16 + fs) * 16 * 128 * 128, [[128, 128], [128 * 128, 16], [1, 128]]),
                                        key=f"d{6 + slot}")
            sc.wait("t", f"d{6 + slot}", wl_ld[(e, fs)])
            tms = {}
            for th in range(2):
                for hc in range(16):
                    sc.mm(pool[4 + th][:, :], wlb[slot][:, hc * 128:(hc + 1) * 128],
                          o_sb[:, hc * TQ + th * 512: hc * TQ + th * 512 + 512],
                          start=(hc == 0), stop=(hc == 15))
                tms[th] = sc.mark("t")
                sc.wait("v", "t", tms[th])
                sl = slice(th * 512, (th + 1) * 512)
                dst = xs[:, fs * TQ:(fs + 1) * TQ][:, sl]
                sc.vop("tensor_add", dst, dst, pool[4 + th][:, :])
                sc.mark("v")
            prev_wl_t[slot] = tms[1]
            sc.wait("t", "v")

    vm = sc.mark("v")
    sc.wait("y", "v", vm)
    sc.dma("y", _fm_ap(x2_out, 16, TQ), xs[:])
    sc._flush("y")
    nc.sync.wait_ge(sc.sem["yd"], sc.cnt["yd"])
    sc.close()
    return nc


def host_prep_C(o_all, x_full, wo, router_logits, gating_einsum, linear, pes, rscale):
    woT = np.ascontiguousarray(wo.T.reshape(16, 128, 16, 128).transpose(2, 0, 1, 3))
    rl = np.ascontiguousarray(router_logits.reshape(16, 128, 8)).astype(np.float32)
    rsc = np.ascontiguousarray(rscale.reshape(16, 128).T).astype(np.float32)
    pes_a = pes.reshape(8, 1).astype(np.float32)
    rowsel = np.zeros((8, 8, 128), np.float32)
    for e in range(8):
        rowsel[e, e, :] = 1.0
    identf = np.eye(128, dtype=np.float32)
    wg_flat = np.transpose(gating_einsum, (0, 3, 1, 2)).reshape(8, 2048, 4096)
    wg = np.ascontiguousarray(
        wg_flat.reshape(8, 16, 128, 2, 16, 128).transpose(0, 4, 1, 2, 3, 5)).reshape(8, 16, 16, 128, 256)
    wl = np.ascontiguousarray(
        linear.reshape(8, 16, 128, 16, 128).transpose(0, 3, 1, 2, 4))
    maps = []
    for core in range(NCORE):
        b, quarter = core // 4, core % 4
        sl = slice(quarter * TQ, (quarter + 1) * TQ)
        o_q = np.ascontiguousarray(o_all[b, :, :, sl])
        maps.append({
            "o_in": o_q,
            "x": _host_fm(x_full[b, sl], 16).astype(np.float32),
            "woT": woT, "rl": rl, "rscale": rsc, "pes": pes_a,
            "rowsel": rowsel, "identf": identf,
            "wgate": wg, "wlin": wl,
        })
    return maps


TE = TQ + 3

def emit_ln2(sc, xe, xlne, tmp, stats, pool, ones_bf, ones1_f32, nchunks, eps=1e-5):
    tiles = [(0, 512), (512, 1024), (1024, 1027)]
    for c in range(nchunks):
        xc = xe[:, c * TE:(c + 1) * TE]
        sc.sop("activation", tmp[:, TE:2 * TE], xc, AF.Square)
        sc.vop("tensor_copy", tmp[:, 0:TE], xc)
        sc.mark("s"); sc.mark("v")
        sc.wait("t", "s"); sc.wait("t", "v")
        for i, (a, b_) in enumerate(tiles):
            w = b_ - a
            sc.mm(pool[i][0:1, 0:w], ones_bf[:, 0:1], tmp[:, a:b_], start=(c == 0), stop=(c == nchunks - 1))
            sc.mm(pool[3 + i][0:1, 0:w], ones_bf[:, 0:1], tmp[:, TE + a:TE + b_], start=(c == 0), stop=(c == nchunks - 1))
        sc.mark("t")
        sc.wait("s", "t"); sc.wait("v", "t")
    sc.mark("t")
    sc.wait("v", "t")
    sumx = stats[0:1, 0:TE]
    var_t = stats[0:1, TE:2 * TE]
    rstd = stats[0:1, 2 * TE:3 * TE]
    mr = stats[0:1, 3 * TE:4 * TE]
    for i, (a, b_) in enumerate(tiles):
        w = b_ - a
        sc.vop("tensor_copy", sumx[:, a:b_], pool[i][0:1, 0:w])
        sc.vop("tensor_copy", var_t[:, a:b_], pool[3 + i][0:1, 0:w])
    sc.vop("scalar_tensor_tensor", mr, sumx, 1.0, sumx, ALU.mult, ALU.mult)
    sc.vop("scalar_tensor_tensor", var_t, mr, -1.0 / D, var_t, ALU.mult, ALU.add)
    sc.mark("v")
    sc.wait("s", "v")
    sc.sop("activation", rstd, var_t, AF.Ln, scale=1.0 / D, bias=sc.const(eps, parts=1))
    sc.sop("activation", rstd, rstd, AF.Exp, scale=-0.5)
    sc.mark("s")
    sc.wait("v", "s")
    sc.vop("scalar_tensor_tensor", mr, sumx, 1.0 / D, rstd, ALU.mult, ALU.mult)
    sc.mark("v")
    for i, (a, b_) in enumerate(tiles):
        w = b_ - a
        sc.wait("t", "v")
        sc.mm(pool[6][:, 0:w], ones1_f32[0:1, :], rstd[:, a:b_], start=True, stop=True)
        sc.mm(pool[7][:, 0:w], ones1_f32[0:1, :], mr[:, a:b_], start=True, stop=True)
        sc.mark("t")
        sc.wait("v", "t")
        for c in range(nchunks):
            xc = xe[:, c * TE + a:c * TE + b_]
            oc = xlne[:, c * TE + a:c * TE + b_]
            sc.vop("tensor_mul", tmp[:, 0:w], xc, pool[6][:, 0:w])
            sc.vop("tensor_sub", oc, tmp[:, 0:w], pool[7][:, 0:w])
        sc.mark("v")
        sc.wait("t", "v")


def build_launchD():
    nc = bass.Bass("TRN2", target_bir_lowering=False)
    x2_in = nc.dram_tensor("x2", [16, 128, TQ], F32, kind="ExternalInput")
    halo_in = nc.dram_tensor("halo", [16, 128, 3], F32, kind="ExternalInput")
    w_in = nc.dram_tensor("wcat", [24, 16, 128, 256], F32, kind="ExternalInput")
    wba_in = nc.dram_tensor("wba", [128, 16 * 24], F32, kind="ExternalInput")
    convw_in = nc.dram_tensor("convw", [128, 40 * 4], F32, kind="ExternalInput")
    alog_in = nc.dram_tensor("alog", [8, 1], F32, kind="ExternalInput")
    dtb_in = nc.dram_tensor("dtb", [8, 1], F32, kind="ExternalInput")
    qn_out = nc.dram_tensor("qn", [8, 128, TQ], BF16, kind="ExternalOutput")
    kn_out = nc.dram_tensor("kn", [16, 128, TQ], BF16, kind="ExternalOutput")
    vv_out = nc.dram_tensor("vv", [16, 128, TQ], BF16, kind="ExternalOutput")
    gate_out = nc.dram_tensor("gate", [8, 128, TQ], BF16, kind="ExternalOutput")
    beta_out = nc.dram_tensor("beta", [16, TQ], F32, kind="ExternalOutput")
    lc_out = nc.dram_tensor("lc", [8, TQ], F32, kind="ExternalOutput")

    xe = nc.alloc_sbuf_tensor("xe", [128, 16 * TE], F32)
    xlne = nc.alloc_sbuf_tensor("xlne", [128, 16 * TE], BF16)
    tmp = nc.alloc_sbuf_tensor("tmp", [128, 2 * TE], BF16)
    stats = nc.alloc_sbuf_tensor("stats", [1, 4 * TE], F32)
    ones_bf = nc.alloc_sbuf_tensor("ones_bf", [128, 1], BF16)
    ones1_f32 = nc.alloc_sbuf_tensor("ones1_f32", [1, 128], F32)
    ones8 = nc.alloc_sbuf_tensor("ones8", [16, 64], F32)
    wbuf = [nc.alloc_sbuf_tensor(f"wbuf{i}", [128, 256], BF16) for i in range(2)]
    wba = nc.alloc_sbuf_tensor("wba_sb", [128, 16 * 24], BF16)
    convw = nc.alloc_sbuf_tensor("convw_sb", [128, 40 * 4], F32)
    alog = nc.alloc_sbuf_tensor("alog_sb", [8, 1], F32)
    dtb = nc.alloc_sbuf_tensor("dtb_sb", [8, 1], F32)
    xq = nc.alloc_sbuf_tensor("xq", [128, 2 * TE], BF16)
    cv = nc.alloc_sbuf_tensor("cv", [128, 2 * TQ], BF16)
    nrm = nc.alloc_sbuf_tensor("nrm", [1, 2 * TQ], F32)
    outstage = nc.alloc_sbuf_tensor("outstage", [128, 2 * TQ], BF16)
    zsb = nc.alloc_sbuf_tensor("zsb", [16, TQ], F32)
    sp1 = nc.alloc_sbuf_tensor("sp1", [16, TQ], F32)
    sp2 = nc.alloc_sbuf_tensor("sp2", [16, TQ], F32)
    beta_sb = nc.alloc_sbuf_tensor("beta_sb", [16, TQ], F32)
    lc_sb = nc.alloc_sbuf_tensor("lc_sb", [16, TQ], F32)
    asc = nc.alloc_sbuf_tensor("asc", [8, 2], F32)

    pool = [nc.alloc_psum_tensor(f"pool{i}", [128, 512], F32) for i in range(8)]

    sc = Sched(nc).open()
    sc.init_consts()
    sc.const(1e-5, parts=1)
    sc.const(1.0)
    sc.vop("memset", ones_bf[:], 1.0)
    sc.vop("memset", ones1_f32[:], 1.0)
    sc.vop("memset", ones8[:], 1.0)
    sc.mark("v")
    sc.wait("t", "v"); sc.wait("s", "v")

    sc.dma("y", bass.AP(xe, 3, [[16 * TE, 128], [TE, 16], [1, TQ]]),
           bass.AP(x2_in, 0, [[TQ, 128], [128 * TQ, 16], [1, TQ]]))
    sc.dma("y", bass.AP(xe, 0, [[16 * TE, 128], [TE, 16], [1, 3]]),
           bass.AP(halo_in, 0, [[3, 128], [128 * 3, 16], [1, 3]]))
    ld0 = sc.cnt["yd"]
    sc.dma("g", wba[:], wba_in.ap())
    sc.dma("g", convw[:], convw_in.ap())
    sc.dma("g", alog[:], alog_in.ap())
    sc.dma("g", dtb[:], dtb_in.ap())
    gld0 = sc.cnt["gd"]
    sc.wait("t", "yd", ld0); sc.wait("v", "yd", ld0); sc.wait("s", "yd", ld0)

    emit_ln2(sc, xe[:], xlne[:], tmp[:], stats[:], pool, ones_bf, ones1_f32, 16)
    sc.barrier()
    sc.wait("v", "gd", gld0); sc.wait("t", "gd", gld0); sc.wait("s", "gd", gld0)

    RSQ = 1.0 / float(np.sqrt(128.0))
    slab_ld = {}
    prev_t = {0: 0, 1: 0}
    slab_ld[0] = sc.dma("g", wbuf[0][:], w_in.ap()[0, 0], key="d0")
    for grp in range(24):
        has_conv = grp < 20
        for d in range(16):
            gi = grp * 16 + d
            slot = gi % 2
            if gi + 1 < 24 * 16:
                nslot = (gi + 1) % 2
                sc.wait("g", "t", prev_t[nslot])
                slab_ld[gi + 1] = sc.dma("g", wbuf[nslot][:], w_in.ap()[(gi + 1) // 16, (gi + 1) % 16], key=f"d{nslot}")
            sc.wait("t", f"d{slot}", slab_ld[gi])
            for oc2 in range(2):
                lh = wbuf[slot][:, oc2 * 128:(oc2 + 1) * 128]
                xc = xlne[:, d * TE:(d + 1) * TE]
                sc.mm(pool[oc2 * 2][:, :], lh, xc[:, 3:515], start=(d == 0), stop=(d == 15))
                sc.mm(pool[oc2 * 2 + 1][:, :], lh, xc[:, 515:1027], start=(d == 0), stop=(d == 15))
                if has_conv:
                    sc.mm(pool[6 + oc2][:, 0:3], lh, xc[:, 0:3], start=(d == 0), stop=(d == 15))
            prev_t[slot] = sc.mark("t")
        tm = sc.mark("t")
        sc.wait("v", "t", tm); sc.wait("s", "t", tm)
        for oc2 in range(2):
            oc = grp * 2 + oc2
            if has_conv:
                xqs = xq[:, oc2 * TE:(oc2 + 1) * TE]
                sc.vop("tensor_copy", xqs[:, 0:3], pool[6 + oc2][:, 0:3])
                sc.vop("tensor_copy", xqs[:, 3:515], pool[oc2 * 2][:, :])
                sc.vop("tensor_copy", xqs[:, 515:1027], pool[oc2 * 2 + 1][:, :])
                cvs = cv[:, oc2 * TQ:(oc2 + 1) * TQ]
                wcol = convw[:, oc * 4:(oc + 1) * 4]
                sc.vop("tensor_scalar_mul", cvs, xqs[:, 0:TQ], wcol[:, 0:1])
                for tau in range(1, 4):
                    sc.vop("scalar_tensor_tensor", cvs, xqs[:, tau:tau + TQ], wcol[:, tau:tau + 1], cvs, ALU.mult, ALU.add)
                sc.mark("v")
                sc.wait("s", "v")
                sc.sop("activation", cvs, cvs, AF.Silu)
                sc.mark("s")
                sc.wait("v", "s"); sc.wait("t", "s")
            else:
                sc.vop("tensor_copy", outstage[:, oc2 * TQ:oc2 * TQ + 512], pool[oc2 * 2][:, :])
                sc.vop("tensor_copy", outstage[:, oc2 * TQ + 512:(oc2 + 1) * TQ], pool[oc2 * 2 + 1][:, :])
                sc.mark("v")
                sc.wait("t", "v")
        if has_conv:
            for oc2 in range(2):
                oc = grp * 2 + oc2
                cvs = cv[:, oc2 * TQ:(oc2 + 1) * TQ]
                dst = outstage[:, oc2 * TQ:(oc2 + 1) * TQ]
                if oc < 24:
                    sc.sop("activation", tmp[:, 0:TQ], cvs, AF.Square)
                    sc.mark("s")
                    sc.wait("t", "s")
                    sc.mm(pool[4][0:1, :], ones_bf[:, 0:1], tmp[:, 0:512], start=True, stop=True)
                    sc.mm(pool[5][0:1, :], ones_bf[:, 0:1], tmp[:, 512:1024], start=True, stop=True)
                    tm = sc.mark("t")
                    sc.wait("s", "t", tm)
                    nr = nrm[0:1, oc2 * TQ:(oc2 + 1) * TQ]
                    sc.sop("activation", nr[:, 0:512], pool[4][0:1, :], AF.Sqrt)
                    sc.sop("activation", nr[:, 512:1024], pool[5][0:1, :], AF.Sqrt)
                    sc.mark("s")
                    sc.wait("v", "s")
                    sc.vop("tensor_scalar_max", nr, nr, 1e-12)
                    sc.vop("reciprocal", nr, nr)
                    if oc < 8:
                        sc.vop("tensor_scalar_mul", nr, nr, RSQ)
                    sc.mark("v")
                    sc.wait("t", "v")
                    sc.mm(pool[4][:, :], ones1_f32[0:1, :], nr[:, 0:512], start=True, stop=True)
                    sc.mm(pool[5][:, :], ones1_f32[0:1, :], nr[:, 512:1024], start=True, stop=True)
                    tm = sc.mark("t")
                    sc.wait("v", "t", tm)
                    sc.vop("tensor_mul", dst[:, 0:512], cvs[:, 0:512], pool[4][:, :])
                    sc.vop("tensor_mul", dst[:, 512:1024], cvs[:, 512:1024], pool[5][:, :])
                    sc.mark("v")
                    sc.wait("t", "v")
                else:
                    sc.vop("tensor_copy", dst, cvs)
                    sc.mark("v")
        vm = sc.mark("v")
        sc.wait("y", "v", vm)
        sc.wait("y", "s", sc.cnt["s"])
        for oc2 in range(2):
            oc = grp * 2 + oc2
            src = outstage[:, oc2 * TQ:(oc2 + 1) * TQ]
            if oc < 8:
                sc.dma("y", qn_out.ap()[oc], src)
            elif oc < 24:
                sc.dma("y", kn_out.ap()[oc - 8], src)
            elif oc < 40:
                sc.dma("y", vv_out.ap()[oc - 24], src)
            else:
                sc.dma("y", gate_out.ap()[oc - 40], src)
        sc.wait("v", "yd", sc.cnt["yd"])
        sc.wait("s", "yd", sc.cnt["yd"])
        sc.wait("t", "v", vm)

    sc.barrier()
    for th in range(2):
        sl = slice(th * 512, (th + 1) * 512)
        for d in range(16):
            xc = xlne[:, d * TE:(d + 1) * TE]
            sc.mm(pool[th][0:16, :], wba[:, d * 24:d * 24 + 16], xc[:, 3 + th * 512:3 + (th + 1) * 512], start=(d == 0), stop=(d == 15))
            sc.mm(pool[2 + th][0:8, :], wba[:, d * 24 + 16:d * 24 + 24], xc[:, 3 + th * 512:3 + (th + 1) * 512], start=(d == 0), stop=(d == 15))
    tm = sc.mark("t")
    sc.wait("s", "t", tm); sc.wait("v", "t", tm)
    for th in range(2):
        sl = slice(th * 512, (th + 1) * 512)
        sc.sop("activation", beta_sb[:, sl], pool[th][0:16, :], AF.Sigmoid)
        sc.vop("tensor_scalar_add", zsb[0:8, sl], pool[2 + th][0:8, :], dtb[0:8, 0:1])
    sc.mark("s"); sc.mark("v")
    sc.wait("v", "s")
    sc.vop("tensor_scalar_mul", beta_sb[:], beta_sb[:], 2.0)
    sc.mark("v")
    sc.wait("s", "v")
    sc.sop("activation", sp1[0:8, :], zsb[0:8, :], AF.Abs)
    sc.sop("activation", sp1[0:8, :], sp1[0:8, :], AF.Exp, scale=-1.0)
    sc.sop("activation", sp1[0:8, :], sp1[0:8, :], AF.Ln, bias=sc.const(1.0, parts=8))
    sc.sop("activation", asc[:, 0:1], alog[:, 0:1], AF.Exp)
    sc.mark("s")
    sc.wait("v", "s")
    sc.vop("tensor_relu", sp2[0:8, :], zsb[0:8, :])
    sc.vop("tensor_add", sp1[0:8, :], sp1[0:8, :], sp2[0:8, :])
    sc.vop("tensor_scalar_mul", asc[:, 1:2], asc[:, 0:1], -1.0)
    sc.vop("tensor_scalar_mul", sp1[0:8, :], sp1[0:8, :], asc[:, 1:2])
    for ch in range(16):
        sl = slice(ch * 64, (ch + 1) * 64)
        sc.vop("tensor_tensor_scan", lc_sb[0:8, sl], ones8[0:8, :], sp1[0:8, sl], 0.0, ALU.mult, ALU.add)
    vm = sc.mark("v")
    sc.wait("y", "v", vm)
    sc.dma("y", beta_out.ap(), beta_sb[:])
    sc.dma("y", lc_out.ap(), lc_sb[0:8, :])

    sc._flush("y")
    nc.sync.wait_ge(sc.sem["yd"], sc.cnt["yd"])
    sc.close()
    return nc


def host_prep_D(x2_all, Wq, Wk, Wv, Wg, Wb, Wa, A_log, dt_bias, cqw, ckw, cvw):
    wcat = np.concatenate([Wq, Wk, Wv, Wg], axis=1)
    w_arr = np.ascontiguousarray(
        wcat.reshape(16, 128, 24, 2, 128).transpose(2, 0, 1, 3, 4)).reshape(24, 16, 128, 256)
    wba = np.ascontiguousarray(
        np.concatenate([Wb, Wa], axis=1).reshape(16, 128, 24).transpose(1, 0, 2)).reshape(128, 16 * 24)
    convw = np.ascontiguousarray(
        np.concatenate([cqw[:, 0, :], ckw[:, 0, :], cvw[:, 0, :]], axis=0)
        .reshape(40, 128, 4).transpose(1, 0, 2)).reshape(128, 160)
    maps = []
    for core in range(NCORE):
        b, quarter = core // 4, core % 4
        if quarter == 0:
            halo = np.zeros((16, 128, 3), np.float32)
        else:
            halo = np.ascontiguousarray(x2_all[core - 1][:, :, TQ - 3:TQ])
        maps.append({
            "x2": x2_all[core], "halo": halo, "wcat": w_arr, "wba": wba,
            "convw": convw, "alog": A_log.reshape(8, 1).astype(np.float32),
            "dtb": dt_bias.reshape(8, 1).astype(np.float32),
        })
    return maps


def build_launchE():
    nc = bass.Bass("TRN2", target_bir_lowering=False)
    NU = 2
    NCH = 64
    qf_in = nc.dram_tensor("qf", [NU, 128, S], BF16, kind="ExternalInput")
    kf_in = nc.dram_tensor("kf", [NU, 128, 2 * S], BF16, kind="ExternalInput")
    ktm_in = nc.dram_tensor("ktm", [NU, NCH, 128, 128], BF16, kind="ExternalInput")
    vtm_in = nc.dram_tensor("vtm", [NU, NCH, 128, 128], BF16, kind="ExternalInput")
    gate_in = nc.dram_tensor("gate", [NU, 128, S], BF16, kind="ExternalInput")
    bc_in = nc.dram_tensor("beta_c", [NU, 128, NCH], F32, kind="ExternalInput")
    lc2_in = nc.dram_tensor("lc2", [NU, 128, NCH], F32, kind="ExternalInput")
    lcq_in = nc.dram_tensor("lcq", [NU, 128, NCH], F32, kind="ExternalInput")
    lcn_in = nc.dram_tensor("lcn", [NU, 128, NCH], F32, kind="ExternalInput")
    identf_in = nc.dram_tensor("identf", [128, 128], F32, kind="ExternalInput")
    normw_in = nc.dram_tensor("normw", [128, 1], F32, kind="ExternalInput")
    maskm_in = nc.dram_tensor("maskm", [128, 128], F32, kind="ExternalInput")
    maskmt_in = nc.dram_tensor("maskmt", [128, 128], F32, kind="ExternalInput")
    maskl_in = nc.dram_tensor("maskl", [128, 64], F32, kind="ExternalInput")
    identb_in = nc.dram_tensor("identb", [128, 128], BF16, kind="ExternalInput")
    of_out = nc.dram_tensor("of", [NU, 128, S], BF16, kind="ExternalOutput")

    qf = [nc.alloc_sbuf_tensor(f"qf{i}", [128, S], BF16) for i in range(2)]
    kf = [nc.alloc_sbuf_tensor(f"kf{i}", [128, 2 * S], BF16) for i in range(2)]
    ktm = [nc.alloc_sbuf_tensor(f"ktm{i}", [128, NCH * 128], BF16) for i in range(2)]
    vtm = [nc.alloc_sbuf_tensor(f"vtm{i}", [128, NCH * 128], BF16) for i in range(2)]
    gate = [nc.alloc_sbuf_tensor(f"gate{i}", [128, S], BF16) for i in range(2)]
    bcw = [nc.alloc_sbuf_tensor(f"bcw{i}", [128, NCH], F32) for i in range(2)]
    lc2w = [nc.alloc_sbuf_tensor(f"lc2w{i}", [128, NCH], F32) for i in range(2)]
    lcqw = [nc.alloc_sbuf_tensor(f"lcqw{i}", [128, NCH], F32) for i in range(2)]
    lcnw = [nc.alloc_sbuf_tensor(f"lcnw{i}", [128, NCH], F32) for i in range(2)]
    o_sb = [nc.alloc_sbuf_tensor(f"osb{i}", [128, S], BF16) for i in range(2)]
    rows_sb = [nc.alloc_sbuf_tensor(f"rows_sb{i}", [1, 320], F32) for i in range(2)]
    identf = nc.alloc_sbuf_tensor("identf_sb", [128, 128], F32)
    normw = nc.alloc_sbuf_tensor("normw_sb", [128, 1], F32)
    maskm = nc.alloc_sbuf_tensor("maskm_sb", [128, 128], F32)
    maskmt = nc.alloc_sbuf_tensor("maskmt_sb", [128, 128], F32)
    maskl = nc.alloc_sbuf_tensor("maskl_sb", [128, 64], F32)
    identb = nc.alloc_sbuf_tensor("identb_sb", [128, 128], BF16)
    ones1_f32 = nc.alloc_sbuf_tensor("ones1_f32", [1, 128], F32)
    ones_bf = nc.alloc_sbuf_tensor("ones_bf", [128, 1], BF16)

    def mkws(i):
        return dict(
            S_st=nc.alloc_sbuf_tensor(f"S_st{i}", [128, 128], F32),
            S_bf=nc.alloc_sbuf_tensor(f"S_bf{i}", [128, 128], BF16),
            t1=nc.alloc_sbuf_tensor(f"t1_{i}", [128, 128], F32),
            t2=nc.alloc_sbuf_tensor(f"t2_{i}", [128, 128], F32),
            Rr=nc.alloc_sbuf_tensor(f"Rr{i}", [128, 128], F32),
            Ff=[nc.alloc_sbuf_tensor(f"Ff{i}_{j}", [128, 128], BF16) for j in range(2)],
            Hh=[nc.alloc_sbuf_tensor(f"Hh{i}_{j}", [128, 128], BF16) for j in range(2)],
            acc=nc.alloc_sbuf_tensor(f"acc{i}", [128, 128], BF16),
            tbt=nc.alloc_sbuf_tensor(f"tbt{i}", [128, 128], BF16),
            w0=nc.alloc_sbuf_tensor(f"w0_{i}", [128, 128], BF16),
            lt=nc.alloc_sbuf_tensor(f"lt{i}", [128, 64], BF16),
            qbar=nc.alloc_sbuf_tensor(f"qbar{i}", [128, 64], BF16),
            xb=nc.alloc_sbuf_tensor(f"xb{i}", [128, 128], BF16),
            uu=nc.alloc_sbuf_tensor(f"uu{i}", [128, 128], BF16),
            khat=nc.alloc_sbuf_tensor(f"khat{i}", [128, 128], BF16),
            scr=nc.alloc_sbuf_tensor(f"scr{i}", [128, 8], F32),
        )
    ws = [mkws(0), mkws(1)]
    tmp = nc.alloc_sbuf_tensor("tmp", [128, 1024], BF16)
    nrm = nc.alloc_sbuf_tensor("nrm", [1, 512], F32)

    pool = [nc.alloc_psum_tensor(f"pool{i}", [128, 512], F32) for i in range(8)]

    sc = Sched(nc).open()
    sc.SELFSYNC = False
    sc.init_consts()
    sc.const(1e-5, parts=1)
    sc.vop("memset", ones1_f32[:], 1.0)
    sc.vop("memset", ones_bf[:], 1.0)
    sc.mark("v")

    sc.dma("g", normw[:], normw_in.ap())
    sc.dma("g", maskm[:], maskm_in.ap())
    sc.dma("g", maskmt[:], maskmt_in.ap())
    sc.dma("g", maskl[:], maskl_in.ap())
    sc.dma("g", identb[:], identb_in.ap())
    sc.dma("g", identf[:], identf_in.ap())
    gld0 = sc.cnt["gd"]
    sc.wait("t", "gd", gld0); sc.wait("v", "gd", gld0); sc.wait("s", "gd", gld0)

    for u in range(NU):
        ks = f"d{u}"
        sc.dma("y", qf[u][:], qf_in.ap()[u], key=ks)
        sc.dma("y", kf[u][:], kf_in.ap()[u], key=ks)
        sc.dma("y", ktm[u][:], bass.AP(ktm_in, u * NCH * 128 * 128, [[128, 128], [128 * 128, NCH], [1, 128]]), key=ks)
        sc.dma("y", vtm[u][:], bass.AP(vtm_in, u * NCH * 128 * 128, [[128, 128], [128 * 128, NCH], [1, 128]]), key=ks)
        sc.dma("y", gate[u][:], gate_in.ap()[u], key=ks)
        kg = f"d{2 + u}"
        sc.dma("g", bcw[u][:], bc_in.ap()[u], key=kg)
        sc.dma("g", lc2w[u][:], lc2_in.ap()[u], key=kg)
        sc.dma("g", lcqw[u][:], lcq_in.ap()[u], key=kg)
        sc.dma("g", lcnw[u][:], lcn_in.ap()[u], key=kg)
        for e in ("t", "v", "s"):
            sc.wait(e, ks, sc.cnt[ks]); sc.wait(e, kg, sc.cnt[kg])
    sc.vop("memset", ws[0]["S_st"][:], 0.0)
    sc.vop("memset", ws[0]["S_bf"][:], 0.0)
    sc.vop("memset", ws[1]["S_st"][:], 0.0)
    sc.vop("memset", ws[1]["S_bf"][:], 0.0)
    sc.mark("v")
    sc.wait("t", "v")

    def ph_scr(u, c):
        w = ws[u]
        scr = w["scr"]
        sc.wait("v", "s", w.get("m10s", 0))
        sc.vop("tensor_sub", scr[:, 0:1], lcnw[u][:, c:c + 1], lc2w[u][:, c:c + 1])
        sc.vop("tensor_scalar_mul", scr[:, 1:2], bcw[u][:, c:c + 1], -1.0)
        w["m1v"] = sc.mark("v")

    def ph_scr_s(u, c):
        w = ws[u]
        scr = w["scr"]
        sc.wait("s", "v", w["m1v"])
        sc.sop("activation", scr[:, 2:3], lc2w[u][:, c:c + 1], AF.Exp)
        sc.sop("activation", scr[:, 3:4], scr[:, 0:1], AF.Exp)
        sc.sop("activation", scr[:, 4:5], lcnw[u][:, c:c + 1], AF.Exp)
        w["m1s"] = sc.mark("s")

    def ph_rows_t(u, c):
        w = ws[u]
        bp = 4 * u
        sc.wait("t", "v", w.get("m10v", 0))
        sc.trans(pool[bp + 3][0:1, 256:384], lc2w[u][:, c:c + 1], identf[:, :])
        sc.trans(pool[bp + 3][0:1, 384:512], bcw[u][:, c:c + 1], identf[:, :])
        sc.trans(pool[bp + 3][0:1, 192:256], lcqw[u][0:64, c:c + 1], identf[0:64, 0:64])
        w["m2t"] = sc.mark("t")

    def ph_rows_v(u, c):
        w = ws[u]
        bp = 4 * u
        sc.wait("v", "t", w["m2t"])
        sc.vop("tensor_copy", rows_sb[u][0:1, 0:128], pool[bp + 3][0:1, 256:384])
        sc.vop("tensor_copy", rows_sb[u][0:1, 128:256], pool[bp + 3][0:1, 384:512])
        sc.vop("tensor_copy", rows_sb[u][0:1, 256:320], pool[bp + 3][0:1, 192:256])
        w["m3v"] = sc.mark("v")

    def ph_A(u, c):
        w = ws[u]
        bp = 4 * u
        kfc = kf[u][:, c * 128:(c + 1) * 128]
        qfc = qf[u][:, c * 64:(c + 1) * 64]
        sc.wait("t", "v", w["m3v"])
        sc.wait("t", "s", w.get("m7s", 0))
        sc.mm(pool[bp + 0][:, 0:128], kfc, kfc, start=True, stop=True)
        sc.mm(pool[bp + 1][:, 0:128], ones1_f32[0:1, :], rows_sb[u][0:1, 0:128], start=True, stop=True)
        sc.mm(pool[bp + 1][:, 128:256], ones1_f32[0:1, :], rows_sb[u][0:1, 128:256], start=True, stop=True)
        sc.mm(pool[bp + 1][:, 256:320], ones1_f32[0:1, :], rows_sb[u][0:1, 256:320], start=True, stop=True)
        sc.mm(pool[bp + 2][:, 0:64], kfc, qfc, start=True, stop=True)
        w["m4t"] = sc.mark("t")

    def ph_m1(u, c):
        w = ws[u]
        bp = 4 * u
        sc.wait("v", "t", w["m4t"])
        sc.vop("tensor_scalar", w["t1"][:], pool[bp + 1][:, 0:128], lc2w[u][:, c:c + 1], None, ALU.subtract)
        sc.vop("tensor_add", w["t2"][:], w["t1"][:], maskm[:])
        w["m5v"] = sc.mark("v")

    def ph_m2(u, c):
        w = ws[u]
        sc.wait("s", "v", w["m5v"])
        sc.sop("activation", w["Rr"][:], w["t2"][:], AF.Exp, scale=-1.0)
        w["m5s"] = sc.mark("s")

    def ph_m3(u, c):
        w = ws[u]
        bp = 4 * u
        sc.wait("v", "s", w["m5s"])
        sc.vop("tensor_mul", w["Rr"][:], w["Rr"][:], pool[bp + 0][:, 0:128])
        sc.vop("tensor_scalar_mul", w["Ff"][0][:], w["Rr"][:], w["scr"][:, 1:2])
        sc.vop("tensor_add", w["t2"][:], w["t1"][:], maskmt[:])
        w["m6v"] = sc.mark("v")

    def ph_m4(u, c):
        w = ws[u]
        sc.wait("s", "v", w["m6v"])
        sc.sop("activation", w["Rr"][:], w["t2"][:], AF.Exp)
        w["m6s"] = sc.mark("s")

    def ph_m5(u, c):
        w = ws[u]
        bp = 4 * u
        sc.wait("v", "s", w["m6s"])
        sc.vop("tensor_mul", w["Rr"][:], w["Rr"][:], pool[bp + 0][:, 0:128])
        sc.vop("scalar_tensor_tensor", w["Hh"][0][:], w["Rr"][:], -1.0, pool[bp + 1][:, 128:256], ALU.mult, ALU.mult)
        sc.vop("tensor_add", w["acc"][:], w["Hh"][0][:], identb[:])
        sc.vop("tensor_scalar", w["t1"][:, 0:64], pool[bp + 1][:, 256:320], lc2w[u][:, c:c + 1], None, ALU.subtract)
        sc.vop("tensor_add", w["t1"][:, 0:64], w["t1"][:, 0:64], maskl[:])
        w["m7v"] = sc.mark("v")

    def ph_m6(u, c):
        w = ws[u]
        bp = 4 * u
        sc.wait("s", "v", w["m7v"])
        sc.sop("activation", w["t2"][:, 0:64], w["t1"][:, 0:64], AF.Exp)
        sc.sop("activation", w["t2"][:, 64:128], pool[bp + 1][:, 256:320], AF.Exp)
        w["m7s"] = sc.mark("s")

    def ch_mm(u, kstep):
        w = ws[u]
        bp = 4 * u
        cur = w["cur"]
        if kstep == 1:
            sc.wait("t", "v", w["m7v"])
        else:
            sc.wait("t", "v", w["chv"]); sc.wait("t", "s", w["chs"])
        sc.mm(pool[bp + 2][:, 128:256], w["Hh"][cur][:], w["Ff"][cur][:], start=True, stop=True)
        sc.mm(pool[bp + 3][:, 256:384], w["Ff"][cur][:], w["Hh"][cur][:], start=True, stop=True)
        w["chmt"] = sc.mark("t")

    def ch_evac(u, kstep):
        w = ws[u]
        bp = 4 * u
        nxt = 1 - w["cur"]
        sc.wait("v", "t", w["chmt"]); sc.wait("s", "t", w["chmt"])
        sc.vop("tensor_copy", w["Ff"][nxt][:], pool[bp + 2][:, 128:256])
        sc.sop("activation", w["Hh"][nxt][:], pool[bp + 3][:, 256:384], AF.Copy)
        w["chev"] = sc.mark("v")
        w["chs"] = sc.mark("s")

    def ch_acc(u, kstep):
        w = ws[u]
        bp = 4 * u
        nxt = 1 - w["cur"]
        sc.wait("t", "v", w["chev"]); sc.wait("t", "s", w["chs"])
        sc.mm(pool[bp + 2][:, 384:512], w["Ff"][nxt][:], w["acc"][:], start=True, stop=True)
        w["chat"] = sc.mark("t")

    def ch_add(u, kstep):
        w = ws[u]
        bp = 4 * u
        sc.wait("v", "t", w["chat"])
        sc.vop("tensor_add", w["acc"][:], w["acc"][:], pool[bp + 2][:, 384:512])
        w["chv"] = sc.mark("v")
        w["cur"] = 1 - w["cur"]

    def ph_fin(u, c):
        w = ws[u]
        bp = 4 * u
        sc.wait("v", "s", w["m7s"])
        sc.vop("tensor_mul", w["lt"][:], w["t2"][:, 0:64], pool[bp + 2][:, 0:64])
        sc.vop("tensor_scalar_mul", w["tbt"][:], w["acc"][:], bcw[u][:, c:c + 1])
        sc.vop("tensor_mul", w["qbar"][:], qf[u][:, c * 64:(c + 1) * 64], w["t2"][:, 64:128])
        sc.vop("tensor_scalar_mul", w["khat"][:], ktm[u][:, c * 128:(c + 1) * 128], w["scr"][:, 3:4])
        w["m8v"] = sc.mark("v")

    def ph_w0x(u, c):
        w = ws[u]
        bp = 4 * u
        sc.wait("t", "v", w["m8v"]); sc.wait("t", "s", w.get("m10s", 0))
        sc.mm(pool[bp + 3][:, 0:128], w["tbt"][:], vtm[u][:, c * 128:(c + 1) * 128], start=True, stop=True)
        sc.mm(pool[bp + 0][:, 0:128], kf[u][:, c * 128:(c + 1) * 128], w["S_bf"][:], start=True, stop=True)
        w["m9t"] = sc.mark("t")

    def ph_w0x_v(u, c):
        w = ws[u]
        bp = 4 * u
        sc.wait("v", "t", w["m9t"])
        sc.vop("tensor_copy", w["w0"][:], pool[bp + 3][:, 0:128])
        sc.vop("tensor_scalar_mul", w["xb"][:], pool[bp + 0][:, 0:128], w["scr"][:, 2:3])
        w["m9v"] = sc.mark("v")

    def ph_y(u, c):
        w = ws[u]
        bp = 4 * u
        sc.wait("t", "v", w["m9v"])
        sc.mm(pool[bp + 0][:, 0:128], w["tbt"][:], w["xb"][:], start=True, stop=True)
        w["m10t"] = sc.mark("t")

    def ph_u(u, c):
        w = ws[u]
        bp = 4 * u
        sc.wait("v", "t", w["m10t"])
        sc.vop("tensor_sub", w["uu"][:], w["w0"][:], pool[bp + 0][:, 0:128])
        w["m11v"] = sc.mark("v")

    def ph_o(u, c):
        w = ws[u]
        bp = 4 * u
        sc.wait("t", "v", w["m11v"])
        sc.mm(pool[bp + 3][:, 128:192], w["S_bf"][:], w["qbar"][:], start=True, stop=False)
        sc.mm(pool[bp + 3][:, 128:192], w["uu"][:], w["lt"][:], start=False, stop=True)
        sc.mm(pool[bp + 0][:, 0:128], w["khat"][:], w["uu"][:], start=True, stop=True)
        w["m12t"] = sc.mark("t")

    def ph_send(u, c):
        w = ws[u]
        bp = 4 * u
        sc.wait("v", "t", w["m12t"])
        sc.vop("tensor_copy", o_sb[u][:, c * 64:(c + 1) * 64], pool[bp + 3][:, 128:192])
        sc.vop("scalar_tensor_tensor", w["S_st"][:], w["S_st"][:], w["scr"][:, 4:5], pool[bp + 0][:, 0:128], ALU.mult, ALU.add)
        w["m10v"] = sc.mark("v")
        sc.wait("s", "v", w["m10v"])
        sc.sop("activation", w["S_bf"][:], w["S_st"][:], AF.Copy)
        w["m10s"] = sc.mark("s")

    ws[0]["cur"] = 0
    ws[1]["cur"] = 0
    phases = [ph_scr, ph_scr_s, ph_rows_t, ph_rows_v, ph_A, ph_m1, ph_m2,
              ph_m3, ph_m4, ph_m5, ph_m6]
    for c in range(NCH):
        ws[0]["cur"] = 0
        ws[1]["cur"] = 0
        for ph in phases:
            ph(0, c); ph(1, c)
        for kstep in range(1, 7):
            ch_mm(0, kstep); ch_mm(1, kstep)
            ch_evac(0, kstep); ch_evac(1, kstep)
            ch_acc(0, kstep); ch_acc(1, kstep)
            ch_add(0, kstep); ch_add(1, kstep)
        for ph in (ph_fin, ph_w0x, ph_w0x_v, ph_y, ph_u, ph_o, ph_send):
            ph(0, c); ph(1, c)

    for u in range(NU):
        for blk in range(8):
            bs = slice(blk * 512, (blk + 1) * 512)
            sc.wait("s", "v", sc.cnt["v"])
            sc.sop("activation", tmp[:, 0:512], o_sb[u][:, bs], AF.Square)
            sc.mark("s"); sc.wait("t", "s")
            sc.mm(pool[0][0:1, :], ones_bf[:, 0:1], tmp[:, 0:512], start=True, stop=True)
            tm = sc.mark("t")
            sc.wait("s", "t", tm)
            sc.sop("activation", nrm[0:1, :], pool[0][0:1, :], AF.Ln, scale=1.0 / 128.0, bias=sc.const(1e-5, parts=1))
            sc.sop("activation", nrm[0:1, :], nrm[0:1, :], AF.Exp, scale=-0.5)
            sc.mark("s"); sc.wait("t", "s")
            sc.mm(pool[1][:, :], ones1_f32[0:1, :], nrm[0:1, :], start=True, stop=True)
            tm = sc.mark("t")
            sc.wait("s", "t", tm); sc.wait("v", "t", tm)
            sc.sop("activation", tmp[:, 512:1024], gate[u][:, bs], AF.Silu)
            sc.mark("s"); sc.wait("v", "s")
            sc.vop("tensor_scalar_mul", tmp[:, 512:1024], tmp[:, 512:1024], normw[:, 0:1])
            sc.vop("tensor_mul", tmp[:, 0:512], o_sb[u][:, bs], pool[1][:, :])
            sc.vop("tensor_mul", o_sb[u][:, bs], tmp[:, 0:512], tmp[:, 512:1024])
            sc.mark("v"); sc.wait("t", "v"); sc.wait("s", "v")
        vm = sc.mark("v")
        sc.wait("y", "v", vm)
        sc.dma("y", of_out.ap()[u], o_sb[u][:])

    sc._flush("y")
    nc.sync.wait_ge(sc.sem["yd"], sc.cnt["yd"])
    sc.close()
    return nc


def host_prep_E(qn_all, kn_all, vv_all, gate_all, beta_all, lc_all, norm_w):
    NCH = 64
    def full(arrs, b, idx):
        return np.concatenate([arrs[b * 4 + q][idx] for q in range(4)], axis=-1)
    maskm = np.where(np.tril(np.ones((128, 128)), -1) > 0, 0.0, 1e30).astype(np.float32)
    maskmt = np.where(np.triu(np.ones((128, 128)), 1) > 0, 0.0, -1e30).astype(np.float32)
    j_idx = np.arange(128)[:, None]
    t_idx = np.arange(64)[None, :]
    maskl = np.where(j_idx <= 2 * t_idx + 1, 0.0, -1e30).astype(np.float32)
    identb = np.eye(128, dtype=np.float32).astype(bf16)
    maps = []
    for core in range(NCORE):
        m = {"normw": norm_w.reshape(128, 1).astype(np.float32), "maskm": maskm, "maskmt": maskmt, "maskl": maskl, "identb": identb}
        qfs, kfs, ktms, vtms, gates = [], [], [], [], []
        bcs, lc2s, lcqs, lcns = [], [], [], []
        for i in range(2):
            u = 2 * core + i
            b, h = u // 8, u % 8
            q_full = full(qn_all, b, h)
            k0 = full(kn_all, b, h); k1 = full(kn_all, b, 8 + h)
            v0 = full(vv_all, b, h); v1 = full(vv_all, b, 8 + h)
            kv = np.stack([k0, k1], axis=-1).reshape(128, 2 * S)
            vv = np.stack([v0, v1], axis=-1).reshape(128, 2 * S)
            g_full = full(gate_all, b, h)
            be0 = full(beta_all, b, h); be1 = full(beta_all, b, 8 + h)
            bv = np.stack([be0, be1], axis=-1).reshape(2 * S)
            lc = full(lc_all, b, h)
            lcv = np.repeat(lc, 2)
            qfs.append(q_full); kfs.append(kv)
            ktms.append(np.ascontiguousarray(kv.T).reshape(NCH, 128, 128))
            vtms.append(np.ascontiguousarray(vv.T).reshape(NCH, 128, 128))
            gates.append(g_full)
            bcs.append(np.ascontiguousarray(bv.reshape(NCH, 128).T).astype(np.float32))
            lc2s.append(np.ascontiguousarray(lcv.reshape(NCH, 128).T).astype(np.float32))
            lcq_c = np.zeros((128, NCH), np.float32)
            lcq_c[0:64, :] = lc.reshape(NCH, 64).T
            lcqs.append(lcq_c)
            lcns.append(np.ascontiguousarray(np.broadcast_to(
                lcv.reshape(NCH, 128)[:, 127], (128, NCH))).astype(np.float32))
        m["identf"] = np.eye(128, dtype=np.float32)
        m.update({
            "qf": np.stack(qfs), "kf": np.stack(kfs), "ktm": np.stack(ktms),
            "vtm": np.stack(vtms), "gate": np.stack(gates),
            "beta_c": np.stack(bcs), "lc2": np.stack(lc2s), "lcq": np.stack(lcqs),
            "lcn": np.stack(lcns),
        })
        maps.append(m)
    return maps


def build_launchF():
    nc = bass.Bass("TRN2", target_bir_lowering=False)
    of_in = nc.dram_tensor("of_in", [8, 128, TQ], BF16, kind="ExternalInput")
    x2_in = nc.dram_tensor("x2", [16, 128, TQ], F32, kind="ExternalInput")
    wo1_in = nc.dram_tensor("wo1", [16, 8, 128, 128], F32, kind="ExternalInput")
    rw_in = nc.dram_tensor("rw", [16, 128, 8], F32, kind="ExternalInput")
    rowsel_in = nc.dram_tensor("rowsel", [8, 8, 128], F32, kind="ExternalInput")
    ident_in = nc.dram_tensor("identf", [128, 128], F32, kind="ExternalInput")
    sgw_in = nc.dram_tensor("sgw", [128, 16], F32, kind="ExternalInput")
    wgu_in = nc.dram_tensor("wgu", [8, 11, 16, 128, 256], F32, kind="ExternalInput")
    wd_in = nc.dram_tensor("wd", [8, 16, 11, 128, 128], F32, kind="ExternalInput")
    swgu_in = nc.dram_tensor("swgu", [44, 16, 128, 256], F32, kind="ExternalInput")
    swd_in = nc.dram_tensor("swd", [16, 44, 128, 128], F32, kind="ExternalInput")
    out_out = nc.dram_tensor("out", [16, 128, TQ], F32, kind="ExternalOutput")

    xs = nc.alloc_sbuf_tensor("xs", [128, 16 * TQ], F32)
    o_sb = nc.alloc_sbuf_tensor("o_sb", [128, 8 * TQ], BF16)
    xln = nc.alloc_sbuf_tensor("xln", [128, 16 * TQ], BF16)
    tmp = nc.alloc_sbuf_tensor("tmp", [128, 2 * TQ], BF16)
    stats = nc.alloc_sbuf_tensor("stats", [1, 4 * TQ], F32)
    ones_bf = nc.alloc_sbuf_tensor("ones_bf", [128, 1], BF16)
    ones1_f32 = nc.alloc_sbuf_tensor("ones1_f32", [1, 128], F32)
    identf = nc.alloc_sbuf_tensor("identf_sb", [128, 128], F32)
    rw_sb = nc.alloc_sbuf_tensor("rw_sb", [128, 16 * 8], BF16)
    rowsel = nc.alloc_sbuf_tensor("rowsel_sb", [8, 8 * 128], BF16)
    sgw_sb = nc.alloc_sbuf_tensor("sgw_sb", [128, 16], BF16)
    lg = nc.alloc_sbuf_tensor("lg", [128, 8], F32)
    m1 = nc.alloc_sbuf_tensor("m1", [128, 4], F32)
    eqm = nc.alloc_sbuf_tensor("eqm", [128, 8], F32)
    ind = nc.alloc_sbuf_tensor("ind", [128, 8], F32)
    ee = nc.alloc_sbuf_tensor("ee", [128, 8], F32)
    w_tm = nc.alloc_sbuf_tensor("w_tm", [128, 8], F32)
    W_fm = nc.alloc_sbuf_tensor("W_fm", [8, TQ], BF16)
    wbc = nc.alloc_sbuf_tensor("wbc", [128, TQ], BF16)
    sg_b = nc.alloc_sbuf_tensor("sg_b", [128, TQ], BF16)
    hg = nc.alloc_sbuf_tensor("hg", [128, TQ], BF16)
    hu = nc.alloc_sbuf_tensor("hu", [128, TQ], BF16)
    hsh = nc.alloc_sbuf_tensor("hsh", [128, 44 * 512], BF16)
    he = hsh
    wov = [nc.alloc_sbuf_tensor(f"wov{i}", [128, 8 * 128], BF16) for i in range(2)]
    _osb_addr = nc.lookup_mloc(o_sb).addr
    wgb = [nc.alloc_sbuf_tensor_at(f"wgb{i}", [128, 16 * 256], BF16, offset=_osb_addr + i * 16 * 256 * 2) for i in range(2)]
    wlb = [nc.alloc_sbuf_tensor(f"wlb{i}", [128, 11 * 128], BF16) for i in range(2)]
    _st_addr = nc.lookup_mloc(stats).addr
    swb = [nc.alloc_sbuf_tensor_at(f"swb{i}", [128, 16 * 256], BF16, offset=_st_addr + i * 16 * 256 * 2) for i in range(2)]
    sdb = [nc.alloc_sbuf_tensor_at("sdb0", [128, 44 * 128], BF16, offset=_osb_addr)]

    pool = [nc.alloc_psum_tensor(f"pool{i}", [128, 512], F32) for i in range(8)]

    sc = Sched(nc).open()
    sc.init_consts()
    sc.const(1e-5, parts=1)
    sc.vop("memset", ones_bf[:], 1.0)
    sc.vop("memset", ones1_f32[:], 1.0)
    sc.mark("v")

    sc.dma("y", xs[:], _fm_ap(x2_in, 16, TQ))
    sc.dma("y", o_sb[:], _fm_ap(of_in, 8, TQ))
    ld0 = sc.cnt["yd"]
    sc.dma("g", rw_sb[:], bass.AP(rw_in, 0, [[8, 128], [128 * 8, 16], [1, 8]]))
    sc.dma("g", rowsel[:], bass.AP(rowsel_in, 0, [[128, 8], [8 * 128, 8], [1, 128]]))
    sc.dma("g", identf[:], ident_in.ap())
    sc.dma("g", sgw_sb[:], sgw_in.ap())
    gld0 = sc.cnt["gd"]

    wo_ld = {}
    wo_ld[0] = sc.dma("g", wov[0][:], bass.AP(wo1_in, 0, [[128, 128], [128 * 128, 8], [1, 128]]), key="d0")
    for ds in range(16):
        slot = ds % 2
        if ds + 1 < 16:
            if ds >= 1:
                sc.wait("g", "t", sc.cnt["t"])
            wo_ld[ds + 1] = sc.dma("g", wov[1 - slot][:],
                                   bass.AP(wo1_in, (ds + 1) * 8 * 128 * 128, [[128, 128], [128 * 128, 8], [1, 128]]),
                                   key=f"d{1 - slot}")
        sc.wait("t", f"d{slot}", wo_ld[ds])
        sc.wait("t", "yd", ld0)
        sc.wait("v", "yd", ld0)
        for th in range(2):
            for ec in range(8):
                sc.mm(pool[th][:, :], wov[slot][:, ec * 128:(ec + 1) * 128],
                      o_sb[:, ec * TQ + th * 512: ec * TQ + (th + 1) * 512],
                      start=(ec == 0), stop=(ec == 7))
            tm = sc.mark("t")
            sc.wait("v", "t", tm)
            sl = slice(th * 512, (th + 1) * 512)
            sc.vop("tensor_add", xs[:, ds * TQ:(ds + 1) * TQ][:, sl], xs[:, ds * TQ:(ds + 1) * TQ][:, sl], pool[th][:, :])
            sc.mark("v")
        sc.wait("t", "v")

    sc.barrier()
    emit_ln(sc, xs[:], xln[:], tmp[:], stats[:], pool, ones_bf, ones1_f32, 16, TQ)
    sc.barrier()

    sc.wait("v", "gd", gld0); sc.wait("t", "gd", gld0); sc.wait("s", "gd", gld0)
    for tt in range(8):
        tsl = slice(tt * 128, (tt + 1) * 128)
        for d in range(16):
            sc.mm(pool[3][:, 0:8], xln[:, d * TQ:(d + 1) * TQ][:, tsl], rw_sb[:, d * 8:(d + 1) * 8], start=(d == 0), stop=(d == 15))
        tm = sc.mark("t")
        sc.wait("v", "t", tm)
        sc.vop("tensor_copy", lg[:], pool[3][:, 0:8])
        sc.vop("reduce_max", m1[:, 0:1], lg[:], axis=AX.X)
        sc.vop("tensor_scalar_mul", m1[:, 1:2], m1[:, 0:1], -1.0)
        sc.vop("tensor_single_scalar", eqm[:], lg[:], m1[:, 0:1], ALU.is_ge)
        sc.vop("scalar_tensor_tensor", eqm[:], eqm[:], -1e30, lg[:], ALU.mult, ALU.add)
        sc.vop("reduce_max", m1[:, 2:3], eqm[:], axis=AX.X)
        sc.vop("tensor_single_scalar", ind[:], lg[:], m1[:, 2:3], ALU.is_ge)
        sc.mark("v")
        sc.wait("s", "v")
        sc.sop("activation", ee[:], lg[:], AF.Exp, bias=m1[:, 1:2])
        sc.mark("s")
        sc.wait("v", "s")
        sc.vop("tensor_mul", ee[:], ee[:], ind[:])
        sc.vop("reduce_sum", m1[:, 3:4], ee[:], axis=AX.X)
        sc.vop("reciprocal", m1[:, 3:4], m1[:, 3:4])
        sc.vop("tensor_scalar_mul", w_tm[:], ee[:], m1[:, 3:4])
        sc.mark("v")
        sc.wait("t", "v")
        sc.trans(pool[5][0:8, 0:128], w_tm[:, 0:8], identf[:, :])
        tm = sc.mark("t")
        sc.wait("v", "t", tm)
        sc.vop("tensor_copy", W_fm[0:8, tsl], pool[5][0:8, 0:128])
        sc.mark("v")
        sc.wait("t", "v")

    sgrow = nc.alloc_sbuf_tensor("sgrow", [1, TQ], F32)
    for th in range(2):
        pp = pool[2] if th == 0 else pool[4]
        for d in range(16):
            sc.mm(pp[0:1, 0:512], sgw_sb[:, d:d + 1],
                  xln[:, d * TQ + th * 512:d * TQ + th * 512 + 512], start=(d == 0), stop=(d == 15))
    tm = sc.mark("t")
    sc.wait("s", "t", tm)
    sc.sop("activation", sgrow[0:1, 0:512], pool[2][0:1, 0:512], AF.Sigmoid)
    sc.sop("activation", sgrow[0:1, 512:1024], pool[4][0:1, 0:512], AF.Sigmoid)
    sc.mark("s")
    sc.wait("t", "s")
    for th in range(2):
        pp = pool[2] if th == 0 else pool[4]
        sc.mm(pp[:, :], ones1_f32[0:1, :], sgrow[0:1, th * 512:(th + 1) * 512], start=True, stop=True)
    tm = sc.mark("t")
    sc.wait("v", "t", tm)
    sc.vop("tensor_copy", sg_b[:, 0:512], pool[2][:, :])
    sc.vop("tensor_copy", sg_b[:, 512:1024], pool[4][:, :])
    sc.mark("v")
    sc.wait("t", "v")

    sw_ld = {}
    sd_ld = {}
    prev_sw_t = {0: 0, 1: 0}
    prev_sd_t = {0: 0, 1: 0}
    shared_vm = {0: 0, 1: 0}
    shared_sm = {0: 0, 1: 0}
    sc.wait("g", "t", sc.cnt["t"]); sc.wait("g", "s", sc.cnt["s"]); sc.wait("g", "v", sc.cnt["v"])
    sw_ld[0] = sc.dma("g", swb[0][:], bass.AP(swgu_in, 0, [[256, 128], [128 * 256, 16], [1, 256]]), key="d4")
    for th in range(2):
        tsl = slice(th * 512, (th + 1) * 512)
        for ic in range(44):
            gi = th * 44 + ic
            slot = gi % 2
            if gi + 1 < 88:
                nslot = (gi + 1) % 2
                ngi = gi + 1
                nic = ngi % 44
                sc.wait("g", "t", prev_sw_t[nslot])
                sw_ld[ngi] = sc.dma("g", swb[nslot][:],
                                    bass.AP(swgu_in, nic * 16 * 128 * 256, [[256, 128], [128 * 256, 16], [1, 256]]),
                                    key=f"d{4 + nslot}")
            sc.wait("t", f"d{4 + slot}", sw_ld[gi])
            pg, pu = (pool[0], pool[1]) if ic % 2 == 0 else (pool[2], pool[3])
            sc.wait("t", "v", shared_vm[ic % 2]); sc.wait("t", "s", shared_sm[ic % 2])
            for d in range(16):
                sc.mm(pg[:, :], swb[slot][:, d * 256:d * 256 + 128], xln[:, d * TQ:(d + 1) * TQ][:, tsl],
                      start=(d == 0), stop=(d == 15))
                sc.mm(pu[:, :], swb[slot][:, d * 256 + 128:d * 256 + 256], xln[:, d * TQ:(d + 1) * TQ][:, tsl],
                      start=(d == 0), stop=(d == 15))
            tm = sc.mark("t")
            prev_sw_t[slot] = tm
            sc.wait("s", "t", tm); sc.wait("v", "t", tm)
            hsl = slice((ic % 2) * 512, (ic % 2) * 512 + 512)
            sc.sop("activation", hg[:, hsl], pg[:, :], AF.Silu)
            sc.vop("tensor_mul", hu[:, hsl], pu[:, :], sg_b[:, tsl])
            shared_sm[ic % 2] = sc.mark("s")
            sc.mark("v")
            sc.wait("v", "s")
            sc.vop("tensor_mul", hsh[:, ic * 512:(ic + 1) * 512], hg[:, hsl], hu[:, hsl])
            shared_vm[ic % 2] = sc.mark("v")
        sdown_vm = {0: 0, 1: 0}
        for fs in range(16):
            gi2 = th * 16 + fs
            sc.wait("g", "t", sc.cnt["t"])
            cnt = sc.dma("g", sdb[0][:], bass.AP(swd_in, fs * 44 * 128 * 128, [[128, 128], [128 * 128, 44], [1, 128]]),
                         key="d6", fence=True)
            sc.wait("t", "d6", cnt)
            sc.wait("t", "v", sdown_vm[fs % 2])
            for ic in range(44):
                sc.mm(pool[4 + (fs % 2)][:, :], sdb[0][:, ic * 128:(ic + 1) * 128], hsh[:, ic * 512:(ic + 1) * 512],
                      start=(ic == 0), stop=(ic == 43))
            tm = sc.mark("t")
            sc.wait("v", "t", tm)
            dst = xs[:, fs * TQ:(fs + 1) * TQ][:, tsl]
            sc.vop("tensor_add", dst, dst, pool[4 + (fs % 2)][:, :])
            sdown_vm[fs % 2] = sc.mark("v")

    wg_ld = {}
    wl_ld = {}
    prev_wg_t = {0: 0, 1: 0}
    prev_wl_t = {0: 0, 1: 0}
    sc.wait("g", "t", sc.cnt["t"])
    wg_ld[(0, 0)] = sc.dma("g", wgb[0][:], bass.AP(wgu_in, 0, [[256, 128], [128 * 256, 16], [1, 256]]), key="d0", fence=True)
    _fenced_d1 = [False]
    for e in range(8):
        sc.wait("t", "v")
        for th in range(2):
            sc.mm(pool[5][:, :], rowsel[0:8, e * 128:(e + 1) * 128], W_fm[0:8, th * 512:(th + 1) * 512], start=True, stop=True)
            tm = sc.mark("t")
            sc.wait("v", "t", tm)
            sc.vop("tensor_copy", wbc[:, th * 512:(th + 1) * 512], pool[5][:, :])
            sc.mark("v")
            sc.wait("t", "v")
        for ic in range(11):
            gi = e * 11 + ic
            slot = gi % 2
            ne, nic = (e, ic + 1) if ic + 1 < 11 else (e + 1, 0)
            if ne < 8:
                nslot = (gi + 1) % 2
                sc.wait("g", "t", prev_wg_t[nslot])
                fence = nslot == 1 and not _fenced_d1[0]
                if fence:
                    _fenced_d1[0] = True
                wg_ld[(ne, nic)] = sc.dma("g", wgb[nslot][:],
                                          bass.AP(wgu_in, (ne * 11 + nic) * 16 * 128 * 256, [[256, 128], [128 * 256, 16], [1, 256]]),
                                          key=f"d{nslot}", fence=fence)
            sc.wait("t", f"d{slot}", wg_ld[(e, ic)])
            tms = {}
            for th in range(2):
                for d in range(16):
                    sc.mm(pool[th][:, :], wgb[slot][:, d * 256:d * 256 + 128],
                          xln[:, d * TQ + th * 512: d * TQ + (th + 1) * 512], start=(d == 0), stop=(d == 15))
                    sc.mm(pool[2 + th][:, :], wgb[slot][:, d * 256 + 128:d * 256 + 256],
                          xln[:, d * TQ + th * 512: d * TQ + (th + 1) * 512], start=(d == 0), stop=(d == 15))
                tms[th] = sc.mark("t")
                sc.wait("s", "t", tms[th]); sc.wait("v", "t", tms[th])
                sl = slice(th * 512, (th + 1) * 512)
                sc.sop("activation", hg[:, sl], pool[th][:, :], AF.Silu)
                sc.vop("tensor_mul", hu[:, sl], pool[2 + th][:, :], wbc[:, sl])
                sc.mark("s"); sc.mark("v")
                sc.wait("v", "s")
                sc.vop("tensor_mul", he[:, ic * TQ + th * 512:ic * TQ + th * 512 + 512], hg[:, sl], hu[:, sl])
                sc.mark("v")
            prev_wg_t[slot] = tms[1]
            sc.wait("t", "v")
        for fs in range(16):
            gi2 = e * 16 + fs
            slot = gi2 % 2
            ne, nfs = (e, fs + 1) if fs + 1 < 16 else (e + 1, 0)
            if ne < 8:
                nslot = (gi2 + 1) % 2
                sc.wait("g", "t", prev_wl_t[nslot])
                wl_ld[(ne, nfs)] = sc.dma("g", wlb[nslot][:],
                                          bass.AP(wd_in, (ne * 16 + nfs) * 11 * 128 * 128, [[128, 128], [128 * 128, 11], [1, 128]]),
                                          key=f"d{2 + nslot}")
            if (e, fs) not in wl_ld:
                wl_ld[(e, fs)] = sc.dma("g", wlb[slot][:],
                                        bass.AP(wd_in, (e * 16 + fs) * 11 * 128 * 128, [[128, 128], [128 * 128, 11], [1, 128]]),
                                        key=f"d{2 + slot}")
            sc.wait("t", f"d{2 + slot}", wl_ld[(e, fs)])
            tms = {}
            for th in range(2):
                for ic in range(11):
                    sc.mm(pool[4 + th][:, :], wlb[slot][:, ic * 128:(ic + 1) * 128],
                          he[:, ic * TQ + th * 512: ic * TQ + th * 512 + 512],
                          start=(ic == 0), stop=(ic == 10))
                tms[th] = sc.mark("t")
                sc.wait("v", "t", tms[th])
                sl = slice(th * 512, (th + 1) * 512)
                dst = xs[:, fs * TQ:(fs + 1) * TQ][:, sl]
                sc.vop("tensor_add", dst, dst, pool[4 + th][:, :])
                sc.mark("v")
            prev_wl_t[slot] = tms[1]
            sc.wait("t", "v")

    vm = sc.mark("v")
    sc.wait("y", "v", vm)
    sc.dma("y", _fm_ap(out_out, 16, TQ), xs[:])
    sc._flush("y")
    nc.sync.wait_ge(sc.sem["yd"], sc.cnt["yd"])
    sc.close()
    return nc


def host_prep_F(of_all, x2_all, Wo, router_w, w_gate, w_up, w_down, sw_gate, sw_up, sw_down, sgw):
    wo1 = np.ascontiguousarray(Wo.reshape(8, 128, 16, 128).transpose(2, 0, 1, 3))
    rw = np.ascontiguousarray(router_w.T.reshape(16, 128, 8)).astype(np.float32)
    rowsel = np.zeros((8, 8, 128), np.float32)
    for e in range(8):
        rowsel[e, e, :] = 1.0
    identf = np.eye(128, dtype=np.float32)
    sgw_a = np.ascontiguousarray(sgw.reshape(16, 128).T).astype(np.float32)
    wgu = np.ascontiguousarray(np.concatenate(
        [w_gate.reshape(8, 16, 128, 11, 128), w_up.reshape(8, 16, 128, 11, 128)],
        axis=4).transpose(0, 3, 1, 2, 4)).reshape(8, 11, 16, 128, 256)
    wd = np.ascontiguousarray(w_down.reshape(8, 11, 128, 16, 128).transpose(0, 3, 1, 2, 4))
    swgu = np.ascontiguousarray(np.concatenate(
        [sw_gate.reshape(16, 128, 44, 128), sw_up.reshape(16, 128, 44, 128)],
        axis=3).transpose(2, 0, 1, 3)).reshape(44, 16, 128, 256)
    swd = np.ascontiguousarray(sw_down.reshape(44, 128, 16, 128).transpose(2, 0, 1, 3))
    of_by_bh = {}
    for core in range(NCORE):
        for i in range(2):
            u = 2 * core + i
            of_by_bh[(u // 8, u % 8)] = of_all[core][i]
    maps = []
    for core in range(NCORE):
        b, quarter = core // 4, core % 4
        sl = slice(quarter * TQ, (quarter + 1) * TQ)
        of_q = np.stack([of_by_bh[(b, h)][:, sl] for h in range(8)])
        maps.append({
            "of_in": np.ascontiguousarray(of_q), "x2": x2_all[core],
            "wo1": wo1, "rw": rw, "rowsel": rowsel, "identf": identf, "sgw": sgw_a,
            "wgu": wgu, "wd": wd, "swgu": swgu, "swd": swd,
        })
    return maps


def kernel(**inputs):
    inp = inputs
    mapsA = host_prep_A(np.asarray(inp["L0_attn_x"]), np.asarray(inp["L0_attn_wq"]),
                        np.asarray(inp["L0_attn_wk"]), np.asarray(inp["L0_attn_wv"]))
    resA = run_bass_kernel_spmd(build_launchA(), mapsA, core_ids=list(range(NCORE))).results
    q_all = np.stack([np.concatenate([np.asarray(resA[b * 4 + q]["q"]) for q in range(4)], axis=2) for b in range(2)])
    k_all = np.stack([np.concatenate([np.asarray(resA[b * 4 + q]["k"]) for q in range(4)], axis=2) for b in range(2)])
    v_all = np.stack([np.concatenate([np.asarray(resA[b * 4 + q]["v"]) for q in range(4)], axis=2) for b in range(2)])
    mapsB = host_prep_B(q_all, k_all, v_all)
    resB = run_bass_kernel_spmd(build_launchB(), mapsB, core_ids=list(range(NCORE))).results
    o_all = np.zeros((2, 16, 128, S), bf16)
    for u in range(32):
        o_all[u // 16, u % 16] = np.asarray(resB[u // 4]["o"])[u % 4]
    mapsC = host_prep_C(o_all, np.asarray(inp["L0_attn_x"]), np.asarray(inp["L0_attn_wo"]),
                        np.asarray(inp["L0_ffn_router_logits"]), np.asarray(inp["L0_ffn_gating_einsum"]),
                        np.asarray(inp["L0_ffn_linear"]), np.asarray(inp["L0_ffn_per_expert_scale"]),
                        np.asarray(inp["L0_ffn_router_scale"]))
    resC = run_bass_kernel_spmd(build_launchC(), mapsC, core_ids=list(range(NCORE))).results
    x2_all = [np.asarray(resC[c]["x2"]) for c in range(NCORE)]
    mapsD = host_prep_D(x2_all, np.asarray(inp["L1_attn_Wq"]), np.asarray(inp["L1_attn_Wk"]),
                        np.asarray(inp["L1_attn_Wv"]), np.asarray(inp["L1_attn_Wg"]),
                        np.asarray(inp["L1_attn_Wb"]), np.asarray(inp["L1_attn_Wa"]),
                        np.asarray(inp["L1_attn_A_log"]), np.asarray(inp["L1_attn_dt_bias"]),
                        np.asarray(inp["L1_attn_conv_q_w"]), np.asarray(inp["L1_attn_conv_k_w"]),
                        np.asarray(inp["L1_attn_conv_v_w"]))
    resD = run_bass_kernel_spmd(build_launchD(), mapsD, core_ids=list(range(NCORE))).results
    mapsE = host_prep_E([np.asarray(resD[c]["qn"]) for c in range(NCORE)],
                        [np.asarray(resD[c]["kn"]) for c in range(NCORE)],
                        [np.asarray(resD[c]["vv"]) for c in range(NCORE)],
                        [np.asarray(resD[c]["gate"]) for c in range(NCORE)],
                        [np.asarray(resD[c]["beta"]) for c in range(NCORE)],
                        [np.asarray(resD[c]["lc"]) for c in range(NCORE)],
                        np.asarray(inp["L1_attn_norm_w"]))
    resE = run_bass_kernel_spmd(build_launchE(), mapsE, core_ids=list(range(NCORE))).results
    mapsF = host_prep_F([np.asarray(resE[c]["of"]) for c in range(NCORE)], x2_all,
                        np.asarray(inp["L1_attn_Wo"]), np.asarray(inp["L1_ffn_router_w"]),
                        np.asarray(inp["L1_ffn_w_gate"]), np.asarray(inp["L1_ffn_w_up"]),
                        np.asarray(inp["L1_ffn_w_down"]), np.asarray(inp["L1_ffn_sw_gate"]),
                        np.asarray(inp["L1_ffn_sw_up"]), np.asarray(inp["L1_ffn_sw_down"]),
                        np.asarray(inp["L1_ffn_shared_gate_w"]))
    resF = run_bass_kernel_spmd(build_launchF(), mapsF, core_ids=list(range(NCORE))).results
    out = np.zeros((B, S, D), np.float32)
    for core in range(NCORE):
        b, q = core // 4, core % 4
        out[b, q * TQ:(q + 1) * TQ] = np.asarray(resF[core]["out"]).reshape(D, TQ).T
    return out


# version 06 (final)
# speedup vs baseline: 1.4174x; 1.0118x over previous
import numpy as np
import concourse.bass as bass
import concourse.mybir as mybir
from concourse.bass_utils import run_bass_kernel_spmd
import ml_dtypes

F32 = mybir.dt.float32
BF16 = mybir.dt.bfloat16
AF = mybir.ActivationFunctionType
ALU = mybir.AluOpType
AX = mybir.AxisListType

B, S, D = 2, 4096, 2048
NCORE = 8
TQ = S // 4
CHUNK = 64
H0, HKV0, DH = 16, 4, 128
bf16 = ml_dtypes.bfloat16


class Sched:

    ENGINES = ("t", "v", "s", "g", "y")

    def __init__(self, nc):
        self.nc = nc
        self.sem = {}
        self.cnt = {}
        self.last = {}
        self.unmarked = {}
        self.pending = {k: {} for k in self.ENGINES}
        self._ctx = []

    def open(self):
        nc = self.nc
        for k in self.ENGINES + ("gd", "yd", "d0", "d1", "d2", "d3", "d4", "d5", "d6", "d7"):
            cm = nc.semaphore("sem_" + k)
            self.sem[k] = cm.__enter__()
            self._ctx.append(cm)
            self.cnt[k] = 0
        blk = nc.Block()
        self.block = blk.__enter__()
        self._ctx.append(blk)
        return self

    def close(self):
        for cm in reversed(self._ctx):
            cm.__exit__(None, None, None)

    def _engine(self, k):
        nc = self.nc
        return {"t": nc.tensor, "v": nc.vector, "s": nc.scalar,
                "g": nc.gpsimd, "y": nc.sync}[k]

    def _flush(self, k):
        e = self._engine(k)
        for sk, val in sorted(self.pending[k].items()):
            if val > 0:
                e.wait_ge(self.sem[sk], val)
        self.pending[k] = {}

    def _record(self, k, inst):
        self.last[k] = inst
        self.unmarked[k] = self.unmarked.get(k, 0) + 1

    def mark(self, k):
        if self.unmarked.get(k, 0) > 0:
            self.last[k].then_inc(self.sem[k], 1)
            self.cnt[k] += 1
            self.unmarked[k] = 0
        return self.cnt[k]

    def wait(self, waiter, prod, val=None):
        if val is None:
            val = self.cnt[prod]
        if val <= 0:
            return
        cur = self.pending[waiter].get(prod, 0)
        self.pending[waiter][prod] = max(cur, val)

    def barrier(self, engines=("t", "v", "s")):
        for k in engines:
            self.mark(k)
        allk = list(engines) + ["gd", "yd", "d0", "d1", "d2", "d3", "d4", "d5", "d6", "d7"]
        for k in engines:
            for p in allk:
                if p == k:
                    continue
                self.wait(k, p, self.cnt[p])

    def dma(self, q, out, in_, key=None, fence=False, **kw):
        if fence:
            semk = key or (q + "d")
            self.wait(q, semk, self.cnt[semk])
        self._flush(q)
        inst = self._engine(q).dma_start(out=out, in_=in_, **kw)
        semk = key or (q + "d")
        inst.then_inc(self.sem[semk], 16)
        self.cnt[semk] += 16
        return self.cnt[semk]

    def mm(self, out, lhsT, rhs, start, stop):
        self._flush("t")
        inst = self.nc.tensor.matmul(out, lhsT, rhs, start=start, stop=stop)
        self._record("t", inst)
        return inst

    def trans(self, out, in_, ident):
        self._flush("t")
        inst = self.nc.tensor.transpose(out, in_, ident)
        self._record("t", inst)
        return inst

    def __init2__(self):
        pass

    def _selfsync(self, k):
        if not getattr(self, "SELFSYNC", True):
            return
        if self.unmarked.get(k, 0) > 0:
            self.mark(k)
        self.pending[k][k] = max(self.pending[k].get(k, 0), self.cnt[k])

    def vop(self, name, *a, **kw):
        self._selfsync("v")
        self._flush("v")
        inst = getattr(self.nc.vector, name)(*a, **kw)
        self._record("v", inst)
        return inst

    def sop(self, name, *a, **kw):
        self._selfsync("s")
        self._flush("s")
        inst = getattr(self.nc.scalar, name)(*a, **kw)
        self._record("s", inst)
        return inst

    def gop(self, name, *a, **kw):
        self._selfsync("g")
        self._flush("g")
        inst = getattr(self.nc.gpsimd, name)(*a, **kw)
        self._record("g", inst)
        return inst

    def init_consts(self):
        self._cf = self.nc.alloc_sbuf_tensor("constsF", [128, 16], F32)
        self._cb = self.nc.alloc_sbuf_tensor("constsB", [128, 16], BF16)
        self._cmap = {}
        self._cn = {F32: 0, BF16: 0}

    def const(self, value, dtype=F32, parts=128):
        key = (dtype, float(value))
        if key not in self._cmap:
            t = self._cf if dtype == F32 else self._cb
            i = self._cn[dtype]
            self._cn[dtype] += 1
            self.vop("memset", t[:, i:i + 1], float(value))
            self._cmap[key] = t[:, i:i + 1]
        return self._cmap[key][0:parts, :]


def emit_ln(sc, xs, xln, tmp, stats, pool, ones_bf, ones1_f32, nchunks, T, eps=1e-5):
    assert T == 1024
    for c in range(nchunks):
        xc = xs[:, c * T:(c + 1) * T]
        sc.sop("activation", tmp[:, T:2 * T], xc, AF.Square)
        sc.vop("tensor_copy", tmp[:, 0:T], xc)
        sc.mark("s"); sc.mark("v")
        sc.wait("t", "s"); sc.wait("t", "v")
        for h in range(2):
            sl = slice(h * 512, (h + 1) * 512)
            sc.mm(pool[h][0:1, :], ones_bf[:, 0:1], tmp[:, 0:T][:, sl], start=(c == 0), stop=(c == nchunks - 1))
            sc.mm(pool[2 + h][0:1, :], ones_bf[:, 0:1], tmp[:, T:2 * T][:, sl], start=(c == 0), stop=(c == nchunks - 1))
        sc.mark("t")
        sc.wait("s", "t"); sc.wait("v", "t")
    sc.mark("t")
    sc.wait("v", "t")
    sumx = stats[0:1, 0:T]
    var_t = stats[0:1, T:2 * T]
    rstd = stats[0:1, 2 * T:3 * T]
    mr = stats[0:1, 3 * T:4 * T]
    for h in range(2):
        sl = slice(h * 512, (h + 1) * 512)
        sc.vop("tensor_copy", sumx[:, sl], pool[h][0:1, :])
        sc.vop("tensor_copy", var_t[:, sl], pool[2 + h][0:1, :])
    sc.vop("scalar_tensor_tensor", mr, sumx, 1.0, sumx, ALU.mult, ALU.mult)
    sc.vop("scalar_tensor_tensor", var_t, mr, -1.0 / D, var_t, ALU.mult, ALU.add)
    sc.mark("v")
    sc.wait("s", "v")
    sc.sop("activation", rstd, var_t, AF.Ln, scale=1.0 / D, bias=sc.const(eps, parts=1))
    sc.sop("activation", rstd, rstd, AF.Exp, scale=-0.5)
    sc.mark("s")
    sc.wait("v", "s")
    sc.vop("scalar_tensor_tensor", mr, sumx, 1.0 / D, rstd, ALU.mult, ALU.mult)
    sc.mark("v")
    sc.wait("t", "v")
    for h in range(2):
        sc.mm(pool[4 + h][:, :], ones1_f32[0:1, :], rstd[:, h * 512:(h + 1) * 512], start=True, stop=True)
        sc.mm(pool[6 + h][:, :], ones1_f32[0:1, :], mr[:, h * 512:(h + 1) * 512], start=True, stop=True)
    sc.mark("t")
    sc.wait("v", "t")
    ftmp = stats[0:1, 0:1]
    for c in range(nchunks):
        for h in range(2):
            sl = slice(h * 512, (h + 1) * 512)
            xc = xs[:, c * T:(c + 1) * T][:, sl]
            oc = xln[:, c * T:(c + 1) * T][:, sl]
            sc.vop("tensor_mul", tmp[:, 0:512], xc, pool[4 + h][:, :])
            sc.vop("tensor_sub", oc, tmp[:, 0:512], pool[6 + h][:, :])
    sc.mark("v")


def _host_fm(a2d, nch):
    T, F = a2d.shape
    assert F == nch * 128
    return np.ascontiguousarray(a2d.T.reshape(nch, 128, T))


def _fm_ap(h, nch, T):
    return bass.AP(h, 0, [[T, 128], [128 * T, nch], [1, T]])


def build_launchA():
    nc = bass.Bass("TRN2", target_bir_lowering=False)
    x_in = nc.dram_tensor("x", [16, 128, TQ], F32, kind="ExternalInput")
    w_in = nc.dram_tensor("wqkv", [8, 16, 128, 384], F32, kind="ExternalInput").ap()
    cos_in = nc.dram_tensor("cos", [128, TQ], F32, kind="ExternalInput").ap()
    sin_in = nc.dram_tensor("sin", [128, TQ], F32, kind="ExternalInput").ap()
    rotp_in = nc.dram_tensor("rotp", [128, 128], F32, kind="ExternalInput").ap()
    q_out = nc.dram_tensor("q", [16, 128, TQ], BF16, kind="ExternalOutput").ap()
    k_out = nc.dram_tensor("k", [4, 128, TQ], BF16, kind="ExternalOutput").ap()
    v_out = nc.dram_tensor("v", [4, 128, TQ], BF16, kind="ExternalOutput").ap()

    xs = nc.alloc_sbuf_tensor("xs", [128, 16 * TQ], F32)
    xln = nc.alloc_sbuf_tensor("xln", [128, 16 * TQ], BF16)
    tmp = nc.alloc_sbuf_tensor("tmp", [128, 2 * TQ], BF16)
    stats = nc.alloc_sbuf_tensor("stats", [1, 4 * TQ], F32)
    cos_t = nc.alloc_sbuf_tensor("cos_t", [128, TQ], BF16)
    sin_t = nc.alloc_sbuf_tensor("sin_t", [128, TQ], BF16)
    rotp = nc.alloc_sbuf_tensor("rotp_sb", [128, 128], BF16)
    ones_bf = nc.alloc_sbuf_tensor("ones_bf", [128, 1], BF16)
    ones1_f32 = nc.alloc_sbuf_tensor("ones1_f32", [1, 128], F32)
    wbuf = [nc.alloc_sbuf_tensor(f"wbuf{i}", [128, 384], BF16) for i in range(2)]
    out_sb = nc.alloc_sbuf_tensor("out_sb", [128, 3 * TQ], BF16)

    pool = [nc.alloc_psum_tensor(f"pool{i}", [128, 512], F32) for i in range(8)]

    sc = Sched(nc).open()
    sc.init_consts()
    sc.const(1e-5, parts=1)

    sc.dma("y", xs[:], _fm_ap(x_in, 16, TQ))
    sc.dma("g", cos_t[:], cos_in)
    sc.dma("g", sin_t[:], sin_in)
    sc.dma("g", rotp[:], rotp_in)
    sc.vop("memset", ones_bf[:], 1.0)
    sc.vop("memset", ones1_f32[:], 1.0)
    sc.mark("v")
    sc.wait("t", "v")
    sc.wait("s", "v")
    sc.wait("s", "yd", 16)
    sc.wait("v", "yd", 16)

    emit_ln(sc, xs[:], xln[:], tmp[:], stats[:], pool, ones_bf, ones1_f32, 16, TQ)

    sc.barrier()
    sc.wait("t", "gd", sc.cnt["gd"])
    for grp in range(8):
        for d in range(16):
            buf = wbuf[d % 2]
            c = sc.dma("g", buf[:], w_in[grp, d], key=f"d{d % 2}")
            sc.wait("t", f"d{d % 2}", c)
            for oc3 in range(3):
                for th in range(2):
                    sc.mm(pool[oc3 * 2 + th][:, :],
                          buf[:, oc3 * 128:(oc3 + 1) * 128],
                          xln[:, d * TQ + th * 512: d * TQ + (th + 1) * 512],
                          start=(d == 0), stop=(d == 15))
            if d >= 1:
                sc.mark("t")
                sc.wait("g", "t")
        tmark = sc.mark("t")
        sc.wait("v", "t", tmark)
        sc.wait("s", "t", tmark)
        for oc3 in range(3):
            oc = grp * 3 + oc3
            dst = out_sb[:, oc3 * TQ:(oc3 + 1) * TQ]
            p0 = pool[oc3 * 2][:, :]
            p1 = pool[oc3 * 2 + 1][:, :]
            if oc < 20:
                sc.vop("tensor_copy", tmp[:, 0:512], p0)
                sc.vop("tensor_copy", tmp[:, 512:1024], p1)
                sc.mark("v")
                sc.wait("t", "v")
                for th in range(2):
                    sc.mm(pool[6 + th][:, :], rotp[:, :], tmp[:, th * 512:(th + 1) * 512], start=True, stop=True)
                sc.mark("t")
                sc.wait("v", "t")
                sc.wait("s", "t")
                sc.vop("tensor_mul", dst, tmp[:, 0:TQ], cos_t[:, :])
                for th in range(2):
                    sc.sop("activation", tmp[:, TQ + th * 512:TQ + (th + 1) * 512], pool[6 + th][:, :], AF.Copy)
                sc.mark("s")
                sc.wait("v", "s")
                sc.vop("tensor_mul", tmp[:, TQ:2 * TQ], tmp[:, TQ:2 * TQ], sin_t[:, :])
                sc.vop("tensor_add", dst, dst, tmp[:, TQ:2 * TQ])
                sc.mark("v")
                sc.wait("t", "v")
            else:
                sc.vop("tensor_copy", dst[:, 0:512], p0)
                sc.vop("tensor_copy", dst[:, 512:1024], p1)
        vmark = sc.mark("v")
        sc.wait("t", "v", vmark)
        sc.wait("y", "v", vmark)
        for oc3 in range(3):
            oc = grp * 3 + oc3
            src = out_sb[:, oc3 * TQ:(oc3 + 1) * TQ]
            if oc < 16:
                sc.dma("y", q_out[oc], src)
            elif oc < 20:
                sc.dma("y", k_out[oc - 16], src)
            else:
                sc.dma("y", v_out[oc - 20], src)
        sc.wait("v", "yd", sc.cnt["yd"])

    sc._flush("y")
    nc.sync.wait_ge(sc.sem["yd"], sc.cnt["yd"])
    sc.close()
    return nc


def host_prep_A(x_full, wq, wk, wv):
    wcat = np.concatenate([wq.T, wk.T, wv.T], axis=1)
    w_arr = np.ascontiguousarray(
        wcat.reshape(16, 128, 8, 3, 128).transpose(2, 0, 1, 3, 4)).reshape(8, 16, 128, 384)
    d = np.arange(64, dtype=np.float64)
    inv = 1.0 / (10000.0 ** (2 * d / 128.0))
    pos = np.arange(S, dtype=np.float64)
    fr = pos[:, None] * inv[None, :]
    cos = np.cos(fr).astype(np.float32)
    sin = np.sin(fr).astype(np.float32)
    cos_full = np.ascontiguousarray(np.concatenate([cos, cos], axis=1).T)
    sin_full = np.ascontiguousarray(np.concatenate([sin, sin], axis=1).T)
    rotp = np.zeros((128, 128), np.float32)
    for r in range(64):
        rotp[r + 64, r] = -1.0
    for r in range(64, 128):
        rotp[r - 64, r] = 1.0
    maps = []
    for core in range(NCORE):
        b, quarter = core // 4, core % 4
        sl = slice(quarter * TQ, (quarter + 1) * TQ)
        maps.append({
            "x": _host_fm(x_full[b, sl], 16).astype(np.float32),
            "wqkv": w_arr,
            "cos": np.ascontiguousarray(cos_full[:, sl]),
            "sin": np.ascontiguousarray(sin_full[:, sl]),
            "rotp": rotp,
        })
    return maps


def build_launchB():
    nc = bass.Bass("TRN2", target_bir_lowering=False)
    NU = 4
    qf_in = nc.dram_tensor("qf", [NU, 128, S], BF16, kind="ExternalInput")
    kf_in = nc.dram_tensor("kf", [NU, 128, S], BF16, kind="ExternalInput")
    ktm_in = nc.dram_tensor("ktm", [NU, 64, 128, 128], BF16, kind="ExternalInput")
    vtm_in = nc.dram_tensor("vtm", [NU, 64, 128, 128], BF16, kind="ExternalInput")
    diagT_in = nc.dram_tensor("diagT", [NU, 64, 64], F32, kind="ExternalInput")
    qdecT_in = nc.dram_tensor("qdecT", [NU, 128, 64], BF16, kind="ExternalInput")
    kdec_in = nc.dram_tensor("kdec", [NU, 128, 1], F32, kind="ExternalInput")
    blkc_in = nc.dram_tensor("blkc", [NU, 128, 1], F32, kind="ExternalInput")
    o_out = nc.dram_tensor("o", [NU, 128, S], BF16, kind="ExternalOutput")

    NC64 = S // 64
    qf = [nc.alloc_sbuf_tensor(f"qf{i}", [128, S], BF16) for i in range(2)]
    kf = [nc.alloc_sbuf_tensor(f"kf{i}", [128, S], BF16) for i in range(2)]
    ktm = [nc.alloc_sbuf_tensor(f"ktm{i}", [128, 64 * 128], BF16) for i in range(2)]
    vtm = [nc.alloc_sbuf_tensor(f"vtm{i}", [128, 64 * 128], BF16) for i in range(2)]
    diagT = [nc.alloc_sbuf_tensor(f"diagT{i}", [64, 64], F32) for i in range(2)]
    qdecT = [nc.alloc_sbuf_tensor(f"qdecT{i}", [128, 64], BF16) for i in range(2)]
    kdec = [nc.alloc_sbuf_tensor(f"kdec{i}", [128, 1], F32) for i in range(2)]
    blkc = [nc.alloc_sbuf_tensor(f"blkc{i}", [128, 1], F32) for i in range(2)]
    o_sb = [nc.alloc_sbuf_tensor(f"osb{i}", [128, S], BF16) for i in range(2)]

    kv = nc.alloc_sbuf_tensor("kv", [128, 128], F32)
    kv_bf = nc.alloc_sbuf_tensor("kv_bf", [128, 128], BF16)
    scT = nc.alloc_sbuf_tensor("scT", [128, 64], BF16)
    qt = nc.alloc_sbuf_tensor("qt", [128, 64], BF16)
    vt = nc.alloc_sbuf_tensor("vt", [128, 128], BF16)

    pool = [nc.alloc_psum_tensor(f"pool{i}", [128, 512], F32) for i in range(4)]

    sc = Sched(nc).open()
    sc.init_consts()
    sc.vop("memset", scT[:], 0.0)
    sc.mark("v")

    def load_unit(u, slot):
        ks = f"d{slot}"
        sc.dma("y", qf[slot][:], qf_in.ap()[u], key=ks)
        sc.dma("y", kf[slot][:], kf_in.ap()[u], key=ks)
        sc.dma("y", ktm[slot][:], bass.AP(ktm_in, u * 64 * 128 * 128, [[128, 128], [128 * 128, 64], [1, 128]]), key=ks)
        sc.dma("y", vtm[slot][:], bass.AP(vtm_in, u * 64 * 128 * 128, [[128, 128], [128 * 128, 64], [1, 128]]), key=ks)
        sc.dma("g", diagT[slot][:], diagT_in.ap()[u], key=f"d{2 + slot}")
        sc.dma("g", qdecT[slot][:], qdecT_in.ap()[u], key=f"d{2 + slot}")
        sc.dma("g", kdec[slot][:], kdec_in.ap()[u], key=f"d{2 + slot}")
        sc.dma("g", blkc[slot][:], blkc_in.ap()[u], key=f"d{2 + slot}")
        return sc.cnt[ks], sc.cnt[f"d{2 + slot}"]

    loads = {}
    done_v = {}
    loads[0] = load_unit(0, 0)
    for u in range(NU):
        slot = u % 2
        if u + 1 < NU:
            if u >= 1:
                sc.wait("g", "v", done_v[u - 1])
            loads[u + 1] = load_unit(u + 1, 1 - slot)
        ydc, gdc = loads[u]
        ks, kg = f"d{slot}", f"d{2 + slot}"
        sc.wait("t", ks, ydc); sc.wait("t", kg, gdc)
        sc.wait("v", ks, ydc); sc.wait("v", kg, gdc)
        sc.vop("memset", kv[:], 0.0)
        sc.vop("memset", kv_bf[:], 0.0)
        sc.mark("v")
        sc.wait("t", "v")
        for c in range(NC64):
            t0, t1 = c * 64, (c + 1) * 64
            sc.mm(pool[0][0:64, 0:64], kf[slot][:, t0:t1], qf[slot][:, t0:t1], start=True, stop=True)
            sc.mark("t")
            sc.wait("v", "t")
            sc.vop("tensor_mul", scT[0:64, :], pool[0][0:64, 0:64], diagT[slot][:, :])
            sc.vop("tensor_mul", qt[:], qf[slot][:, t0:t1], qdecT[slot][:, :])
            sc.vop("tensor_scalar_mul", vt[:], vtm[slot][:, c * 128:(c + 1) * 128], kdec[slot][:, 0:1])
            sc.mark("v")
            sc.wait("t", "v")
            sc.mm(pool[1][:, 0:64], vtm[slot][:, c * 128:(c + 1) * 128], scT[:, :], start=True, stop=False)
            sc.mm(pool[1][:, 0:64], kv_bf[:], qt[:], start=False, stop=True)
            sc.mm(pool[2][:, 0:128], ktm[slot][:, c * 128:(c + 1) * 128], vt[:], start=True, stop=True)
            sc.mark("t")
            sc.wait("v", "t")
            sc.vop("scalar_tensor_tensor", kv[:], kv[:], blkc[slot][:, 0:1], pool[2][:, 0:128], ALU.mult, ALU.add)
            sc.vop("tensor_copy", o_sb[slot][:, t0:t1], pool[1][:, 0:64])
            sc.mark("v")
            sc.wait("s", "v")
            sc.sop("activation", kv_bf[:], kv[:], AF.Copy)
            sc.mark("s")
            sc.wait("t", "s")
            sc.wait("t", "v")
        vm = sc.mark("v")
        done_v[u] = vm
        sc.wait("y", "v", vm)
        sc.dma("y", o_out.ap()[u], o_sb[slot][:])
        sc.wait("y", "yd", sc.cnt["yd"])

    sc._flush("y")
    nc.sync.wait_ge(sc.sem["yd"], sc.cnt["yd"])
    sc.close()
    return nc


def host_prep_B(q_all, k_all, v_all):
    lam_all = np.exp2(-8.0 * np.arange(1, 17) / 16.0).astype(np.float64)
    r = np.arange(64, dtype=np.float64)
    maps = []
    for core in range(NCORE):
        units = [4 * core + i for i in range(4)]
        qf = np.stack([q_all[u // 16, u % 16] for u in units])
        kf = np.stack([k_all[u // 16, (u % 16) // 4] for u in units])
        ktm = np.zeros((4, 64, 128, 128), bf16)
        vtm = np.zeros((4, 64, 128, 128), bf16)
        diagT = np.zeros((4, 64, 64), np.float32)
        qdecT = np.zeros((4, 128, 64), np.float32)
        kdec = np.zeros((4, 128, 1), np.float32)
        blkc = np.zeros((4, 128, 1), np.float32)
        for i, u in enumerate(units):
            b, h = u // 16, u % 16
            lam = lam_all[h]
            kfm = k_all[b, h // 4]
            vfm = v_all[b, h // 4]
            ktm[i, :, 0:64, :] = kfm.T.reshape(64, 64, 128)
            vtm[i, :, 0:64, :] = vfm.T.reshape(64, 64, 128)
            diff = r[:, None] - r[None, :]
            diagT[i] = np.where(diff >= 0, np.exp(-lam * diff), 0.0).T.astype(np.float32)
            qdecT[i] = np.broadcast_to(np.exp(-lam * (r + 1.0)).astype(np.float32), (128, 64))
            kdec[i, 0:64, 0] = np.exp(-lam * (63.0 - r)).astype(np.float32)
            blkc[i, :, 0] = np.exp(-lam * 64.0)
        maps.append({
            "qf": qf, "kf": kf, "ktm": ktm, "vtm": vtm,
            "diagT": diagT, "qdecT": qdecT.astype(bf16), "kdec": kdec, "blkc": blkc,
        })
    return maps


def build_launchC():
    nc = bass.Bass("TRN2", target_bir_lowering=False)
    o_in = nc.dram_tensor("o_in", [16, 128, TQ], BF16, kind="ExternalInput")
    x_in = nc.dram_tensor("x", [16, 128, TQ], F32, kind="ExternalInput")
    woT_in = nc.dram_tensor("woT", [16, 16, 128, 128], F32, kind="ExternalInput")
    rl_in = nc.dram_tensor("rl", [16, 128, 8], F32, kind="ExternalInput")
    rsc_in = nc.dram_tensor("rscale", [128, 16], F32, kind="ExternalInput")
    pes_in = nc.dram_tensor("pes", [8, 1], F32, kind="ExternalInput")
    rowsel_in = nc.dram_tensor("rowsel", [8, 8, 128], F32, kind="ExternalInput")
    ident_in = nc.dram_tensor("identf", [128, 128], F32, kind="ExternalInput")
    wg_in = nc.dram_tensor("wgate", [8, 16, 16, 128, 256], F32, kind="ExternalInput")
    wl_in = nc.dram_tensor("wlin", [8, 16, 16, 128, 128], F32, kind="ExternalInput")
    x2_out = nc.dram_tensor("x2", [16, 128, TQ], F32, kind="ExternalOutput")
    dbgw_out = nc.dram_tensor("dbgw", [8, TQ], F32, kind="ExternalOutput")
    dbgx1_out = nc.dram_tensor("dbgx1", [16, 128, TQ], F32, kind="ExternalOutput")
    dbglg_out = nc.dram_tensor("dbglg", [8, 128, 8], F32, kind="ExternalOutput")

    xs = nc.alloc_sbuf_tensor("xs", [128, 16 * TQ], F32)
    o_sb = nc.alloc_sbuf_tensor("o_sb", [128, 16 * TQ], BF16)
    xln = nc.alloc_sbuf_tensor("xln", [128, 16 * TQ], BF16)
    tmp = nc.alloc_sbuf_tensor("tmp", [128, 2 * TQ], BF16)
    stats = nc.alloc_sbuf_tensor("stats", [1, 4 * TQ], F32)
    ones_bf = nc.alloc_sbuf_tensor("ones_bf", [128, 1], BF16)
    ones1_f32 = nc.alloc_sbuf_tensor("ones1_f32", [1, 128], F32)
    identf = nc.alloc_sbuf_tensor("identf_sb", [128, 128], F32)
    rlp = nc.alloc_sbuf_tensor("rlp", [128, 16 * 8], BF16)
    rl_sb = nc.alloc_sbuf_tensor("rl_sb", [128, 16 * 8], F32)
    rsc_sb = nc.alloc_sbuf_tensor("rsc_sb", [128, 16], F32)
    pes_sb = nc.alloc_sbuf_tensor("pes_sb", [8, 1], F32)
    rowsel = nc.alloc_sbuf_tensor("rowsel_sb", [8, 8 * 128], BF16)
    lg = nc.alloc_sbuf_tensor("lg", [128, 8], F32)
    m1 = nc.alloc_sbuf_tensor("m1", [128, 4], F32)
    eqm = nc.alloc_sbuf_tensor("eqm", [128, 8], F32)
    ind = nc.alloc_sbuf_tensor("ind", [128, 8], F32)
    ee = nc.alloc_sbuf_tensor("ee", [128, 8], F32)
    w_tm = nc.alloc_sbuf_tensor("w_tm", [128, 8], F32)
    rs_sb = nc.alloc_sbuf_tensor("rs_sb", [128, 1], F32)
    W_fm = nc.alloc_sbuf_tensor("W_fm", [8, TQ], BF16)
    W_dbg = nc.alloc_sbuf_tensor("W_dbg", [8, TQ], F32)
    wbc = nc.alloc_sbuf_tensor("wbc", [128, TQ], BF16)
    hg = nc.alloc_sbuf_tensor("hg", [128, TQ], BF16)
    hu = nc.alloc_sbuf_tensor("hu", [128, TQ], BF16)
    wov = [nc.alloc_sbuf_tensor(f"wov{i}", [128, 16 * 128], BF16) for i in range(2)]
    wgb = [nc.alloc_sbuf_tensor(f"wgb{i}", [128, 16 * 256], BF16) for i in range(2)]
    wlb = [nc.alloc_sbuf_tensor(f"wlb{i}", [128, 16 * 128], BF16) for i in range(2)]

    pool = [nc.alloc_psum_tensor(f"pool{i}", [128, 512], F32) for i in range(8)]

    sc = Sched(nc).open()
    sc.init_consts()
    sc.const(1e-5, parts=1)
    sc.const(1e-6, parts=1)
    sc.const(-0.5 * float(np.log(2048.0)))
    sc.vop("memset", ones_bf[:], 1.0)
    sc.vop("memset", ones1_f32[:], 1.0)
    sc.mark("v")

    sc.dma("y", xs[:], _fm_ap(x_in, 16, TQ))
    sc.dma("y", o_sb[:], _fm_ap(o_in, 16, TQ))
    ld0 = sc.cnt["yd"]
    sc.dma("g", rl_sb[:], bass.AP(rl_in, 0, [[8, 128], [128 * 8, 16], [1, 8]]))
    sc.dma("g", rsc_sb[:], rsc_in.ap())
    sc.dma("g", pes_sb[:], pes_in.ap())
    sc.dma("g", rowsel[:], bass.AP(rowsel_in, 0, [[128, 8], [8 * 128, 8], [1, 128]]))
    sc.dma("g", identf[:], ident_in.ap())
    gld0 = sc.cnt["gd"]

    wo_ld = {}
    wo_ld[0] = sc.dma("g", wov[0][:], bass.AP(woT_in, 0, [[128, 128], [128 * 128, 16], [1, 128]]), key="d0")
    for ds in range(16):
        slot = ds % 2
        if ds + 1 < 16:
            if ds >= 1:
                sc.wait("g", "t", sc.cnt["t"])
            wo_ld[ds + 1] = sc.dma("g", wov[1 - slot][:],
                                   bass.AP(woT_in, (ds + 1) * 16 * 128 * 128, [[128, 128], [128 * 128, 16], [1, 128]]),
                                   key=f"d{1 - slot}")
        sc.wait("t", f"d{slot}", wo_ld[ds])
        sc.wait("t", "yd", ld0)
        sc.wait("v", "yd", ld0)
        for th in range(2):
            for ec in range(16):
                sc.mm(pool[th][:, :], wov[slot][:, ec * 128:(ec + 1) * 128],
                      o_sb[:, ec * TQ + th * 512: ec * TQ + (th + 1) * 512],
                      start=(ec == 0), stop=(ec == 15))
            tm = sc.mark("t")
            sc.wait("v", "t", tm)
            sl = slice(th * 512, (th + 1) * 512)
            sc.vop("tensor_add", xs[:, ds * TQ:(ds + 1) * TQ][:, sl], xs[:, ds * TQ:(ds + 1) * TQ][:, sl], pool[th][:, :])
            sc.mark("v")
        sc.wait("t", "v")

    sc.barrier()
    sc.wait("y", "v", sc.cnt["v"])
    sc.dma("y", _fm_ap(dbgx1_out, 16, TQ), xs[:])
    sc.wait("v", "yd", sc.cnt["yd"])
    emit_ln(sc, xs[:], xln[:], tmp[:], stats[:], pool, ones_bf, ones1_f32, 16, TQ)
    sc.barrier()

    sc.wait("v", "gd", gld0)
    sc.wait("t", "gd", gld0)
    sc.wait("s", "gd", gld0)
    for d in range(16):
        sc.vop("tensor_scalar_mul", rlp[:, d * 8:(d + 1) * 8], rl_sb[:, d * 8:(d + 1) * 8], rsc_sb[:, d:d + 1])
    sc.mark("v")
    sc.wait("t", "v"); sc.wait("s", "v")
    for tt in range(8):
        tsl = slice(tt * 128, (tt + 1) * 128)
        for d in range(16):
            sc.sop("activation", tmp[:, d * 128:(d + 1) * 128], xln[:, d * TQ:(d + 1) * TQ][:, tsl], AF.Square)
        sm = sc.mark("s")
        sc.wait("t", "s", sm)
        for d in range(16):
            sc.mm(pool[2][:, 0:1], tmp[:, d * 128:(d + 1) * 128], ones_bf[:, 0:1], start=(d == 0), stop=(d == 15))
            sc.mm(pool[3][:, 0:8], xln[:, d * TQ:(d + 1) * TQ][:, tsl], rlp[:, d * 8:(d + 1) * 8], start=(d == 0), stop=(d == 15))
        tm = sc.mark("t")
        sc.wait("s", "t", tm); sc.wait("v", "t", tm)
        sc.sop("activation", rs_sb[:], pool[2][:, 0:1], AF.Ln, scale=1.0 / D, bias=sc.const(1e-6))
        sc.sop("activation", rs_sb[:], rs_sb[:], AF.Exp, scale=-0.5,
               bias=sc.const(-0.5 * float(np.log(2048.0))))
        sc.mark("s")
        sc.wait("v", "s")
        sc.vop("tensor_scalar_mul", lg[:], pool[3][:, 0:8], rs_sb[:, 0:1])
        sc.mark("v"); sc.wait("y", "v", sc.cnt["v"])
        sc.dma("y", dbglg_out.ap()[tt], lg[:])
        sc.wait("v", "yd", sc.cnt["yd"])
        sc.vop("reduce_max", m1[:, 0:1], lg[:], axis=AX.X)
        sc.vop("tensor_scalar_mul", m1[:, 1:2], m1[:, 0:1], -1.0)
        sc.vop("tensor_single_scalar", eqm[:], lg[:], m1[:, 0:1], ALU.is_ge)
        sc.vop("scalar_tensor_tensor", eqm[:], eqm[:], -1e30, lg[:], ALU.mult, ALU.add)
        sc.vop("reduce_max", m1[:, 2:3], eqm[:], axis=AX.X)
        sc.vop("tensor_single_scalar", ind[:], lg[:], m1[:, 2:3], ALU.is_ge)
        sc.mark("v")
        sc.wait("s", "v")
        sc.sop("activation", ee[:], lg[:], AF.Exp, bias=m1[:, 1:2])
        sc.mark("s")
        sc.wait("v", "s")
        sc.vop("tensor_mul", ee[:], ee[:], ind[:])
        sc.vop("reduce_sum", m1[:, 3:4], ee[:], axis=AX.X)
        sc.vop("reciprocal", m1[:, 3:4], m1[:, 3:4])
        sc.vop("tensor_scalar_mul", w_tm[:], ee[:], m1[:, 3:4])
        sc.mark("v")
        sc.wait("t", "v")
        sc.trans(pool[5][0:8, 0:128], w_tm[:, 0:8], identf[:, :])
        tm = sc.mark("t")
        sc.wait("v", "t", tm)
        sc.vop("tensor_scalar_mul", W_fm[0:8, tsl], pool[5][0:8, 0:128], pes_sb[:, 0:1])
        sc.vop("tensor_scalar_mul", W_dbg[0:8, tsl], pool[5][0:8, 0:128], pes_sb[:, 0:1])
        sc.mark("v")
        sc.wait("t", "v")

    vm = sc.mark("v")
    sc.wait("y", "v", vm)
    sc.dma("y", dbgw_out.ap(), W_dbg[0:8, :])

    wg_ld = {}
    wl_ld = {}
    wg_ld[(0, 0)] = sc.dma("g", wgb[0][:], bass.AP(wg_in, 0, [[256, 128], [128 * 256, 16], [1, 256]]), key="d4")
    prev_t_for_slot = {0: 0, 1: 0}
    prev_wl_t = {0: 0, 1: 0}
    for e in range(8):
        sc.wait("t", "v")
        for th in range(2):
            sc.mm(pool[5][:, :], rowsel[0:8, e * 128:(e + 1) * 128], W_fm[0:8, th * 512:(th + 1) * 512], start=True, stop=True)
            tm = sc.mark("t")
            sc.wait("v", "t", tm)
            sc.vop("tensor_copy", wbc[:, th * 512:(th + 1) * 512], pool[5][:, :])
            sc.mark("v")
            sc.wait("t", "v")
        for hc in range(16):
            slot = (e * 16 + hc) % 2
            nslot = 1 - slot
            ne, nhc = (e, hc + 1) if hc + 1 < 16 else (e + 1, 0)
            if ne < 8:
                sc.wait("g", "t", prev_t_for_slot[nslot])
                wg_ld[(ne, nhc)] = sc.dma("g", wgb[nslot][:],
                                          bass.AP(wg_in, (ne * 16 + nhc) * 16 * 128 * 256, [[256, 128], [128 * 256, 16], [1, 256]]),
                                          key=f"d{4 + nslot}")
            sc.wait("t", f"d{4 + slot}", wg_ld[(e, hc)])
            tms = {}
            for th in range(2):
                for d in range(16):
                    sc.mm(pool[th][:, :], wgb[slot][:, d * 256:d * 256 + 128],
                          xln[:, d * TQ + th * 512: d * TQ + (th + 1) * 512], start=(d == 0), stop=(d == 15))
                    sc.mm(pool[2 + th][:, :], wgb[slot][:, d * 256 + 128:d * 256 + 256],
                          xln[:, d * TQ + th * 512: d * TQ + (th + 1) * 512], start=(d == 0), stop=(d == 15))
                tms[th] = sc.mark("t")
                sc.wait("s", "t", tms[th]); sc.wait("v", "t", tms[th])
                sl = slice(th * 512, (th + 1) * 512)
                sc.sop("activation", hg[:, sl], pool[th][:, :], AF.Gelu_apprx_tanh)
                sc.vop("tensor_mul", hu[:, sl], pool[2 + th][:, :], wbc[:, sl])
                sc.mark("s"); sc.mark("v")
                sc.wait("v", "s")
                sc.vop("tensor_mul", o_sb[:, hc * TQ + th * 512:hc * TQ + th * 512 + 512], hg[:, sl], hu[:, sl])
                sc.mark("v")
            prev_t_for_slot[slot] = tms[1]
            sc.wait("t", "v")
        for fs in range(16):
            slot = (e * 16 + fs) % 2
            nslot = 1 - slot
            ne, nfs = (e, fs + 1) if fs + 1 < 16 else (e + 1, 0)
            if ne < 8:
                sc.wait("g", "t", prev_wl_t[nslot])
                wl_ld[(ne, nfs)] = sc.dma("g", wlb[nslot][:],
                                          bass.AP(wl_in, (ne * 16 + nfs) * 16 * 128 * 128, [[128, 128], [128 * 128, 16], [1, 128]]),
                                          key=f"d{6 + nslot}")
            if (e, fs) not in wl_ld:
                wl_ld[(e, fs)] = sc.dma("g", wlb[slot][:],
                                        bass.AP(wl_in, (e * 16 + fs) * 16 * 128 * 128, [[128, 128], [128 * 128, 16], [1, 128]]),
                                        key=f"d{6 + slot}")
            sc.wait("t", f"d{6 + slot}", wl_ld[(e, fs)])
            tms = {}
            for th in range(2):
                for hc in range(16):
                    sc.mm(pool[4 + th][:, :], wlb[slot][:, hc * 128:(hc + 1) * 128],
                          o_sb[:, hc * TQ + th * 512: hc * TQ + th * 512 + 512],
                          start=(hc == 0), stop=(hc == 15))
                tms[th] = sc.mark("t")
                sc.wait("v", "t", tms[th])
                sl = slice(th * 512, (th + 1) * 512)
                dst = xs[:, fs * TQ:(fs + 1) * TQ][:, sl]
                sc.vop("tensor_add", dst, dst, pool[4 + th][:, :])
                sc.mark("v")
            prev_wl_t[slot] = tms[1]
            sc.wait("t", "v")

    vm = sc.mark("v")
    sc.wait("y", "v", vm)
    sc.dma("y", _fm_ap(x2_out, 16, TQ), xs[:])
    sc._flush("y")
    nc.sync.wait_ge(sc.sem["yd"], sc.cnt["yd"])
    sc.close()
    return nc


def host_prep_C(o_all, x_full, wo, router_logits, gating_einsum, linear, pes, rscale):
    woT = np.ascontiguousarray(wo.T.reshape(16, 128, 16, 128).transpose(2, 0, 1, 3))
    rl = np.ascontiguousarray(router_logits.reshape(16, 128, 8)).astype(np.float32)
    rsc = np.ascontiguousarray(rscale.reshape(16, 128).T).astype(np.float32)
    pes_a = pes.reshape(8, 1).astype(np.float32)
    rowsel = np.zeros((8, 8, 128), np.float32)
    for e in range(8):
        rowsel[e, e, :] = 1.0
    identf = np.eye(128, dtype=np.float32)
    wg_flat = np.transpose(gating_einsum, (0, 3, 1, 2)).reshape(8, 2048, 4096)
    wg = np.ascontiguousarray(
        wg_flat.reshape(8, 16, 128, 2, 16, 128).transpose(0, 4, 1, 2, 3, 5)).reshape(8, 16, 16, 128, 256)
    wl = np.ascontiguousarray(
        linear.reshape(8, 16, 128, 16, 128).transpose(0, 3, 1, 2, 4))
    maps = []
    for core in range(NCORE):
        b, quarter = core // 4, core % 4
        sl = slice(quarter * TQ, (quarter + 1) * TQ)
        o_q = np.ascontiguousarray(o_all[b, :, :, sl])
        maps.append({
            "o_in": o_q,
            "x": _host_fm(x_full[b, sl], 16).astype(np.float32),
            "woT": woT, "rl": rl, "rscale": rsc, "pes": pes_a,
            "rowsel": rowsel, "identf": identf,
            "wgate": wg, "wlin": wl,
        })
    return maps


TE = TQ + 3

def emit_ln2(sc, xe, xlne, tmp, stats, pool, ones_bf, ones1_f32, nchunks, eps=1e-5):
    tiles = [(0, 512), (512, 1024), (1024, 1027)]
    for c in range(nchunks):
        xc = xe[:, c * TE:(c + 1) * TE]
        sc.sop("activation", tmp[:, TE:2 * TE], xc, AF.Square)
        sc.vop("tensor_copy", tmp[:, 0:TE], xc)
        sc.mark("s"); sc.mark("v")
        sc.wait("t", "s"); sc.wait("t", "v")
        for i, (a, b_) in enumerate(tiles):
            w = b_ - a
            sc.mm(pool[i][0:1, 0:w], ones_bf[:, 0:1], tmp[:, a:b_], start=(c == 0), stop=(c == nchunks - 1))
            sc.mm(pool[3 + i][0:1, 0:w], ones_bf[:, 0:1], tmp[:, TE + a:TE + b_], start=(c == 0), stop=(c == nchunks - 1))
        sc.mark("t")
        sc.wait("s", "t"); sc.wait("v", "t")
    sc.mark("t")
    sc.wait("v", "t")
    sumx = stats[0:1, 0:TE]
    var_t = stats[0:1, TE:2 * TE]
    rstd = stats[0:1, 2 * TE:3 * TE]
    mr = stats[0:1, 3 * TE:4 * TE]
    for i, (a, b_) in enumerate(tiles):
        w = b_ - a
        sc.vop("tensor_copy", sumx[:, a:b_], pool[i][0:1, 0:w])
        sc.vop("tensor_copy", var_t[:, a:b_], pool[3 + i][0:1, 0:w])
    sc.vop("scalar_tensor_tensor", mr, sumx, 1.0, sumx, ALU.mult, ALU.mult)
    sc.vop("scalar_tensor_tensor", var_t, mr, -1.0 / D, var_t, ALU.mult, ALU.add)
    sc.mark("v")
    sc.wait("s", "v")
    sc.sop("activation", rstd, var_t, AF.Ln, scale=1.0 / D, bias=sc.const(eps, parts=1))
    sc.sop("activation", rstd, rstd, AF.Exp, scale=-0.5)
    sc.mark("s")
    sc.wait("v", "s")
    sc.vop("scalar_tensor_tensor", mr, sumx, 1.0 / D, rstd, ALU.mult, ALU.mult)
    sc.mark("v")
    for i, (a, b_) in enumerate(tiles):
        w = b_ - a
        sc.wait("t", "v")
        sc.mm(pool[6][:, 0:w], ones1_f32[0:1, :], rstd[:, a:b_], start=True, stop=True)
        sc.mm(pool[7][:, 0:w], ones1_f32[0:1, :], mr[:, a:b_], start=True, stop=True)
        sc.mark("t")
        sc.wait("v", "t")
        for c in range(nchunks):
            xc = xe[:, c * TE + a:c * TE + b_]
            oc = xlne[:, c * TE + a:c * TE + b_]
            sc.vop("tensor_mul", tmp[:, 0:w], xc, pool[6][:, 0:w])
            sc.vop("tensor_sub", oc, tmp[:, 0:w], pool[7][:, 0:w])
        sc.mark("v")
        sc.wait("t", "v")


def build_launchD():
    nc = bass.Bass("TRN2", target_bir_lowering=False)
    x2_in = nc.dram_tensor("x2", [16, 128, TQ], F32, kind="ExternalInput")
    halo_in = nc.dram_tensor("halo", [16, 128, 3], F32, kind="ExternalInput")
    w_in = nc.dram_tensor("wcat", [24, 16, 128, 256], F32, kind="ExternalInput")
    wba_in = nc.dram_tensor("wba", [128, 16 * 24], F32, kind="ExternalInput")
    convw_in = nc.dram_tensor("convw", [128, 40 * 4], F32, kind="ExternalInput")
    alog_in = nc.dram_tensor("alog", [8, 1], F32, kind="ExternalInput")
    dtb_in = nc.dram_tensor("dtb", [8, 1], F32, kind="ExternalInput")
    qn_out = nc.dram_tensor("qn", [8, 128, TQ], BF16, kind="ExternalOutput")
    kn_out = nc.dram_tensor("kn", [16, 128, TQ], BF16, kind="ExternalOutput")
    vv_out = nc.dram_tensor("vv", [16, 128, TQ], BF16, kind="ExternalOutput")
    gate_out = nc.dram_tensor("gate", [8, 128, TQ], BF16, kind="ExternalOutput")
    beta_out = nc.dram_tensor("beta", [16, TQ], F32, kind="ExternalOutput")
    lc_out = nc.dram_tensor("lc", [8, TQ], F32, kind="ExternalOutput")

    xe = nc.alloc_sbuf_tensor("xe", [128, 16 * TE], F32)
    xlne = nc.alloc_sbuf_tensor("xlne", [128, 16 * TE], BF16)
    tmp = nc.alloc_sbuf_tensor("tmp", [128, 2 * TE], BF16)
    stats = nc.alloc_sbuf_tensor("stats", [1, 4 * TE], F32)
    ones_bf = nc.alloc_sbuf_tensor("ones_bf", [128, 1], BF16)
    ones1_f32 = nc.alloc_sbuf_tensor("ones1_f32", [1, 128], F32)
    ones8 = nc.alloc_sbuf_tensor("ones8", [16, 64], F32)
    wbuf = [nc.alloc_sbuf_tensor(f"wbuf{i}", [128, 256], BF16) for i in range(2)]
    wba = nc.alloc_sbuf_tensor("wba_sb", [128, 16 * 24], BF16)
    convw = nc.alloc_sbuf_tensor("convw_sb", [128, 40 * 4], F32)
    alog = nc.alloc_sbuf_tensor("alog_sb", [8, 1], F32)
    dtb = nc.alloc_sbuf_tensor("dtb_sb", [8, 1], F32)
    xq = nc.alloc_sbuf_tensor("xq", [128, 2 * TE], BF16)
    cv = nc.alloc_sbuf_tensor("cv", [128, 2 * TQ], BF16)
    nrm = nc.alloc_sbuf_tensor("nrm", [1, 2 * TQ], F32)
    outstage = nc.alloc_sbuf_tensor("outstage", [128, 2 * TQ], BF16)
    zsb = nc.alloc_sbuf_tensor("zsb", [16, TQ], F32)
    sp1 = nc.alloc_sbuf_tensor("sp1", [16, TQ], F32)
    sp2 = nc.alloc_sbuf_tensor("sp2", [16, TQ], F32)
    beta_sb = nc.alloc_sbuf_tensor("beta_sb", [16, TQ], F32)
    lc_sb = nc.alloc_sbuf_tensor("lc_sb", [16, TQ], F32)
    asc = nc.alloc_sbuf_tensor("asc", [8, 2], F32)

    pool = [nc.alloc_psum_tensor(f"pool{i}", [128, 512], F32) for i in range(8)]

    sc = Sched(nc).open()
    sc.init_consts()
    sc.const(1e-5, parts=1)
    sc.const(1.0)
    sc.vop("memset", ones_bf[:], 1.0)
    sc.vop("memset", ones1_f32[:], 1.0)
    sc.vop("memset", ones8[:], 1.0)
    sc.mark("v")
    sc.wait("t", "v"); sc.wait("s", "v")

    sc.dma("y", bass.AP(xe, 3, [[16 * TE, 128], [TE, 16], [1, TQ]]),
           bass.AP(x2_in, 0, [[TQ, 128], [128 * TQ, 16], [1, TQ]]))
    sc.dma("y", bass.AP(xe, 0, [[16 * TE, 128], [TE, 16], [1, 3]]),
           bass.AP(halo_in, 0, [[3, 128], [128 * 3, 16], [1, 3]]))
    ld0 = sc.cnt["yd"]
    sc.dma("g", wba[:], wba_in.ap())
    sc.dma("g", convw[:], convw_in.ap())
    sc.dma("g", alog[:], alog_in.ap())
    sc.dma("g", dtb[:], dtb_in.ap())
    gld0 = sc.cnt["gd"]
    sc.wait("t", "yd", ld0); sc.wait("v", "yd", ld0); sc.wait("s", "yd", ld0)

    emit_ln2(sc, xe[:], xlne[:], tmp[:], stats[:], pool, ones_bf, ones1_f32, 16)
    sc.barrier()
    sc.wait("v", "gd", gld0); sc.wait("t", "gd", gld0); sc.wait("s", "gd", gld0)

    RSQ = 1.0 / float(np.sqrt(128.0))
    slab_ld = {}
    prev_t = {0: 0, 1: 0}
    slab_ld[0] = sc.dma("g", wbuf[0][:], w_in.ap()[0, 0], key="d0")
    for grp in range(24):
        has_conv = grp < 20
        for d in range(16):
            gi = grp * 16 + d
            slot = gi % 2
            if gi + 1 < 24 * 16:
                nslot = (gi + 1) % 2
                sc.wait("g", "t", prev_t[nslot])
                slab_ld[gi + 1] = sc.dma("g", wbuf[nslot][:], w_in.ap()[(gi + 1) // 16, (gi + 1) % 16], key=f"d{nslot}")
            sc.wait("t", f"d{slot}", slab_ld[gi])
            for oc2 in range(2):
                lh = wbuf[slot][:, oc2 * 128:(oc2 + 1) * 128]
                xc = xlne[:, d * TE:(d + 1) * TE]
                sc.mm(pool[oc2 * 2][:, :], lh, xc[:, 3:515], start=(d == 0), stop=(d == 15))
                sc.mm(pool[oc2 * 2 + 1][:, :], lh, xc[:, 515:1027], start=(d == 0), stop=(d == 15))
                if has_conv:
                    sc.mm(pool[6 + oc2][:, 0:3], lh, xc[:, 0:3], start=(d == 0), stop=(d == 15))
            prev_t[slot] = sc.mark("t")
        tm = sc.mark("t")
        sc.wait("v", "t", tm); sc.wait("s", "t", tm)
        for oc2 in range(2):
            oc = grp * 2 + oc2
            if has_conv:
                xqs = xq[:, oc2 * TE:(oc2 + 1) * TE]
                sc.vop("tensor_copy", xqs[:, 0:3], pool[6 + oc2][:, 0:3])
                sc.vop("tensor_copy", xqs[:, 3:515], pool[oc2 * 2][:, :])
                sc.vop("tensor_copy", xqs[:, 515:1027], pool[oc2 * 2 + 1][:, :])
                cvs = cv[:, oc2 * TQ:(oc2 + 1) * TQ]
                wcol = convw[:, oc * 4:(oc + 1) * 4]
                sc.vop("tensor_scalar_mul", cvs, xqs[:, 0:TQ], wcol[:, 0:1])
                for tau in range(1, 4):
                    sc.vop("scalar_tensor_tensor", cvs, xqs[:, tau:tau + TQ], wcol[:, tau:tau + 1], cvs, ALU.mult, ALU.add)
                sc.mark("v")
                sc.wait("s", "v")
                sc.sop("activation", cvs, cvs, AF.Silu)
                sc.mark("s")
                sc.wait("v", "s"); sc.wait("t", "s")
            else:
                sc.vop("tensor_copy", outstage[:, oc2 * TQ:oc2 * TQ + 512], pool[oc2 * 2][:, :])
                sc.vop("tensor_copy", outstage[:, oc2 * TQ + 512:(oc2 + 1) * TQ], pool[oc2 * 2 + 1][:, :])
                sc.mark("v")
                sc.wait("t", "v")
        if has_conv:
            for oc2 in range(2):
                oc = grp * 2 + oc2
                cvs = cv[:, oc2 * TQ:(oc2 + 1) * TQ]
                dst = outstage[:, oc2 * TQ:(oc2 + 1) * TQ]
                if oc < 24:
                    sc.sop("activation", tmp[:, 0:TQ], cvs, AF.Square)
                    sc.mark("s")
                    sc.wait("t", "s")
                    sc.mm(pool[4][0:1, :], ones_bf[:, 0:1], tmp[:, 0:512], start=True, stop=True)
                    sc.mm(pool[5][0:1, :], ones_bf[:, 0:1], tmp[:, 512:1024], start=True, stop=True)
                    tm = sc.mark("t")
                    sc.wait("s", "t", tm)
                    nr = nrm[0:1, oc2 * TQ:(oc2 + 1) * TQ]
                    sc.sop("activation", nr[:, 0:512], pool[4][0:1, :], AF.Sqrt)
                    sc.sop("activation", nr[:, 512:1024], pool[5][0:1, :], AF.Sqrt)
                    sc.mark("s")
                    sc.wait("v", "s")
                    sc.vop("tensor_scalar_max", nr, nr, 1e-12)
                    sc.vop("reciprocal", nr, nr)
                    if oc < 8:
                        sc.vop("tensor_scalar_mul", nr, nr, RSQ)
                    sc.mark("v")
                    sc.wait("t", "v")
                    sc.mm(pool[4][:, :], ones1_f32[0:1, :], nr[:, 0:512], start=True, stop=True)
                    sc.mm(pool[5][:, :], ones1_f32[0:1, :], nr[:, 512:1024], start=True, stop=True)
                    tm = sc.mark("t")
                    sc.wait("v", "t", tm)
                    sc.vop("tensor_mul", dst[:, 0:512], cvs[:, 0:512], pool[4][:, :])
                    sc.vop("tensor_mul", dst[:, 512:1024], cvs[:, 512:1024], pool[5][:, :])
                    sc.mark("v")
                    sc.wait("t", "v")
                else:
                    sc.vop("tensor_copy", dst, cvs)
                    sc.mark("v")
        vm = sc.mark("v")
        sc.wait("y", "v", vm)
        sc.wait("y", "s", sc.cnt["s"])
        for oc2 in range(2):
            oc = grp * 2 + oc2
            src = outstage[:, oc2 * TQ:(oc2 + 1) * TQ]
            if oc < 8:
                sc.dma("y", qn_out.ap()[oc], src)
            elif oc < 24:
                sc.dma("y", kn_out.ap()[oc - 8], src)
            elif oc < 40:
                sc.dma("y", vv_out.ap()[oc - 24], src)
            else:
                sc.dma("y", gate_out.ap()[oc - 40], src)
        sc.wait("v", "yd", sc.cnt["yd"])
        sc.wait("s", "yd", sc.cnt["yd"])
        sc.wait("t", "v", vm)

    sc.barrier()
    for th in range(2):
        sl = slice(th * 512, (th + 1) * 512)
        for d in range(16):
            xc = xlne[:, d * TE:(d + 1) * TE]
            sc.mm(pool[th][0:16, :], wba[:, d * 24:d * 24 + 16], xc[:, 3 + th * 512:3 + (th + 1) * 512], start=(d == 0), stop=(d == 15))
            sc.mm(pool[2 + th][0:8, :], wba[:, d * 24 + 16:d * 24 + 24], xc[:, 3 + th * 512:3 + (th + 1) * 512], start=(d == 0), stop=(d == 15))
    tm = sc.mark("t")
    sc.wait("s", "t", tm); sc.wait("v", "t", tm)
    for th in range(2):
        sl = slice(th * 512, (th + 1) * 512)
        sc.sop("activation", beta_sb[:, sl], pool[th][0:16, :], AF.Sigmoid)
        sc.vop("tensor_scalar_add", zsb[0:8, sl], pool[2 + th][0:8, :], dtb[0:8, 0:1])
    sc.mark("s"); sc.mark("v")
    sc.wait("v", "s")
    sc.vop("tensor_scalar_mul", beta_sb[:], beta_sb[:], 2.0)
    sc.mark("v")
    sc.wait("s", "v")
    sc.sop("activation", sp1[0:8, :], zsb[0:8, :], AF.Abs)
    sc.sop("activation", sp1[0:8, :], sp1[0:8, :], AF.Exp, scale=-1.0)
    sc.sop("activation", sp1[0:8, :], sp1[0:8, :], AF.Ln, bias=sc.const(1.0, parts=8))
    sc.sop("activation", asc[:, 0:1], alog[:, 0:1], AF.Exp)
    sc.mark("s")
    sc.wait("v", "s")
    sc.vop("tensor_relu", sp2[0:8, :], zsb[0:8, :])
    sc.vop("tensor_add", sp1[0:8, :], sp1[0:8, :], sp2[0:8, :])
    sc.vop("tensor_scalar_mul", asc[:, 1:2], asc[:, 0:1], -1.0)
    sc.vop("tensor_scalar_mul", sp1[0:8, :], sp1[0:8, :], asc[:, 1:2])
    for ch in range(16):
        sl = slice(ch * 64, (ch + 1) * 64)
        sc.vop("tensor_tensor_scan", lc_sb[0:8, sl], ones8[0:8, :], sp1[0:8, sl], 0.0, ALU.mult, ALU.add)
    vm = sc.mark("v")
    sc.wait("y", "v", vm)
    sc.dma("y", beta_out.ap(), beta_sb[:])
    sc.dma("y", lc_out.ap(), lc_sb[0:8, :])

    sc._flush("y")
    nc.sync.wait_ge(sc.sem["yd"], sc.cnt["yd"])
    sc.close()
    return nc


def host_prep_D(x2_all, Wq, Wk, Wv, Wg, Wb, Wa, A_log, dt_bias, cqw, ckw, cvw):
    wcat = np.concatenate([Wq, Wk, Wv, Wg], axis=1)
    w_arr = np.ascontiguousarray(
        wcat.reshape(16, 128, 24, 2, 128).transpose(2, 0, 1, 3, 4)).reshape(24, 16, 128, 256)
    wba = np.ascontiguousarray(
        np.concatenate([Wb, Wa], axis=1).reshape(16, 128, 24).transpose(1, 0, 2)).reshape(128, 16 * 24)
    convw = np.ascontiguousarray(
        np.concatenate([cqw[:, 0, :], ckw[:, 0, :], cvw[:, 0, :]], axis=0)
        .reshape(40, 128, 4).transpose(1, 0, 2)).reshape(128, 160)
    maps = []
    for core in range(NCORE):
        b, quarter = core // 4, core % 4
        if quarter == 0:
            halo = np.zeros((16, 128, 3), np.float32)
        else:
            halo = np.ascontiguousarray(x2_all[core - 1][:, :, TQ - 3:TQ])
        maps.append({
            "x2": x2_all[core], "halo": halo, "wcat": w_arr, "wba": wba,
            "convw": convw, "alog": A_log.reshape(8, 1).astype(np.float32),
            "dtb": dt_bias.reshape(8, 1).astype(np.float32),
        })
    return maps


def build_launchE():
    nc = bass.Bass("TRN2", target_bir_lowering=False)
    NU = 2
    NCH = 64
    qf_in = nc.dram_tensor("qf", [NU, 128, S], BF16, kind="ExternalInput")
    kf_in = nc.dram_tensor("kf", [NU, 128, 2 * S], BF16, kind="ExternalInput")
    ktm_in = nc.dram_tensor("ktm", [NU, NCH, 128, 128], BF16, kind="ExternalInput")
    vtm_in = nc.dram_tensor("vtm", [NU, NCH, 128, 128], BF16, kind="ExternalInput")
    gate_in = nc.dram_tensor("gate", [NU, 128, S], BF16, kind="ExternalInput")
    bc_in = nc.dram_tensor("beta_c", [NU, 128, NCH], F32, kind="ExternalInput")
    lc2_in = nc.dram_tensor("lc2", [NU, 128, NCH], F32, kind="ExternalInput")
    lcq_in = nc.dram_tensor("lcq", [NU, 128, NCH], F32, kind="ExternalInput")
    lcn_in = nc.dram_tensor("lcn", [NU, 128, NCH], F32, kind="ExternalInput")
    identf_in = nc.dram_tensor("identf", [128, 128], F32, kind="ExternalInput")
    normw_in = nc.dram_tensor("normw", [128, 1], F32, kind="ExternalInput")
    maskm_in = nc.dram_tensor("maskm", [128, 128], F32, kind="ExternalInput")
    maskmt_in = nc.dram_tensor("maskmt", [128, 128], F32, kind="ExternalInput")
    maskl_in = nc.dram_tensor("maskl", [128, 64], F32, kind="ExternalInput")
    identb_in = nc.dram_tensor("identb", [128, 128], BF16, kind="ExternalInput")
    of_out = nc.dram_tensor("of", [NU, 128, S], BF16, kind="ExternalOutput")

    qf = [nc.alloc_sbuf_tensor(f"qf{i}", [128, S], BF16) for i in range(2)]
    kf = [nc.alloc_sbuf_tensor(f"kf{i}", [128, 2 * S], BF16) for i in range(2)]
    ktm = [nc.alloc_sbuf_tensor(f"ktm{i}", [128, NCH * 128], BF16) for i in range(2)]
    vtm = [nc.alloc_sbuf_tensor(f"vtm{i}", [128, NCH * 128], BF16) for i in range(2)]
    gate = [nc.alloc_sbuf_tensor(f"gate{i}", [128, S], BF16) for i in range(2)]
    bcw = [nc.alloc_sbuf_tensor(f"bcw{i}", [128, NCH], F32) for i in range(2)]
    lc2w = [nc.alloc_sbuf_tensor(f"lc2w{i}", [128, NCH], F32) for i in range(2)]
    lcqw = [nc.alloc_sbuf_tensor(f"lcqw{i}", [128, NCH], F32) for i in range(2)]
    lcnw = [nc.alloc_sbuf_tensor(f"lcnw{i}", [128, NCH], F32) for i in range(2)]
    o_sb = [nc.alloc_sbuf_tensor(f"osb{i}", [128, S], BF16) for i in range(2)]
    rows_sb = [nc.alloc_sbuf_tensor(f"rows_sb{i}", [1, 320], F32) for i in range(2)]
    identf = nc.alloc_sbuf_tensor("identf_sb", [128, 128], F32)
    normw = nc.alloc_sbuf_tensor("normw_sb", [128, 1], F32)
    maskm = nc.alloc_sbuf_tensor("maskm_sb", [128, 128], F32)
    maskmt = nc.alloc_sbuf_tensor("maskmt_sb", [128, 128], F32)
    maskl = nc.alloc_sbuf_tensor("maskl_sb", [128, 64], F32)
    identb = nc.alloc_sbuf_tensor("identb_sb", [128, 128], BF16)
    ones1_f32 = nc.alloc_sbuf_tensor("ones1_f32", [1, 128], F32)
    ones_bf = nc.alloc_sbuf_tensor("ones_bf", [128, 1], BF16)

    def mkws(i):
        return dict(
            S_st=nc.alloc_sbuf_tensor(f"S_st{i}", [128, 128], F32),
            S_bf=nc.alloc_sbuf_tensor(f"S_bf{i}", [128, 128], BF16),
            t1=nc.alloc_sbuf_tensor(f"t1_{i}", [128, 128], F32),
            t2=nc.alloc_sbuf_tensor(f"t2_{i}", [128, 128], F32),
            Rr=nc.alloc_sbuf_tensor(f"Rr{i}", [128, 128], F32),
            Ff=[nc.alloc_sbuf_tensor(f"Ff{i}_{j}", [128, 128], BF16) for j in range(2)],
            Hh=[nc.alloc_sbuf_tensor(f"Hh{i}_{j}", [128, 128], BF16) for j in range(2)],
            acc=nc.alloc_sbuf_tensor(f"acc{i}", [128, 128], BF16),
            tbt=nc.alloc_sbuf_tensor(f"tbt{i}", [128, 128], BF16),
            w0=nc.alloc_sbuf_tensor(f"w0_{i}", [128, 128], BF16),
            lt=nc.alloc_sbuf_tensor(f"lt{i}", [128, 64], BF16),
            qbar=nc.alloc_sbuf_tensor(f"qbar{i}", [128, 64], BF16),
            xb=nc.alloc_sbuf_tensor(f"xb{i}", [128, 128], BF16),
            uu=nc.alloc_sbuf_tensor(f"uu{i}", [128, 128], BF16),
            khat=nc.alloc_sbuf_tensor(f"khat{i}", [128, 128], BF16),
            scr=nc.alloc_sbuf_tensor(f"scr{i}", [128, 8], F32),
        )
    ws = [mkws(0), mkws(1)]
    tmp = nc.alloc_sbuf_tensor("tmp", [128, 1024], BF16)
    nrm = nc.alloc_sbuf_tensor("nrm", [1, 512], F32)

    pool = [nc.alloc_psum_tensor(f"pool{i}", [128, 512], F32) for i in range(8)]

    sc = Sched(nc).open()
    sc.SELFSYNC = False
    sc.init_consts()
    sc.const(1e-5, parts=1)
    sc.vop("memset", ones1_f32[:], 1.0)
    sc.vop("memset", ones_bf[:], 1.0)
    sc.mark("v")

    sc.dma("g", normw[:], normw_in.ap())
    sc.dma("g", maskm[:], maskm_in.ap())
    sc.dma("g", maskmt[:], maskmt_in.ap())
    sc.dma("g", maskl[:], maskl_in.ap())
    sc.dma("g", identb[:], identb_in.ap())
    sc.dma("g", identf[:], identf_in.ap())
    gld0 = sc.cnt["gd"]
    sc.wait("t", "gd", gld0); sc.wait("v", "gd", gld0); sc.wait("s", "gd", gld0)

    for u in range(NU):
        ks = f"d{u}"
        sc.dma("y", qf[u][:], qf_in.ap()[u], key=ks)
        sc.dma("y", kf[u][:], kf_in.ap()[u], key=ks)
        sc.dma("y", ktm[u][:], bass.AP(ktm_in, u * NCH * 128 * 128, [[128, 128], [128 * 128, NCH], [1, 128]]), key=ks)
        sc.dma("y", vtm[u][:], bass.AP(vtm_in, u * NCH * 128 * 128, [[128, 128], [128 * 128, NCH], [1, 128]]), key=ks)
        sc.dma("y", gate[u][:], gate_in.ap()[u], key=ks)
        kg = f"d{2 + u}"
        sc.dma("g", bcw[u][:], bc_in.ap()[u], key=kg)
        sc.dma("g", lc2w[u][:], lc2_in.ap()[u], key=kg)
        sc.dma("g", lcqw[u][:], lcq_in.ap()[u], key=kg)
        sc.dma("g", lcnw[u][:], lcn_in.ap()[u], key=kg)
        for e in ("t", "v", "s"):
            sc.wait(e, ks, sc.cnt[ks]); sc.wait(e, kg, sc.cnt[kg])
    sc.vop("memset", ws[0]["S_st"][:], 0.0)
    sc.vop("memset", ws[0]["S_bf"][:], 0.0)
    sc.vop("memset", ws[1]["S_st"][:], 0.0)
    sc.vop("memset", ws[1]["S_bf"][:], 0.0)
    sc.mark("v")
    sc.wait("t", "v")

    def ph_scr(u, c):
        w = ws[u]
        scr = w["scr"]
        sc.wait("v", "s", w.get("m10s", 0))
        sc.vop("tensor_sub", scr[:, 0:1], lcnw[u][:, c:c + 1], lc2w[u][:, c:c + 1])
        sc.vop("tensor_scalar_mul", scr[:, 1:2], bcw[u][:, c:c + 1], -1.0)
        w["m1v"] = sc.mark("v")

    def ph_scr_s(u, c):
        w = ws[u]
        scr = w["scr"]
        sc.wait("s", "v", w["m1v"])
        sc.sop("activation", scr[:, 2:3], lc2w[u][:, c:c + 1], AF.Exp)
        sc.sop("activation", scr[:, 3:4], scr[:, 0:1], AF.Exp)
        sc.sop("activation", scr[:, 4:5], lcnw[u][:, c:c + 1], AF.Exp)
        w["m1s"] = sc.mark("s")

    def ph_rows_t(u, c):
        w = ws[u]
        bp = 4 * u
        sc.wait("t", "v", w.get("m10v", 0))
        sc.trans(pool[bp + 3][0:1, 256:384], lc2w[u][:, c:c + 1], identf[:, :])
        sc.trans(pool[bp + 3][0:1, 384:512], bcw[u][:, c:c + 1], identf[:, :])
        sc.trans(pool[bp + 3][0:1, 192:256], lcqw[u][0:64, c:c + 1], identf[0:64, 0:64])
        w["m2t"] = sc.mark("t")

    def ph_rows_v(u, c):
        w = ws[u]
        bp = 4 * u
        sc.wait("v", "t", w["m2t"])
        sc.vop("tensor_copy", rows_sb[u][0:1, 0:128], pool[bp + 3][0:1, 256:384])
        sc.vop("tensor_copy", rows_sb[u][0:1, 128:256], pool[bp + 3][0:1, 384:512])
        sc.vop("tensor_copy", rows_sb[u][0:1, 256:320], pool[bp + 3][0:1, 192:256])
        w["m3v"] = sc.mark("v")

    def ph_A(u, c):
        w = ws[u]
        bp = 4 * u
        kfc = kf[u][:, c * 128:(c + 1) * 128]
        qfc = qf[u][:, c * 64:(c + 1) * 64]
        sc.wait("t", "v", w["m3v"])
        sc.wait("t", "s", w.get("m7s", 0))
        sc.mm(pool[bp + 0][:, 0:128], kfc, kfc, start=True, stop=True)
        sc.mm(pool[bp + 1][:, 0:128], ones1_f32[0:1, :], rows_sb[u][0:1, 0:128], start=True, stop=True)
        sc.mm(pool[bp + 1][:, 128:256], ones1_f32[0:1, :], rows_sb[u][0:1, 128:256], start=True, stop=True)
        sc.mm(pool[bp + 1][:, 256:320], ones1_f32[0:1, :], rows_sb[u][0:1, 256:320], start=True, stop=True)
        sc.mm(pool[bp + 2][:, 0:64], kfc, qfc, start=True, stop=True)
        w["m4t"] = sc.mark("t")

    def ph_m1(u, c):
        w = ws[u]
        bp = 4 * u
        sc.wait("v", "t", w["m4t"])
        sc.vop("tensor_scalar", w["t1"][:], pool[bp + 1][:, 0:128], lc2w[u][:, c:c + 1], None, ALU.subtract)
        sc.vop("tensor_add", w["t2"][:], w["t1"][:], maskm[:])
        w["m5v"] = sc.mark("v")

    def ph_m2(u, c):
        w = ws[u]
        sc.wait("s", "v", w["m5v"])
        sc.sop("activation", w["Rr"][:], w["t2"][:], AF.Exp, scale=-1.0)
        w["m5s"] = sc.mark("s")

    def ph_m3(u, c):
        w = ws[u]
        bp = 4 * u
        sc.wait("v", "s", w["m5s"])
        sc.vop("tensor_mul", w["Rr"][:], w["Rr"][:], pool[bp + 0][:, 0:128])
        sc.vop("tensor_scalar_mul", w["Ff"][0][:], w["Rr"][:], w["scr"][:, 1:2])
        sc.vop("tensor_add", w["t2"][:], w["t1"][:], maskmt[:])
        w["m6v"] = sc.mark("v")

    def ph_m4(u, c):
        w = ws[u]
        sc.wait("s", "v", w["m6v"])
        sc.sop("activation", w["Rr"][:], w["t2"][:], AF.Exp)
        w["m6s"] = sc.mark("s")

    def ph_m5(u, c):
        w = ws[u]
        bp = 4 * u
        sc.wait("v", "s", w["m6s"])
        sc.vop("tensor_mul", w["Rr"][:], w["Rr"][:], pool[bp + 0][:, 0:128])
        sc.vop("scalar_tensor_tensor", w["Hh"][0][:], w["Rr"][:], -1.0, pool[bp + 1][:, 128:256], ALU.mult, ALU.mult)
        sc.vop("tensor_add", w["acc"][:], w["Hh"][0][:], identb[:])
        sc.vop("tensor_scalar", w["t1"][:, 0:64], pool[bp + 1][:, 256:320], lc2w[u][:, c:c + 1], None, ALU.subtract)
        sc.vop("tensor_add", w["t1"][:, 0:64], w["t1"][:, 0:64], maskl[:])
        w["m7v"] = sc.mark("v")

    def ph_m6(u, c):
        w = ws[u]
        bp = 4 * u
        sc.wait("s", "v", w["m7v"])
        sc.sop("activation", w["t2"][:, 0:64], w["t1"][:, 0:64], AF.Exp)
        sc.sop("activation", w["t2"][:, 64:128], pool[bp + 1][:, 256:320], AF.Exp)
        w["m7s"] = sc.mark("s")

    def ch_mm(u, kstep):
        w = ws[u]
        bp = 4 * u
        cur = w["cur"]
        if kstep == 1:
            sc.wait("t", "v", w["m7v"])
        else:
            sc.wait("t", "v", w["chv"]); sc.wait("t", "s", w["chs"])
        sc.mm(pool[bp + 2][:, 128:256], w["Hh"][cur][:], w["Ff"][cur][:], start=True, stop=True)
        sc.mm(pool[bp + 3][:, 256:384], w["Ff"][cur][:], w["Hh"][cur][:], start=True, stop=True)
        w["chmt"] = sc.mark("t")

    def ch_evac(u, kstep):
        w = ws[u]
        bp = 4 * u
        nxt = 1 - w["cur"]
        sc.wait("v", "t", w["chmt"]); sc.wait("s", "t", w["chmt"])
        sc.vop("tensor_copy", w["Ff"][nxt][:], pool[bp + 2][:, 128:256])
        sc.sop("activation", w["Hh"][nxt][:], pool[bp + 3][:, 256:384], AF.Copy)
        w["chev"] = sc.mark("v")
        w["chs"] = sc.mark("s")

    def ch_acc(u, kstep):
        w = ws[u]
        bp = 4 * u
        nxt = 1 - w["cur"]
        sc.wait("t", "v", w["chev"]); sc.wait("t", "s", w["chs"])
        sc.mm(pool[bp + 2][:, 384:512], w["Ff"][nxt][:], w["acc"][:], start=True, stop=True)
        w["chat"] = sc.mark("t")

    def ch_add(u, kstep):
        w = ws[u]
        bp = 4 * u
        sc.wait("v", "t", w["chat"])
        sc.vop("tensor_add", w["acc"][:], w["acc"][:], pool[bp + 2][:, 384:512])
        w["chv"] = sc.mark("v")
        w["cur"] = 1 - w["cur"]

    def ph_fin(u, c):
        w = ws[u]
        bp = 4 * u
        sc.wait("v", "s", w["m7s"])
        sc.vop("tensor_mul", w["lt"][:], w["t2"][:, 0:64], pool[bp + 2][:, 0:64])
        sc.vop("tensor_scalar_mul", w["tbt"][:], w["acc"][:], bcw[u][:, c:c + 1])
        sc.vop("tensor_mul", w["qbar"][:], qf[u][:, c * 64:(c + 1) * 64], w["t2"][:, 64:128])
        sc.vop("tensor_scalar_mul", w["khat"][:], ktm[u][:, c * 128:(c + 1) * 128], w["scr"][:, 3:4])
        w["m8v"] = sc.mark("v")

    def ph_w0x(u, c):
        w = ws[u]
        bp = 4 * u
        sc.wait("t", "v", w["m8v"]); sc.wait("t", "s", w.get("m10s", 0))
        sc.mm(pool[bp + 3][:, 0:128], w["tbt"][:], vtm[u][:, c * 128:(c + 1) * 128], start=True, stop=True)
        sc.mm(pool[bp + 0][:, 0:128], kf[u][:, c * 128:(c + 1) * 128], w["S_bf"][:], start=True, stop=True)
        w["m9t"] = sc.mark("t")

    def ph_w0x_v(u, c):
        w = ws[u]
        bp = 4 * u
        sc.wait("v", "t", w["m9t"])
        sc.vop("tensor_copy", w["w0"][:], pool[bp + 3][:, 0:128])
        sc.vop("tensor_scalar_mul", w["xb"][:], pool[bp + 0][:, 0:128], w["scr"][:, 2:3])
        w["m9v"] = sc.mark("v")

    def ph_y(u, c):
        w = ws[u]
        bp = 4 * u
        sc.wait("t", "v", w["m9v"])
        sc.mm(pool[bp + 0][:, 0:128], w["tbt"][:], w["xb"][:], start=True, stop=True)
        w["m10t"] = sc.mark("t")

    def ph_u(u, c):
        w = ws[u]
        bp = 4 * u
        sc.wait("v", "t", w["m10t"])
        sc.vop("tensor_sub", w["uu"][:], w["w0"][:], pool[bp + 0][:, 0:128])
        w["m11v"] = sc.mark("v")

    def ph_o(u, c):
        w = ws[u]
        bp = 4 * u
        sc.wait("t", "v", w["m11v"])
        sc.mm(pool[bp + 3][:, 128:192], w["S_bf"][:], w["qbar"][:], start=True, stop=False)
        sc.mm(pool[bp + 3][:, 128:192], w["uu"][:], w["lt"][:], start=False, stop=True)
        sc.mm(pool[bp + 0][:, 0:128], w["khat"][:], w["uu"][:], start=True, stop=True)
        w["m12t"] = sc.mark("t")

    def ph_send(u, c):
        w = ws[u]
        bp = 4 * u
        sc.wait("v", "t", w["m12t"])
        sc.vop("tensor_copy", o_sb[u][:, c * 64:(c + 1) * 64], pool[bp + 3][:, 128:192])
        sc.vop("scalar_tensor_tensor", w["S_st"][:], w["S_st"][:], w["scr"][:, 4:5], pool[bp + 0][:, 0:128], ALU.mult, ALU.add)
        w["m10v"] = sc.mark("v")
        sc.wait("s", "v", w["m10v"])
        sc.sop("activation", w["S_bf"][:], w["S_st"][:], AF.Copy)
        w["m10s"] = sc.mark("s")

    ws[0]["cur"] = 0
    ws[1]["cur"] = 0
    phases = [ph_scr, ph_scr_s, ph_rows_t, ph_rows_v, ph_A, ph_m1, ph_m2,
              ph_m3, ph_m4, ph_m5, ph_m6]
    for c in range(NCH):
        ws[0]["cur"] = 0
        ws[1]["cur"] = 0
        for ph in phases:
            ph(0, c); ph(1, c)
        for kstep in range(1, 7):
            ch_mm(0, kstep); ch_mm(1, kstep)
            ch_evac(0, kstep); ch_evac(1, kstep)
            ch_acc(0, kstep); ch_acc(1, kstep)
            ch_add(0, kstep); ch_add(1, kstep)
        for ph in (ph_fin, ph_w0x, ph_w0x_v, ph_y, ph_u, ph_o, ph_send):
            ph(0, c); ph(1, c)

    for u in range(NU):
        for blk in range(8):
            bs = slice(blk * 512, (blk + 1) * 512)
            sc.wait("s", "v", sc.cnt["v"])
            sc.sop("activation", tmp[:, 0:512], o_sb[u][:, bs], AF.Square)
            sc.mark("s"); sc.wait("t", "s")
            sc.mm(pool[0][0:1, :], ones_bf[:, 0:1], tmp[:, 0:512], start=True, stop=True)
            tm = sc.mark("t")
            sc.wait("s", "t", tm)
            sc.sop("activation", nrm[0:1, :], pool[0][0:1, :], AF.Ln, scale=1.0 / 128.0, bias=sc.const(1e-5, parts=1))
            sc.sop("activation", nrm[0:1, :], nrm[0:1, :], AF.Exp, scale=-0.5)
            sc.mark("s"); sc.wait("t", "s")
            sc.mm(pool[1][:, :], ones1_f32[0:1, :], nrm[0:1, :], start=True, stop=True)
            tm = sc.mark("t")
            sc.wait("s", "t", tm); sc.wait("v", "t", tm)
            sc.sop("activation", tmp[:, 512:1024], gate[u][:, bs], AF.Silu)
            sc.mark("s"); sc.wait("v", "s")
            sc.vop("tensor_scalar_mul", tmp[:, 512:1024], tmp[:, 512:1024], normw[:, 0:1])
            sc.vop("tensor_mul", tmp[:, 0:512], o_sb[u][:, bs], pool[1][:, :])
            sc.vop("tensor_mul", o_sb[u][:, bs], tmp[:, 0:512], tmp[:, 512:1024])
            sc.mark("v"); sc.wait("t", "v"); sc.wait("s", "v")
        vm = sc.mark("v")
        sc.wait("y", "v", vm)
        sc.dma("y", of_out.ap()[u], o_sb[u][:])

    sc._flush("y")
    nc.sync.wait_ge(sc.sem["yd"], sc.cnt["yd"])
    sc.close()
    return nc


def host_prep_E(qn_all, kn_all, vv_all, gate_all, beta_all, lc_all, norm_w):
    NCH = 64
    def full(arrs, b, idx):
        return np.concatenate([arrs[b * 4 + q][idx] for q in range(4)], axis=-1)
    maskm = np.where(np.tril(np.ones((128, 128)), -1) > 0, 0.0, 1e30).astype(np.float32)
    maskmt = np.where(np.triu(np.ones((128, 128)), 1) > 0, 0.0, -1e30).astype(np.float32)
    j_idx = np.arange(128)[:, None]
    t_idx = np.arange(64)[None, :]
    maskl = np.where(j_idx <= 2 * t_idx + 1, 0.0, -1e30).astype(np.float32)
    identb = np.eye(128, dtype=np.float32).astype(bf16)
    maps = []
    for core in range(NCORE):
        m = {"normw": norm_w.reshape(128, 1).astype(np.float32), "maskm": maskm, "maskmt": maskmt, "maskl": maskl, "identb": identb}
        qfs, kfs, ktms, vtms, gates = [], [], [], [], []
        bcs, lc2s, lcqs, lcns = [], [], [], []
        for i in range(2):
            u = 2 * core + i
            b, h = u // 8, u % 8
            q_full = full(qn_all, b, h)
            k0 = full(kn_all, b, h); k1 = full(kn_all, b, 8 + h)
            v0 = full(vv_all, b, h); v1 = full(vv_all, b, 8 + h)
            kv = np.stack([k0, k1], axis=-1).reshape(128, 2 * S)
            vv = np.stack([v0, v1], axis=-1).reshape(128, 2 * S)
            g_full = full(gate_all, b, h)
            be0 = full(beta_all, b, h); be1 = full(beta_all, b, 8 + h)
            bv = np.stack([be0, be1], axis=-1).reshape(2 * S)
            lc = full(lc_all, b, h)
            lcv = np.repeat(lc, 2)
            qfs.append(q_full); kfs.append(kv)
            ktms.append(np.ascontiguousarray(kv.T).reshape(NCH, 128, 128))
            vtms.append(np.ascontiguousarray(vv.T).reshape(NCH, 128, 128))
            gates.append(g_full)
            bcs.append(np.ascontiguousarray(bv.reshape(NCH, 128).T).astype(np.float32))
            lc2s.append(np.ascontiguousarray(lcv.reshape(NCH, 128).T).astype(np.float32))
            lcq_c = np.zeros((128, NCH), np.float32)
            lcq_c[0:64, :] = lc.reshape(NCH, 64).T
            lcqs.append(lcq_c)
            lcns.append(np.ascontiguousarray(np.broadcast_to(
                lcv.reshape(NCH, 128)[:, 127], (128, NCH))).astype(np.float32))
        m["identf"] = np.eye(128, dtype=np.float32)
        m.update({
            "qf": np.stack(qfs), "kf": np.stack(kfs), "ktm": np.stack(ktms),
            "vtm": np.stack(vtms), "gate": np.stack(gates),
            "beta_c": np.stack(bcs), "lc2": np.stack(lc2s), "lcq": np.stack(lcqs),
            "lcn": np.stack(lcns),
        })
        maps.append(m)
    return maps


def build_launchF():
    nc = bass.Bass("TRN2", target_bir_lowering=False)
    of_in = nc.dram_tensor("of_in", [8, 128, TQ], BF16, kind="ExternalInput")
    x2_in = nc.dram_tensor("x2", [16, 128, TQ], F32, kind="ExternalInput")
    wo1_in = nc.dram_tensor("wo1", [16, 8, 128, 128], F32, kind="ExternalInput")
    rw_in = nc.dram_tensor("rw", [16, 128, 8], F32, kind="ExternalInput")
    rowsel_in = nc.dram_tensor("rowsel", [8, 8, 128], F32, kind="ExternalInput")
    ident_in = nc.dram_tensor("identf", [128, 128], F32, kind="ExternalInput")
    sgw_in = nc.dram_tensor("sgw", [128, 16], F32, kind="ExternalInput")
    wgu_in = nc.dram_tensor("wgu", [8, 11, 16, 128, 256], F32, kind="ExternalInput")
    wd_in = nc.dram_tensor("wd", [8, 16, 11, 128, 128], F32, kind="ExternalInput")
    swgu_in = nc.dram_tensor("swgu", [44, 16, 128, 256], F32, kind="ExternalInput")
    swd_in = nc.dram_tensor("swd", [16, 44, 128, 128], F32, kind="ExternalInput")
    out_out = nc.dram_tensor("out", [16, 128, TQ], F32, kind="ExternalOutput")

    xs = nc.alloc_sbuf_tensor("xs", [128, 16 * TQ], F32)
    o_sb = nc.alloc_sbuf_tensor("o_sb", [128, 8 * TQ], BF16)
    xln = nc.alloc_sbuf_tensor("xln", [128, 16 * TQ], BF16)
    tmp = nc.alloc_sbuf_tensor("tmp", [128, 2 * TQ], BF16)
    stats = nc.alloc_sbuf_tensor("stats", [1, 4 * TQ], F32)
    ones_bf = nc.alloc_sbuf_tensor("ones_bf", [128, 1], BF16)
    ones1_f32 = nc.alloc_sbuf_tensor("ones1_f32", [1, 128], F32)
    identf = nc.alloc_sbuf_tensor("identf_sb", [128, 128], F32)
    rw_sb = nc.alloc_sbuf_tensor("rw_sb", [128, 16 * 8], BF16)
    rowsel = nc.alloc_sbuf_tensor("rowsel_sb", [8, 8 * 128], BF16)
    sgw_sb = nc.alloc_sbuf_tensor("sgw_sb", [128, 16], BF16)
    lg = nc.alloc_sbuf_tensor("lg", [128, 8], F32)
    m1 = nc.alloc_sbuf_tensor("m1", [128, 4], F32)
    eqm = nc.alloc_sbuf_tensor("eqm", [128, 8], F32)
    ind = nc.alloc_sbuf_tensor("ind", [128, 8], F32)
    ee = nc.alloc_sbuf_tensor("ee", [128, 8], F32)
    w_tm = nc.alloc_sbuf_tensor("w_tm", [128, 8], F32)
    W_fm = nc.alloc_sbuf_tensor("W_fm", [8, TQ], BF16)
    wbc = nc.alloc_sbuf_tensor("wbc", [128, TQ], BF16)
    sg_b = nc.alloc_sbuf_tensor("sg_b", [128, TQ], BF16)
    hg = nc.alloc_sbuf_tensor("hg", [128, TQ], BF16)
    hu = nc.alloc_sbuf_tensor("hu", [128, TQ], BF16)
    hsh = nc.alloc_sbuf_tensor("hsh", [128, 44 * 512], BF16)
    he = hsh
    wov = [nc.alloc_sbuf_tensor(f"wov{i}", [128, 8 * 128], BF16) for i in range(2)]
    _osb_addr = nc.lookup_mloc(o_sb).addr
    wgb = [nc.alloc_sbuf_tensor_at(f"wgb{i}", [128, 16 * 256], BF16, offset=_osb_addr + i * 16 * 256 * 2) for i in range(2)]
    wlb = [nc.alloc_sbuf_tensor(f"wlb{i}", [128, 11 * 128], BF16) for i in range(2)]
    _st_addr = nc.lookup_mloc(stats).addr
    swb = [nc.alloc_sbuf_tensor_at(f"swb{i}", [128, 16 * 256], BF16, offset=_st_addr + i * 16 * 256 * 2) for i in range(2)]
    sdb = [nc.alloc_sbuf_tensor_at("sdb0", [128, 44 * 128], BF16, offset=_osb_addr)]

    pool = [nc.alloc_psum_tensor(f"pool{i}", [128, 512], F32) for i in range(8)]

    sc = Sched(nc).open()
    sc.init_consts()
    sc.const(1e-5, parts=1)
    sc.vop("memset", ones_bf[:], 1.0)
    sc.vop("memset", ones1_f32[:], 1.0)
    sc.mark("v")

    sc.dma("y", xs[:], _fm_ap(x2_in, 16, TQ))
    sc.dma("y", o_sb[:], _fm_ap(of_in, 8, TQ))
    ld0 = sc.cnt["yd"]
    sc.dma("g", rw_sb[:], bass.AP(rw_in, 0, [[8, 128], [128 * 8, 16], [1, 8]]))
    sc.dma("g", rowsel[:], bass.AP(rowsel_in, 0, [[128, 8], [8 * 128, 8], [1, 128]]))
    sc.dma("g", identf[:], ident_in.ap())
    sc.dma("g", sgw_sb[:], sgw_in.ap())
    gld0 = sc.cnt["gd"]

    wo_ld = {}
    wo_ld[0] = sc.dma("g", wov[0][:], bass.AP(wo1_in, 0, [[128, 128], [128 * 128, 8], [1, 128]]), key="d0")
    for ds in range(16):
        slot = ds % 2
        if ds + 1 < 16:
            if ds >= 1:
                sc.wait("g", "t", sc.cnt["t"])
            wo_ld[ds + 1] = sc.dma("g", wov[1 - slot][:],
                                   bass.AP(wo1_in, (ds + 1) * 8 * 128 * 128, [[128, 128], [128 * 128, 8], [1, 128]]),
                                   key=f"d{1 - slot}")
        sc.wait("t", f"d{slot}", wo_ld[ds])
        sc.wait("t", "yd", ld0)
        sc.wait("v", "yd", ld0)
        for th in range(2):
            for ec in range(8):
                sc.mm(pool[th][:, :], wov[slot][:, ec * 128:(ec + 1) * 128],
                      o_sb[:, ec * TQ + th * 512: ec * TQ + (th + 1) * 512],
                      start=(ec == 0), stop=(ec == 7))
            tm = sc.mark("t")
            sc.wait("v", "t", tm)
            sl = slice(th * 512, (th + 1) * 512)
            sc.vop("tensor_add", xs[:, ds * TQ:(ds + 1) * TQ][:, sl], xs[:, ds * TQ:(ds + 1) * TQ][:, sl], pool[th][:, :])
            sc.mark("v")
        sc.wait("t", "v")

    sc.barrier()
    emit_ln(sc, xs[:], xln[:], tmp[:], stats[:], pool, ones_bf, ones1_f32, 16, TQ)
    sc.barrier()

    sc.wait("v", "gd", gld0); sc.wait("t", "gd", gld0); sc.wait("s", "gd", gld0)
    for tt in range(8):
        tsl = slice(tt * 128, (tt + 1) * 128)
        for d in range(16):
            sc.mm(pool[3][:, 0:8], xln[:, d * TQ:(d + 1) * TQ][:, tsl], rw_sb[:, d * 8:(d + 1) * 8], start=(d == 0), stop=(d == 15))
        tm = sc.mark("t")
        sc.wait("v", "t", tm)
        sc.vop("tensor_copy", lg[:], pool[3][:, 0:8])
        sc.vop("reduce_max", m1[:, 0:1], lg[:], axis=AX.X)
        sc.vop("tensor_scalar_mul", m1[:, 1:2], m1[:, 0:1], -1.0)
        sc.vop("tensor_single_scalar", eqm[:], lg[:], m1[:, 0:1], ALU.is_ge)
        sc.vop("scalar_tensor_tensor", eqm[:], eqm[:], -1e30, lg[:], ALU.mult, ALU.add)
        sc.vop("reduce_max", m1[:, 2:3], eqm[:], axis=AX.X)
        sc.vop("tensor_single_scalar", ind[:], lg[:], m1[:, 2:3], ALU.is_ge)
        sc.mark("v")
        sc.wait("s", "v")
        sc.sop("activation", ee[:], lg[:], AF.Exp, bias=m1[:, 1:2])
        sc.mark("s")
        sc.wait("v", "s")
        sc.vop("tensor_mul", ee[:], ee[:], ind[:])
        sc.vop("reduce_sum", m1[:, 3:4], ee[:], axis=AX.X)
        sc.vop("reciprocal", m1[:, 3:4], m1[:, 3:4])
        sc.vop("tensor_scalar_mul", w_tm[:], ee[:], m1[:, 3:4])
        sc.mark("v")
        sc.wait("t", "v")
        sc.trans(pool[5][0:8, 0:128], w_tm[:, 0:8], identf[:, :])
        tm = sc.mark("t")
        sc.wait("v", "t", tm)
        sc.vop("tensor_copy", W_fm[0:8, tsl], pool[5][0:8, 0:128])
        sc.mark("v")
        sc.wait("t", "v")

    sgrow = nc.alloc_sbuf_tensor("sgrow", [1, TQ], F32)
    for th in range(2):
        pp = pool[2] if th == 0 else pool[4]
        for d in range(16):
            sc.mm(pp[0:1, 0:512], sgw_sb[:, d:d + 1],
                  xln[:, d * TQ + th * 512:d * TQ + th * 512 + 512], start=(d == 0), stop=(d == 15))
    tm = sc.mark("t")
    sc.wait("s", "t", tm)
    sc.sop("activation", sgrow[0:1, 0:512], pool[2][0:1, 0:512], AF.Sigmoid)
    sc.sop("activation", sgrow[0:1, 512:1024], pool[4][0:1, 0:512], AF.Sigmoid)
    sc.mark("s")
    sc.wait("t", "s")
    for th in range(2):
        pp = pool[2] if th == 0 else pool[4]
        sc.mm(pp[:, :], ones1_f32[0:1, :], sgrow[0:1, th * 512:(th + 1) * 512], start=True, stop=True)
    tm = sc.mark("t")
    sc.wait("v", "t", tm)
    sc.vop("tensor_copy", sg_b[:, 0:512], pool[2][:, :])
    sc.vop("tensor_copy", sg_b[:, 512:1024], pool[4][:, :])
    sc.mark("v")
    sc.wait("t", "v")

    sw_ld = {}
    sd_ld = {}
    prev_sw_t = {0: 0, 1: 0}
    prev_sd_t = {0: 0, 1: 0}
    shared_vm = {0: 0, 1: 0}
    shared_sm = {0: 0, 1: 0}
    sc.wait("g", "t", sc.cnt["t"]); sc.wait("g", "s", sc.cnt["s"]); sc.wait("g", "v", sc.cnt["v"])
    sw_ld[0] = sc.dma("g", swb[0][:], bass.AP(swgu_in, 0, [[256, 128], [128 * 256, 16], [1, 256]]), key="d4")
    for th in range(2):
        tsl = slice(th * 512, (th + 1) * 512)
        for ic in range(44):
            gi = th * 44 + ic
            slot = gi % 2
            if gi + 1 < 88:
                nslot = (gi + 1) % 2
                ngi = gi + 1
                nic = ngi % 44
                sc.wait("g", "t", prev_sw_t[nslot])
                sw_ld[ngi] = sc.dma("g", swb[nslot][:],
                                    bass.AP(swgu_in, nic * 16 * 128 * 256, [[256, 128], [128 * 256, 16], [1, 256]]),
                                    key=f"d{4 + nslot}")
            sc.wait("t", f"d{4 + slot}", sw_ld[gi])
            pg, pu = (pool[0], pool[1]) if ic % 2 == 0 else (pool[2], pool[3])
            sc.wait("t", "v", shared_vm[ic % 2]); sc.wait("t", "s", shared_sm[ic % 2])
            for d in range(16):
                sc.mm(pg[:, :], swb[slot][:, d * 256:d * 256 + 128], xln[:, d * TQ:(d + 1) * TQ][:, tsl],
                      start=(d == 0), stop=(d == 15))
                sc.mm(pu[:, :], swb[slot][:, d * 256 + 128:d * 256 + 256], xln[:, d * TQ:(d + 1) * TQ][:, tsl],
                      start=(d == 0), stop=(d == 15))
            tm = sc.mark("t")
            prev_sw_t[slot] = tm
            sc.wait("s", "t", tm); sc.wait("v", "t", tm)
            hsl = slice((ic % 2) * 512, (ic % 2) * 512 + 512)
            sc.sop("activation", hg[:, hsl], pg[:, :], AF.Silu)
            sc.vop("tensor_mul", hu[:, hsl], pu[:, :], sg_b[:, tsl])
            shared_sm[ic % 2] = sc.mark("s")
            sc.mark("v")
            sc.wait("v", "s")
            sc.vop("tensor_mul", hsh[:, ic * 512:(ic + 1) * 512], hg[:, hsl], hu[:, hsl])
            shared_vm[ic % 2] = sc.mark("v")
        sdown_vm = {0: 0, 1: 0}
        half_t = {0: 0, 1: 0}
        half_ld = {}
        sc.wait("g", "t", sc.cnt["t"])
        half_ld[(0, 0)] = sc.dma("g", sdb[0][:, 0:22 * 128],
                                 bass.AP(swd_in, 0 * 44 * 128 * 128, [[128, 128], [128 * 128, 22], [1, 128]]),
                                 key="d6", fence=True)
        for fs in range(16):
            for hf in range(2):
                nfs, nhf = (fs, 1) if hf == 0 else (fs + 1, 0)
                if nfs < 16 and (th, nfs, nhf) != (1, 16, 0):
                    sc.wait("g", "t", half_t[nhf])
                    half_ld[(nfs, nhf)] = sc.dma(
                        "g", sdb[0][:, nhf * 22 * 128:(nhf + 1) * 22 * 128],
                        bass.AP(swd_in, (nfs * 44 + nhf * 22) * 128 * 128, [[128, 128], [128 * 128, 22], [1, 128]]),
                        key=f"d{6 + nhf}", fence=True)
                sc.wait("t", f"d{6 + hf}", half_ld[(fs, hf)])
                if hf == 0:
                    sc.wait("t", "v", sdown_vm[fs % 2])
                for j in range(22):
                    ic = hf * 22 + j
                    sc.mm(pool[4 + (fs % 2)][:, :], sdb[0][:, ic * 128:(ic + 1) * 128], hsh[:, ic * 512:(ic + 1) * 512],
                          start=(ic == 0), stop=(ic == 43))
                half_t[hf] = sc.mark("t")
            tm = half_t[1]
            sc.wait("v", "t", tm)
            dst = xs[:, fs * TQ:(fs + 1) * TQ][:, tsl]
            sc.vop("tensor_add", dst, dst, pool[4 + (fs % 2)][:, :])
            sdown_vm[fs % 2] = sc.mark("v")

    wg_ld = {}
    wl_ld = {}
    prev_wg_t = {0: 0, 1: 0}
    prev_wl_t = {0: 0, 1: 0}
    sc.wait("g", "t", sc.cnt["t"])
    wg_ld[(0, 0)] = sc.dma("g", wgb[0][:], bass.AP(wgu_in, 0, [[256, 128], [128 * 256, 16], [1, 256]]), key="d0", fence=True)
    _fenced_d1 = [False]
    for e in range(8):
        sc.wait("t", "v")
        for th in range(2):
            sc.mm(pool[5][:, :], rowsel[0:8, e * 128:(e + 1) * 128], W_fm[0:8, th * 512:(th + 1) * 512], start=True, stop=True)
            tm = sc.mark("t")
            sc.wait("v", "t", tm)
            sc.vop("tensor_copy", wbc[:, th * 512:(th + 1) * 512], pool[5][:, :])
            sc.mark("v")
            sc.wait("t", "v")
        for ic in range(11):
            gi = e * 11 + ic
            slot = gi % 2
            ne, nic = (e, ic + 1) if ic + 1 < 11 else (e + 1, 0)
            if ne < 8:
                nslot = (gi + 1) % 2
                sc.wait("g", "t", prev_wg_t[nslot])
                fence = nslot == 1 and not _fenced_d1[0]
                if fence:
                    _fenced_d1[0] = True
                wg_ld[(ne, nic)] = sc.dma("g", wgb[nslot][:],
                                          bass.AP(wgu_in, (ne * 11 + nic) * 16 * 128 * 256, [[256, 128], [128 * 256, 16], [1, 256]]),
                                          key=f"d{nslot}", fence=fence)
            sc.wait("t", f"d{slot}", wg_ld[(e, ic)])
            tms = {}
            for th in range(2):
                for d in range(16):
                    sc.mm(pool[th][:, :], wgb[slot][:, d * 256:d * 256 + 128],
                          xln[:, d * TQ + th * 512: d * TQ + (th + 1) * 512], start=(d == 0), stop=(d == 15))
                    sc.mm(pool[2 + th][:, :], wgb[slot][:, d * 256 + 128:d * 256 + 256],
                          xln[:, d * TQ + th * 512: d * TQ + (th + 1) * 512], start=(d == 0), stop=(d == 15))
                tms[th] = sc.mark("t")
                sc.wait("s", "t", tms[th]); sc.wait("v", "t", tms[th])
                sl = slice(th * 512, (th + 1) * 512)
                sc.sop("activation", hg[:, sl], pool[th][:, :], AF.Silu)
                sc.vop("tensor_mul", hu[:, sl], pool[2 + th][:, :], wbc[:, sl])
                sc.mark("s"); sc.mark("v")
                sc.wait("v", "s")
                sc.vop("tensor_mul", he[:, ic * TQ + th * 512:ic * TQ + th * 512 + 512], hg[:, sl], hu[:, sl])
                sc.mark("v")
            prev_wg_t[slot] = tms[1]
            sc.wait("t", "v")
        for fs in range(16):
            gi2 = e * 16 + fs
            slot = gi2 % 2
            ne, nfs = (e, fs + 1) if fs + 1 < 16 else (e + 1, 0)
            if ne < 8:
                nslot = (gi2 + 1) % 2
                sc.wait("g", "t", prev_wl_t[nslot])
                wl_ld[(ne, nfs)] = sc.dma("g", wlb[nslot][:],
                                          bass.AP(wd_in, (ne * 16 + nfs) * 11 * 128 * 128, [[128, 128], [128 * 128, 11], [1, 128]]),
                                          key=f"d{2 + nslot}")
            if (e, fs) not in wl_ld:
                wl_ld[(e, fs)] = sc.dma("g", wlb[slot][:],
                                        bass.AP(wd_in, (e * 16 + fs) * 11 * 128 * 128, [[128, 128], [128 * 128, 11], [1, 128]]),
                                        key=f"d{2 + slot}")
            sc.wait("t", f"d{2 + slot}", wl_ld[(e, fs)])
            tms = {}
            for th in range(2):
                for ic in range(11):
                    sc.mm(pool[4 + th][:, :], wlb[slot][:, ic * 128:(ic + 1) * 128],
                          he[:, ic * TQ + th * 512: ic * TQ + th * 512 + 512],
                          start=(ic == 0), stop=(ic == 10))
                tms[th] = sc.mark("t")
                sc.wait("v", "t", tms[th])
                sl = slice(th * 512, (th + 1) * 512)
                dst = xs[:, fs * TQ:(fs + 1) * TQ][:, sl]
                sc.vop("tensor_add", dst, dst, pool[4 + th][:, :])
                sc.mark("v")
            prev_wl_t[slot] = tms[1]
            sc.wait("t", "v")

    vm = sc.mark("v")
    sc.wait("y", "v", vm)
    sc.dma("y", _fm_ap(out_out, 16, TQ), xs[:])
    sc._flush("y")
    nc.sync.wait_ge(sc.sem["yd"], sc.cnt["yd"])
    sc.close()
    return nc


def host_prep_F(of_all, x2_all, Wo, router_w, w_gate, w_up, w_down, sw_gate, sw_up, sw_down, sgw):
    wo1 = np.ascontiguousarray(Wo.reshape(8, 128, 16, 128).transpose(2, 0, 1, 3))
    rw = np.ascontiguousarray(router_w.T.reshape(16, 128, 8)).astype(np.float32)
    rowsel = np.zeros((8, 8, 128), np.float32)
    for e in range(8):
        rowsel[e, e, :] = 1.0
    identf = np.eye(128, dtype=np.float32)
    sgw_a = np.ascontiguousarray(sgw.reshape(16, 128).T).astype(np.float32)
    wgu = np.ascontiguousarray(np.concatenate(
        [w_gate.reshape(8, 16, 128, 11, 128), w_up.reshape(8, 16, 128, 11, 128)],
        axis=4).transpose(0, 3, 1, 2, 4)).reshape(8, 11, 16, 128, 256)
    wd = np.ascontiguousarray(w_down.reshape(8, 11, 128, 16, 128).transpose(0, 3, 1, 2, 4))
    swgu = np.ascontiguousarray(np.concatenate(
        [sw_gate.reshape(16, 128, 44, 128), sw_up.reshape(16, 128, 44, 128)],
        axis=3).transpose(2, 0, 1, 3)).reshape(44, 16, 128, 256)
    swd = np.ascontiguousarray(sw_down.reshape(44, 128, 16, 128).transpose(2, 0, 1, 3))
    of_by_bh = {}
    for core in range(NCORE):
        for i in range(2):
            u = 2 * core + i
            of_by_bh[(u // 8, u % 8)] = of_all[core][i]
    maps = []
    for core in range(NCORE):
        b, quarter = core // 4, core % 4
        sl = slice(quarter * TQ, (quarter + 1) * TQ)
        of_q = np.stack([of_by_bh[(b, h)][:, sl] for h in range(8)])
        maps.append({
            "of_in": np.ascontiguousarray(of_q), "x2": x2_all[core],
            "wo1": wo1, "rw": rw, "rowsel": rowsel, "identf": identf, "sgw": sgw_a,
            "wgu": wgu, "wd": wd, "swgu": swgu, "swd": swd,
        })
    return maps


def kernel(**inputs):
    inp = inputs
    mapsA = host_prep_A(np.asarray(inp["L0_attn_x"]), np.asarray(inp["L0_attn_wq"]),
                        np.asarray(inp["L0_attn_wk"]), np.asarray(inp["L0_attn_wv"]))
    resA = run_bass_kernel_spmd(build_launchA(), mapsA, core_ids=list(range(NCORE))).results
    q_all = np.stack([np.concatenate([np.asarray(resA[b * 4 + q]["q"]) for q in range(4)], axis=2) for b in range(2)])
    k_all = np.stack([np.concatenate([np.asarray(resA[b * 4 + q]["k"]) for q in range(4)], axis=2) for b in range(2)])
    v_all = np.stack([np.concatenate([np.asarray(resA[b * 4 + q]["v"]) for q in range(4)], axis=2) for b in range(2)])
    mapsB = host_prep_B(q_all, k_all, v_all)
    resB = run_bass_kernel_spmd(build_launchB(), mapsB, core_ids=list(range(NCORE))).results
    o_all = np.zeros((2, 16, 128, S), bf16)
    for u in range(32):
        o_all[u // 16, u % 16] = np.asarray(resB[u // 4]["o"])[u % 4]
    mapsC = host_prep_C(o_all, np.asarray(inp["L0_attn_x"]), np.asarray(inp["L0_attn_wo"]),
                        np.asarray(inp["L0_ffn_router_logits"]), np.asarray(inp["L0_ffn_gating_einsum"]),
                        np.asarray(inp["L0_ffn_linear"]), np.asarray(inp["L0_ffn_per_expert_scale"]),
                        np.asarray(inp["L0_ffn_router_scale"]))
    resC = run_bass_kernel_spmd(build_launchC(), mapsC, core_ids=list(range(NCORE))).results
    x2_all = [np.asarray(resC[c]["x2"]) for c in range(NCORE)]
    mapsD = host_prep_D(x2_all, np.asarray(inp["L1_attn_Wq"]), np.asarray(inp["L1_attn_Wk"]),
                        np.asarray(inp["L1_attn_Wv"]), np.asarray(inp["L1_attn_Wg"]),
                        np.asarray(inp["L1_attn_Wb"]), np.asarray(inp["L1_attn_Wa"]),
                        np.asarray(inp["L1_attn_A_log"]), np.asarray(inp["L1_attn_dt_bias"]),
                        np.asarray(inp["L1_attn_conv_q_w"]), np.asarray(inp["L1_attn_conv_k_w"]),
                        np.asarray(inp["L1_attn_conv_v_w"]))
    resD = run_bass_kernel_spmd(build_launchD(), mapsD, core_ids=list(range(NCORE))).results
    mapsE = host_prep_E([np.asarray(resD[c]["qn"]) for c in range(NCORE)],
                        [np.asarray(resD[c]["kn"]) for c in range(NCORE)],
                        [np.asarray(resD[c]["vv"]) for c in range(NCORE)],
                        [np.asarray(resD[c]["gate"]) for c in range(NCORE)],
                        [np.asarray(resD[c]["beta"]) for c in range(NCORE)],
                        [np.asarray(resD[c]["lc"]) for c in range(NCORE)],
                        np.asarray(inp["L1_attn_norm_w"]))
    resE = run_bass_kernel_spmd(build_launchE(), mapsE, core_ids=list(range(NCORE))).results
    mapsF = host_prep_F([np.asarray(resE[c]["of"]) for c in range(NCORE)], x2_all,
                        np.asarray(inp["L1_attn_Wo"]), np.asarray(inp["L1_ffn_router_w"]),
                        np.asarray(inp["L1_ffn_w_gate"]), np.asarray(inp["L1_ffn_w_up"]),
                        np.asarray(inp["L1_ffn_w_down"]), np.asarray(inp["L1_ffn_sw_gate"]),
                        np.asarray(inp["L1_ffn_sw_up"]), np.asarray(inp["L1_ffn_sw_down"]),
                        np.asarray(inp["L1_ffn_shared_gate_w"]))
    resF = run_bass_kernel_spmd(build_launchF(), mapsF, core_ids=list(range(NCORE))).results
    out = np.zeros((B, S, D), np.float32)
    for core in range(NCORE):
        b, q = core // 4, core % 4
        out[b, q * TQ:(q + 1) * TQ] = np.asarray(resF[core]["out"]).reshape(D, TQ).T
    return out
